# version 05
# baseline (speedup 1.0000x reference)
import numpy as np
import ml_dtypes
import concourse.bass as bass
import concourse.mybir as mybir
from concourse.bass_utils import run_bass_kernel_spmd
from contextlib import ExitStack

F32 = mybir.dt.float32
BF16 = mybir.dt.bfloat16
AF = mybir.ActivationFunctionType
ALU = mybir.AluOpType
AX = mybir.AxisListType

T, NT, C, NC = 2048, 16, 1024, 8
H, DH, G, E, FF, NF = 16, 64, 204, 8, 4096, 32
LTOK = T - G
ENGINES = ("sync", "gpsimd", "tensor", "vector", "scalar")
KB = 512


class Build:
    def __init__(self, nc):
        self.nc = nc
        self.q = {e: [] for e in ENGINES}
        self.val = {e: 0 for e in ENGINES}
        self.step = {"sync": 16, "gpsimd": 16, "tensor": 1, "vector": 1, "scalar": 1}
        self.sem = {}

    def wait(self, eng, tok):
        if tok is not None:
            self.q[eng].append(("w", tok[0], tok[1]))

    def waits(self, eng, toks):
        best = {}
        for t in toks:
            if t is None:
                continue
            best[t[0]] = max(best.get(t[0], 0), t[1])
        for s, v in best.items():
            self.q[eng].append(("w", s, v))

    def do(self, eng, fn, tok=False):
        if eng in ("vector", "scalar"):
            self.q[eng].append(("w", eng, self.val[eng]))
            self.val[eng] += self.step[eng]
            self.q[eng].append(("i", fn, self.step[eng]))
            return (eng, self.val[eng])
        if tok or eng in ("sync", "gpsimd"):
            self.q[eng].append(("w", eng, self.val[eng]))
            self.val[eng] += self.step[eng]
            self.q[eng].append(("i", fn, self.step[eng]))
            return (eng, self.val[eng])
        self.q[eng].append(("e", fn))
        return None

    def tok(self, eng):
        return (eng, self.val[eng])

    def mark(self, eng):
        for i in range(len(self.q[eng]) - 1, -1, -1):
            item = self.q[eng][i]
            if item[0] == "w":
                continue
            if item[0] == "i":
                return (eng, self._val_at(eng, i))
            self.val[eng] += self.step[eng]
            self.q[eng][i] = ("i", item[1], self.step[eng])
            return (eng, self.val[eng])
        raise RuntimeError("mark: no instruction on " + eng)

    def _val_at(self, eng, idx):
        v = 0
        for item in self.q[eng][:idx + 1]:
            if item[0] == "i":
                v += item[2]
        return v

    def emit(self, block):
        sem = self.sem
        for eng in ENGINES:
            queue = self.q[eng]

            def body(e, queue=queue):
                for item in queue:
                    if item[0] == "w":
                        e.wait_ge(sem[item[1]], item[2])
                    elif item[0] == "e":
                        item[1](e)
                    else:
                        item[1](e).then_inc(sem[eng], item[2])
            getattr(block, eng)(body)


def build_program(debug=(), stop=99):
    dbg = set(debug)
    p0cut = next((int(d[5:]) for d in dbg if d.startswith("p0cut")), 99)
    nc = bass.Bass("TRN2", target_bir_lowering=False, detect_race_conditions=False)
    D = {}

    def din(name, shape, dt=F32):
        D[name] = nc.dram_tensor(name, shape, dt, kind="ExternalInput").ap()
        return D[name]

    x_d = din("x", [T, C])
    qkvwT_d = din("qkvwT", [C, 3 * C])
    projT_d = din("projT", [C, C])
    pb_d = din("pb", [1, C])
    qnw_d = din("d_qnw", [128, DH])
    knw_d = din("d_knw", [128, DH])
    rhs_d = din("rhs", [C, C])
    wq_d = din("wq", [C, C])
    wk_d = din("wk", [C, C])
    wv_d = din("wv", [C, C])
    wo_d = din("wo", [C, C])
    gw_d = din("gw", [C, E])
    vwr_d = din("vwr", [8, 128, NC * 128])
    vwr1_d = din("vwr1", [8, 128, NC * 128])
    w1_d = din("w1", [E, NF, 128, NC * 128])
    b1c_d = din("d_b1c", [128, E * NF])
    w2_d = din("w2", [E, FF, C])
    b2_d = din("b2", [E, C])
    cos_d = din("cos", [128, NT * 32])
    sin_d = din("sin", [128, NT * 32])
    ident_d = din("d_ident", [128, 128])
    identb_d = din("d_identb", [128, 128], BF16)
    onesrow_d = din("d_onesrow", [1, 128])
    onesrowb_d = din("d_onesrowb", [1, 128], BF16)
    onescolb_d = din("d_onescolb", [128, 1], BF16)
    selallb_d = din("d_selallb", [16, NC * 128], BF16)
    selj_d = din("d_selj", [8, NC * 128])
    cvec_d = din("d_cvec", [128, 8])
    xs_d = nc.dram_tensor("xs", [T, C], F32, kind="Internal").ap()
    ao_d = nc.dram_tensor("ao", [NC, 128, T], BF16, kind="Internal").ap()
    out_d = nc.dram_tensor("out", [T, C], F32, kind="ExternalOutput").ap()
    D["out"] = out_d

    es = ExitStack()

    def sb(name, shape, dt):
        return es.enter_context(nc.sbuf_tensor(name, shape, dt))

    def psum(name, shape, dt=F32):
        return es.enter_context(nc.psum_tensor(name, shape, dt))

    B = Build(nc)

    ident = sb("ident", [128, 128], F32)
    identb = sb("identb", [128, 128], BF16)
    onesrow = sb("onesrow", [1, 128], F32)
    onesrowb = sb("onesrowb", [1, 128], BF16)
    onescolb = sb("onescolb", [128, 1], BF16)
    cvec = sb("cvec", [128, 8], F32)
    r127 = sb("r127", [128, 1], F32)
    selallb = sb("selallb", [16, NC * 128], BF16)
    qnw = sb("qnw", [128, DH], F32)
    knw = sb("knw", [128, DH], F32)
    pb_sb = sb("pb_sb", [1, C], BF16)
    b2row = sb("b2row", [1, C], BF16)
    b1c = sb("b1c", [128, E * NF], F32)
    gsml = sb("gsml", [128, NC * E], F32)
    ghi = sb("ghi", [128, NC * E], BF16)
    glo = sb("glo", [128, NC * E], BF16)
    wte = sb("wte", [128, NT * E], F32)
    stats = sb("stats", [128, 16], F32)
    colsA = sb("colsA", [128, 8], F32)
    colsB = sb("colsB", [128, 16], F32)
    sl_all = sb("sl_all", [128, NT], F32)
    isl_all = sb("isl_all", [128, NT], F32)
    ssq_q = sb("ssq_q", [128, NT * H], F32)
    ssq_k = sb("ssq_k", [128, NT * H], F32)
    negc = sb("negc", [128, NT * H], F32)
    zall = sb("zall", [128, NT * H], F32)
    zpart = sb("zpart", [128, 8], F32)
    m2b = sb("m2b", [128, H], F32)
    s8 = sb("s8", [8, 128], F32)
    sr_cols = sb("sr_cols", [128, NC], F32)
    agT0 = sb("agT0", [128, G], BF16)
    agT1 = sb("agT1", [76, G], BF16)
    kv2 = [sb(f"kv2_{i}", [128, 128], BF16) for i in range(2)]
    iS8 = sb("iS8", [128, NC], F32)
    zgbuf = sb("zgbuf", [128, 24], F32)
    zs1 = sb("zs1", [128, 8], F32)
    zs2 = sb("zs2", [128, 8], F32)
    aoev = sb("aoev", [128, 512], BF16)
    tb4 = [sb(f"tb4_{i}", [128, C], BF16) for i in range(4)]

    AR = sb("AR", [128, 176 * KB], BF16)

    def zbv(kb_off, kb_len):
        return AR[:, kb_off * KB:(kb_off + kb_len) * KB]

    def zfv(kb_off, kb_len):
        return AR[:, kb_off * KB:(kb_off + kb_len) * KB].bitcast(F32)

    xnT = zbv(0, 32)
    attn_s = zbv(0, 16)
    attnT_s = zbv(16, 8)
    izTb = zbv(160, 4)[0:16, 0:T]
    izbb = zbv(164, 4)
    qxT = zbv(32, 32)
    qsT = zbv(32, 32)
    loT = zbv(32, 32)
    qrT = zbv(32, 32)
    qrhs = zbv(64, 16)
    ksT = zbv(64, 32)
    ekl = zbv(64, 30)
    x_sbf = zfv(32, 64)
    v_sb = zbv(96, 32)
    projw = zbv(96, 16)
    woutw = zbv(96, 16)
    aold = [zbv(112, 2), zbv(114, 2)]
    hT = zbv(96, 32)
    rhsf = zfv(128, 32)
    xt = [zfv(128, 4), zfv(132, 4)]
    xn_s = [zfv(136, 4), zfv(140, 4)]
    rr0 = zfv(144, 4)
    rr = zfv(148, 4)
    rt = zfv(152, 4)
    spare4 = zfv(156, 4)
    qf = zfv(144, 4)
    kf = zfv(148, 4)
    sq_s = zfv(152, 4)
    ropeA = zfv(156, 2)
    ropeB = zfv(158, 2)
    cos_sb = zfv(128, 2)
    sin_sb = zfv(130, 2)
    p0rr0 = zfv(32, 4)
    p0rr = zfv(36, 4)
    p0rt = zfv(40, 4)
    sr_b = zfv(160, 4)
    isr_b = zfv(164, 4)
    selj = zfv(168, 4)[0:8, 0:1024]
    wqk = zfv(128, 32)
    wqkb = zbv(128, 32)
    vwstr = [zbv(160, 2), zbv(162, 2)]
    p3qf = zfv(96, 4)
    p3kf = zfv(100, 4)
    p3sq = zfv(104, 4)
    wstr = [zbv(160, 6), zbv(166, 6)]
    l1w = zbv(144, 16)
    l1qf = zfv(132, 4)
    ropeA2 = zfv(136, 2)
    ropeB2 = zfv(138, 2)
    cosv = zfv(128, 2)
    sinv = zfv(130, 2)
    xt10 = zfv(132, 4)
    vw1str = [zbv(140, 2), zbv(142, 2)]
    kgT = zbv(172, 4)
    w1b = [zbv(160, 2), zbv(162, 2)]
    w2b = [zbv(164, 2), zbv(166, 2)]
    w1b4 = [zbv(128, 2), zbv(130, 2), zbv(132, 2), zbv(134, 2)]
    w2b4 = [zbv(136, 2), zbv(138, 2), zbv(140, 2), zbv(142, 2)]

    PBK = [psum(f"pbk{i}", [128, 512], F32) for i in range(8)]
    PTB = [PBK[6][:].bitcast(BF16), PBK[7][:].bitcast(BF16)]

    left = nc.sbuf_bytes_remaining
    assert left >= 0, f"SBUF over budget by {-left} bytes/partition"

    for s in ENGINES:
        B.sem[s] = es.enter_context(nc.semaphore(f"sem_{s}"))

    dbg_out = {}

    def add_dbg(name, shape, dt=F32):
        if name in dbg:
            dbg_out[name] = nc.dram_tensor("dbg_" + name, shape, dt, kind="ExternalOutput").ap()

    add_dbg("xn", [T, C]); add_dbg("qx", [T, C], BF16); add_dbg("sl", [128, NT])
    add_dbg("srb", [128, C]); add_dbg("qrhs", [C, C], BF16); add_dbg("xffn", [T, C])
    add_dbg("qs", [128, NC * T], BF16); add_dbg("ks", [128, NC * T], BF16)
    add_dbg("v0", [128, NT * C], BF16)
    add_dbg("negc", [128, NT * H]); add_dbg("ao", [NC, 128, T], BF16)
    add_dbg("zall", [128, NT * H]); add_dbg("x1", [T, C]); add_dbg("wte", [128, NT * E])
    add_dbg("qrT", [128, NC * T], BF16); add_dbg("ekl", [128, 15 * C], BF16)
    add_dbg("x2", [T, C]); add_dbg("xnT", [128, NC * T], BF16)
    add_dbg("kgT", [128, NC * G], BF16); add_dbg("iS8", [128, NC])
    add_dbg("kv2", [2, 128, 128], BF16); add_dbg("hT0", [128, NF * 512], BF16)
    add_dbg("srfranspose", [128, NC])
    add_dbg("src", [128, NC]); add_dbg("s8d", [8, 128]); add_dbg("rhsfd", [128, NC * C])
    add_dbg("ao1", [NC, 128, T], BF16); add_dbg("attn00", [128, 4 * T], BF16)
    add_dbg("aoev0", [128, 512], BF16)
    add_dbg("glog", [128, NT * E]); add_dbg("xnT1", [128, NC * T], BF16); add_dbg("loT1", [128, NC * T], BF16)

    ln_eps_col = cvec[:, 0:1]
    rms_eps_col = cvec[:, 1:2]

    def chT(tens, cc, c0, n, p0=0, np_=128):
        return tens[p0:p0 + np_, cc * T + c0: cc * T + c0 + n]

    def tp(e, out_ap, in_ap, idn, first, last):
        return e.matmul(out_ap, in_ap, idn, start=first, stop=last, is_transpose=True)

    ld0 = []
    for dst, src in [(ident, ident_d), (identb, identb_d), (onesrow, onesrow_d),
                     (onesrowb, onesrowb_d), (onescolb, onescolb_d), (cvec, cvec_d),
                     (selallb, selallb_d), (qnw, qnw_d), (knw, knw_d), (b1c, b1c_d)]:
        ld0.append(B.do("sync", lambda e, d=dst, s=src: e.dma_start(out=d[:], in_=s[:]), tok=True))
    t_consts = B.tok("sync")
    t_selj = B.do("sync", lambda e: e.dma_start(out=selj, in_=selj_d[:]), tok=True)
    t_rhs = B.do("sync", lambda e: e.dma_start(
        out=rhsf.rearrange("p (cc c) -> p cc c", cc=NC),
        in_=rhs_d.rearrange("(cc p) c -> p cc c", p=128)), tok=True)
    t_gw = B.do("sync", lambda e: e.dma_start(
        out=gsml[:].rearrange("p (cc c) -> p cc c", cc=NC),
        in_=gw_d.rearrange("(cc p) c -> p cc c", p=128)), tok=True)
    t_pb = B.do("gpsimd", lambda e: e.dma_start(out=pb_sb[:], in_=pb_d[:]), tok=True)

    if p0cut >= 2:
        B.waits("vector", [t_consts])
        t_r127 = B.do("vector", lambda e: e.reciprocal(r127[:], cvec[:, 2:3]), tok=True)
        B.wait("vector", t_r127)
        B.waits("vector", [t_gw])
        t_ghi = B.do("vector", lambda e: e.tensor_copy(ghi[:], gsml[:]), tok=True)
        B.wait("vector", t_ghi)
        t_glo = B.do("vector", lambda e: e.tensor_tensor(glo[:], gsml[:], ghi[:], ALU.subtract), tok=True)

    B.waits("tensor", [t_rhs, t_consts])
    for jc in range(NC if p0cut >= 3 else 0):
        for cc in range(NC):
            bank, off = PBK[cc // 4], (cc % 4) * 128
            B.do("tensor", lambda e, b=bank, o=off, c=cc, j=jc: tp(e,
                b[:, o:o + 128], rhsf[:, c * C + j * 128:c * C + (j + 1) * 128], ident[:],
                c % 4 == 0, c % 4 == 3))
        tpe = B.mark("tensor")
        B.waits("vector", [tpe])
        B.do("vector", lambda e: e.tensor_copy(p0rr[:, 0:512], PBK[0][:]))
        tev = B.do("vector", lambda e: e.tensor_copy(p0rt[:, 0:512], PBK[1][:]), tok=True)
        B.waits("tensor", [tev])
        B.do("vector", lambda e: e.reduce_max(colsA[:, 0:1], p0rr[:, 0:512], AX.X, apply_absolute_value=True))
        B.do("vector", lambda e: e.reduce_max(colsA[:, 1:2], p0rt[:, 0:512], AX.X, apply_absolute_value=True))
        B.do("vector", lambda e, j=jc: e.tensor_tensor(
            sr_cols[:, j:j + 1], colsA[:, 0:1], colsA[:, 1:2], ALU.max))
    if p0cut >= 4:
        tsr = B.do("vector", lambda e: e.tensor_scalar(
            sr_cols[:], sr_cols[:], 1e-12, r127[:], ALU.max, ALU.mult), tok=True)
        B.waits("tensor", [tsr])
        tpe = B.do("tensor", lambda e: e.transpose(PBK[2][0:8, 0:128], sr_cols[:], ident[:]), tok=True)
        B.waits("vector", [tpe])
        tev = B.do("vector", lambda e: e.tensor_copy(s8[:], PBK[2][0:8, 0:128]), tok=True)
        B.waits("tensor", [tev, t_selj])
        for jc in range(NC):
            bank = PBK[3 + (jc % 2)]
            tpe = B.do("tensor", lambda e, b=bank, j=jc: e.matmul(
                b[:, 0:128], selj[0:8, j * 128:(j + 1) * 128], s8[:], start=True, stop=True), tok=True)
            B.waits("vector", [tpe])
            tev = B.do("vector", lambda e, b=bank, j=jc: e.tensor_copy(
                sr_b[:, j * 128:(j + 1) * 128], b[:, 0:128]), tok=True)
            B.waits("tensor", [tev])
        t_isr = B.do("vector", lambda e: e.reciprocal(isr_b, sr_b), tok=True)
        B.wait("vector", t_isr)

    def emit_round(dst_ap, R0=None, R1=None, R2=None):
        R0 = R0 if R0 is not None else rr0
        R1 = R1 if R1 is not None else rr
        R2 = R2 if R2 is not None else rt
        B.do("vector", lambda e: e.tensor_scalar(R0, R0, 127.25, -127.25, ALU.min, ALU.max))
        B.do("vector", lambda e: e.tensor_scalar(R0, R0, 128.5, None, ALU.add))
        B.do("vector", lambda e: e.tensor_scalar(R2, R0, 128.0, 128.0, ALU.is_ge, ALU.mult))
        B.do("vector", lambda e: e.tensor_tensor(R1, R0, R2, ALU.subtract))
        for b in (64.0, 32.0, 16.0, 8.0, 4.0, 2.0, 1.0):
            B.do("vector", lambda e, b=b: e.tensor_scalar(R2, R1, b, b, ALU.is_ge, ALU.mult))
            B.do("vector", lambda e: e.tensor_tensor(R1, R1, R2, ALU.subtract))
        B.do("vector", lambda e: e.tensor_tensor(R2, R0, R1, ALU.subtract))
        return B.do("vector", lambda e: e.tensor_scalar(dst_ap, R2, 128.0, None, ALU.subtract), tok=True)

    for cc in range(NC if p0cut >= 5 else 0):
        B.do("vector", lambda e, c=cc: e.tensor_tensor(
            p0rr0, rhsf[:, c * C:(c + 1) * C], isr_b, ALU.mult))
        emit_round(qrhs[:, cc * C:(cc + 1) * C], p0rr0, p0rr, p0rt)
    t_qrhs = B.tok("vector")

    if "srb" in dbg:
        B.waits("sync", [t_isr])
        B.do("sync", lambda e: e.dma_start(out=dbg_out["srb"], in_=sr_b))
    if "src" in dbg:
        B.waits("sync", [t_isr])
        B.do("sync", lambda e: e.dma_start(out=dbg_out["src"], in_=sr_cols[:]))
    if "s8d" in dbg:
        B.waits("sync", [t_isr])
        B.do("sync", lambda e: e.dma_start(out=dbg_out["s8d"], in_=s8[:]))
    if "rhsfd" in dbg:
        B.waits("sync", [t_rhs])
        B.do("sync", lambda e: e.dma_start(out=dbg_out["rhsfd"], in_=rhsf))
    if "qrhs" in dbg:
        B.waits("sync", [t_qrhs])
        B.do("sync", lambda e: e.dma_start(
            out=dbg_out["qrhs"].rearrange("(cc p) c -> p cc c", p=128),
            in_=qrhs[:].rearrange("p (cc c) -> p cc c", cc=NC)))

    def emit_ln(xbuf, xn_out, dep_toks, cb, prev_tsq):
        B.waits("vector", dep_toks + ([prev_tsq] if prev_tsq else []))
        B.do("vector", lambda e: e.bn_stats(stats[:, 0:6], xbuf[:, 0:512]))
        B.do("vector", lambda e: e.bn_stats(stats[:, 6:12], xbuf[:, 512:1024]))
        tagg = B.do("vector", lambda e, cb=cb: e.bn_aggr(colsA[:, cb:cb + 2], stats[:, 0:12]), tok=True)
        B.waits("scalar", [tagg])
        tsq = B.do("scalar", lambda e, cb=cb: e.activation(
            colsA[:, cb + 2:cb + 3], colsA[:, cb + 1:cb + 2], AF.Sqrt,
            bias=ln_eps_col, scale=1.0), tok=True)
        B.waits("vector", [tsq])
        trc = B.do("vector", lambda e, cb=cb: e.reciprocal(
            colsA[:, cb + 3:cb + 4], colsA[:, cb + 2:cb + 3]), tok=True)
        B.wait("vector", trc)
        tn = B.do("vector", lambda e, cb=cb: e.tensor_scalar(
            xn_out, xbuf, colsA[:, cb:cb + 1], colsA[:, cb + 3:cb + 4],
            ALU.subtract, ALU.mult), tok=True)
        return tn, tsq

    if stop >= 1:
        prev_tsq = None
        ev1_tok = {}
        tp0s, tp1s = {}, {}
        t_xn_toks = [None] * NT
        for tc in range(NT):
            bi = tc % 2
            cb = bi * 4
            B.waits("sync", [t_qrhs] + ([t_xn_toks[tc - 2]] if tc >= 2 else []))
            t_xld = B.do("sync", lambda e, t=tc, b=bi: e.dma_start(
                out=xt[b], in_=x_d[t * 128:(t + 1) * 128, :]), tok=True)
            t_xn, prev_tsq = emit_ln(xt[bi], xn_s[bi], [t_xld], cb, prev_tsq)
            t_xn_toks[tc] = t_xn
            B.do("vector", lambda e, b=bi: e.reduce_max(
                colsB[:, 0:1], xn_s[b], AX.X, apply_absolute_value=True))
            tsl = B.do("vector", lambda e, t=tc: e.tensor_scalar(
                sl_all[:, t:t + 1], colsB[:, 0:1], 1e-12, r127[:], ALU.max, ALU.mult), tok=True)
            B.wait("vector", tsl)
            tisl = B.do("vector", lambda e, t=tc: e.reciprocal(
                isl_all[:, t:t + 1], sl_all[:, t:t + 1]), tok=True)
            B.wait("vector", tisl)
            B.waits("vector", [tp1s.get(tc - 2)])
            B.do("vector", lambda e, b=bi, t=tc: e.tensor_scalar(
                rr0, xn_s[b], isl_all[:, t:t + 1], None, ALU.mult))
            tq = emit_round(tb4[2 + bi][:])
            B.waits("scalar", [t_xn, tp0s.get(tc - 2)])
            txb = B.do("scalar", lambda e, b=bi: e.copy(tb4[b][:], xn_s[b]), tok=True)
            B.waits("tensor", [txb, tq, t_consts, ev1_tok.get(tc - 1)])
            for cc in range(NC):
                B.do("tensor", lambda e, c=cc, b=bi: tp(e,
                    PTB[0][:, c * 128:(c + 1) * 128], tb4[b][:, c * 128:(c + 1) * 128], identb[:],
                    c == 0, c == NC - 1))
            tp0s[tc] = B.mark("tensor")
            for cc in range(NC):
                B.do("tensor", lambda e, c=cc, b=bi: tp(e,
                    PTB[1][:, c * 128:(c + 1) * 128], tb4[2 + b][:, c * 128:(c + 1) * 128], identb[:],
                    c == 0, c == NC - 1))
            tp1s[tc] = B.mark("tensor")
            B.waits("vector", [tp0s[tc], tp1s[tc]])
            B.do("vector", lambda e, t=tc: e.tensor_copy(
                xnT.rearrange("p (cc tt) -> p cc tt", cc=NC)[:, :, t * 128:(t + 1) * 128],
                PTB[0][:].rearrange("p (cc tt) -> p cc tt", cc=NC)))
            ev1_tok[tc] = B.do("vector", lambda e, t=tc: e.tensor_copy(
                qxT.rearrange("p (cc tt) -> p cc tt", cc=NC)[:, :, t * 128:(t + 1) * 128],
                PTB[1][:].rearrange("p (cc tt) -> p cc tt", cc=NC)), tok=True)
            if "xn" in dbg:
                B.waits("sync", [t_xn])
                B.do("sync", lambda e, t=tc, b=bi: e.dma_start(
                    out=dbg_out["xn"][t * 128:(t + 1) * 128, :], in_=xn_s[b]))
            if "qx" in dbg:
                B.waits("sync", [tq])
                B.do("sync", lambda e, t=tc, b=bi: e.dma_start(
                    out=dbg_out["qx"][t * 128:(t + 1) * 128, :], in_=tb4[2 + b][:]))
        if "sl" in dbg:
            B.waits("sync", [B.tok("vector")])
            B.do("sync", lambda e: e.dma_start(out=dbg_out["sl"], in_=sl_all[:]))
        if "xnT" in dbg:
            B.waits("sync", [B.tok("vector")])
            B.do("sync", lambda e: e.dma_start(out=dbg_out["xnT"], in_=xnT))

    if stop >= 2:
        prev_ffn = {}
        for tc in range(NT):
            bb = (tc % 2) * 2
            bi = tc % 2
            deps = [t_qrhs, ev1_tok[tc]]
            if tc >= 2:
                deps.append(prev_ffn[tc - 2][1])
            B.waits("tensor", deps)
            for cc in range(NC):
                for jb in range(2):
                    B.do("tensor", lambda e, c=cc, j=jb, b=bb, t=tc: e.matmul(
                        PBK[b + j][:], chT(qxT, c, t * 128, 128),
                        qrhs[:, c * C + j * 512:c * C + (j + 1) * 512],
                        start=(c == 0), stop=(c == NC - 1)))
            tmm = B.mark("tensor")
            B.waits("sync", [t_xn_toks[14 + bi]] + ([prev_ffn[tc - 2][0]] if tc >= 2 else []))
            t_xld = B.do("sync", lambda e, t=tc, b=bi: e.dma_start(
                out=xt[b], in_=x_d[t * 128:(t + 1) * 128, :]), tok=True)
            B.waits("vector", [tmm, t_xld])
            for jb in range(2):
                B.do("vector", lambda e, j=jb, b=bb, t=tc: e.scalar_tensor_tensor(
                    rt[:, j * 512:(j + 1) * 512], PBK[b + j][:], sl_all[:, t:t + 1],
                    sr_b[:, j * 512:(j + 1) * 512], ALU.mult, ALU.mult))
            tev = B.do("vector", lambda e, b=bi: e.tensor_tensor(
                xt[b], xt[b], rt, ALU.add), tok=True)
            B.waits("sync", [tev])
            tst = B.do("sync", lambda e, t=tc, b=bi: e.dma_start(
                out=xs_d[t * 128:(t + 1) * 128, :], in_=xt[b]), tok=True)
            prev_ffn[tc] = (tst, tev)
            if "xffn" in dbg:
                B.do("sync", lambda e, t=tc, b=bi: e.dma_start(
                    out=dbg_out["xffn"][t * 128:(t + 1) * 128, :], in_=xt[b]))
        t_ffn_done_s = B.tok("sync")
        t_ffn_done_v = B.tok("vector")

    if stop >= 3:
        def rms_block(src_f32, w_rep, dst_bf, extra_scale, dep_tok, sq_buf):
            B.waits("scalar", [dep_tok])
            tsq = B.do("scalar", lambda e: e.activation(sq_buf, src_f32, AF.Square,
                                                        bias=0.0, scale=1.0), tok=True)
            B.waits("vector", [tsq])
            trd = B.do("vector", lambda e: e.tensor_reduce(
                colsB[:, 0:16].rearrange("p (h o) -> p h o", o=1),
                sq_buf.rearrange("p (h d) -> p h d", h=H), AX.X, ALU.add), tok=True)
            B.waits("scalar", [trd])
            ts1 = B.do("scalar", lambda e: e.activation(
                colsA[:, 0:8], colsB[:, 0:8], AF.Sqrt, bias=rms_eps_col, scale=1.0 / DH), tok=True)
            ts2 = B.do("scalar", lambda e: e.activation(
                stats[:, 0:8], colsB[:, 8:16], AF.Sqrt, bias=rms_eps_col, scale=1.0 / DH), tok=True)
            B.waits("vector", [ts1, ts2])
            B.do("vector", lambda e: e.reciprocal(stats[:, 8:16], colsA[:, 0:8]))
            trc = B.do("vector", lambda e: e.reciprocal(colsB[:, 0:8], stats[:, 0:8]), tok=True)
            B.wait("vector", trc)
            if extra_scale != 1.0:
                B.do("vector", lambda e: e.tensor_scalar(
                    stats[:, 8:16], stats[:, 8:16], extra_scale, None, ALU.mult))
                tsc = B.do("vector", lambda e: e.tensor_scalar(
                    colsB[:, 0:8], colsB[:, 0:8], extra_scale, None, ALU.mult), tok=True)
                B.wait("vector", tsc)
            B.do("vector", lambda e: e.tensor_tensor(
                sq_buf.rearrange("p (h d) -> p h d", h=H), src_f32.rearrange("p (h d) -> p h d", h=H),
                w_rep[:].unsqueeze(1).broadcast_to([128, H, DH]),
                ALU.mult))
            B.do("vector", lambda e: e.tensor_tensor(
                dst_bf.rearrange("p (h d) -> p h d", h=H)[:, 0:8],
                sq_buf.rearrange("p (h d) -> p h d", h=H)[:, 0:8],
                stats[:, 8:16].unsqueeze(2).broadcast_to([128, 8, DH]),
                ALU.mult))
            return B.do("vector", lambda e: e.tensor_tensor(
                dst_bf.rearrange("p (h d) -> p h d", h=H)[:, 8:16],
                sq_buf.rearrange("p (h d) -> p h d", h=H)[:, 8:16],
                colsB[:, 0:8].unsqueeze(2).broadcast_to([128, 8, DH]),
                ALU.mult), tok=True)

        def ssq_block(src_bf, dst_cols, sq_buf, dep_tok):
            B.waits("scalar", [dep_tok])
            tsq = B.do("scalar", lambda e: e.activation(sq_buf, src_bf, AF.Square,
                                                        bias=0.0, scale=1.0), tok=True)
            B.waits("vector", [tsq])
            return B.do("vector", lambda e: e.tensor_reduce(
                dst_cols.rearrange("p (h o) -> p h o", o=1),
                sq_buf.rearrange("p (h d) -> p h d", h=H), AX.X, ALU.add), tok=True)

        B.waits("gpsimd", [t_ffn_done_v, t_ffn_done_s])
        t_wqk = B.do("gpsimd", lambda e: e.dma_start(
            out=wqkb[:].rearrange("p (cc c) -> p cc c", cc=NC),
            in_=qkvwT_d[:, 0:2 * C].rearrange("(cc p) c -> p cc c", p=128)), tok=True)

        prev_qkv = {}
        tqk_ev = {}
        for tc in range(NT):
            bi = tc % 2
            deps = [ev1_tok[NT - 1], t_ffn_done_v, t_wqk]
            if tc >= 2:
                deps.append(prev_qkv[tc - 2])
            B.waits("tensor", deps)
            for cc in range(NC):
                for part in range(2):
                    for jb in range(2):
                        B.do("tensor", lambda e, c=cc, p=part, j=jb, t=tc: e.matmul(
                            PBK[p * 2 + j][:], chT(xnT, c, t * 128, 128),
                            wqkb[:, c * 2 * C + p * C + j * 512:c * 2 * C + p * C + (j + 1) * 512],
                            start=(c == 0), stop=(c == NC - 1)))
            tmm = B.mark("tensor")
            B.waits("vector", [tmm])
            B.do("vector", lambda e: e.tensor_copy(p3qf[:, 0:512], PBK[0][:]))
            B.do("vector", lambda e: e.tensor_copy(p3qf[:, 512:1024], PBK[1][:]))
            B.do("vector", lambda e: e.tensor_copy(p3kf[:, 0:512], PBK[2][:]))
            tevv = B.do("vector", lambda e: e.tensor_copy(p3kf[:, 512:1024], PBK[3][:]), tok=True)
            prev_qkv[tc] = tevv

            tqs = rms_block(p3qf, qnw, tb4[2 + bi][:], 0.125, tevv, p3sq)
            tssq = ssq_block(tb4[2 + bi][:], ssq_q[:, tc * H:(tc + 1) * H], p3sq, tqs)
            tks = rms_block(p3kf, knw, tb4[bi][:], 1.0, tssq, p3sq)
            tssk = ssq_block(tb4[bi][:], ssq_k[:, tc * H:(tc + 1) * H], p3sq, tks)

            B.waits("tensor", [tqs, tks, tqk_ev.get(tc - 1)])
            for cc in range(NC):
                B.do("tensor", lambda e, c=cc, b=bi: tp(e,
                    PTB[0][:, c * 128:(c + 1) * 128], tb4[2 + b][:, c * 128:(c + 1) * 128], identb[:],
                    c == 0, c == NC - 1))
            tq0 = B.mark("tensor")
            for cc in range(NC):
                B.do("tensor", lambda e, c=cc, b=bi: tp(e,
                    PTB[1][:, c * 128:(c + 1) * 128], tb4[b][:, c * 128:(c + 1) * 128], identb[:],
                    c == 0, c == NC - 1))
            tq1 = B.mark("tensor")
            B.waits("vector", [tq0, tq1, tssk])
            B.do("vector", lambda e, t=tc: e.tensor_copy(
                qsT.rearrange("p (cc tt) -> p cc tt", cc=NC)[:, :, t * 128:(t + 1) * 128],
                PTB[0][:].rearrange("p (cc tt) -> p cc tt", cc=NC)))
            tqk_ev[tc] = B.do("vector", lambda e, t=tc: e.tensor_copy(
                ksT.rearrange("p (cc tt) -> p cc tt", cc=NC)[:, :, t * 128:(t + 1) * 128],
                PTB[1][:].rearrange("p (cc tt) -> p cc tt", cc=NC)), tok=True)
        t_p3a_v = B.tok("vector")

        t_vw = [None] * 8
        t_vw_used = {}
        B.waits("gpsimd", [t_ffn_done_v])
        for v0 in (0, 1):
            t_vw[v0] = B.do("gpsimd", lambda e, v=v0, w=v0: e.dma_start(
                out=vwstr[w][:], in_=vwr_d[v]), tok=True)
        prev_vev = {}
        for vd in range(8):
            wb = vd % 2
            for tb in range(4):
                key = (vd, tb)
                B.waits("tensor", [t_vw[vd], t_p3a_v if vd == 0 and tb == 0 else None,
                                   prev_vev.get((vd, tb - 2)) or prev_vev.get((vd - 1, 2 + tb))])
                bank = PBK[(vd * 4 + tb) % 2]
                for cc in range(NC):
                    B.do("tensor", lambda e, c=cc, v=vd, t=tb, w=wb, bk=bank: e.matmul(
                        bk[:], vwstr[w][:, c * 128:(c + 1) * 128],
                        chT(xnT, c, t * 512, 512),
                        start=(c == 0), stop=(c == NC - 1)))
                tmm = B.mark("tensor")
                if tb == 3:
                    t_vw_used[vd] = tmm
                B.waits("vector", [tmm])
                tev = B.do("vector", lambda e, bk=bank: e.tensor_copy(aoev[:], bk[:]), tok=True)
                B.waits("tensor", [tev])
                for q4 in range(4):
                    B.do("tensor", lambda e, q=q4: tp(e,
                        PTB[0][:, q * 128:(q + 1) * 128], aoev[:, q * 128:(q + 1) * 128],
                        identb[:], q == 0, q == 3))
                tpe = B.mark("tensor")
                B.waits("vector", [tpe])
                tev2 = B.do("vector", lambda e, v=vd, t=tb: e.tensor_copy(
                    v_sb[:].rearrange("p (tt c) -> p tt c", tt=NT)[:, t * 4:(t + 1) * 4,
                                                                  v * 128:(v + 1) * 128],
                    PTB[0][:, 0:512].rearrange("p (q c) -> p q c", q=4)), tok=True)
                B.waits("tensor", [tev2])
                prev_vev[key] = tev2
            if vd + 2 < 8:
                B.waits("gpsimd", [t_vw_used[vd]])
                t_vw[vd + 2] = B.do("gpsimd", lambda e, v=vd + 2, w=(vd % 2): e.dma_start(
                    out=vwstr[w][:], in_=vwr_d[v]), tok=True)
        t_p3_done = B.tok("vector")

        B.do("vector", lambda e: e.tensor_copy(stats[:, 0:16], ssq_k[:, 0:16]))
        tmax = None
        for tc in range(1, NT):
            tmax = B.do("vector", lambda e, t=tc: e.tensor_tensor(
                stats[:, 0:16], stats[:, 0:16], ssq_k[:, t * H:(t + 1) * H], ALU.max), tok=True)
        B.waits("tensor", [tmax])
        tpe = B.do("tensor", lambda e: e.transpose(PBK[0][0:16, 0:128], stats[:, 0:16], ident[:]), tok=True)
        B.waits("vector", [tpe])
        tev = B.do("vector", lambda e: e.tensor_copy(rt[0:16, 0:128], PBK[0][0:16, 0:128]), tok=True)
        B.wait("vector", tev)
        tred = B.do("vector", lambda e: e.reduce_max(rt[0:16, 128:129], rt[0:16, 0:128], AX.X), tok=True)
        B.waits("tensor", [tred])
        tpe = B.do("tensor", lambda e: e.transpose(
            PBK[1][0:1, 0:16], rt[0:16, 128:129], ident[0:16, 0:16]), tok=True)
        B.waits("vector", [tpe])
        tev = B.do("vector", lambda e: e.tensor_copy(rt[0:1, 256:272], PBK[1][0:1, 0:16]), tok=True)
        B.waits("tensor", [tev])
        tpe = B.do("tensor", lambda e: e.matmul(
            PBK[2][:, 0:16], onesrow[:], rt[0:1, 256:272], start=True, stop=True), tok=True)
        B.waits("vector", [tpe])
        tev = B.do("vector", lambda e: e.tensor_copy(m2b[:], PBK[2][:, 0:16]), tok=True)
        B.wait("vector", tev)
        tm2 = B.do("vector", lambda e: e.tensor_scalar(m2b[:], m2b[:], -0.0625, None, ALU.mult), tok=True)
        B.wait("vector", tm2)
        for tc in range(NT):
            B.do("vector", lambda e, t=tc: e.scalar_tensor_tensor(
                negc[:, t * H:(t + 1) * H], ssq_q[:, t * H:(t + 1) * H], -4.0, m2b[:],
                ALU.mult, ALU.add))
        t_negc = B.mark("vector")

        if "qs" in dbg:
            B.waits("sync", [B.tok("vector")])
            B.do("sync", lambda e: e.dma_start(out=dbg_out["qs"], in_=qsT))
        if "ks" in dbg:
            B.waits("sync", [B.tok("vector")])
            B.do("sync", lambda e: e.dma_start(out=dbg_out["ks"], in_=ksT))
        if "v0" in dbg:
            B.waits("sync", [B.tok("vector")])
            B.do("sync", lambda e: e.dma_start(out=dbg_out["v0"], in_=v_sb))
        if "negc" in dbg:
            B.waits("sync", [t_negc])
            B.do("sync", lambda e: e.dma_start(out=dbg_out["negc"], in_=negc[:]))

    if stop >= 4:
        t_aodma = {}
        texp_half = {}
        tz_done = {}
        t_trdone = {}
        last_av_ev = None
        for h in range(H):
            hc, off = h // 2, (h % 2) * 64
            avb = 4 + (h % 2)
            for ib in range(4):
                key = (h, ib)
                for ii in range(4):
                    ic = ib * 4 + ii
                    for jh in range(2):
                        g = jh * 2
                        B.waits("tensor", [t_negc, texp_half.get((h, ib, ii - 1, jh))
                                           if ii > 0 else texp_half.get((h, ib - 1, 3, jh))
                                           if ib > 0 else texp_half.get((h - 1, 3, 3, jh))])
                        if ii == 0 and jh == 0:
                            B.waits("scalar", [t_trdone.get((h, ib - 1)) if ib > 0
                                               else t_trdone.get((h - 1, 3))])
                        for jb in range(2):
                            B.do("tensor", lambda e, j=g + jb, i=ic, hc2=hc, o=off: e.matmul(
                                PBK[j][:],
                                chT(qsT, hc2, i * 128, 128, o, 64),
                                chT(ksT, hc2, j * 512, 512, o, 64),
                                start=True, stop=True))
                        tsm = B.mark("tensor")
                        B.waits("scalar", [tsm, tz_done.get((h, ic - 2))])
                        for jb in range(2):
                            B.do("scalar", lambda e, j=g + jb, i=ii, icc=ic, hh=h: e.activation(
                                attn_s[:, i * T + j * 512:i * T + (j + 1) * 512], PBK[j][:],
                                AF.Exp, bias=negc[:, icc * H + hh:icc * H + hh + 1], scale=1.0,
                                accum_out=zpart[:, (icc % 2) * 4 + j:(icc % 2) * 4 + j + 1]))
                        texp_half[(h, ib, ii, jh)] = B.mark("scalar")
                    zb_ = (ic % 2) * 4
                    B.waits("vector", [texp_half[(h, ib, ii, 0)], texp_half[(h, ib, ii, 1)]])
                    B.do("vector", lambda e, z=zb_: e.tensor_tensor(
                        colsA[:, 0:1], zpart[:, z:z + 1], zpart[:, z + 1:z + 2], ALU.add))
                    B.do("vector", lambda e, z=zb_: e.tensor_tensor(
                        colsA[:, 1:2], zpart[:, z + 2:z + 3], zpart[:, z + 3:z + 4], ALU.add))
                    tz_done[(h, ic)] = B.do("vector", lambda e, icc=ic, hh=h: e.tensor_tensor(
                        zall[:, icc * H + hh:icc * H + hh + 1], colsA[:, 0:1], colsA[:, 1:2],
                        ALU.add), tok=True)
                texp_all = texp_half[(h, ib, 3, 1)]
                if h == 0 and ib == 0 and "attn00" in dbg:
                    B.waits("sync", [texp_all, tz_done[(0, 3)]])
                    tdmp = B.do("sync", lambda e: e.dma_start(out=dbg_out["attn00"], in_=attn_s), tok=True)
                    B.waits("tensor", [tdmp])
                    B.waits("scalar", [tdmp])
                    B.waits("vector", [tdmp])
                t_tr_ev = {}
                t_avmm = {}
                for jc in range(16):
                    slot = jc % 8
                    B.waits("tensor", [texp_all, t_tr_ev.get(jc - 8)])
                    for ii in range(4):
                        B.do("tensor", lambda e, i=ii, j=jc: tp(e,
                            PTB[j % 2][:, i * 128:(i + 1) * 128],
                            attn_s[:, i * T + j * 128:i * T + (j + 1) * 128], identb[:],
                            i == 0, i == 3))
                    tpe = B.mark("tensor")
                    B.waits("vector", [tpe, t_avmm.get(jc - 8)])
                    tev = B.do("vector", lambda e, j=jc, s=slot: e.tensor_copy(
                        attnT_s[:, s * 512:(s + 1) * 512], PTB[j % 2][:, 0:512]), tok=True)
                    t_tr_ev[jc] = tev
                    B.waits("tensor", [tev] + ([last_av_ev] if jc == 0 else []))
                    t_avmm[jc] = B.do("tensor", lambda e, j=jc, s=slot, hh=h, o=off, b=avb: e.matmul(
                        PBK[b][o:o + 64, :], v_sb[:, j * C + hh * DH:j * C + (hh + 1) * DH],
                        attnT_s[:, s * 512:(s + 1) * 512],
                        start=(j == 0), stop=(j == 15),
                        **({"tile_position": (0, 64)} if o else {})), tok=True)
                tavd = B.mark("tensor")
                t_trdone[key] = tavd
                B.waits("vector", [tavd, t_aodma.get((h, ib - 1)) if ib > 0 else t_aodma.get((h - 1, 3))])
                tev = B.do("vector", lambda e, o=off, b=avb: e.tensor_copy(
                    aoev[o:o + 64, :], PBK[b][o:o + 64, :]), tok=True)
                last_av_ev = tev
                B.waits("sync", [tev])
                t_aodma[key] = B.do("sync", lambda e, hcc=hc, o=off, i=ib: e.dma_start(
                    out=ao_d[hcc, o:o + 64, i * 512:(i + 1) * 512], in_=aoev[o:o + 64, :]), tok=True)

        tiz1 = B.do("vector", lambda e: e.reciprocal(zall[:, 0:128], zall[:, 0:128]), tok=True)
        tiz = B.do("vector", lambda e: e.reciprocal(zall[:, 128:256], zall[:, 128:256]), tok=True)
        B.wait("vector", tiz)
        t_izld = {}
        for ic in range(NT):
            B.waits("tensor", [tiz] + ([t_izld.get(ic - 1)] if ic else []))
            tpe = B.do("tensor", lambda e, i=ic: e.transpose(
                PBK[0][0:16, 0:128], zall[:, i * 16:(i + 1) * 16], ident[:]), tok=True)
            B.waits("vector", [tpe])
            t_izld[ic] = B.do("vector", lambda e, i=ic: e.tensor_copy(
                izTb[:, i * 128:(i + 1) * 128], PBK[0][0:16, 0:128]), tok=True)
        t_izT = B.tok("vector")

        if "zall" in dbg:
            B.waits("sync", [tiz])
            B.do("sync", lambda e: e.dma_start(out=dbg_out["zall"], in_=zall[:]))
        if "ao" in dbg:
            B.waits("sync", [t_aodma[(H - 1, 3)]])
            B.do("sync", lambda e: e.dma_start(out=dbg_out["ao"], in_=ao_d))

    if stop >= 5:
        B.waits("gpsimd", [B.tok("tensor")])
        t_projw = B.do("gpsimd", lambda e: e.dma_start(
            out=projw[:].rearrange("p (cc c) -> p cc c", cc=NC),
            in_=projT_d.rearrange("(cc p) c -> p cc c", p=128)), tok=True)
        prev_p5 = {}
        t_hi_ev = {}
        t_lo_used = {}
        t_lnq5 = {}
        t_izb_used = None
        t_xn1_toks = [None] * NT
        for tc in range(NT):
            bi = tc % 2
            cb = bi * 4
            B.waits("sync", [t_aodma[(H - 1, 3)]] + ([prev_p5[tc - 2][0]] if tc >= 2 else []))
            tld = B.do("sync", lambda e, t=tc, b=bi: e.dma_start(
                out=aold[b][:].rearrange("p (hc t) -> p hc t", hc=NC),
                in_=ao_d[:, :, t * 128:(t + 1) * 128].rearrange("hc p t -> p hc t")), tok=True)
            B.waits("tensor", [t_izT, tld] + ([t_izb_used] if t_izb_used else []))
            for hcc in range(NC):
                B.do("tensor", lambda e, hc2=hcc, t=tc: e.matmul(
                    PBK[5][:, 0:128], selallb[:, hc2 * 128:(hc2 + 1) * 128],
                    izTb[:, t * 128:(t + 1) * 128], start=True, stop=True))
                tpe = B.mark("tensor")
                B.waits("vector", [tpe])
                tev = B.do("vector", lambda e, hc2=hcc: e.tensor_copy(
                    izbb[:, hc2 * 128:(hc2 + 1) * 128], PBK[5][:, 0:128]), tok=True)
                B.waits("tensor", [tev])
            tsc = B.do("vector", lambda e, b=bi: e.tensor_tensor(
                aold[b][:], aold[b][:], izbb[:, 0:1024], ALU.mult), tok=True)
            t_izb_used = tsc
            B.waits("tensor", [tsc, t_projw, t_pb]
                    + ([prev_p5[tc - 2][1]] if tc >= 2 else []))
            for jb in range(2):
                B.do("tensor", lambda e, j=jb: e.matmul(
                    PBK[j][:], onesrowb[:], pb_sb[0:1, j * 512:(j + 1) * 512],
                    start=True, stop=False))
            for hcc in range(NC):
                for jb in range(2):
                    B.do("tensor", lambda e, hc2=hcc, j=jb, b=bi: e.matmul(
                        PBK[j][:], aold[b][:, hc2 * 128:(hc2 + 1) * 128],
                        projw[:, hc2 * C + j * 512:hc2 * C + (j + 1) * 512],
                        start=False, stop=(hc2 == NC - 1)))
            tmm = B.mark("tensor")
            B.waits("sync", [t_ffn_done_s] + ([prev_p5[tc - 2][2]] if tc >= 2 else []))
            txld = B.do("sync", lambda e, t=tc, b=bi: e.dma_start(
                out=xt[b], in_=xs_d[t * 128:(t + 1) * 128, :]), tok=True)
            B.waits("vector", [tmm, txld])
            B.do("vector", lambda e, b=bi: e.tensor_tensor(
                xt[b][:, 0:512], xt[b][:, 0:512], PBK[0][:], ALU.add))
            tx1 = B.do("vector", lambda e, b=bi: e.tensor_tensor(
                xt[b][:, 512:1024], xt[b][:, 512:1024], PBK[1][:], ALU.add), tok=True)
            B.waits("sync", [tx1])
            tst = B.do("sync", lambda e, t=tc, b=bi: e.dma_start(
                out=xs_d[t * 128:(t + 1) * 128, :], in_=xt[b]), tok=True)
            if "x1" in dbg:
                B.do("sync", lambda e, t=tc, b=bi: e.dma_start(
                    out=dbg_out["x1"][t * 128:(t + 1) * 128, :], in_=xt[b]))
            t_xn1, _p = emit_ln(xt[bi], xn_s[bi], [tx1], cb, t_lnq5.get(tc - 2))
            t_lnq5[tc] = _p
            t_xn1_toks[tc] = t_xn1
            B.waits("scalar", [t_xn1, t_hi_ev.get(tc - 2)])
            thi = B.do("scalar", lambda e, b=bi: e.copy(tb4[b][:], xn_s[b]), tok=True)
            B.waits("vector", [thi])
            tlo = B.do("vector", lambda e, b=bi: e.tensor_tensor(
                tb4[2 + b][:], xn_s[b], tb4[b][:], ALU.subtract), tok=True)
            B.waits("tensor", [thi, tlo, t_hi_ev.get(tc - 1)])
            for cc in range(NC):
                B.do("tensor", lambda e, c=cc, b=bi: tp(e,
                    PTB[0][:, c * 128:(c + 1) * 128], tb4[b][:, c * 128:(c + 1) * 128], identb[:],
                    c == 0, c == NC - 1))
            th0 = B.mark("tensor")
            for cc in range(NC):
                B.do("tensor", lambda e, c=cc, b=bi: tp(e,
                    PTB[1][:, c * 128:(c + 1) * 128], tb4[2 + b][:, c * 128:(c + 1) * 128], identb[:],
                    c == 0, c == NC - 1))
            th1 = B.mark("tensor")
            B.waits("vector", [th0, th1])
            B.do("vector", lambda e, t=tc: e.tensor_copy(
                xnT.rearrange("p (cc tt) -> p cc tt", cc=NC)[:, :, t * 128:(t + 1) * 128],
                PTB[0][:].rearrange("p (cc tt) -> p cc tt", cc=NC)))
            t_hi_ev[tc] = B.do("vector", lambda e, t=tc: e.tensor_copy(
                loT.rearrange("p (cc tt) -> p cc tt", cc=NC)[:, :, t * 128:(t + 1) * 128],
                PTB[1][:].rearrange("p (cc tt) -> p cc tt", cc=NC)), tok=True)
            prev_p5[tc] = (tld, tmm, tst)
        t_p5_v = B.tok("vector")
        if "xnT1" in dbg:
            B.waits("sync", [t_p5_v])
            B.do("sync", lambda e: e.dma_start(out=dbg_out["xnT1"], in_=xnT))
        if "loT1" in dbg:
            B.waits("sync", [t_p5_v])
            B.do("sync", lambda e: e.dma_start(out=dbg_out["loT1"], in_=loT))
        t_p5_s = B.tok("sync")

    if stop >= 6:
        t_wte = [None] * NT
        prev_g = {}
        for tc in range(NT):
            B.waits("tensor", [t_hi_ev[NT - 1], t_glo] + ([prev_g.get(tc - 2)] if tc >= 2 else []))
            gbk = PBK[2 + tc % 2]
            for cc in range(NC):
                B.do("tensor", lambda e, c=cc, t=tc, bk=gbk: e.matmul(
                    bk[:, 0:8], chT(xnT, c, t * 128, 128), ghi[:, c * E:(c + 1) * E],
                    start=(c == 0), stop=False))
                B.do("tensor", lambda e, c=cc, t=tc, bk=gbk: e.matmul(
                    bk[:, 0:8], chT(xnT, c, t * 128, 128), glo[:, c * E:(c + 1) * E],
                    start=False, stop=False))
            for cc in range(NC):
                B.do("tensor", lambda e, c=cc, t=tc, bk=gbk: e.matmul(
                    bk[:, 0:8], chT(loT, c, t * 128, 128), ghi[:, c * E:(c + 1) * E],
                    start=False, stop=(c == NC - 1)))
            tgm = B.mark("tensor")
            B.waits("vector", [tgm])
            tz0 = B.do("vector", lambda e, bk=gbk: e.tensor_copy(zgbuf[:, 0:8], bk[:, 0:8]), tok=True)
            prev_g[tc] = tz0
            B.wait("vector", tz0)
            if "glog" in dbg:
                B.do("vector", lambda e, t=tc: e.tensor_copy(
                    wte[:, 0:1] if False else zs2[:], zgbuf[:, 0:8]))
                B.waits("sync", [B.tok("vector")])
                B.do("sync", lambda e, t=tc: e.dma_start(
                    out=dbg_out["glog"][:, t * E:(t + 1) * E], in_=zgbuf[:, 0:8]))
            t1 = B.do("vector", lambda e: e.reduce_max(zgbuf[:, 8:9], zgbuf[:, 0:8], AX.X), tok=True)
            B.wait("vector", t1)
            t2 = B.do("vector", lambda e: e.tensor_scalar(
                zgbuf[:, 16:24], zgbuf[:, 0:8], zgbuf[:, 8:9], -1e30, ALU.is_equal, ALU.mult), tok=True)
            B.wait("vector", t2)
            t3 = B.do("vector", lambda e: e.tensor_tensor(
                zgbuf[:, 16:24], zgbuf[:, 16:24], zgbuf[:, 0:8], ALU.add), tok=True)
            B.wait("vector", t3)
            t4 = B.do("vector", lambda e: e.reduce_max(zgbuf[:, 9:10], zgbuf[:, 16:24], AX.X), tok=True)
            B.wait("vector", t4)
            t5 = B.do("vector", lambda e: e.tensor_scalar(
                zgbuf[:, 10:11], zgbuf[:, 8:9], -1.0, None, ALU.mult), tok=True)
            B.waits("scalar", [t5])
            t6 = B.do("scalar", lambda e: e.activation(
                zgbuf[:, 16:24], zgbuf[:, 0:8], AF.Exp, bias=zgbuf[:, 10:11], scale=1.0), tok=True)
            B.waits("vector", [t6])
            t7 = B.do("vector", lambda e: e.tensor_scalar(
                zs1[:], zgbuf[:, 0:8], zgbuf[:, 9:10], None, ALU.is_ge), tok=True)
            B.wait("vector", t7)
            t8 = B.do("vector", lambda e: e.tensor_tensor(
                zs2[:], zgbuf[:, 16:24], zs1[:], ALU.mult), tok=True)
            B.wait("vector", t8)
            t9 = B.do("vector", lambda e: e.tensor_reduce(zgbuf[:, 11:12], zs2[:], AX.X, ALU.add), tok=True)
            B.wait("vector", t9)
            t10 = B.do("vector", lambda e: e.reciprocal(zgbuf[:, 12:13], zgbuf[:, 11:12]), tok=True)
            B.wait("vector", t10)
            t_wte[tc] = B.do("vector", lambda e, t=tc: e.tensor_scalar(
                wte[:, t * E:(t + 1) * E], zs2[:], zgbuf[:, 12:13], None, ALU.mult), tok=True)
        t_p6_v = B.tok("vector")
        if "wte" in dbg:
            B.waits("sync", [t_p6_v])
            B.do("sync", lambda e: e.dma_start(out=dbg_out["wte"], in_=wte[:]))

    if stop >= 7:
        t_cs = B.do("sync", lambda e: e.dma_start(out=cosv, in_=cos_d[:]), tok=True)
        t_sn = B.do("sync", lambda e: e.dma_start(out=sinv, in_=sin_d[:]), tok=True)
        B.waits("sync", [t_p5_v, t_p6_v])

        def rope_tile(tc, src_f32, dst_bf, dep_toks):
            cs = cosv[:, tc * 32:(tc + 1) * 32].unsqueeze(1).broadcast_to([128, H, 32])
            sn = sinv[:, tc * 32:(tc + 1) * 32].unsqueeze(1).broadcast_to([128, H, 32])
            s3 = src_f32.rearrange("p (h d) -> p h d", h=H)
            d3 = dst_bf.rearrange("p (h d) -> p h d", h=H)
            rA = ropeA2.rearrange("p (h d) -> p h d", h=H, d=32)
            rB = ropeB2.rearrange("p (h d) -> p h d", h=H, d=32)
            B.waits("vector", dep_toks)
            B.do("vector", lambda e: e.tensor_tensor(rA, s3[:, :, 0:32], cs, ALU.mult))
            B.do("vector", lambda e: e.tensor_tensor(rB, s3[:, :, 32:64], sn, ALU.mult))
            B.do("vector", lambda e: e.tensor_tensor(d3[:, :, 0:32], rA, rB, ALU.subtract))
            B.do("vector", lambda e: e.tensor_tensor(rA, s3[:, :, 0:32], sn, ALU.mult))
            B.do("vector", lambda e: e.tensor_tensor(rB, s3[:, :, 32:64], cs, ALU.mult))
            return B.do("vector", lambda e: e.tensor_tensor(d3[:, :, 32:64], rA, rB, ALU.add), tok=True)

        B.waits("gpsimd", [t_p5_v, t_p6_v])
        t_lw = B.do("gpsimd", lambda e: e.dma_start(
            out=l1w[:].rearrange("p (cc c) -> p cc c", cc=NC),
            in_=wq_d.rearrange("(cc p) c -> p cc c", p=128)), tok=True)
        prev_q7 = {}
        tq7_ev = {}
        tq0s = {}
        for tc in range(NT):
            bi = tc % 2
            qb0 = 2 * (tc % 2)
            B.waits("tensor", [t_lw] + ([prev_q7.get(tc - 2)]))
            for cc in range(NC):
                for jb in range(2):
                    B.do("tensor", lambda e, c=cc, j=jb, t=tc, b0=qb0: e.matmul(
                        PBK[b0 + j][:], chT(xnT, c, t * 128, 128),
                        l1w[:, c * C + j * 512:c * C + (j + 1) * 512],
                        start=(c == 0), stop=(c == NC - 1)))
            tmm = B.mark("tensor")
            B.waits("vector", [tmm, tq7_ev.get(tc - 2)])
            B.do("vector", lambda e, b0=qb0: e.tensor_copy(l1qf[:, 0:512], PBK[b0][:]))
            tqe = B.do("vector", lambda e, b0=qb0: e.tensor_copy(l1qf[:, 512:1024], PBK[b0 + 1][:]), tok=True)
            prev_q7[tc] = tqe
            trp = rope_tile(tc, l1qf, tb4[bi][:], [tqe, tq0s.get(tc - 2)])
            B.waits("tensor", [trp, tq7_ev.get(tc - 1)])
            for cc in range(NC):
                B.do("tensor", lambda e, c=cc, b=bi: tp(e,
                    PTB[0][:, c * 128:(c + 1) * 128], tb4[b][:, c * 128:(c + 1) * 128], identb[:],
                    c == 0, c == NC - 1))
            tq0 = B.mark("tensor")
            tq0s[tc] = tq0
            B.waits("vector", [tq0])
            tq7_ev[tc] = B.do("vector", lambda e, t=tc: e.tensor_copy(
                qrT.rearrange("p (cc tt) -> p cc tt", cc=NC)[:, :, t * 128:(t + 1) * 128],
                PTB[0][:].rearrange("p (cc tt) -> p cc tt", cc=NC)), tok=True)
        t_qr_done = B.tok("vector")
        if "qrT" in dbg:
            B.waits("sync", [t_qr_done])
            B.do("sync", lambda e: e.dma_start(out=dbg_out["qrT"], in_=qrT))

        B.waits("gpsimd", [B.tok("tensor")])
        t_lwk = B.do("gpsimd", lambda e: e.dma_start(
            out=l1w[:].rearrange("p (cc c) -> p cc c", cc=NC),
            in_=wk_d.rearrange("(cc p) c -> p cc c", p=128)), tok=True)
        tz_ekl = B.do("vector", lambda e: e.memset(ekl[0:76, 0:C], 0.0), tok=True)
        prev_k7 = {}
        t_kg_ev = {}
        t_exk = {}
        t_kgpe = {}
        t_ekl_toks = []
        for tc in range(NT):
            bi = tc % 2
            qb0 = 2 * (tc % 2)
            B.waits("tensor", [t_lwk, prev_q7[NT - 2] if tc == 0 else None,
                               prev_q7[NT - 1] if tc == 1 else None] + ([prev_k7.get(tc - 2)]))
            for cc in range(NC):
                for jb in range(2):
                    B.do("tensor", lambda e, c=cc, j=jb, t=tc, b0=qb0: e.matmul(
                        PBK[b0 + j][:], chT(xnT, c, t * 128, 128),
                        l1w[:, c * C + j * 512:c * C + (j + 1) * 512],
                        start=(c == 0), stop=(c == NC - 1)))
            tmm = B.mark("tensor")
            B.waits("vector", [tmm, prev_k7.get(tc - 2)])
            B.do("vector", lambda e, b0=qb0: e.tensor_copy(l1qf[:, 0:512], PBK[b0][:]))
            tke = B.do("vector", lambda e, b0=qb0: e.tensor_copy(l1qf[:, 512:1024], PBK[b0 + 1][:]), tok=True)
            prev_k7[tc] = tke
            trp = rope_tile(tc, l1qf, tb4[bi][:],
                            [tke, tz_ekl, t_exk.get(tc - 2), t_kgpe.get(tc - 2)])
            if tc >= 1:
                B.waits("scalar", [trp])
                tex = B.do("scalar", lambda e, t=tc, b=bi: e.activation(
                    ekl[:, (t - 1) * C:t * C], tb4[b][:], AF.Exp, bias=0.0, scale=1.0), tok=True)
                t_exk[tc] = tex
                if tc == 1:
                    B.waits("vector", [tex])
                    tex = B.do("vector", lambda e: e.memset(ekl[0:76, 0:C], 0.0), tok=True)
                t_ekl_toks.append(tex)
            if tc <= 1:
                B.waits("tensor", [trp, t_kg_ev.get(tc - 1)])
                nrow = 128 if tc == 0 else 76
                for cc in range(NC):
                    B.do("tensor", lambda e, c=cc, b=bi, n=nrow: tp(e,
                        PTB[0][:, c * 128:c * 128 + n],
                        tb4[b][0:n, c * 128:(c + 1) * 128], identb[0:n, 0:n],
                        c == 0, c == NC - 1))
                tkp = B.mark("tensor")
                t_kgpe[tc] = tkp
                B.waits("vector", [tkp])
                co = 0 if tc == 0 else 128
                nrow2 = 128 if tc == 0 else 76
                t_kg_ev[tc] = B.do("vector", lambda e, c0=co, n=nrow2: e.tensor_copy(
                    kgT[:].rearrange("p (hc g) -> p hc g", hc=NC)[:, :, c0:c0 + n],
                    PTB[0][:].rearrange("p (hc g) -> p hc g", hc=NC)[:, :, 0:n]), tok=True)
        t_ekl_done_s = B.tok("scalar")
        t_ekl_done_v = B.tok("vector")
        t_kg_done = B.tok("vector")
        if "ekl" in dbg:
            B.waits("sync", [t_ekl_done_s, t_kg_done])
            B.do("sync", lambda e: e.dma_start(out=dbg_out["ekl"], in_=ekl[:, 0:15 * C]))
        if "kgT" in dbg:
            B.waits("sync", [t_kg_done])
            B.do("sync", lambda e: e.dma_start(
                out=dbg_out["kgT"].rearrange("p (hc g) -> p hc g", hc=NC),
                in_=kgT[:].rearrange("p (hc g) -> p hc g", hc=NC)[:, :, 0:G]))

        t_vw1 = [None] * 8
        t_vw1_used = {}
        B.waits("gpsimd", [B.tok("tensor")])
        for v0 in (0, 1):
            t_vw1[v0] = B.do("gpsimd", lambda e, v=v0, w=v0: e.dma_start(
                out=vw1str[w][:], in_=vwr1_d[v]), tok=True)
        prev_vev1 = {}
        for vd in range(8):
            wb = vd % 2
            for tb in range(4):
                key = (vd, tb)
                B.waits("tensor", [t_vw1[vd],
                                   prev_k7[NT - 2] if (vd == 0 and tb == 0) else None,
                                   prev_k7[NT - 1] if (vd == 0 and tb == 0) else None,
                                   prev_vev1.get((vd, tb - 2)) or prev_vev1.get((vd - 1, 2 + tb))])
                bank = PBK[(vd * 4 + tb) % 2 + 2]
                for cc in range(NC):
                    B.do("tensor", lambda e, c=cc, t=tb, w=wb, bk=bank: e.matmul(
                        bk[:], vw1str[w][:, c * 128:(c + 1) * 128],
                        chT(xnT, c, t * 512, 512),
                        start=(c == 0), stop=(c == NC - 1)))
                tmm = B.mark("tensor")
                if tb == 3:
                    t_vw1_used[vd] = tmm
                B.waits("vector", [tmm])
                tev = B.do("vector", lambda e, bk=bank: e.tensor_copy(aoev[:], bk[:]), tok=True)
                B.waits("tensor", [tev])
                for q4 in range(4):
                    B.do("tensor", lambda e, q=q4: tp(e,
                        PTB[0][:, q * 128:(q + 1) * 128], aoev[:, q * 128:(q + 1) * 128],
                        identb[:], q == 0, q == 3))
                tpe = B.mark("tensor")
                B.waits("vector", [tpe])
                tev2 = B.do("vector", lambda e, v=vd, t=tb: e.tensor_copy(
                    v_sb[:].rearrange("p (tt c) -> p tt c", tt=NT)[:, t * 4:(t + 1) * 4,
                                                                  v * 128:(v + 1) * 128],
                    PTB[0][:, 0:512].rearrange("p (q c) -> p q c", q=4)), tok=True)
                B.waits("tensor", [tev2])
                prev_vev1[key] = tev2
            if vd + 2 < 8:
                B.waits("gpsimd", [t_vw1_used[vd]])
                t_vw1[vd + 2] = B.do("gpsimd", lambda e, v=vd + 2, w=(vd % 2): e.dma_start(
                    out=vw1str[w][:], in_=vwr1_d[v]), tok=True)
        t_v1_done = B.tok("vector")
        if "v0" in dbg and stop >= 7:
            B.waits("sync", [t_v1_done])
            B.do("sync", lambda e: e.dma_start(out=dbg_out["v0"], in_=v_sb))

    if stop >= 8:
        tS_ev = {}
        for hc in range(NC):
            B.waits("tensor", [t_ekl_done_s, t_ekl_done_v, t_consts] + ([tS_ev.get(hc - 1)]))
            for lt in range(15):
                B.do("tensor", lambda e, l=lt, c=hc: e.matmul(
                    PBK[4][:, 0:1], ekl[:, l * C + c * 128:l * C + (c + 1) * 128],
                    onescolb[:], start=(l == 0), stop=(l == 14)))
            tS = B.mark("tensor")
            B.waits("vector", [tS])
            tS_ev[hc] = B.do("vector", lambda e, c=hc: e.tensor_copy(
                iS8[:, c:c + 1], PBK[4][:, 0:1]), tok=True)
        B.wait("vector", tS_ev[NC - 1])
        t_iS = B.do("vector", lambda e: e.reciprocal(iS8[:], iS8[:]), tok=True)
        B.wait("vector", t_iS)
        if "iS8" in dbg:
            B.waits("sync", [t_iS])
            B.do("sync", lambda e: e.dma_start(out=dbg_out["iS8"], in_=iS8[:]))

        t_lodma = {}
        t_kvev = {}
        t_kvev_s = {}
        t_tlm = {}
        t_tle = {}
        for hc in range(NC):
            kb = kv2[hc % 2]
            B.waits("vector", [t_tlm.get((hc - 2, 3))])
            tzm = B.do("vector", lambda e, k=kb: e.memset(k[:], 0.0), tok=True)
            for sub in range(2):
                h = 2 * hc + sub
                off = sub * 64
                bk = PBK[5 - sub]
                B.waits("tensor", [t_v1_done, tzm, t_kvev_s.get((hc - 1, sub)),
                                   tS_ev[NC - 1] if (hc == 0 and sub == 1) else None])
                for lt in range(15):
                    B.do("tensor", lambda e, l=lt, hh=h, o=off, bb=bk: e.matmul(
                        bb[o:o + 64, o:o + 64],
                        v_sb[:, (l + 1) * C + hh * DH:(l + 1) * C + (hh + 1) * DH],
                        ekl[:, l * C + hh * DH:l * C + (hh + 1) * DH],
                        start=(l == 0), stop=(l == 14),
                        **({"tile_position": (0, 64)} if o else {})))
                tkv = B.mark("tensor")
                B.waits("vector", [tkv])
                tev = B.do("vector", lambda e, k=kb, o=off, bb=bk: e.tensor_copy(
                    k[o:o + 64, o:o + 64], bb[o:o + 64, o:o + 64]), tok=True)
                t_kvev_s[(hc, sub)] = tev
                t_kvev[hc] = tev
            if "kv2" in dbg and hc <= 1:
                B.waits("sync", [t_kvev[hc]])
                B.do("sync", lambda e, c=hc: e.dma_start(out=dbg_out["kv2"][c], in_=kv2[c][:]))
            blocks = [(G, 512), (716, 512), (1228, 512), (1740, 308)]
            for nb, (c0, w) in enumerate(blocks):
                B.waits("tensor", [t_kvev[hc], t_qr_done if hc == 0 and nb == 0 else None,
                                   t_tle.get((hc, nb - 2)) or t_tle.get((hc - 1, nb + 2))])
                tlm = B.do("tensor", lambda e, k=kb, c=hc, c0=c0, w=w, n=nb: e.matmul(
                    PBK[n % 2][:, 0:w], k[:], chT(qrT, c, c0, w),
                    start=True, stop=True), tok=True)
                t_tlm[(hc, nb)] = tlm
                B.waits("vector", [tlm])
                tle = B.do("vector", lambda e, c=hc, w=w, n=nb: e.tensor_scalar(
                    aoev[:, 0:w], PBK[n % 2][:, 0:w], iS8[:, c:c + 1], None, ALU.mult), tok=True)
                t_tle[(hc, nb)] = tle
                if "aoev0" in dbg and hc == 0 and nb == 0:
                    B.waits("sync", [tle])
                    tdmp = B.do("sync", lambda e: e.dma_start(
                        out=dbg_out["aoev0"], in_=aoev[:]), tok=True)
                    B.waits("vector", [tdmp])
                B.waits("sync", [tle])
                t_lodma[(hc, nb)] = B.do("sync", lambda e, c=hc, c0=c0, w=w: e.dma_start(
                    out=ao_d[c, :, c0:c0 + w], in_=aoev[:, 0:w]), tok=True)
                B.waits("vector", [t_lodma[(hc, nb)]])
        t_p8_pe = B.tok("tensor")

    if stop >= 9:
        t_godma = {}
        for h in range(H):
            hc, off = h // 2, (h % 2) * 64
            B.waits("tensor", [t_kg_done, t_qr_done, t_p8_pe if h == 0 else None,
                               t_godma.get(h - 2)])
            B.do("tensor", lambda e, c=hc, o=off: e.matmul(
                PBK[2][:, 0:G], chT(qrT, c, 0, 128, o, 64),
                kgT[o:o + 64, c * 256:c * 256 + G], start=True, stop=True))
            tg1 = B.do("tensor", lambda e, c=hc, o=off: e.matmul(
                PBK[3][0:76, 0:G], chT(qrT, c, 128, 76, o, 64),
                kgT[o:o + 64, c * 256:c * 256 + G], start=True, stop=True), tok=True)
            B.waits("vector", [tg1])
            tm1 = B.do("vector", lambda e: e.reduce_max(zgbuf[:, 0:1], PBK[2][:, 0:G], AX.X), tok=True)
            tm2 = B.do("vector", lambda e: e.reduce_max(zgbuf[0:76, 1:2], PBK[3][0:76, 0:G], AX.X), tok=True)
            B.waits("vector", [tm1, tm2])
            tn1 = B.do("vector", lambda e: e.tensor_scalar(
                zgbuf[:, 2:3], zgbuf[:, 0:1], -0.125, None, ALU.mult), tok=True)
            tn2 = B.do("vector", lambda e: e.tensor_scalar(
                zgbuf[0:76, 3:4], zgbuf[0:76, 1:2], -0.125, None, ALU.mult), tok=True)
            B.waits("scalar", [tn1, tn2])
            te1 = B.do("scalar", lambda e: e.activation(
                aoev[:, 0:G], PBK[2][:, 0:G], AF.Exp, bias=zgbuf[:, 2:3], scale=0.125,
                accum_out=zgbuf[:, 4:5]), tok=True)
            te2 = B.do("scalar", lambda e: e.activation(
                aoev[0:76, 256:256 + G], PBK[3][0:76, 0:G], AF.Exp, bias=zgbuf[0:76, 3:4],
                scale=0.125, accum_out=zgbuf[0:76, 5:6]), tok=True)
            B.waits("vector", [te1, te2])
            tr1 = B.do("vector", lambda e: e.reciprocal(zgbuf[:, 6:7], zgbuf[:, 4:5]), tok=True)
            tr2 = B.do("vector", lambda e: e.reciprocal(zgbuf[0:76, 7:8], zgbuf[0:76, 5:6]), tok=True)
            B.waits("vector", [tr1, tr2])
            ts1 = B.do("vector", lambda e: e.tensor_scalar(
                aoev[:, 0:G], aoev[:, 0:G], zgbuf[:, 6:7], None, ALU.mult), tok=True)
            ts2 = B.do("vector", lambda e: e.tensor_scalar(
                aoev[0:76, 256:256 + G], aoev[0:76, 256:256 + G], zgbuf[0:76, 7:8],
                None, ALU.mult), tok=True)
            B.waits("tensor", [ts1, ts2])
            B.do("tensor", lambda e: tp(e, PTB[0][:, 0:128], aoev[:, 0:128], identb[:], True, False))
            B.do("tensor", lambda e: tp(e, PTB[0][:, 128:204], aoev[0:76, 256:256 + 128],
                                        identb[0:76, 0:76], False, True))
            B.do("tensor", lambda e: tp(e, PTB[1][0:76, 0:128], aoev[:, 128:204], identb[:], True, False))
            tg2 = B.do("tensor", lambda e: tp(e, PTB[1][0:76, 128:204],
                                              aoev[0:76, 256 + 128:256 + 204],
                                              identb[0:76, 0:76], False, True), tok=True)
            B.waits("vector", [tg2])
            tga = B.do("vector", lambda e: e.tensor_copy(agT0[:, 0:G], PTB[0][:, 0:G]), tok=True)
            tgb = B.do("vector", lambda e: e.tensor_copy(agT1[:, 0:G], PTB[1][0:76, 0:G]), tok=True)
            B.waits("tensor", [tga, tgb])
            B.do("tensor", lambda e, hh=h, o=off: e.matmul(
                PBK[4][o:o + 64, 0:G], v_sb[:, 0 * C + hh * DH:0 * C + (hh + 1) * DH],
                agT0[:, 0:G], start=True, stop=False,
                **({"tile_position": (0, 64)} if o else {})))
            tgo = B.do("tensor", lambda e, hh=h, o=off: e.matmul(
                PBK[4][o:o + 64, 0:G], v_sb[0:76, 1 * C + hh * DH:1 * C + (hh + 1) * DH],
                agT1[0:76, 0:G], start=False, stop=True,
                **({"tile_position": (0, 64)} if o else {})), tok=True)
            B.waits("vector", [tgo])
            tge = B.do("vector", lambda e, o=off: e.tensor_copy(
                aoev[o:o + 64, 256:256 + G], PBK[4][o:o + 64, 0:G]), tok=True)
            B.waits("sync", [tge])
            t_godma[h] = B.do("sync", lambda e, c=hc, o=off: e.dma_start(
                out=ao_d[c, o:o + 64, 0:G], in_=aoev[o:o + 64, 256:256 + G]), tok=True)
            B.waits("vector", [t_godma[h]])
            B.waits("scalar", [t_godma[h]])
        t_p9_pe = B.tok("tensor")
        t_p9_sc = B.tok("scalar")
        if "ao1" in dbg:
            B.waits("sync", [t_godma[H - 1], t_lodma[(NC - 1, 3)]])
            B.do("sync", lambda e: e.dma_start(out=dbg_out["ao1"], in_=ao_d))

    if stop >= 10:
        B.waits("gpsimd", [t_p9_pe])
        t_wo = B.do("gpsimd", lambda e: e.dma_start(
            out=woutw[:].rearrange("p (cc c) -> p cc c", cc=NC),
            in_=wo_d.rearrange("(cc p) c -> p cc c", p=128)), tok=True)
        prev_w = {}
        for tc in range(NT):
            bi = tc % 2
            jp = 2 * (tc % 2)
            B.waits("sync", [t_godma[H - 1], t_lodma[(NC - 1, 3)]]
                    + ([prev_w[tc - 2][0]] if tc >= 2 else []))
            tld = B.do("sync", lambda e, t=tc, b=bi: e.dma_start(
                out=aold[b][:].rearrange("p (hc t) -> p hc t", hc=NC),
                in_=ao_d[:, :, t * 128:(t + 1) * 128].rearrange("hc p t -> p hc t")), tok=True)
            B.waits("tensor", [t_wo, tld, t_p9_pe,
                               t_tle[(NC - 1, 3)] if tc == 0 else None,
                               t_p9_sc if tc == 1 else None]
                    + ([prev_w[tc - 2][3]] if tc >= 2 else []))
            for hcc in range(NC):
                for jb in range(2):
                    B.do("tensor", lambda e, hc2=hcc, j=jb, b=bi, p=jp: e.matmul(
                        PBK[p + j][:], aold[b][:, hc2 * 128:(hc2 + 1) * 128],
                        woutw[:, hc2 * C + j * 512:hc2 * C + (j + 1) * 512],
                        start=(hc2 == 0), stop=(hc2 == NC - 1)))
            tmm = B.mark("tensor")
            B.waits("sync", [t_p5_s] + ([prev_w[tc - 1][3]] if tc >= 1 else []))
            txld = B.do("sync", lambda e, t=tc: e.dma_start(
                out=xt10, in_=xs_d[t * 128:(t + 1) * 128, :]), tok=True)
            B.waits("vector", [tmm, txld, t_p8_pe if tc == 0 else None])
            B.do("vector", lambda e, t=tc, p=jp: e.tensor_tensor(
                x_sbf[:, t * C:t * C + 512], xt10[:, 0:512], PBK[p][:], ALU.add))
            tx2 = B.do("vector", lambda e, t=tc, p=jp: e.tensor_tensor(
                x_sbf[:, t * C + 512:(t + 1) * C], xt10[:, 512:1024], PBK[p + 1][:], ALU.add), tok=True)
            tst = tx2
            prev_w[tc] = (tld, tmm, txld, tx2)
            if "x2" in dbg:
                B.waits("sync", [tx2])
                B.do("sync", lambda e, t=tc: e.dma_start(
                    out=dbg_out["x2"][t * 128:(t + 1) * 128, :], in_=x_sbf[:, t * C:(t + 1) * C]))
        t_p10_v = B.tok("vector")

    if stop >= 11:
        t_b2 = {}
        t_w1ld = {}
        t_w2ld = {}
        t_gelu = {}
        t_yev = {}
        t_p1g = {}
        t_ph2mm = {}
        t_bias = {}
        for ei in range(E):
            B.waits("gpsimd", [t_p10_v if ei == 0 else None, t_bias.get((ei - 1, 3))])
            t_b2[ei] = B.do("gpsimd", lambda e, x=ei: e.dma_start(
                out=b2row[:], in_=b2_d[x:x + 1, :]), tok=True)
            for tb in range(4):
                for fc in range(NF):
                    fls = [0, 1, 2, 3] if fc == 0 else ([fc + 3] if fc + 3 < NF else [])
                    for fl in fls:
                        prev_use = (t_p1g.get((ei, tb, fl - 4))
                                    or t_p1g.get((ei, tb - 1, NF - 4 + fl))
                                    or t_p1g.get((ei - 1, 3, NF - 4 + fl)))
                        B.waits("gpsimd", [prev_use,
                                           t_p10_v if (ei == 0 and tb == 0 and fc == 0) else None])
                        t_w1ld[(ei, tb, fl)] = B.do("gpsimd", lambda e, x=ei, f=fl: e.dma_start(
                            out=w1b4[f % 4][:], in_=w1_d[x, f]), tok=True)
                    bank = PBK[4 + (fc % 4)]
                    B.waits("tensor", [t_w1ld[(ei, tb, fc)],
                                       t_gelu.get((ei, tb, fc - 4))
                                       or t_gelu.get((ei, tb - 1, NF - 4 + fc))
                                       or t_gelu.get((ei - 1, 3, NF - 4 + fc)),
                                       (t_yev.get((ei, tb - 1)) or t_yev.get((ei - 1, 3)))
                                       if fc < 4 else None])
                    for cc in range(NC):
                        B.do("tensor", lambda e, c=cc, f=fc, t=tb, bk=bank: e.matmul(
                            bk[:], w1b4[f % 4][:, c * 128:(c + 1) * 128],
                            chT(xnT, c, t * 512, 512),
                            start=(c == 0), stop=(c == NC - 1)))
                    t_p1g[(ei, tb, fc)] = B.mark("tensor")
                    B.waits("scalar", [t_p1g[(ei, tb, fc)]])
                    t_gelu[(ei, tb, fc)] = B.do("scalar", lambda e, x=ei, f=fc, bk=bank: e.activation(
                        hT[:, f * 512:(f + 1) * 512], bk[:], AF.Gelu,
                        bias=b1c[:, x * NF + f:x * NF + f + 1], scale=1.0), tok=True)
                B.waits("tensor", [t_b2[ei], t_gelu[(ei, tb, NF - 1)], t_gelu[(ei, tb, NF - 2)],
                                   t_gelu[(ei, tb, NF - 3)], t_gelu[(ei, tb, NF - 4)],
                                   t_yev.get((ei, tb - 1)) or t_yev.get((ei - 1, 3))])
                for tc4 in range(4):
                    for jb in range(2):
                        B.do("tensor", lambda e, t4=tc4, j=jb: e.matmul(
                            PBK[t4 * 2 + j][:], onesrowb[:], b2row[0:1, j * 512:(j + 1) * 512],
                            start=True, stop=False))
                t_bias[(ei, tb)] = B.mark("tensor")
                for fc in range(NF):
                    fls = [0, 1, 2, 3] if fc == 0 else ([fc + 3] if fc + 3 < NF else [])
                    for fl in fls:
                        prev_use = (t_ph2mm.get((ei, tb, fl - 4))
                                    or t_ph2mm.get((ei, tb - 1, NF - 4 + fl))
                                    or t_ph2mm.get((ei - 1, 3, NF - 4 + fl)))
                        B.waits("gpsimd", [prev_use])
                        t_w2ld[(ei, tb, fl)] = B.do("gpsimd", lambda e, x=ei, f=fl: e.dma_start(
                            out=w2b4[f % 4][:], in_=w2_d[x, f * 128:(f + 1) * 128, :]), tok=True)
                    B.waits("tensor", [t_w2ld[(ei, tb, fc)]])
                    for tc4 in range(4):
                        for jb in range(2):
                            B.do("tensor", lambda e, f=fc, t4=tc4, j=jb: e.matmul(
                                PBK[t4 * 2 + j][:],
                                hT[:, f * 512 + t4 * 128:f * 512 + (t4 + 1) * 128],
                                w2b4[f % 4][:, j * 512:(j + 1) * 512],
                                start=False, stop=(f == NF - 1)))
                    t_ph2mm[(ei, tb, fc)] = B.mark("tensor")
                B.waits("vector", [t_ph2mm[(ei, tb, NF - 1)], t_p10_v])
                for tc4 in range(4):
                    tg = tb * 4 + tc4
                    for jb in range(2):
                        B.do("vector", lambda e, t=tg, t4=tc4, j=jb, x=ei: e.scalar_tensor_tensor(
                            x_sbf[:, t * C + j * 512:t * C + (j + 1) * 512],
                            PBK[t4 * 2 + j][:], wte[:, t * E + x:t * E + x + 1],
                            x_sbf[:, t * C + j * 512:t * C + (j + 1) * 512],
                            ALU.mult, ALU.add))
                t_yev[(ei, tb)] = B.mark("vector")
        t_moe_v = B.tok("vector")
        if "hT0" in dbg:
            B.waits("sync", [t_gelu[(0, 0, NF - 1)]])
            B.do("sync", lambda e: e.dma_start(out=dbg_out["hT0"], in_=hT))

    if stop >= 12:
        B.waits("sync", [t_moe_v])
        for tc in range(NT):
            B.do("sync", lambda e, t=tc: e.dma_start(
                out=out_d[t * 128:(t + 1) * 128, :], in_=x_sbf[:, t * C:(t + 1) * C]),
                tok=(tc == NT - 1))
        B.wait("sync", B.tok("sync"))

    if stop < 12:
        B.waits("sync", [B.tok("vector"), B.tok("tensor"), B.tok("scalar"), B.tok("gpsimd")])
        if stop >= 2:
            for tc in range(NT):
                B.do("sync", lambda e, t=tc: e.dma_start(
                    out=out_d[t * 128:(t + 1) * 128, :], in_=xs_d[t * 128:(t + 1) * 128, :]))
            B.do("sync", lambda e: e.dma_start(out=out_d[0:1, 0:1], in_=xs_d[0:1, 0:1]), tok=True)
        else:
            tz = B.do("vector", lambda e: e.memset(rr0, 0.0), tok=True)
            B.waits("sync", [tz])
            for tc in range(NT):
                B.do("sync", lambda e, t=tc: e.dma_start(
                    out=out_d[t * 128:(t + 1) * 128, :], in_=rr0), tok=(tc == NT - 1))
        B.wait("sync", B.tok("sync"))

    with nc.Block() as block:
        B.emit(block)
    es.close()
    return nc, D, dbg_out


def _host_inputs(inputs, n_cores=8):
    f32 = np.float32
    bf16 = ml_dtypes.bfloat16
    x = np.ascontiguousarray(inputs["L0_attn_x"])
    qkvwT = np.ascontiguousarray(np.asarray(inputs["L0_attn_qkv_w"]).T)
    projT = np.ascontiguousarray(np.asarray(inputs["L0_attn_proj_w"]).T)
    pb = np.ascontiguousarray(np.asarray(inputs["L0_attn_proj_b"]).reshape(1, C))
    qnw = np.ascontiguousarray(np.broadcast_to(inputs["L0_attn_q_norm_w"], (128, DH)))
    knw = np.ascontiguousarray(np.broadcast_to(inputs["L0_attn_k_norm_w"], (128, DH)))
    rhs = np.ascontiguousarray(inputs["L0_ffn_rhs"])
    wq = np.ascontiguousarray(inputs["L1_attn_Wq"])
    wk = np.ascontiguousarray(inputs["L1_attn_Wk"])
    wv = np.ascontiguousarray(inputs["L1_attn_Wv"])
    wo = np.ascontiguousarray(inputs["L1_attn_Wout"])
    gw = np.ascontiguousarray(inputs["L1_ffn_gate_w"])
    w1 = np.asarray(inputs["L1_ffn_w1"]).reshape(E, NC, 128, NF, 128).transpose(0, 3, 2, 1, 4)
    w1 = np.ascontiguousarray(w1.reshape(E, NF, 128, NC * 128))
    b1 = np.asarray(inputs["L1_ffn_b1"]).reshape(E, NF, 128).transpose(2, 0, 1)
    b1c = np.ascontiguousarray(b1.reshape(128, E * NF))
    w2 = np.ascontiguousarray(inputs["L1_ffn_w2"])
    vw = np.asarray(inputs["L0_attn_qkv_w"]).T[:, 2 * C:3 * C]
    vwr = np.ascontiguousarray(vw.reshape(NC, 128, 8, 128).transpose(2, 1, 0, 3).reshape(8, 128, NC * 128))
    vw1 = np.asarray(inputs["L1_attn_Wv"])
    vwr1 = np.ascontiguousarray(vw1.reshape(NC, 128, 8, 128).transpose(2, 1, 0, 3).reshape(8, 128, NC * 128))
    b2 = np.ascontiguousarray(inputs["L1_ffn_b2"])

    inv = (f32(1.0) / (f32(10000.0) ** (np.arange(0, DH, 2, dtype=f32) / f32(DH)))).astype(f32)
    fr = (np.arange(T, dtype=f32)[:, None] * inv[None, :]).astype(f32)
    cos = np.cos(fr.astype(np.float64)).astype(f32)
    sin = np.sin(fr.astype(np.float64)).astype(f32)
    cos_l = np.ascontiguousarray(cos.reshape(NT, 128, 32).transpose(1, 0, 2).reshape(128, NT * 32))
    sin_l = np.ascontiguousarray(sin.reshape(NT, 128, 32).transpose(1, 0, 2).reshape(128, NT * 32))

    selallb = np.zeros((16, NC * 128), bf16)
    for hc in range(NC):
        for p in range(128):
            selallb[2 * hc + p // 64, hc * 128 + p] = 1.0
    selj = np.zeros((8, NC * 128), f32)
    for jc in range(NC):
        selj[jc, jc * 128:(jc + 1) * 128] = 1.0
    cvec = np.zeros((128, 8), f32)
    cvec[:, 0] = 1e-5
    cvec[:, 1] = 1e-6
    cvec[:, 2] = 127.0

    shared = dict(qkvwT=qkvwT, projT=projT, pb=pb, d_qnw=qnw, d_knw=knw, rhs=rhs, vwr=vwr, vwr1=vwr1,
                  wq=wq, wk=wk, wv=wv, wo=wo, gw=gw, w1=w1, d_b1c=b1c, w2=w2, b2=b2,
                  cos=cos_l, sin=sin_l, d_ident=np.eye(128, dtype=f32),
                  d_identb=np.eye(128, dtype=bf16),
                  d_onesrow=np.ones((1, 128), f32), d_onesrowb=np.ones((1, 128), bf16),
                  d_onescolb=np.ones((128, 1), bf16),
                  d_selallb=selallb, d_selj=selj, d_cvec=cvec)
    in_maps = []
    for i in range(n_cores):
        m = dict(shared)
        m["x"] = np.ascontiguousarray(x[i])
        in_maps.append(m)
    return in_maps


def kernel(**inputs):
    nc, D, dbg_out = build_program()
    in_maps = _host_inputs(inputs)
    res = run_bass_kernel_spmd(nc, in_maps, core_ids=list(range(8)))
    return np.stack([np.asarray(r["out"]) for r in res.results], axis=0)


# version 07
# speedup vs baseline: 1.3741x; 1.3741x over previous
import numpy as np
import ml_dtypes
import concourse.bass as bass
import concourse.mybir as mybir
from concourse.bass_utils import run_bass_kernel_spmd
from contextlib import ExitStack

F32 = mybir.dt.float32
BF16 = mybir.dt.bfloat16
AF = mybir.ActivationFunctionType
ALU = mybir.AluOpType
AX = mybir.AxisListType

T, NT, C, NC = 2048, 16, 1024, 8
H, DH, G, E, FF, NF = 16, 64, 204, 8, 4096, 32
LTOK = T - G
ENGINES = ("sync", "gpsimd", "tensor", "vector", "scalar")
KB = 512


class Build:
    def __init__(self, nc):
        self.nc = nc
        self.q = {e: [] for e in ENGINES}
        self.val = {e: 0 for e in ENGINES}
        self.step = {"sync": 16, "gpsimd": 16, "tensor": 1, "vector": 1, "scalar": 1}
        self.sem = {}

    def wait(self, eng, tok):
        if tok is not None:
            self.q[eng].append(("w", tok[0], tok[1]))

    def waits(self, eng, toks):
        best = {}
        for t in toks:
            if t is None:
                continue
            best[t[0]] = max(best.get(t[0], 0), t[1])
        for s, v in best.items():
            self.q[eng].append(("w", s, v))

    def do(self, eng, fn, tok=False, sem=None):
        if eng in ("vector", "scalar"):
            self.q[eng].append(("w", eng, self.val[eng]))
            self.val[eng] += self.step[eng]
            self.q[eng].append(("i", fn, self.step[eng]))
            return (eng, self.val[eng])
        if tok or eng in ("sync", "gpsimd"):
            if sem is not None:
                self.val[sem] = self.val.get(sem, 0) + 16
                self.q[eng].append(("I", fn, 16, sem))
                return (sem, self.val[sem])
            self.q[eng].append(("w", eng, self.val[eng]))
            self.val[eng] += self.step[eng]
            self.q[eng].append(("i", fn, self.step[eng]))
            return (eng, self.val[eng])
        self.q[eng].append(("e", fn))
        return None

    def tok(self, eng):
        return (eng, self.val[eng])

    def mark(self, eng):
        for i in range(len(self.q[eng]) - 1, -1, -1):
            item = self.q[eng][i]
            if item[0] == "w":
                continue
            if item[0] == "i":
                return (eng, self._val_at(eng, i))
            self.val[eng] += self.step[eng]
            self.q[eng][i] = ("i", item[1], self.step[eng])
            return (eng, self.val[eng])
        raise RuntimeError("mark: no instruction on " + eng)

    def _val_at(self, eng, idx):
        v = 0
        for item in self.q[eng][:idx + 1]:
            if item[0] == "i":
                v += item[2]
        return v

    def emit(self, block):
        sem = self.sem
        for eng in ENGINES:
            queue = self.q[eng]

            def body(e, queue=queue):
                for item in queue:
                    if item[0] == "w":
                        e.wait_ge(sem[item[1]], item[2])
                    elif item[0] == "e":
                        item[1](e)
                    elif item[0] == "I":
                        item[1](e).then_inc(sem[item[3]], item[2])
                    else:
                        item[1](e).then_inc(sem[eng], item[2])
            getattr(block, eng)(body)


def build_program(debug=(), stop=99):
    dbg = set(debug)
    p0cut = next((int(d[5:]) for d in dbg if d.startswith("p0cut")), 99)
    nc = bass.Bass("TRN2", target_bir_lowering=False, detect_race_conditions=False)
    D = {}

    def din(name, shape, dt=F32):
        D[name] = nc.dram_tensor(name, shape, dt, kind="ExternalInput").ap()
        return D[name]

    x_d = din("x", [T, C])
    qkvwT_d = din("qkvwT", [C, 3 * C])
    projT_d = din("projT", [C, C])
    pb_d = din("pb", [1, C])
    qnw_d = din("d_qnw", [128, DH])
    knw_d = din("d_knw", [128, DH])
    rhs_d = din("rhs", [C, C])
    wq_d = din("wq", [C, C])
    wk_d = din("wk", [C, C])
    wv_d = din("wv", [C, C])
    wo_d = din("wo", [C, C])
    gw_d = din("gw", [C, E])
    vwr_d = din("vwr", [8, 128, NC * 128])
    vwr1_d = din("vwr1", [8, 128, NC * 128])
    w1_d = din("w1", [E, NF, 128, NC * 128])
    b1c_d = din("d_b1c", [128, E * NF])
    w2_d = din("w2", [E, FF, C])
    b2_d = din("b2", [E, C])
    cos_d = din("cos", [128, NT * 32])
    sin_d = din("sin", [128, NT * 32])
    ident_d = din("d_ident", [128, 128])
    identb_d = din("d_identb", [128, 128], BF16)
    onesrow_d = din("d_onesrow", [1, 128])
    onesrowb_d = din("d_onesrowb", [1, 128], BF16)
    onescolb_d = din("d_onescolb", [128, 1], BF16)
    selallb_d = din("d_selallb", [16, NC * 128], BF16)
    selj_d = din("d_selj", [8, NC * 128])
    cvec_d = din("d_cvec", [128, 8])
    xs_d = nc.dram_tensor("xs", [T, C], F32, kind="Internal").ap()
    ao_d = nc.dram_tensor("ao", [NC, 128, T], BF16, kind="Internal").ap()
    out_d = nc.dram_tensor("out", [T, C], F32, kind="ExternalOutput").ap()
    D["out"] = out_d

    es = ExitStack()

    def sb(name, shape, dt):
        return es.enter_context(nc.sbuf_tensor(name, shape, dt))

    def psum(name, shape, dt=F32):
        return es.enter_context(nc.psum_tensor(name, shape, dt))

    B = Build(nc)

    ident = sb("ident", [128, 128], F32)
    identb = sb("identb", [128, 128], BF16)
    onesrow = sb("onesrow", [1, 128], F32)
    onesrowb = sb("onesrowb", [1, 128], BF16)
    onescolb = sb("onescolb", [128, 1], BF16)
    cvec = sb("cvec", [128, 8], F32)
    r127 = sb("r127", [128, 1], F32)
    selallb = sb("selallb", [16, NC * 128], BF16)
    qnw = sb("qnw", [128, DH], F32)
    knw = sb("knw", [128, DH], F32)
    pb_sb = sb("pb_sb", [1, C], BF16)
    b2row = sb("b2row", [1, C], BF16)
    b1c = sb("b1c", [128, E * NF], F32)
    gsml = sb("gsml", [128, NC * E], F32)
    ghi = sb("ghi", [128, NC * E], BF16)
    glo = sb("glo", [128, NC * E], BF16)
    wte = sb("wte", [128, NT * E], F32)
    stats = sb("stats", [128, 16], F32)
    colsA = sb("colsA", [128, 8], F32)
    colsB = sb("colsB", [128, 16], F32)
    sl_all = sb("sl_all", [128, NT], F32)
    isl_all = sb("isl_all", [128, NT], F32)
    ssq_q = sb("ssq_q", [128, NT * H], F32)
    ssq_k = sb("ssq_k", [128, NT * H], F32)
    negc = sb("negc", [128, NT * H], F32)
    zall = sb("zall", [128, NT * H], F32)
    zpart = sb("zpart", [128, 8], F32)
    m2b = sb("m2b", [128, H], F32)
    s8 = sb("s8", [8, 128], F32)
    sr_cols = sb("sr_cols", [128, NC], F32)
    agT0 = sb("agT0", [128, G], BF16)
    agT1 = sb("agT1", [76, G], BF16)
    kv2 = [sb(f"kv2_{i}", [128, 128], BF16) for i in range(2)]
    iS8 = sb("iS8", [128, NC], F32)
    zgbuf = sb("zgbuf", [128, 24], F32)
    zs1 = sb("zs1", [128, 8], F32)
    zs2 = sb("zs2", [128, 8], F32)
    aoev = sb("aoev", [128, 512], BF16)
    tb4 = [sb(f"tb4_{i}", [128, C], BF16) for i in range(4)]

    AR = sb("AR", [128, 176 * KB], BF16)

    def zbv(kb_off, kb_len):
        return AR[:, kb_off * KB:(kb_off + kb_len) * KB]

    def zfv(kb_off, kb_len):
        return AR[:, kb_off * KB:(kb_off + kb_len) * KB].bitcast(F32)

    xnT = zbv(0, 32)
    attn_s = zbv(0, 16)
    attnT_s = zbv(16, 8)
    izTb = zbv(160, 4)[0:16, 0:T]
    izbb = zbv(164, 4)
    qxT = zbv(32, 32)
    qsT = zbv(32, 32)
    loT = zbv(32, 32)
    qrT = zbv(32, 32)
    qrhs = zbv(64, 16)
    ksT = zbv(64, 32)
    ekl = zbv(64, 30)
    x_sbf = zfv(32, 64)
    v_sb = zbv(96, 32)
    projw = zbv(96, 16)
    woutw = zbv(96, 16)
    aold = [zbv(112, 2), zbv(114, 2)]
    hT = zbv(96, 32)
    rhsf = zfv(128, 32)
    xt = [zfv(128, 4), zfv(132, 4)]
    xn_s = [zfv(136, 4), zfv(140, 4)]
    rr0 = zfv(144, 4)
    rr = zfv(148, 4)
    rt = zfv(152, 4)
    spare4 = zfv(156, 4)
    qf = zfv(144, 4)
    kf = zfv(148, 4)
    sq_s = zfv(152, 4)
    ropeA = zfv(156, 2)
    ropeB = zfv(158, 2)
    cos_sb = zfv(128, 2)
    sin_sb = zfv(130, 2)
    p0rr0 = zfv(32, 4)
    p0rr = zfv(36, 4)
    p0rt = zfv(40, 4)
    sr_b = zfv(160, 4)
    isr_b = zfv(164, 4)
    selj = zfv(168, 4)[0:8, 0:1024]
    wqk = zfv(128, 32)
    wqkb = zbv(128, 32)
    vwstr = [zbv(160, 2), zbv(162, 2)]
    p3qf = zfv(96, 4)
    p3kf = zfv(100, 4)
    p3sq = zfv(104, 4)
    wstr = [zbv(160, 6), zbv(166, 6)]
    l1w = zbv(144, 16)
    l1qf = zfv(132, 4)
    ropeA2 = zfv(136, 2)
    ropeB2 = zfv(138, 2)
    cosv = zfv(128, 2)
    sinv = zfv(130, 2)
    xt10 = zfv(132, 4)
    vw1str = [zbv(140, 2), zbv(142, 2)]
    kgT = zbv(172, 4)
    w1b = [zbv(160, 2), zbv(162, 2)]
    w2b = [zbv(164, 2), zbv(166, 2)]
    w1b4 = [zbv(128, 2), zbv(130, 2), zbv(132, 2), zbv(134, 2)]
    w2b4 = [zbv(136, 2), zbv(138, 2), zbv(140, 2), zbv(142, 2)]

    PBK = [psum(f"pbk{i}", [128, 512], F32) for i in range(8)]
    PTB = [PBK[6][:].bitcast(BF16), PBK[7][:].bitcast(BF16)]

    left = nc.sbuf_bytes_remaining
    assert left >= 0, f"SBUF over budget by {-left} bytes/partition"

    for s in ENGINES:
        B.sem[s] = es.enter_context(nc.semaphore(f"sem_{s}"))
    for s in ["m1s0", "m1s1", "m1s2", "m1s3", "m2s0", "m2s1", "m2s2", "m2s3"]:
        B.sem[s] = es.enter_context(nc.semaphore(f"sem_{s}"))

    dbg_out = {}

    def add_dbg(name, shape, dt=F32):
        if name in dbg:
            dbg_out[name] = nc.dram_tensor("dbg_" + name, shape, dt, kind="ExternalOutput").ap()

    add_dbg("xn", [T, C]); add_dbg("qx", [T, C], BF16); add_dbg("sl", [128, NT])
    add_dbg("srb", [128, C]); add_dbg("qrhs", [C, C], BF16); add_dbg("xffn", [T, C])
    add_dbg("qs", [128, NC * T], BF16); add_dbg("ks", [128, NC * T], BF16)
    add_dbg("v0", [128, NT * C], BF16)
    add_dbg("negc", [128, NT * H]); add_dbg("ao", [NC, 128, T], BF16)
    add_dbg("zall", [128, NT * H]); add_dbg("x1", [T, C]); add_dbg("wte", [128, NT * E])
    add_dbg("qrT", [128, NC * T], BF16); add_dbg("ekl", [128, 15 * C], BF16)
    add_dbg("x2", [T, C]); add_dbg("xnT", [128, NC * T], BF16)
    add_dbg("kgT", [128, NC * G], BF16); add_dbg("iS8", [128, NC])
    add_dbg("kv2", [2, 128, 128], BF16); add_dbg("hT0", [128, NF * 512], BF16)
    add_dbg("srfranspose", [128, NC])
    add_dbg("src", [128, NC]); add_dbg("s8d", [8, 128]); add_dbg("rhsfd", [128, NC * C])
    add_dbg("ao1", [NC, 128, T], BF16); add_dbg("attn00", [128, 4 * T], BF16)
    add_dbg("aoev0", [128, 512], BF16)
    add_dbg("glog", [128, NT * E]); add_dbg("xnT1", [128, NC * T], BF16); add_dbg("loT1", [128, NC * T], BF16)

    ln_eps_col = cvec[:, 0:1]
    rms_eps_col = cvec[:, 1:2]

    def chT(tens, cc, c0, n, p0=0, np_=128):
        return tens[p0:p0 + np_, cc * T + c0: cc * T + c0 + n]

    def tp(e, out_ap, in_ap, idn, first, last):
        return e.matmul(out_ap, in_ap, idn, start=first, stop=last, is_transpose=True)

    ld0 = []
    for dst, src in [(ident, ident_d), (identb, identb_d), (onesrow, onesrow_d),
                     (onesrowb, onesrowb_d), (onescolb, onescolb_d), (cvec, cvec_d),
                     (selallb, selallb_d), (qnw, qnw_d), (knw, knw_d), (b1c, b1c_d)]:
        ld0.append(B.do("sync", lambda e, d=dst, s=src: e.dma_start(out=d[:], in_=s[:]), tok=True))
    t_consts = B.tok("sync")
    t_selj = B.do("sync", lambda e: e.dma_start(out=selj, in_=selj_d[:]), tok=True)
    t_rhs = B.do("sync", lambda e: e.dma_start(
        out=rhsf.rearrange("p (cc c) -> p cc c", cc=NC),
        in_=rhs_d.rearrange("(cc p) c -> p cc c", p=128)), tok=True)
    t_gw = B.do("sync", lambda e: e.dma_start(
        out=gsml[:].rearrange("p (cc c) -> p cc c", cc=NC),
        in_=gw_d.rearrange("(cc p) c -> p cc c", p=128)), tok=True)
    t_pb = B.do("gpsimd", lambda e: e.dma_start(out=pb_sb[:], in_=pb_d[:]), tok=True)

    if p0cut >= 2:
        B.waits("vector", [t_consts])
        t_r127 = B.do("vector", lambda e: e.reciprocal(r127[:], cvec[:, 2:3]), tok=True)
        B.wait("vector", t_r127)
        B.waits("vector", [t_gw])
        t_ghi = B.do("vector", lambda e: e.tensor_copy(ghi[:], gsml[:]), tok=True)
        B.wait("vector", t_ghi)
        t_glo = B.do("vector", lambda e: e.tensor_tensor(glo[:], gsml[:], ghi[:], ALU.subtract), tok=True)

    B.waits("tensor", [t_rhs, t_consts])
    for jc in range(NC if p0cut >= 3 else 0):
        for cc in range(NC):
            bank, off = PBK[cc // 4], (cc % 4) * 128
            B.do("tensor", lambda e, b=bank, o=off, c=cc, j=jc: tp(e,
                b[:, o:o + 128], rhsf[:, c * C + j * 128:c * C + (j + 1) * 128], ident[:],
                c % 4 == 0, c % 4 == 3))
        tpe = B.mark("tensor")
        B.waits("vector", [tpe])
        B.do("vector", lambda e: e.tensor_copy(p0rr[:, 0:512], PBK[0][:]))
        tev = B.do("vector", lambda e: e.tensor_copy(p0rt[:, 0:512], PBK[1][:]), tok=True)
        B.waits("tensor", [tev])
        B.do("vector", lambda e: e.reduce_max(colsA[:, 0:1], p0rr[:, 0:512], AX.X, apply_absolute_value=True))
        B.do("vector", lambda e: e.reduce_max(colsA[:, 1:2], p0rt[:, 0:512], AX.X, apply_absolute_value=True))
        B.do("vector", lambda e, j=jc: e.tensor_tensor(
            sr_cols[:, j:j + 1], colsA[:, 0:1], colsA[:, 1:2], ALU.max))
    if p0cut >= 4:
        tsr = B.do("vector", lambda e: e.tensor_scalar(
            sr_cols[:], sr_cols[:], 1e-12, r127[:], ALU.max, ALU.mult), tok=True)
        B.waits("tensor", [tsr])
        tpe = B.do("tensor", lambda e: e.transpose(PBK[2][0:8, 0:128], sr_cols[:], ident[:]), tok=True)
        B.waits("vector", [tpe])
        tev = B.do("vector", lambda e: e.tensor_copy(s8[:], PBK[2][0:8, 0:128]), tok=True)
        B.waits("tensor", [tev, t_selj])
        for jc in range(NC):
            bank = PBK[3 + (jc % 2)]
            tpe = B.do("tensor", lambda e, b=bank, j=jc: e.matmul(
                b[:, 0:128], selj[0:8, j * 128:(j + 1) * 128], s8[:], start=True, stop=True), tok=True)
            B.waits("vector", [tpe])
            tev = B.do("vector", lambda e, b=bank, j=jc: e.tensor_copy(
                sr_b[:, j * 128:(j + 1) * 128], b[:, 0:128]), tok=True)
            B.waits("tensor", [tev])
        t_isr = B.do("vector", lambda e: e.reciprocal(isr_b, sr_b), tok=True)
        B.wait("vector", t_isr)

    def emit_round(dst_ap, R0=None, R1=None, R2=None):
        R0 = R0 if R0 is not None else rr0
        R1 = R1 if R1 is not None else rr
        R2 = R2 if R2 is not None else rt
        B.do("vector", lambda e: e.tensor_scalar(R0, R0, 127.25, -127.25, ALU.min, ALU.max))
        B.do("vector", lambda e: e.tensor_scalar(R0, R0, 128.5, None, ALU.add))
        B.do("vector", lambda e: e.tensor_scalar(R2, R0, 128.0, 128.0, ALU.is_ge, ALU.mult))
        B.do("vector", lambda e: e.tensor_tensor(R1, R0, R2, ALU.subtract))
        for b in (64.0, 32.0, 16.0, 8.0, 4.0, 2.0, 1.0):
            B.do("vector", lambda e, b=b: e.tensor_scalar(R2, R1, b, b, ALU.is_ge, ALU.mult))
            B.do("vector", lambda e: e.tensor_tensor(R1, R1, R2, ALU.subtract))
        B.do("vector", lambda e: e.tensor_tensor(R2, R0, R1, ALU.subtract))
        return B.do("vector", lambda e: e.tensor_scalar(dst_ap, R2, 128.0, None, ALU.subtract), tok=True)

    for cc in range(NC if p0cut >= 5 else 0):
        B.do("vector", lambda e, c=cc: e.tensor_tensor(
            p0rr0, rhsf[:, c * C:(c + 1) * C], isr_b, ALU.mult))
        emit_round(qrhs[:, cc * C:(cc + 1) * C], p0rr0, p0rr, p0rt)
    t_qrhs = B.tok("vector")

    if "srb" in dbg:
        B.waits("sync", [t_isr])
        B.do("sync", lambda e: e.dma_start(out=dbg_out["srb"], in_=sr_b))
    if "src" in dbg:
        B.waits("sync", [t_isr])
        B.do("sync", lambda e: e.dma_start(out=dbg_out["src"], in_=sr_cols[:]))
    if "s8d" in dbg:
        B.waits("sync", [t_isr])
        B.do("sync", lambda e: e.dma_start(out=dbg_out["s8d"], in_=s8[:]))
    if "rhsfd" in dbg:
        B.waits("sync", [t_rhs])
        B.do("sync", lambda e: e.dma_start(out=dbg_out["rhsfd"], in_=rhsf))
    if "qrhs" in dbg:
        B.waits("sync", [t_qrhs])
        B.do("sync", lambda e: e.dma_start(
            out=dbg_out["qrhs"].rearrange("(cc p) c -> p cc c", p=128),
            in_=qrhs[:].rearrange("p (cc c) -> p cc c", cc=NC)))

    def emit_ln(xbuf, xn_out, dep_toks, cb, prev_tsq):
        B.waits("vector", dep_toks + ([prev_tsq] if prev_tsq else []))
        B.do("vector", lambda e: e.bn_stats(stats[:, 0:6], xbuf[:, 0:512]))
        B.do("vector", lambda e: e.bn_stats(stats[:, 6:12], xbuf[:, 512:1024]))
        tagg = B.do("vector", lambda e, cb=cb: e.bn_aggr(colsA[:, cb:cb + 2], stats[:, 0:12]), tok=True)
        B.waits("scalar", [tagg])
        tsq = B.do("scalar", lambda e, cb=cb: e.activation(
            colsA[:, cb + 2:cb + 3], colsA[:, cb + 1:cb + 2], AF.Sqrt,
            bias=ln_eps_col, scale=1.0), tok=True)
        B.waits("vector", [tsq])
        trc = B.do("vector", lambda e, cb=cb: e.reciprocal(
            colsA[:, cb + 3:cb + 4], colsA[:, cb + 2:cb + 3]), tok=True)
        B.wait("vector", trc)
        tn = B.do("vector", lambda e, cb=cb: e.tensor_scalar(
            xn_out, xbuf, colsA[:, cb:cb + 1], colsA[:, cb + 3:cb + 4],
            ALU.subtract, ALU.mult), tok=True)
        return tn, tsq

    if stop >= 1:
        prev_tsq = None
        ev1_tok = {}
        tp0s, tp1s = {}, {}
        t_xn_toks = [None] * NT
        for tc in range(NT):
            bi = tc % 2
            cb = bi * 4
            B.waits("sync", [t_qrhs] + ([t_xn_toks[tc - 2]] if tc >= 2 else []))
            t_xld = B.do("sync", lambda e, t=tc, b=bi: e.dma_start(
                out=xt[b], in_=x_d[t * 128:(t + 1) * 128, :]), tok=True)
            t_xn, prev_tsq = emit_ln(xt[bi], xn_s[bi], [t_xld], cb, prev_tsq)
            t_xn_toks[tc] = t_xn
            B.do("vector", lambda e, b=bi: e.reduce_max(
                colsB[:, 0:1], xn_s[b], AX.X, apply_absolute_value=True))
            tsl = B.do("vector", lambda e, t=tc: e.tensor_scalar(
                sl_all[:, t:t + 1], colsB[:, 0:1], 1e-12, r127[:], ALU.max, ALU.mult), tok=True)
            B.wait("vector", tsl)
            tisl = B.do("vector", lambda e, t=tc: e.reciprocal(
                isl_all[:, t:t + 1], sl_all[:, t:t + 1]), tok=True)
            B.wait("vector", tisl)
            B.waits("vector", [tp1s.get(tc - 2)])
            B.do("vector", lambda e, b=bi, t=tc: e.tensor_scalar(
                rr0, xn_s[b], isl_all[:, t:t + 1], None, ALU.mult))
            tq = emit_round(tb4[2 + bi][:])
            B.waits("scalar", [t_xn, tp0s.get(tc - 2)])
            txb = B.do("scalar", lambda e, b=bi: e.copy(tb4[b][:], xn_s[b]), tok=True)
            B.waits("tensor", [txb, tq, t_consts, ev1_tok.get(tc - 1)])
            for cc in range(NC):
                B.do("tensor", lambda e, c=cc, b=bi: tp(e,
                    PTB[0][:, c * 128:(c + 1) * 128], tb4[b][:, c * 128:(c + 1) * 128], identb[:],
                    c == 0, c == NC - 1))
            tp0s[tc] = B.mark("tensor")
            for cc in range(NC):
                B.do("tensor", lambda e, c=cc, b=bi: tp(e,
                    PTB[1][:, c * 128:(c + 1) * 128], tb4[2 + b][:, c * 128:(c + 1) * 128], identb[:],
                    c == 0, c == NC - 1))
            tp1s[tc] = B.mark("tensor")
            B.waits("vector", [tp0s[tc], tp1s[tc]])
            B.do("vector", lambda e, t=tc: e.tensor_copy(
                xnT.rearrange("p (cc tt) -> p cc tt", cc=NC)[:, :, t * 128:(t + 1) * 128],
                PTB[0][:].rearrange("p (cc tt) -> p cc tt", cc=NC)))
            ev1_tok[tc] = B.do("vector", lambda e, t=tc: e.tensor_copy(
                qxT.rearrange("p (cc tt) -> p cc tt", cc=NC)[:, :, t * 128:(t + 1) * 128],
                PTB[1][:].rearrange("p (cc tt) -> p cc tt", cc=NC)), tok=True)
            if "xn" in dbg:
                B.waits("sync", [t_xn])
                B.do("sync", lambda e, t=tc, b=bi: e.dma_start(
                    out=dbg_out["xn"][t * 128:(t + 1) * 128, :], in_=xn_s[b]))
            if "qx" in dbg:
                B.waits("sync", [tq])
                B.do("sync", lambda e, t=tc, b=bi: e.dma_start(
                    out=dbg_out["qx"][t * 128:(t + 1) * 128, :], in_=tb4[2 + b][:]))
        if "sl" in dbg:
            B.waits("sync", [B.tok("vector")])
            B.do("sync", lambda e: e.dma_start(out=dbg_out["sl"], in_=sl_all[:]))
        if "xnT" in dbg:
            B.waits("sync", [B.tok("vector")])
            B.do("sync", lambda e: e.dma_start(out=dbg_out["xnT"], in_=xnT))

    if stop >= 2:
        prev_ffn = {}
        for tc in range(NT):
            bb = (tc % 2) * 2
            bi = tc % 2
            deps = [t_qrhs, ev1_tok[tc]]
            if tc >= 2:
                deps.append(prev_ffn[tc - 2][1])
            B.waits("tensor", deps)
            for cc in range(NC):
                for jb in range(2):
                    B.do("tensor", lambda e, c=cc, j=jb, b=bb, t=tc: e.matmul(
                        PBK[b + j][:], chT(qxT, c, t * 128, 128),
                        qrhs[:, c * C + j * 512:c * C + (j + 1) * 512],
                        start=(c == 0), stop=(c == NC - 1)))
            tmm = B.mark("tensor")
            B.waits("sync", [t_xn_toks[14 + bi]] + ([prev_ffn[tc - 2][0]] if tc >= 2 else []))
            t_xld = B.do("sync", lambda e, t=tc, b=bi: e.dma_start(
                out=xt[b], in_=x_d[t * 128:(t + 1) * 128, :]), tok=True)
            B.waits("vector", [tmm, t_xld])
            for jb in range(2):
                B.do("vector", lambda e, j=jb, b=bb, t=tc: e.scalar_tensor_tensor(
                    rt[:, j * 512:(j + 1) * 512], PBK[b + j][:], sl_all[:, t:t + 1],
                    sr_b[:, j * 512:(j + 1) * 512], ALU.mult, ALU.mult))
            tev = B.do("vector", lambda e, b=bi: e.tensor_tensor(
                xt[b], xt[b], rt, ALU.add), tok=True)
            B.waits("sync", [tev])
            tst = B.do("sync", lambda e, t=tc, b=bi: e.dma_start(
                out=xs_d[t * 128:(t + 1) * 128, :], in_=xt[b]), tok=True)
            prev_ffn[tc] = (tst, tev)
            if "xffn" in dbg:
                B.do("sync", lambda e, t=tc, b=bi: e.dma_start(
                    out=dbg_out["xffn"][t * 128:(t + 1) * 128, :], in_=xt[b]))
        t_ffn_done_s = B.tok("sync")
        t_ffn_done_v = B.tok("vector")

    if stop >= 3:
        def rms_block(src_f32, w_rep, dst_bf, extra_scale, dep_tok, sq_buf):
            B.waits("scalar", [dep_tok])
            tsq = B.do("scalar", lambda e: e.activation(sq_buf, src_f32, AF.Square,
                                                        bias=0.0, scale=1.0), tok=True)
            B.waits("vector", [tsq])
            trd = B.do("vector", lambda e: e.tensor_reduce(
                colsB[:, 0:16].rearrange("p (h o) -> p h o", o=1),
                sq_buf.rearrange("p (h d) -> p h d", h=H), AX.X, ALU.add), tok=True)
            B.waits("scalar", [trd])
            ts1 = B.do("scalar", lambda e: e.activation(
                colsA[:, 0:8], colsB[:, 0:8], AF.Sqrt, bias=rms_eps_col, scale=1.0 / DH), tok=True)
            ts2 = B.do("scalar", lambda e: e.activation(
                stats[:, 0:8], colsB[:, 8:16], AF.Sqrt, bias=rms_eps_col, scale=1.0 / DH), tok=True)
            B.waits("vector", [ts1, ts2])
            B.do("vector", lambda e: e.reciprocal(stats[:, 8:16], colsA[:, 0:8]))
            trc = B.do("vector", lambda e: e.reciprocal(colsB[:, 0:8], stats[:, 0:8]), tok=True)
            B.wait("vector", trc)
            if extra_scale != 1.0:
                B.do("vector", lambda e: e.tensor_scalar(
                    stats[:, 8:16], stats[:, 8:16], extra_scale, None, ALU.mult))
                tsc = B.do("vector", lambda e: e.tensor_scalar(
                    colsB[:, 0:8], colsB[:, 0:8], extra_scale, None, ALU.mult), tok=True)
                B.wait("vector", tsc)
            B.do("vector", lambda e: e.tensor_tensor(
                sq_buf.rearrange("p (h d) -> p h d", h=H), src_f32.rearrange("p (h d) -> p h d", h=H),
                w_rep[:].unsqueeze(1).broadcast_to([128, H, DH]),
                ALU.mult))
            B.do("vector", lambda e: e.tensor_tensor(
                dst_bf.rearrange("p (h d) -> p h d", h=H)[:, 0:8],
                sq_buf.rearrange("p (h d) -> p h d", h=H)[:, 0:8],
                stats[:, 8:16].unsqueeze(2).broadcast_to([128, 8, DH]),
                ALU.mult))
            return B.do("vector", lambda e: e.tensor_tensor(
                dst_bf.rearrange("p (h d) -> p h d", h=H)[:, 8:16],
                sq_buf.rearrange("p (h d) -> p h d", h=H)[:, 8:16],
                colsB[:, 0:8].unsqueeze(2).broadcast_to([128, 8, DH]),
                ALU.mult), tok=True)

        def ssq_block(src_bf, dst_cols, sq_buf, dep_tok):
            B.waits("scalar", [dep_tok])
            tsq = B.do("scalar", lambda e: e.activation(sq_buf, src_bf, AF.Square,
                                                        bias=0.0, scale=1.0), tok=True)
            B.waits("vector", [tsq])
            return B.do("vector", lambda e: e.tensor_reduce(
                dst_cols.rearrange("p (h o) -> p h o", o=1),
                sq_buf.rearrange("p (h d) -> p h d", h=H), AX.X, ALU.add), tok=True)

        B.waits("gpsimd", [t_ffn_done_v, t_ffn_done_s])
        t_wqk = B.do("gpsimd", lambda e: e.dma_start(
            out=wqkb[:].rearrange("p (cc c) -> p cc c", cc=NC),
            in_=qkvwT_d[:, 0:2 * C].rearrange("(cc p) c -> p cc c", p=128)), tok=True)

        prev_qkv = {}
        tqk_ev = {}
        for tc in range(NT):
            bi = tc % 2
            deps = [ev1_tok[NT - 1], t_ffn_done_v, t_wqk]
            if tc >= 2:
                deps.append(prev_qkv[tc - 2])
            B.waits("tensor", deps)
            for cc in range(NC):
                for part in range(2):
                    for jb in range(2):
                        B.do("tensor", lambda e, c=cc, p=part, j=jb, t=tc: e.matmul(
                            PBK[p * 2 + j][:], chT(xnT, c, t * 128, 128),
                            wqkb[:, c * 2 * C + p * C + j * 512:c * 2 * C + p * C + (j + 1) * 512],
                            start=(c == 0), stop=(c == NC - 1)))
            tmm = B.mark("tensor")
            B.waits("vector", [tmm])
            B.do("vector", lambda e: e.tensor_copy(p3qf[:, 0:512], PBK[0][:]))
            B.do("vector", lambda e: e.tensor_copy(p3qf[:, 512:1024], PBK[1][:]))
            B.do("vector", lambda e: e.tensor_copy(p3kf[:, 0:512], PBK[2][:]))
            tevv = B.do("vector", lambda e: e.tensor_copy(p3kf[:, 512:1024], PBK[3][:]), tok=True)
            prev_qkv[tc] = tevv

            tqs = rms_block(p3qf, qnw, tb4[2 + bi][:], 0.125, tevv, p3sq)
            tssq = ssq_block(tb4[2 + bi][:], ssq_q[:, tc * H:(tc + 1) * H], p3sq, tqs)
            tks = rms_block(p3kf, knw, tb4[bi][:], 1.0, tssq, p3sq)
            tssk = ssq_block(tb4[bi][:], ssq_k[:, tc * H:(tc + 1) * H], p3sq, tks)

            B.waits("tensor", [tqs, tks, tqk_ev.get(tc - 1)])
            for cc in range(NC):
                B.do("tensor", lambda e, c=cc, b=bi: tp(e,
                    PTB[0][:, c * 128:(c + 1) * 128], tb4[2 + b][:, c * 128:(c + 1) * 128], identb[:],
                    c == 0, c == NC - 1))
            tq0 = B.mark("tensor")
            for cc in range(NC):
                B.do("tensor", lambda e, c=cc, b=bi: tp(e,
                    PTB[1][:, c * 128:(c + 1) * 128], tb4[b][:, c * 128:(c + 1) * 128], identb[:],
                    c == 0, c == NC - 1))
            tq1 = B.mark("tensor")
            B.waits("vector", [tq0, tq1, tssk])
            B.do("vector", lambda e, t=tc: e.tensor_copy(
                qsT.rearrange("p (cc tt) -> p cc tt", cc=NC)[:, :, t * 128:(t + 1) * 128],
                PTB[0][:].rearrange("p (cc tt) -> p cc tt", cc=NC)))
            tqk_ev[tc] = B.do("vector", lambda e, t=tc: e.tensor_copy(
                ksT.rearrange("p (cc tt) -> p cc tt", cc=NC)[:, :, t * 128:(t + 1) * 128],
                PTB[1][:].rearrange("p (cc tt) -> p cc tt", cc=NC)), tok=True)
        t_p3a_v = B.tok("vector")

        t_vw = [None] * 8
        t_vw_used = {}
        B.waits("gpsimd", [t_ffn_done_v])
        for v0 in (0, 1):
            t_vw[v0] = B.do("gpsimd", lambda e, v=v0, w=v0: e.dma_start(
                out=vwstr[w][:], in_=vwr_d[v]), tok=True)
        prev_vev = {}
        for vd in range(8):
            wb = vd % 2
            for tb in range(4):
                key = (vd, tb)
                B.waits("tensor", [t_vw[vd], t_p3a_v if vd == 0 and tb == 0 else None,
                                   prev_vev.get((vd, tb - 2)) or prev_vev.get((vd - 1, 2 + tb))])
                bank = PBK[(vd * 4 + tb) % 2]
                for cc in range(NC):
                    B.do("tensor", lambda e, c=cc, v=vd, t=tb, w=wb, bk=bank: e.matmul(
                        bk[:], vwstr[w][:, c * 128:(c + 1) * 128],
                        chT(xnT, c, t * 512, 512),
                        start=(c == 0), stop=(c == NC - 1)))
                tmm = B.mark("tensor")
                if tb == 3:
                    t_vw_used[vd] = tmm
                B.waits("vector", [tmm])
                tev = B.do("vector", lambda e, bk=bank: e.tensor_copy(aoev[:], bk[:]), tok=True)
                B.waits("tensor", [tev])
                for q4 in range(4):
                    B.do("tensor", lambda e, q=q4: tp(e,
                        PTB[0][:, q * 128:(q + 1) * 128], aoev[:, q * 128:(q + 1) * 128],
                        identb[:], q == 0, q == 3))
                tpe = B.mark("tensor")
                B.waits("vector", [tpe])
                tev2 = B.do("vector", lambda e, v=vd, t=tb: e.tensor_copy(
                    v_sb[:].rearrange("p (tt c) -> p tt c", tt=NT)[:, t * 4:(t + 1) * 4,
                                                                  v * 128:(v + 1) * 128],
                    PTB[0][:, 0:512].rearrange("p (q c) -> p q c", q=4)), tok=True)
                B.waits("tensor", [tev2])
                prev_vev[key] = tev2
            if vd + 2 < 8:
                B.waits("gpsimd", [t_vw_used[vd]])
                t_vw[vd + 2] = B.do("gpsimd", lambda e, v=vd + 2, w=(vd % 2): e.dma_start(
                    out=vwstr[w][:], in_=vwr_d[v]), tok=True)
        t_p3_done = B.tok("vector")

        B.do("vector", lambda e: e.tensor_copy(stats[:, 0:16], ssq_k[:, 0:16]))
        tmax = None
        for tc in range(1, NT):
            tmax = B.do("vector", lambda e, t=tc: e.tensor_tensor(
                stats[:, 0:16], stats[:, 0:16], ssq_k[:, t * H:(t + 1) * H], ALU.max), tok=True)
        B.waits("tensor", [tmax])
        tpe = B.do("tensor", lambda e: e.transpose(PBK[0][0:16, 0:128], stats[:, 0:16], ident[:]), tok=True)
        B.waits("vector", [tpe])
        tev = B.do("vector", lambda e: e.tensor_copy(rt[0:16, 0:128], PBK[0][0:16, 0:128]), tok=True)
        B.wait("vector", tev)
        tred = B.do("vector", lambda e: e.reduce_max(rt[0:16, 128:129], rt[0:16, 0:128], AX.X), tok=True)
        B.waits("tensor", [tred])
        tpe = B.do("tensor", lambda e: e.transpose(
            PBK[1][0:1, 0:16], rt[0:16, 128:129], ident[0:16, 0:16]), tok=True)
        B.waits("vector", [tpe])
        tev = B.do("vector", lambda e: e.tensor_copy(rt[0:1, 256:272], PBK[1][0:1, 0:16]), tok=True)
        B.waits("tensor", [tev])
        tpe = B.do("tensor", lambda e: e.matmul(
            PBK[2][:, 0:16], onesrow[:], rt[0:1, 256:272], start=True, stop=True), tok=True)
        B.waits("vector", [tpe])
        tev = B.do("vector", lambda e: e.tensor_copy(m2b[:], PBK[2][:, 0:16]), tok=True)
        B.wait("vector", tev)
        tm2 = B.do("vector", lambda e: e.tensor_scalar(m2b[:], m2b[:], -0.0625, None, ALU.mult), tok=True)
        B.wait("vector", tm2)
        for tc in range(NT):
            B.do("vector", lambda e, t=tc: e.scalar_tensor_tensor(
                negc[:, t * H:(t + 1) * H], ssq_q[:, t * H:(t + 1) * H], -4.0, m2b[:],
                ALU.mult, ALU.add))
        t_negc = B.mark("vector")

        if "qs" in dbg:
            B.waits("sync", [B.tok("vector")])
            B.do("sync", lambda e: e.dma_start(out=dbg_out["qs"], in_=qsT))
        if "ks" in dbg:
            B.waits("sync", [B.tok("vector")])
            B.do("sync", lambda e: e.dma_start(out=dbg_out["ks"], in_=ksT))
        if "v0" in dbg:
            B.waits("sync", [B.tok("vector")])
            B.do("sync", lambda e: e.dma_start(out=dbg_out["v0"], in_=v_sb))
        if "negc" in dbg:
            B.waits("sync", [t_negc])
            B.do("sync", lambda e: e.dma_start(out=dbg_out["negc"], in_=negc[:]))

    if stop >= 4:
        t_aodma = {}
        texp_half = {}
        tz_done = {}
        t_trdone = {}
        last_av_ev = None
        for h in range(H):
            hc, off = h // 2, (h % 2) * 64
            avb = 4 + (h % 2)
            for ib in range(4):
                key = (h, ib)
                for ii in range(4):
                    ic = ib * 4 + ii
                    for jh in range(2):
                        g = jh * 2
                        B.waits("tensor", [t_negc, texp_half.get((h, ib, ii - 1, jh))
                                           if ii > 0 else texp_half.get((h, ib - 1, 3, jh))
                                           if ib > 0 else texp_half.get((h - 1, 3, 3, jh))])
                        if ii == 0 and jh == 0:
                            B.waits("scalar", [t_trdone.get((h, ib - 1)) if ib > 0
                                               else t_trdone.get((h - 1, 3))])
                        for jb in range(2):
                            B.do("tensor", lambda e, j=g + jb, i=ic, hc2=hc, o=off: e.matmul(
                                PBK[j][:],
                                chT(qsT, hc2, i * 128, 128, o, 64),
                                chT(ksT, hc2, j * 512, 512, o, 64),
                                start=True, stop=True))
                        tsm = B.mark("tensor")
                        B.waits("scalar", [tsm, tz_done.get((h, ic - 2))])
                        for jb in range(2):
                            B.do("scalar", lambda e, j=g + jb, i=ii, icc=ic, hh=h: e.activation(
                                attn_s[:, i * T + j * 512:i * T + (j + 1) * 512], PBK[j][:],
                                AF.Exp, bias=negc[:, icc * H + hh:icc * H + hh + 1], scale=1.0,
                                accum_out=zpart[:, (icc % 2) * 4 + j:(icc % 2) * 4 + j + 1]))
                        texp_half[(h, ib, ii, jh)] = B.mark("scalar")
                    zb_ = (ic % 2) * 4
                    B.waits("vector", [texp_half[(h, ib, ii, 0)], texp_half[(h, ib, ii, 1)]])
                    B.do("vector", lambda e, z=zb_: e.tensor_tensor(
                        colsA[:, 0:1], zpart[:, z:z + 1], zpart[:, z + 1:z + 2], ALU.add))
                    B.do("vector", lambda e, z=zb_: e.tensor_tensor(
                        colsA[:, 1:2], zpart[:, z + 2:z + 3], zpart[:, z + 3:z + 4], ALU.add))
                    tz_done[(h, ic)] = B.do("vector", lambda e, icc=ic, hh=h: e.tensor_tensor(
                        zall[:, icc * H + hh:icc * H + hh + 1], colsA[:, 0:1], colsA[:, 1:2],
                        ALU.add), tok=True)
                texp_all = texp_half[(h, ib, 3, 1)]
                if h == 0 and ib == 0 and "attn00" in dbg:
                    B.waits("sync", [texp_all, tz_done[(0, 3)]])
                    tdmp = B.do("sync", lambda e: e.dma_start(out=dbg_out["attn00"], in_=attn_s), tok=True)
                    B.waits("tensor", [tdmp])
                    B.waits("scalar", [tdmp])
                    B.waits("vector", [tdmp])
                t_tr_ev = {}
                t_avmm = {}
                for jc in range(16):
                    slot = jc % 8
                    B.waits("tensor", [texp_all, t_tr_ev.get(jc - 8)])
                    for ii in range(4):
                        B.do("tensor", lambda e, i=ii, j=jc: tp(e,
                            PTB[j % 2][:, i * 128:(i + 1) * 128],
                            attn_s[:, i * T + j * 128:i * T + (j + 1) * 128], identb[:],
                            i == 0, i == 3))
                    tpe = B.mark("tensor")
                    B.waits("vector", [tpe, t_avmm.get(jc - 8)])
                    tev = B.do("vector", lambda e, j=jc, s=slot: e.tensor_copy(
                        attnT_s[:, s * 512:(s + 1) * 512], PTB[j % 2][:, 0:512]), tok=True)
                    t_tr_ev[jc] = tev
                    B.waits("tensor", [tev] + ([last_av_ev] if jc == 0 else []))
                    t_avmm[jc] = B.do("tensor", lambda e, j=jc, s=slot, hh=h, o=off, b=avb: e.matmul(
                        PBK[b][o:o + 64, :], v_sb[:, j * C + hh * DH:j * C + (hh + 1) * DH],
                        attnT_s[:, s * 512:(s + 1) * 512],
                        start=(j == 0), stop=(j == 15),
                        **({"tile_position": (0, 64)} if o else {})), tok=True)
                tavd = B.mark("tensor")
                t_trdone[key] = tavd
                B.waits("vector", [tavd, t_aodma.get((h, ib - 1)) if ib > 0 else t_aodma.get((h - 1, 3))])
                tev = B.do("vector", lambda e, o=off, b=avb: e.tensor_copy(
                    aoev[o:o + 64, :], PBK[b][o:o + 64, :]), tok=True)
                last_av_ev = tev
                B.waits("sync", [tev])
                t_aodma[key] = B.do("sync", lambda e, hcc=hc, o=off, i=ib: e.dma_start(
                    out=ao_d[hcc, o:o + 64, i * 512:(i + 1) * 512], in_=aoev[o:o + 64, :]), tok=True)

        tiz1 = B.do("vector", lambda e: e.reciprocal(zall[:, 0:128], zall[:, 0:128]), tok=True)
        tiz = B.do("vector", lambda e: e.reciprocal(zall[:, 128:256], zall[:, 128:256]), tok=True)
        B.wait("vector", tiz)
        t_izld = {}
        for ic in range(NT):
            B.waits("tensor", [tiz] + ([t_izld.get(ic - 1)] if ic else []))
            tpe = B.do("tensor", lambda e, i=ic: e.transpose(
                PBK[0][0:16, 0:128], zall[:, i * 16:(i + 1) * 16], ident[:]), tok=True)
            B.waits("vector", [tpe])
            t_izld[ic] = B.do("vector", lambda e, i=ic: e.tensor_copy(
                izTb[:, i * 128:(i + 1) * 128], PBK[0][0:16, 0:128]), tok=True)
        t_izT = B.tok("vector")

        if "zall" in dbg:
            B.waits("sync", [tiz])
            B.do("sync", lambda e: e.dma_start(out=dbg_out["zall"], in_=zall[:]))
        if "ao" in dbg:
            B.waits("sync", [t_aodma[(H - 1, 3)]])
            B.do("sync", lambda e: e.dma_start(out=dbg_out["ao"], in_=ao_d))

    if stop >= 5:
        B.waits("gpsimd", [B.tok("tensor")])
        t_projw = B.do("gpsimd", lambda e: e.dma_start(
            out=projw[:].rearrange("p (cc c) -> p cc c", cc=NC),
            in_=projT_d.rearrange("(cc p) c -> p cc c", p=128)), tok=True)
        prev_p5 = {}
        t_hi_ev = {}
        t_lo_used = {}
        t_lnq5 = {}
        t_izb_used = None
        t_xn1_toks = [None] * NT
        for tc in range(NT):
            bi = tc % 2
            cb = bi * 4
            B.waits("sync", [t_aodma[(H - 1, 3)]] + ([prev_p5[tc - 2][0]] if tc >= 2 else []))
            tld = B.do("sync", lambda e, t=tc, b=bi: e.dma_start(
                out=aold[b][:].rearrange("p (hc t) -> p hc t", hc=NC),
                in_=ao_d[:, :, t * 128:(t + 1) * 128].rearrange("hc p t -> p hc t")), tok=True)
            B.waits("tensor", [t_izT, tld] + ([t_izb_used] if t_izb_used else []))
            for hcc in range(NC):
                B.do("tensor", lambda e, hc2=hcc, t=tc: e.matmul(
                    PBK[5][:, 0:128], selallb[:, hc2 * 128:(hc2 + 1) * 128],
                    izTb[:, t * 128:(t + 1) * 128], start=True, stop=True))
                tpe = B.mark("tensor")
                B.waits("vector", [tpe])
                tev = B.do("vector", lambda e, hc2=hcc: e.tensor_copy(
                    izbb[:, hc2 * 128:(hc2 + 1) * 128], PBK[5][:, 0:128]), tok=True)
                B.waits("tensor", [tev])
            tsc = B.do("vector", lambda e, b=bi: e.tensor_tensor(
                aold[b][:], aold[b][:], izbb[:, 0:1024], ALU.mult), tok=True)
            t_izb_used = tsc
            B.waits("tensor", [tsc, t_projw, t_pb]
                    + ([prev_p5[tc - 2][1]] if tc >= 2 else []))
            for jb in range(2):
                B.do("tensor", lambda e, j=jb: e.matmul(
                    PBK[j][:], onesrowb[:], pb_sb[0:1, j * 512:(j + 1) * 512],
                    start=True, stop=False))
            for hcc in range(NC):
                for jb in range(2):
                    B.do("tensor", lambda e, hc2=hcc, j=jb, b=bi: e.matmul(
                        PBK[j][:], aold[b][:, hc2 * 128:(hc2 + 1) * 128],
                        projw[:, hc2 * C + j * 512:hc2 * C + (j + 1) * 512],
                        start=False, stop=(hc2 == NC - 1)))
            tmm = B.mark("tensor")
            B.waits("sync", [t_ffn_done_s] + ([prev_p5[tc - 2][2]] if tc >= 2 else []))
            txld = B.do("sync", lambda e, t=tc, b=bi: e.dma_start(
                out=xt[b], in_=xs_d[t * 128:(t + 1) * 128, :]), tok=True)
            B.waits("vector", [tmm, txld])
            B.do("vector", lambda e, b=bi: e.tensor_tensor(
                xt[b][:, 0:512], xt[b][:, 0:512], PBK[0][:], ALU.add))
            tx1 = B.do("vector", lambda e, b=bi: e.tensor_tensor(
                xt[b][:, 512:1024], xt[b][:, 512:1024], PBK[1][:], ALU.add), tok=True)
            B.waits("sync", [tx1])
            tst = B.do("sync", lambda e, t=tc, b=bi: e.dma_start(
                out=xs_d[t * 128:(t + 1) * 128, :], in_=xt[b]), tok=True)
            if "x1" in dbg:
                B.do("sync", lambda e, t=tc, b=bi: e.dma_start(
                    out=dbg_out["x1"][t * 128:(t + 1) * 128, :], in_=xt[b]))
            t_xn1, _p = emit_ln(xt[bi], xn_s[bi], [tx1], cb, t_lnq5.get(tc - 2))
            t_lnq5[tc] = _p
            t_xn1_toks[tc] = t_xn1
            B.waits("scalar", [t_xn1, t_hi_ev.get(tc - 2)])
            thi = B.do("scalar", lambda e, b=bi: e.copy(tb4[b][:], xn_s[b]), tok=True)
            B.waits("vector", [thi])
            tlo = B.do("vector", lambda e, b=bi: e.tensor_tensor(
                tb4[2 + b][:], xn_s[b], tb4[b][:], ALU.subtract), tok=True)
            B.waits("tensor", [thi, tlo, t_hi_ev.get(tc - 1)])
            for cc in range(NC):
                B.do("tensor", lambda e, c=cc, b=bi: tp(e,
                    PTB[0][:, c * 128:(c + 1) * 128], tb4[b][:, c * 128:(c + 1) * 128], identb[:],
                    c == 0, c == NC - 1))
            th0 = B.mark("tensor")
            for cc in range(NC):
                B.do("tensor", lambda e, c=cc, b=bi: tp(e,
                    PTB[1][:, c * 128:(c + 1) * 128], tb4[2 + b][:, c * 128:(c + 1) * 128], identb[:],
                    c == 0, c == NC - 1))
            th1 = B.mark("tensor")
            B.waits("vector", [th0, th1])
            B.do("vector", lambda e, t=tc: e.tensor_copy(
                xnT.rearrange("p (cc tt) -> p cc tt", cc=NC)[:, :, t * 128:(t + 1) * 128],
                PTB[0][:].rearrange("p (cc tt) -> p cc tt", cc=NC)))
            t_hi_ev[tc] = B.do("vector", lambda e, t=tc: e.tensor_copy(
                loT.rearrange("p (cc tt) -> p cc tt", cc=NC)[:, :, t * 128:(t + 1) * 128],
                PTB[1][:].rearrange("p (cc tt) -> p cc tt", cc=NC)), tok=True)
            prev_p5[tc] = (tld, tmm, tst)
        t_p5_v = B.tok("vector")
        if "xnT1" in dbg:
            B.waits("sync", [t_p5_v])
            B.do("sync", lambda e: e.dma_start(out=dbg_out["xnT1"], in_=xnT))
        if "loT1" in dbg:
            B.waits("sync", [t_p5_v])
            B.do("sync", lambda e: e.dma_start(out=dbg_out["loT1"], in_=loT))
        t_p5_s = B.tok("sync")

    if stop >= 6:
        t_wte = [None] * NT
        prev_g = {}
        for tc in range(NT):
            B.waits("tensor", [t_hi_ev[NT - 1], t_glo] + ([prev_g.get(tc - 2)] if tc >= 2 else []))
            gbk = PBK[2 + tc % 2]
            for cc in range(NC):
                B.do("tensor", lambda e, c=cc, t=tc, bk=gbk: e.matmul(
                    bk[:, 0:8], chT(xnT, c, t * 128, 128), ghi[:, c * E:(c + 1) * E],
                    start=(c == 0), stop=False))
                B.do("tensor", lambda e, c=cc, t=tc, bk=gbk: e.matmul(
                    bk[:, 0:8], chT(xnT, c, t * 128, 128), glo[:, c * E:(c + 1) * E],
                    start=False, stop=False))
            for cc in range(NC):
                B.do("tensor", lambda e, c=cc, t=tc, bk=gbk: e.matmul(
                    bk[:, 0:8], chT(loT, c, t * 128, 128), ghi[:, c * E:(c + 1) * E],
                    start=False, stop=(c == NC - 1)))
            tgm = B.mark("tensor")
            B.waits("vector", [tgm])
            tz0 = B.do("vector", lambda e, bk=gbk: e.tensor_copy(zgbuf[:, 0:8], bk[:, 0:8]), tok=True)
            prev_g[tc] = tz0
            B.wait("vector", tz0)
            if "glog" in dbg:
                B.do("vector", lambda e, t=tc: e.tensor_copy(
                    wte[:, 0:1] if False else zs2[:], zgbuf[:, 0:8]))
                B.waits("sync", [B.tok("vector")])
                B.do("sync", lambda e, t=tc: e.dma_start(
                    out=dbg_out["glog"][:, t * E:(t + 1) * E], in_=zgbuf[:, 0:8]))
            t1 = B.do("vector", lambda e: e.reduce_max(zgbuf[:, 8:9], zgbuf[:, 0:8], AX.X), tok=True)
            B.wait("vector", t1)
            t2 = B.do("vector", lambda e: e.tensor_scalar(
                zgbuf[:, 16:24], zgbuf[:, 0:8], zgbuf[:, 8:9], -1e30, ALU.is_equal, ALU.mult), tok=True)
            B.wait("vector", t2)
            t3 = B.do("vector", lambda e: e.tensor_tensor(
                zgbuf[:, 16:24], zgbuf[:, 16:24], zgbuf[:, 0:8], ALU.add), tok=True)
            B.wait("vector", t3)
            t4 = B.do("vector", lambda e: e.reduce_max(zgbuf[:, 9:10], zgbuf[:, 16:24], AX.X), tok=True)
            B.wait("vector", t4)
            t5 = B.do("vector", lambda e: e.tensor_scalar(
                zgbuf[:, 10:11], zgbuf[:, 8:9], -1.0, None, ALU.mult), tok=True)
            B.waits("scalar", [t5])
            t6 = B.do("scalar", lambda e: e.activation(
                zgbuf[:, 16:24], zgbuf[:, 0:8], AF.Exp, bias=zgbuf[:, 10:11], scale=1.0), tok=True)
            B.waits("vector", [t6])
            t7 = B.do("vector", lambda e: e.tensor_scalar(
                zs1[:], zgbuf[:, 0:8], zgbuf[:, 9:10], None, ALU.is_ge), tok=True)
            B.wait("vector", t7)
            t8 = B.do("vector", lambda e: e.tensor_tensor(
                zs2[:], zgbuf[:, 16:24], zs1[:], ALU.mult), tok=True)
            B.wait("vector", t8)
            t9 = B.do("vector", lambda e: e.tensor_reduce(zgbuf[:, 11:12], zs2[:], AX.X, ALU.add), tok=True)
            B.wait("vector", t9)
            t10 = B.do("vector", lambda e: e.reciprocal(zgbuf[:, 12:13], zgbuf[:, 11:12]), tok=True)
            B.wait("vector", t10)
            t_wte[tc] = B.do("vector", lambda e, t=tc: e.tensor_scalar(
                wte[:, t * E:(t + 1) * E], zs2[:], zgbuf[:, 12:13], None, ALU.mult), tok=True)
        t_p6_v = B.tok("vector")
        if "wte" in dbg:
            B.waits("sync", [t_p6_v])
            B.do("sync", lambda e: e.dma_start(out=dbg_out["wte"], in_=wte[:]))

    if stop >= 7:
        t_cs = B.do("sync", lambda e: e.dma_start(out=cosv, in_=cos_d[:]), tok=True)
        t_sn = B.do("sync", lambda e: e.dma_start(out=sinv, in_=sin_d[:]), tok=True)
        B.waits("sync", [t_p5_v, t_p6_v])

        def rope_tile(tc, src_f32, dst_bf, dep_toks):
            cs = cosv[:, tc * 32:(tc + 1) * 32].unsqueeze(1).broadcast_to([128, H, 32])
            sn = sinv[:, tc * 32:(tc + 1) * 32].unsqueeze(1).broadcast_to([128, H, 32])
            s3 = src_f32.rearrange("p (h d) -> p h d", h=H)
            d3 = dst_bf.rearrange("p (h d) -> p h d", h=H)
            rA = ropeA2.rearrange("p (h d) -> p h d", h=H, d=32)
            rB = ropeB2.rearrange("p (h d) -> p h d", h=H, d=32)
            B.waits("vector", dep_toks)
            B.do("vector", lambda e: e.tensor_tensor(rA, s3[:, :, 0:32], cs, ALU.mult))
            B.do("vector", lambda e: e.tensor_tensor(rB, s3[:, :, 32:64], sn, ALU.mult))
            B.do("vector", lambda e: e.tensor_tensor(d3[:, :, 0:32], rA, rB, ALU.subtract))
            B.do("vector", lambda e: e.tensor_tensor(rA, s3[:, :, 0:32], sn, ALU.mult))
            B.do("vector", lambda e: e.tensor_tensor(rB, s3[:, :, 32:64], cs, ALU.mult))
            return B.do("vector", lambda e: e.tensor_tensor(d3[:, :, 32:64], rA, rB, ALU.add), tok=True)

        B.waits("gpsimd", [t_p5_v, t_p6_v])
        t_lw = B.do("gpsimd", lambda e: e.dma_start(
            out=l1w[:].rearrange("p (cc c) -> p cc c", cc=NC),
            in_=wq_d.rearrange("(cc p) c -> p cc c", p=128)), tok=True)
        prev_q7 = {}
        tq7_ev = {}
        tq0s = {}
        for tc in range(NT):
            bi = tc % 2
            qb0 = 2 * (tc % 2)
            B.waits("tensor", [t_lw] + ([prev_q7.get(tc - 2)]))
            for cc in range(NC):
                for jb in range(2):
                    B.do("tensor", lambda e, c=cc, j=jb, t=tc, b0=qb0: e.matmul(
                        PBK[b0 + j][:], chT(xnT, c, t * 128, 128),
                        l1w[:, c * C + j * 512:c * C + (j + 1) * 512],
                        start=(c == 0), stop=(c == NC - 1)))
            tmm = B.mark("tensor")
            B.waits("vector", [tmm, tq7_ev.get(tc - 2)])
            B.do("vector", lambda e, b0=qb0: e.tensor_copy(l1qf[:, 0:512], PBK[b0][:]))
            tqe = B.do("vector", lambda e, b0=qb0: e.tensor_copy(l1qf[:, 512:1024], PBK[b0 + 1][:]), tok=True)
            prev_q7[tc] = tqe
            trp = rope_tile(tc, l1qf, tb4[bi][:], [tqe, tq0s.get(tc - 2)])
            B.waits("tensor", [trp, tq7_ev.get(tc - 1)])
            for cc in range(NC):
                B.do("tensor", lambda e, c=cc, b=bi: tp(e,
                    PTB[0][:, c * 128:(c + 1) * 128], tb4[b][:, c * 128:(c + 1) * 128], identb[:],
                    c == 0, c == NC - 1))
            tq0 = B.mark("tensor")
            tq0s[tc] = tq0
            B.waits("vector", [tq0])
            tq7_ev[tc] = B.do("vector", lambda e, t=tc: e.tensor_copy(
                qrT.rearrange("p (cc tt) -> p cc tt", cc=NC)[:, :, t * 128:(t + 1) * 128],
                PTB[0][:].rearrange("p (cc tt) -> p cc tt", cc=NC)), tok=True)
        t_qr_done = B.tok("vector")
        if "qrT" in dbg:
            B.waits("sync", [t_qr_done])
            B.do("sync", lambda e: e.dma_start(out=dbg_out["qrT"], in_=qrT))

        B.waits("gpsimd", [B.tok("tensor")])
        t_lwk = B.do("gpsimd", lambda e: e.dma_start(
            out=l1w[:].rearrange("p (cc c) -> p cc c", cc=NC),
            in_=wk_d.rearrange("(cc p) c -> p cc c", p=128)), tok=True)
        tz_ekl = B.do("vector", lambda e: e.memset(ekl[0:76, 0:C], 0.0), tok=True)
        prev_k7 = {}
        t_kg_ev = {}
        t_exk = {}
        t_kgpe = {}
        t_ekl_toks = []
        for tc in range(NT):
            bi = tc % 2
            qb0 = 2 * (tc % 2)
            B.waits("tensor", [t_lwk, prev_q7[NT - 2] if tc == 0 else None,
                               prev_q7[NT - 1] if tc == 1 else None] + ([prev_k7.get(tc - 2)]))
            for cc in range(NC):
                for jb in range(2):
                    B.do("tensor", lambda e, c=cc, j=jb, t=tc, b0=qb0: e.matmul(
                        PBK[b0 + j][:], chT(xnT, c, t * 128, 128),
                        l1w[:, c * C + j * 512:c * C + (j + 1) * 512],
                        start=(c == 0), stop=(c == NC - 1)))
            tmm = B.mark("tensor")
            B.waits("vector", [tmm, prev_k7.get(tc - 2)])
            B.do("vector", lambda e, b0=qb0: e.tensor_copy(l1qf[:, 0:512], PBK[b0][:]))
            tke = B.do("vector", lambda e, b0=qb0: e.tensor_copy(l1qf[:, 512:1024], PBK[b0 + 1][:]), tok=True)
            prev_k7[tc] = tke
            trp = rope_tile(tc, l1qf, tb4[bi][:],
                            [tke, tz_ekl, t_exk.get(tc - 2), t_kgpe.get(tc - 2)])
            if tc >= 1:
                B.waits("scalar", [trp])
                tex = B.do("scalar", lambda e, t=tc, b=bi: e.activation(
                    ekl[:, (t - 1) * C:t * C], tb4[b][:], AF.Exp, bias=0.0, scale=1.0), tok=True)
                t_exk[tc] = tex
                if tc == 1:
                    B.waits("vector", [tex])
                    tex = B.do("vector", lambda e: e.memset(ekl[0:76, 0:C], 0.0), tok=True)
                t_ekl_toks.append(tex)
            if tc <= 1:
                B.waits("tensor", [trp, t_kg_ev.get(tc - 1)])
                nrow = 128 if tc == 0 else 76
                for cc in range(NC):
                    B.do("tensor", lambda e, c=cc, b=bi, n=nrow: tp(e,
                        PTB[0][:, c * 128:c * 128 + n],
                        tb4[b][0:n, c * 128:(c + 1) * 128], identb[0:n, 0:n],
                        c == 0, c == NC - 1))
                tkp = B.mark("tensor")
                t_kgpe[tc] = tkp
                B.waits("vector", [tkp])
                co = 0 if tc == 0 else 128
                nrow2 = 128 if tc == 0 else 76
                t_kg_ev[tc] = B.do("vector", lambda e, c0=co, n=nrow2: e.tensor_copy(
                    kgT[:].rearrange("p (hc g) -> p hc g", hc=NC)[:, :, c0:c0 + n],
                    PTB[0][:].rearrange("p (hc g) -> p hc g", hc=NC)[:, :, 0:n]), tok=True)
        t_ekl_done_s = B.tok("scalar")
        t_ekl_done_v = B.tok("vector")
        t_kg_done = B.tok("vector")
        if "ekl" in dbg:
            B.waits("sync", [t_ekl_done_s, t_kg_done])
            B.do("sync", lambda e: e.dma_start(out=dbg_out["ekl"], in_=ekl[:, 0:15 * C]))
        if "kgT" in dbg:
            B.waits("sync", [t_kg_done])
            B.do("sync", lambda e: e.dma_start(
                out=dbg_out["kgT"].rearrange("p (hc g) -> p hc g", hc=NC),
                in_=kgT[:].rearrange("p (hc g) -> p hc g", hc=NC)[:, :, 0:G]))

        t_vw1 = [None] * 8
        t_vw1_used = {}
        B.waits("gpsimd", [B.tok("tensor")])
        for v0 in (0, 1):
            t_vw1[v0] = B.do("gpsimd", lambda e, v=v0, w=v0: e.dma_start(
                out=vw1str[w][:], in_=vwr1_d[v]), tok=True)
        prev_vev1 = {}
        for vd in range(8):
            wb = vd % 2
            for tb in range(4):
                key = (vd, tb)
                B.waits("tensor", [t_vw1[vd],
                                   prev_k7[NT - 2] if (vd == 0 and tb == 0) else None,
                                   prev_k7[NT - 1] if (vd == 0 and tb == 0) else None,
                                   prev_vev1.get((vd, tb - 2)) or prev_vev1.get((vd - 1, 2 + tb))])
                bank = PBK[(vd * 4 + tb) % 2 + 2]
                for cc in range(NC):
                    B.do("tensor", lambda e, c=cc, t=tb, w=wb, bk=bank: e.matmul(
                        bk[:], vw1str[w][:, c * 128:(c + 1) * 128],
                        chT(xnT, c, t * 512, 512),
                        start=(c == 0), stop=(c == NC - 1)))
                tmm = B.mark("tensor")
                if tb == 3:
                    t_vw1_used[vd] = tmm
                B.waits("vector", [tmm])
                tev = B.do("vector", lambda e, bk=bank: e.tensor_copy(aoev[:], bk[:]), tok=True)
                B.waits("tensor", [tev])
                for q4 in range(4):
                    B.do("tensor", lambda e, q=q4: tp(e,
                        PTB[0][:, q * 128:(q + 1) * 128], aoev[:, q * 128:(q + 1) * 128],
                        identb[:], q == 0, q == 3))
                tpe = B.mark("tensor")
                B.waits("vector", [tpe])
                tev2 = B.do("vector", lambda e, v=vd, t=tb: e.tensor_copy(
                    v_sb[:].rearrange("p (tt c) -> p tt c", tt=NT)[:, t * 4:(t + 1) * 4,
                                                                  v * 128:(v + 1) * 128],
                    PTB[0][:, 0:512].rearrange("p (q c) -> p q c", q=4)), tok=True)
                B.waits("tensor", [tev2])
                prev_vev1[key] = tev2
            if vd + 2 < 8:
                B.waits("gpsimd", [t_vw1_used[vd]])
                t_vw1[vd + 2] = B.do("gpsimd", lambda e, v=vd + 2, w=(vd % 2): e.dma_start(
                    out=vw1str[w][:], in_=vwr1_d[v]), tok=True)
        t_v1_done = B.tok("vector")
        if "v0" in dbg and stop >= 7:
            B.waits("sync", [t_v1_done])
            B.do("sync", lambda e: e.dma_start(out=dbg_out["v0"], in_=v_sb))

    if stop >= 8:
        tS_ev = {}
        for hc in range(NC):
            B.waits("tensor", [t_ekl_done_s, t_ekl_done_v, t_consts] + ([tS_ev.get(hc - 1)]))
            for lt in range(15):
                B.do("tensor", lambda e, l=lt, c=hc: e.matmul(
                    PBK[4][:, 0:1], ekl[:, l * C + c * 128:l * C + (c + 1) * 128],
                    onescolb[:], start=(l == 0), stop=(l == 14)))
            tS = B.mark("tensor")
            B.waits("vector", [tS])
            tS_ev[hc] = B.do("vector", lambda e, c=hc: e.tensor_copy(
                iS8[:, c:c + 1], PBK[4][:, 0:1]), tok=True)
        B.wait("vector", tS_ev[NC - 1])
        t_iS = B.do("vector", lambda e: e.reciprocal(iS8[:], iS8[:]), tok=True)
        B.wait("vector", t_iS)
        if "iS8" in dbg:
            B.waits("sync", [t_iS])
            B.do("sync", lambda e: e.dma_start(out=dbg_out["iS8"], in_=iS8[:]))

        t_lodma = {}
        t_kvev = {}
        t_kvev_s = {}
        t_tlm = {}
        t_tle = {}
        for hc in range(NC):
            kb = kv2[hc % 2]
            B.waits("vector", [t_tlm.get((hc - 2, 3))])
            tzm = B.do("vector", lambda e, k=kb: e.memset(k[:], 0.0), tok=True)
            for sub in range(2):
                h = 2 * hc + sub
                off = sub * 64
                bk = PBK[5 - sub]
                B.waits("tensor", [t_v1_done, tzm, t_kvev_s.get((hc - 1, sub)),
                                   tS_ev[NC - 1] if (hc == 0 and sub == 1) else None])
                for lt in range(15):
                    B.do("tensor", lambda e, l=lt, hh=h, o=off, bb=bk: e.matmul(
                        bb[o:o + 64, o:o + 64],
                        v_sb[:, (l + 1) * C + hh * DH:(l + 1) * C + (hh + 1) * DH],
                        ekl[:, l * C + hh * DH:l * C + (hh + 1) * DH],
                        start=(l == 0), stop=(l == 14),
                        **({"tile_position": (0, 64)} if o else {})))
                tkv = B.mark("tensor")
                B.waits("vector", [tkv])
                tev = B.do("vector", lambda e, k=kb, o=off, bb=bk: e.tensor_copy(
                    k[o:o + 64, o:o + 64], bb[o:o + 64, o:o + 64]), tok=True)
                t_kvev_s[(hc, sub)] = tev
                t_kvev[hc] = tev
            if "kv2" in dbg and hc <= 1:
                B.waits("sync", [t_kvev[hc]])
                B.do("sync", lambda e, c=hc: e.dma_start(out=dbg_out["kv2"][c], in_=kv2[c][:]))
            blocks = [(G, 512), (716, 512), (1228, 512), (1740, 308)]
            for nb, (c0, w) in enumerate(blocks):
                B.waits("tensor", [t_kvev[hc], t_qr_done if hc == 0 and nb == 0 else None,
                                   t_tle.get((hc, nb - 2)) or t_tle.get((hc - 1, nb + 2))])
                tlm = B.do("tensor", lambda e, k=kb, c=hc, c0=c0, w=w, n=nb: e.matmul(
                    PBK[n % 2][:, 0:w], k[:], chT(qrT, c, c0, w),
                    start=True, stop=True), tok=True)
                t_tlm[(hc, nb)] = tlm
                B.waits("vector", [tlm])
                tle = B.do("vector", lambda e, c=hc, w=w, n=nb: e.tensor_scalar(
                    aoev[:, 0:w], PBK[n % 2][:, 0:w], iS8[:, c:c + 1], None, ALU.mult), tok=True)
                t_tle[(hc, nb)] = tle
                if "aoev0" in dbg and hc == 0 and nb == 0:
                    B.waits("sync", [tle])
                    tdmp = B.do("sync", lambda e: e.dma_start(
                        out=dbg_out["aoev0"], in_=aoev[:]), tok=True)
                    B.waits("vector", [tdmp])
                B.waits("sync", [tle])
                t_lodma[(hc, nb)] = B.do("sync", lambda e, c=hc, c0=c0, w=w: e.dma_start(
                    out=ao_d[c, :, c0:c0 + w], in_=aoev[:, 0:w]), tok=True)
                B.waits("vector", [t_lodma[(hc, nb)]])
        t_p8_pe = B.tok("tensor")

    if stop >= 9:
        t_godma = {}
        for h in range(H):
            hc, off = h // 2, (h % 2) * 64
            B.waits("tensor", [t_kg_done, t_qr_done, t_p8_pe if h == 0 else None,
                               t_godma.get(h - 2)])
            B.do("tensor", lambda e, c=hc, o=off: e.matmul(
                PBK[2][:, 0:G], chT(qrT, c, 0, 128, o, 64),
                kgT[o:o + 64, c * 256:c * 256 + G], start=True, stop=True))
            tg1 = B.do("tensor", lambda e, c=hc, o=off: e.matmul(
                PBK[3][0:76, 0:G], chT(qrT, c, 128, 76, o, 64),
                kgT[o:o + 64, c * 256:c * 256 + G], start=True, stop=True), tok=True)
            B.waits("vector", [tg1])
            tm1 = B.do("vector", lambda e: e.reduce_max(zgbuf[:, 0:1], PBK[2][:, 0:G], AX.X), tok=True)
            tm2 = B.do("vector", lambda e: e.reduce_max(zgbuf[0:76, 1:2], PBK[3][0:76, 0:G], AX.X), tok=True)
            B.waits("vector", [tm1, tm2])
            tn1 = B.do("vector", lambda e: e.tensor_scalar(
                zgbuf[:, 2:3], zgbuf[:, 0:1], -0.125, None, ALU.mult), tok=True)
            tn2 = B.do("vector", lambda e: e.tensor_scalar(
                zgbuf[0:76, 3:4], zgbuf[0:76, 1:2], -0.125, None, ALU.mult), tok=True)
            B.waits("scalar", [tn1, tn2])
            te1 = B.do("scalar", lambda e: e.activation(
                aoev[:, 0:G], PBK[2][:, 0:G], AF.Exp, bias=zgbuf[:, 2:3], scale=0.125,
                accum_out=zgbuf[:, 4:5]), tok=True)
            te2 = B.do("scalar", lambda e: e.activation(
                aoev[0:76, 256:256 + G], PBK[3][0:76, 0:G], AF.Exp, bias=zgbuf[0:76, 3:4],
                scale=0.125, accum_out=zgbuf[0:76, 5:6]), tok=True)
            B.waits("vector", [te1, te2])
            tr1 = B.do("vector", lambda e: e.reciprocal(zgbuf[:, 6:7], zgbuf[:, 4:5]), tok=True)
            tr2 = B.do("vector", lambda e: e.reciprocal(zgbuf[0:76, 7:8], zgbuf[0:76, 5:6]), tok=True)
            B.waits("vector", [tr1, tr2])
            ts1 = B.do("vector", lambda e: e.tensor_scalar(
                aoev[:, 0:G], aoev[:, 0:G], zgbuf[:, 6:7], None, ALU.mult), tok=True)
            ts2 = B.do("vector", lambda e: e.tensor_scalar(
                aoev[0:76, 256:256 + G], aoev[0:76, 256:256 + G], zgbuf[0:76, 7:8],
                None, ALU.mult), tok=True)
            B.waits("tensor", [ts1, ts2])
            B.do("tensor", lambda e: tp(e, PTB[0][:, 0:128], aoev[:, 0:128], identb[:], True, False))
            B.do("tensor", lambda e: tp(e, PTB[0][:, 128:204], aoev[0:76, 256:256 + 128],
                                        identb[0:76, 0:76], False, True))
            B.do("tensor", lambda e: tp(e, PTB[1][0:76, 0:128], aoev[:, 128:204], identb[:], True, False))
            tg2 = B.do("tensor", lambda e: tp(e, PTB[1][0:76, 128:204],
                                              aoev[0:76, 256 + 128:256 + 204],
                                              identb[0:76, 0:76], False, True), tok=True)
            B.waits("vector", [tg2])
            tga = B.do("vector", lambda e: e.tensor_copy(agT0[:, 0:G], PTB[0][:, 0:G]), tok=True)
            tgb = B.do("vector", lambda e: e.tensor_copy(agT1[:, 0:G], PTB[1][0:76, 0:G]), tok=True)
            B.waits("tensor", [tga, tgb])
            B.do("tensor", lambda e, hh=h, o=off: e.matmul(
                PBK[4][o:o + 64, 0:G], v_sb[:, 0 * C + hh * DH:0 * C + (hh + 1) * DH],
                agT0[:, 0:G], start=True, stop=False,
                **({"tile_position": (0, 64)} if o else {})))
            tgo = B.do("tensor", lambda e, hh=h, o=off: e.matmul(
                PBK[4][o:o + 64, 0:G], v_sb[0:76, 1 * C + hh * DH:1 * C + (hh + 1) * DH],
                agT1[0:76, 0:G], start=False, stop=True,
                **({"tile_position": (0, 64)} if o else {})), tok=True)
            B.waits("vector", [tgo])
            tge = B.do("vector", lambda e, o=off: e.tensor_copy(
                aoev[o:o + 64, 256:256 + G], PBK[4][o:o + 64, 0:G]), tok=True)
            B.waits("sync", [tge])
            t_godma[h] = B.do("sync", lambda e, c=hc, o=off: e.dma_start(
                out=ao_d[c, o:o + 64, 0:G], in_=aoev[o:o + 64, 256:256 + G]), tok=True)
            B.waits("vector", [t_godma[h]])
            B.waits("scalar", [t_godma[h]])
        t_p9_pe = B.tok("tensor")
        t_p9_sc = B.tok("scalar")
        if "ao1" in dbg:
            B.waits("sync", [t_godma[H - 1], t_lodma[(NC - 1, 3)]])
            B.do("sync", lambda e: e.dma_start(out=dbg_out["ao1"], in_=ao_d))

    if stop >= 10:
        B.waits("gpsimd", [t_p9_pe])
        t_wo = B.do("gpsimd", lambda e: e.dma_start(
            out=woutw[:].rearrange("p (cc c) -> p cc c", cc=NC),
            in_=wo_d.rearrange("(cc p) c -> p cc c", p=128)), tok=True)
        prev_w = {}
        for tc in range(NT):
            bi = tc % 2
            jp = 2 * (tc % 2)
            B.waits("sync", [t_godma[H - 1], t_lodma[(NC - 1, 3)]]
                    + ([prev_w[tc - 2][0]] if tc >= 2 else []))
            tld = B.do("sync", lambda e, t=tc, b=bi: e.dma_start(
                out=aold[b][:].rearrange("p (hc t) -> p hc t", hc=NC),
                in_=ao_d[:, :, t * 128:(t + 1) * 128].rearrange("hc p t -> p hc t")), tok=True)
            B.waits("tensor", [t_wo, tld, t_p9_pe,
                               t_tle[(NC - 1, 3)] if tc == 0 else None,
                               t_p9_sc if tc == 1 else None]
                    + ([prev_w[tc - 2][3]] if tc >= 2 else []))
            for hcc in range(NC):
                for jb in range(2):
                    B.do("tensor", lambda e, hc2=hcc, j=jb, b=bi, p=jp: e.matmul(
                        PBK[p + j][:], aold[b][:, hc2 * 128:(hc2 + 1) * 128],
                        woutw[:, hc2 * C + j * 512:hc2 * C + (j + 1) * 512],
                        start=(hc2 == 0), stop=(hc2 == NC - 1)))
            tmm = B.mark("tensor")
            B.waits("sync", [t_p5_s] + ([prev_w[tc - 1][3]] if tc >= 1 else []))
            txld = B.do("sync", lambda e, t=tc: e.dma_start(
                out=xt10, in_=xs_d[t * 128:(t + 1) * 128, :]), tok=True)
            B.waits("vector", [tmm, txld, t_p8_pe if tc == 0 else None])
            B.do("vector", lambda e, t=tc, p=jp: e.tensor_tensor(
                x_sbf[:, t * C:t * C + 512], xt10[:, 0:512], PBK[p][:], ALU.add))
            tx2 = B.do("vector", lambda e, t=tc, p=jp: e.tensor_tensor(
                x_sbf[:, t * C + 512:(t + 1) * C], xt10[:, 512:1024], PBK[p + 1][:], ALU.add), tok=True)
            tst = tx2
            prev_w[tc] = (tld, tmm, txld, tx2)
            if "x2" in dbg:
                B.waits("sync", [tx2])
                B.do("sync", lambda e, t=tc: e.dma_start(
                    out=dbg_out["x2"][t * 128:(t + 1) * 128, :], in_=x_sbf[:, t * C:(t + 1) * C]))
        t_p10_v = B.tok("vector")

    if stop >= 11:
        t_b2 = {}
        t_w1ld = {}
        t_w2ld = {}
        t_gelu = {}
        t_yev = {}
        t_p1g = {}
        t_ph2mm = {}
        t_bias = {}
        for ei in range(E):
            B.waits("gpsimd", [t_p10_v if ei == 0 else None, t_bias.get((ei - 1, 3))])
            t_b2[ei] = B.do("gpsimd", lambda e, x=ei: e.dma_start(
                out=b2row[:], in_=b2_d[x:x + 1, :]), tok=True)
            for tb in range(4):
                for fc in range(NF):
                    fls = [0, 1, 2, 3] if fc == 0 else ([fc + 3] if fc + 3 < NF else [])
                    for fl in fls:
                        prev_use = (t_p1g.get((ei, tb, fl - 4))
                                    or t_p1g.get((ei, tb - 1, NF - 4 + fl))
                                    or t_p1g.get((ei - 1, 3, NF - 4 + fl)))
                        B.waits("gpsimd", [prev_use,
                                           t_p10_v if (ei == 0 and tb == 0 and fc == 0) else None])
                        t_w1ld[(ei, tb, fl)] = B.do("gpsimd", lambda e, x=ei, f=fl: e.dma_start(
                            out=w1b4[f % 4][:], in_=w1_d[x, f]), sem=f"m1s{fl % 4}")
                    bank = PBK[4 + (fc % 4)]
                    B.waits("tensor", [t_w1ld[(ei, tb, fc)],
                                       t_gelu.get((ei, tb, fc - 4))
                                       or t_gelu.get((ei, tb - 1, NF - 4 + fc))
                                       or t_gelu.get((ei - 1, 3, NF - 4 + fc)),
                                       (t_yev.get((ei, tb - 1)) or t_yev.get((ei - 1, 3)))
                                       if fc < 4 else None])
                    for cc in range(NC):
                        B.do("tensor", lambda e, c=cc, f=fc, t=tb, bk=bank: e.matmul(
                            bk[:], w1b4[f % 4][:, c * 128:(c + 1) * 128],
                            chT(xnT, c, t * 512, 512),
                            start=(c == 0), stop=(c == NC - 1)))
                    t_p1g[(ei, tb, fc)] = B.mark("tensor")
                    B.waits("scalar", [t_p1g[(ei, tb, fc)]])
                    t_gelu[(ei, tb, fc)] = B.do("scalar", lambda e, x=ei, f=fc, bk=bank: e.activation(
                        hT[:, f * 512:(f + 1) * 512], bk[:], AF.Gelu,
                        bias=b1c[:, x * NF + f:x * NF + f + 1], scale=1.0), tok=True)
                B.waits("tensor", [t_b2[ei], t_gelu[(ei, tb, NF - 1)], t_gelu[(ei, tb, NF - 2)],
                                   t_gelu[(ei, tb, NF - 3)], t_gelu[(ei, tb, NF - 4)],
                                   t_yev.get((ei, tb - 1)) or t_yev.get((ei - 1, 3))])
                for tc4 in range(4):
                    for jb in range(2):
                        B.do("tensor", lambda e, t4=tc4, j=jb: e.matmul(
                            PBK[t4 * 2 + j][:], onesrowb[:], b2row[0:1, j * 512:(j + 1) * 512],
                            start=True, stop=False))
                t_bias[(ei, tb)] = B.mark("tensor")
                for fc in range(NF):
                    fls = [0, 1, 2, 3] if fc == 0 else ([fc + 3] if fc + 3 < NF else [])
                    for fl in fls:
                        prev_use = (t_ph2mm.get((ei, tb, fl - 4))
                                    or t_ph2mm.get((ei, tb - 1, NF - 4 + fl))
                                    or t_ph2mm.get((ei - 1, 3, NF - 4 + fl)))
                        B.waits("gpsimd", [prev_use])
                        t_w2ld[(ei, tb, fl)] = B.do("gpsimd", lambda e, x=ei, f=fl: e.dma_start(
                            out=w2b4[f % 4][:], in_=w2_d[x, f * 128:(f + 1) * 128, :]), sem=f"m2s{fl % 4}")
                    B.waits("tensor", [t_w2ld[(ei, tb, fc)]])
                    for tc4 in range(4):
                        for jb in range(2):
                            B.do("tensor", lambda e, f=fc, t4=tc4, j=jb: e.matmul(
                                PBK[t4 * 2 + j][:],
                                hT[:, f * 512 + t4 * 128:f * 512 + (t4 + 1) * 128],
                                w2b4[f % 4][:, j * 512:(j + 1) * 512],
                                start=False, stop=(f == NF - 1)))
                    t_ph2mm[(ei, tb, fc)] = B.mark("tensor")
                B.waits("vector", [t_ph2mm[(ei, tb, NF - 1)], t_p10_v])
                for tc4 in range(4):
                    tg = tb * 4 + tc4
                    for jb in range(2):
                        B.do("vector", lambda e, t=tg, t4=tc4, j=jb, x=ei: e.scalar_tensor_tensor(
                            x_sbf[:, t * C + j * 512:t * C + (j + 1) * 512],
                            PBK[t4 * 2 + j][:], wte[:, t * E + x:t * E + x + 1],
                            x_sbf[:, t * C + j * 512:t * C + (j + 1) * 512],
                            ALU.mult, ALU.add))
                t_yev[(ei, tb)] = B.mark("vector")
        t_moe_v = B.tok("vector")
        if "hT0" in dbg:
            B.waits("sync", [t_gelu[(0, 0, NF - 1)]])
            B.do("sync", lambda e: e.dma_start(out=dbg_out["hT0"], in_=hT))

    if stop >= 12:
        B.waits("sync", [t_moe_v])
        for tc in range(NT):
            B.do("sync", lambda e, t=tc: e.dma_start(
                out=out_d[t * 128:(t + 1) * 128, :], in_=x_sbf[:, t * C:(t + 1) * C]),
                tok=(tc == NT - 1))
        B.wait("sync", B.tok("sync"))

    if stop < 12:
        B.waits("sync", [B.tok("vector"), B.tok("tensor"), B.tok("scalar"), B.tok("gpsimd")])
        if stop >= 2:
            for tc in range(NT):
                B.do("sync", lambda e, t=tc: e.dma_start(
                    out=out_d[t * 128:(t + 1) * 128, :], in_=xs_d[t * 128:(t + 1) * 128, :]))
            B.do("sync", lambda e: e.dma_start(out=out_d[0:1, 0:1], in_=xs_d[0:1, 0:1]), tok=True)
        else:
            tz = B.do("vector", lambda e: e.memset(rr0, 0.0), tok=True)
            B.waits("sync", [tz])
            for tc in range(NT):
                B.do("sync", lambda e, t=tc: e.dma_start(
                    out=out_d[t * 128:(t + 1) * 128, :], in_=rr0), tok=(tc == NT - 1))
        B.wait("sync", B.tok("sync"))

    with nc.Block() as block:
        B.emit(block)
    es.close()
    return nc, D, dbg_out


def _host_inputs(inputs, n_cores=8):
    f32 = np.float32
    bf16 = ml_dtypes.bfloat16
    x = np.ascontiguousarray(inputs["L0_attn_x"])
    qkvwT = np.ascontiguousarray(np.asarray(inputs["L0_attn_qkv_w"]).T)
    projT = np.ascontiguousarray(np.asarray(inputs["L0_attn_proj_w"]).T)
    pb = np.ascontiguousarray(np.asarray(inputs["L0_attn_proj_b"]).reshape(1, C))
    qnw = np.ascontiguousarray(np.broadcast_to(inputs["L0_attn_q_norm_w"], (128, DH)))
    knw = np.ascontiguousarray(np.broadcast_to(inputs["L0_attn_k_norm_w"], (128, DH)))
    rhs = np.ascontiguousarray(inputs["L0_ffn_rhs"])
    wq = np.ascontiguousarray(inputs["L1_attn_Wq"])
    wk = np.ascontiguousarray(inputs["L1_attn_Wk"])
    wv = np.ascontiguousarray(inputs["L1_attn_Wv"])
    wo = np.ascontiguousarray(inputs["L1_attn_Wout"])
    gw = np.ascontiguousarray(inputs["L1_ffn_gate_w"])
    w1 = np.asarray(inputs["L1_ffn_w1"]).reshape(E, NC, 128, NF, 128).transpose(0, 3, 2, 1, 4)
    w1 = np.ascontiguousarray(w1.reshape(E, NF, 128, NC * 128))
    b1 = np.asarray(inputs["L1_ffn_b1"]).reshape(E, NF, 128).transpose(2, 0, 1)
    b1c = np.ascontiguousarray(b1.reshape(128, E * NF))
    w2 = np.ascontiguousarray(inputs["L1_ffn_w2"])
    vw = np.asarray(inputs["L0_attn_qkv_w"]).T[:, 2 * C:3 * C]
    vwr = np.ascontiguousarray(vw.reshape(NC, 128, 8, 128).transpose(2, 1, 0, 3).reshape(8, 128, NC * 128))
    vw1 = np.asarray(inputs["L1_attn_Wv"])
    vwr1 = np.ascontiguousarray(vw1.reshape(NC, 128, 8, 128).transpose(2, 1, 0, 3).reshape(8, 128, NC * 128))
    b2 = np.ascontiguousarray(inputs["L1_ffn_b2"])

    inv = (f32(1.0) / (f32(10000.0) ** (np.arange(0, DH, 2, dtype=f32) / f32(DH)))).astype(f32)
    fr = (np.arange(T, dtype=f32)[:, None] * inv[None, :]).astype(f32)
    cos = np.cos(fr.astype(np.float64)).astype(f32)
    sin = np.sin(fr.astype(np.float64)).astype(f32)
    cos_l = np.ascontiguousarray(cos.reshape(NT, 128, 32).transpose(1, 0, 2).reshape(128, NT * 32))
    sin_l = np.ascontiguousarray(sin.reshape(NT, 128, 32).transpose(1, 0, 2).reshape(128, NT * 32))

    selallb = np.zeros((16, NC * 128), bf16)
    for hc in range(NC):
        for p in range(128):
            selallb[2 * hc + p // 64, hc * 128 + p] = 1.0
    selj = np.zeros((8, NC * 128), f32)
    for jc in range(NC):
        selj[jc, jc * 128:(jc + 1) * 128] = 1.0
    cvec = np.zeros((128, 8), f32)
    cvec[:, 0] = 1e-5
    cvec[:, 1] = 1e-6
    cvec[:, 2] = 127.0

    shared = dict(qkvwT=qkvwT, projT=projT, pb=pb, d_qnw=qnw, d_knw=knw, rhs=rhs, vwr=vwr, vwr1=vwr1,
                  wq=wq, wk=wk, wv=wv, wo=wo, gw=gw, w1=w1, d_b1c=b1c, w2=w2, b2=b2,
                  cos=cos_l, sin=sin_l, d_ident=np.eye(128, dtype=f32),
                  d_identb=np.eye(128, dtype=bf16),
                  d_onesrow=np.ones((1, 128), f32), d_onesrowb=np.ones((1, 128), bf16),
                  d_onescolb=np.ones((128, 1), bf16),
                  d_selallb=selallb, d_selj=selj, d_cvec=cvec)
    in_maps = []
    for i in range(n_cores):
        m = dict(shared)
        m["x"] = np.ascontiguousarray(x[i])
        in_maps.append(m)
    return in_maps


def kernel(**inputs):
    nc, D, dbg_out = build_program()
    in_maps = _host_inputs(inputs)
    res = run_bass_kernel_spmd(nc, in_maps, core_ids=list(range(8)))
    return np.stack([np.asarray(r["out"]) for r in res.results], axis=0)


# version 08
# speedup vs baseline: 1.4290x; 1.0399x over previous
import numpy as np
import ml_dtypes
import concourse.bass as bass
import concourse.mybir as mybir
from concourse.bass_utils import run_bass_kernel_spmd
from contextlib import ExitStack

F32 = mybir.dt.float32
BF16 = mybir.dt.bfloat16
AF = mybir.ActivationFunctionType
ALU = mybir.AluOpType
AX = mybir.AxisListType

T, NT, C, NC = 2048, 16, 1024, 8
H, DH, G, E, FF, NF = 16, 64, 204, 8, 4096, 32
LTOK = T - G
ENGINES = ("sync", "gpsimd", "tensor", "vector", "scalar")
KB = 512


class Build:
    def __init__(self, nc):
        self.nc = nc
        self.q = {e: [] for e in ENGINES}
        self.val = {e: 0 for e in ENGINES}
        self.step = {"sync": 16, "gpsimd": 16, "tensor": 1, "vector": 1, "scalar": 1}
        self.sem = {}

    def wait(self, eng, tok):
        if tok is not None:
            self.q[eng].append(("w", tok[0], tok[1]))

    def waits(self, eng, toks):
        best = {}
        for t in toks:
            if t is None:
                continue
            best[t[0]] = max(best.get(t[0], 0), t[1])
        for s, v in best.items():
            self.q[eng].append(("w", s, v))

    def do(self, eng, fn, tok=False, sem=None):
        if eng in ("vector", "scalar"):
            self.q[eng].append(("w", eng, self.val[eng]))
            self.val[eng] += self.step[eng]
            self.q[eng].append(("i", fn, self.step[eng]))
            return (eng, self.val[eng])
        if tok or eng in ("sync", "gpsimd"):
            if sem is not None:
                self.val[sem] = self.val.get(sem, 0) + 16
                self.q[eng].append(("I", fn, 16, sem))
                return (sem, self.val[sem])
            self.q[eng].append(("w", eng, self.val[eng]))
            self.val[eng] += self.step[eng]
            self.q[eng].append(("i", fn, self.step[eng]))
            return (eng, self.val[eng])
        self.q[eng].append(("e", fn))
        return None

    def tok(self, eng):
        return (eng, self.val[eng])

    def mark(self, eng):
        for i in range(len(self.q[eng]) - 1, -1, -1):
            item = self.q[eng][i]
            if item[0] == "w":
                continue
            if item[0] == "i":
                return (eng, self._val_at(eng, i))
            self.val[eng] += self.step[eng]
            self.q[eng][i] = ("i", item[1], self.step[eng])
            return (eng, self.val[eng])
        raise RuntimeError("mark: no instruction on " + eng)

    def _val_at(self, eng, idx):
        v = 0
        for item in self.q[eng][:idx + 1]:
            if item[0] == "i":
                v += item[2]
        return v

    def emit(self, block):
        sem = self.sem
        for eng in ENGINES:
            queue = self.q[eng]

            def body(e, queue=queue):
                for item in queue:
                    if item[0] == "w":
                        e.wait_ge(sem[item[1]], item[2])
                    elif item[0] == "e":
                        item[1](e)
                    elif item[0] == "I":
                        item[1](e).then_inc(sem[item[3]], item[2])
                    else:
                        item[1](e).then_inc(sem[eng], item[2])
            getattr(block, eng)(body)


def build_program(debug=(), stop=99):
    dbg = set(debug)
    p0cut = next((int(d[5:]) for d in dbg if d.startswith("p0cut")), 99)
    nc = bass.Bass("TRN2", target_bir_lowering=False, detect_race_conditions=False)
    D = {}

    def din(name, shape, dt=F32):
        D[name] = nc.dram_tensor(name, shape, dt, kind="ExternalInput").ap()
        return D[name]

    x_d = din("x", [T, C])
    qkvwT_d = din("qkvwT", [C, 3 * C])
    projT_d = din("projT", [C, C])
    pb_d = din("pb", [1, C])
    qnw_d = din("d_qnw", [128, DH])
    knw_d = din("d_knw", [128, DH])
    rhs_d = din("rhs", [C, C])
    wq_d = din("wq", [C, C])
    wk_d = din("wk", [C, C])
    wv_d = din("wv", [C, C])
    wo_d = din("wo", [C, C])
    gw_d = din("gw", [C, E])
    vwr_d = din("vwr", [8, 128, NC * 128])
    vwr1_d = din("vwr1", [8, 128, NC * 128])
    w1_d = din("w1", [E, NF, 128, NC * 128])
    b1c_d = din("d_b1c", [128, E * NF])
    w2_d = din("w2", [E, FF, C])
    b2_d = din("b2", [E, C])
    cos_d = din("cos", [128, NT * 32])
    sin_d = din("sin", [128, NT * 32])
    ident_d = din("d_ident", [128, 128])
    identb_d = din("d_identb", [128, 128], BF16)
    onesrow_d = din("d_onesrow", [1, 128])
    onesrowb_d = din("d_onesrowb", [1, 128], BF16)
    onescolb_d = din("d_onescolb", [128, 1], BF16)
    selallb_d = din("d_selallb", [16, NC * 128], BF16)
    selz_d = din("d_selz", [128, H * 16], BF16)
    selj_d = din("d_selj", [8, NC * 128])
    cvec_d = din("d_cvec", [128, 8])
    xs_d = nc.dram_tensor("xs", [T, C], F32, kind="Internal").ap()
    ao_d = nc.dram_tensor("ao", [NC, 128, T], BF16, kind="Internal").ap()
    out_d = nc.dram_tensor("out", [T, C], F32, kind="ExternalOutput").ap()
    D["out"] = out_d

    es = ExitStack()

    def sb(name, shape, dt):
        return es.enter_context(nc.sbuf_tensor(name, shape, dt))

    def psum(name, shape, dt=F32):
        return es.enter_context(nc.psum_tensor(name, shape, dt))

    B = Build(nc)

    ident = sb("ident", [128, 128], F32)
    identb = sb("identb", [128, 128], BF16)
    onesrow = sb("onesrow", [1, 128], F32)
    onesrowb = sb("onesrowb", [1, 128], BF16)
    onescolb = sb("onescolb", [128, 1], BF16)
    cvec = sb("cvec", [128, 8], F32)
    r127 = sb("r127", [128, 1], F32)
    selallb = sb("selallb", [16, NC * 128], BF16)
    selz = sb("selz", [128, H * 16], BF16)
    qnw = sb("qnw", [128, DH], F32)
    knw = sb("knw", [128, DH], F32)
    pb_sb = sb("pb_sb", [1, C], BF16)
    b2row = sb("b2row", [1, C], BF16)
    b1c = sb("b1c", [128, E * NF], F32)
    gsml = sb("gsml", [128, NC * E], F32)
    ghi = sb("ghi", [128, NC * E], BF16)
    glo = sb("glo", [128, NC * E], BF16)
    wte = sb("wte", [128, NT * E], F32)
    stats = sb("stats", [128, 16], F32)
    colsA = sb("colsA", [128, 8], F32)
    colsB = sb("colsB", [128, 16], F32)
    sl_all = sb("sl_all", [128, NT], F32)
    isl_all = sb("isl_all", [128, NT], F32)
    ssq_q = sb("ssq_q", [128, NT * H], F32)
    ssq_k = sb("ssq_k", [128, NT * H], F32)
    negc = sb("negc", [128, NT * H], F32)
    zall = sb("zall", [128, NT * H], F32)
    zpart = sb("zpart", [128, 8], F32)
    m2b = sb("m2b", [128, H], F32)
    s8 = sb("s8", [8, 128], F32)
    sr_cols = sb("sr_cols", [128, NC], F32)
    agT0 = sb("agT0", [128, G], BF16)
    agT1 = sb("agT1", [76, G], BF16)
    kv2 = [sb(f"kv2_{i}", [128, 128], BF16) for i in range(2)]
    iS8 = sb("iS8", [128, NC], F32)
    zgbuf = sb("zgbuf", [128, 24], F32)
    zs1 = sb("zs1", [128, 8], F32)
    zs2 = sb("zs2", [128, 8], F32)
    aoev = sb("aoev", [128, 512], BF16)
    tb4 = [sb(f"tb4_{i}", [128, C], BF16) for i in range(4)]

    AR = sb("AR", [128, 176 * KB], BF16)

    def zbv(kb_off, kb_len):
        return AR[:, kb_off * KB:(kb_off + kb_len) * KB]

    def zfv(kb_off, kb_len):
        return AR[:, kb_off * KB:(kb_off + kb_len) * KB].bitcast(F32)

    xnT = zbv(0, 32)
    attn_s = zbv(0, 16)
    attnT_s = zbv(16, 8)
    izTb = zbv(160, 4)[0:16, 0:T]
    izbb = zbv(164, 4)
    qxT = zbv(32, 32)
    qsT = zbv(32, 32)
    loT = zbv(32, 32)
    qrT = zbv(32, 32)
    qrhs = zbv(64, 16)
    ksT = zbv(64, 32)
    ekl = zbv(64, 30)
    x_sbf = zfv(32, 64)
    v_sb = zbv(96, 32)
    projw = zbv(96, 16)
    woutw = zbv(96, 16)
    aold = [zbv(112, 2), zbv(114, 2)]
    hT = zbv(96, 32)
    rhsf = zfv(128, 32)
    xt = [zfv(128, 4), zfv(132, 4)]
    xn_s = [zfv(136, 4), zfv(140, 4)]
    rr0 = zfv(144, 4)
    rr = zfv(148, 4)
    rt = zfv(152, 4)
    spare4 = zfv(156, 4)
    qf = zfv(144, 4)
    kf = zfv(148, 4)
    sq_s = zfv(152, 4)
    ropeA = zfv(156, 2)
    ropeB = zfv(158, 2)
    cos_sb = zfv(128, 2)
    sin_sb = zfv(130, 2)
    p0rr0 = zfv(32, 4)
    p0rr = zfv(36, 4)
    p0rt = zfv(40, 4)
    sr_b = zfv(160, 4)
    isr_b = zfv(164, 4)
    selj = zfv(168, 4)[0:8, 0:1024]
    wqk = zfv(128, 32)
    wqkb = zbv(128, 32)
    vwstr = [zbv(160, 2), zbv(162, 2)]
    p3qf = zfv(96, 4)
    p3kf = zfv(100, 4)
    p3sq = zfv(104, 4)
    wstr = [zbv(160, 6), zbv(166, 6)]
    l1w = zbv(144, 16)
    l1qf = zfv(132, 4)
    ropeA2 = zfv(136, 2)
    ropeB2 = zfv(138, 2)
    cosv = zfv(128, 2)
    sinv = zfv(130, 2)
    xt10 = zfv(132, 4)
    vw1str = [zbv(140, 2), zbv(142, 2)]
    kgT = zbv(172, 4)
    w1b = [zbv(160, 2), zbv(162, 2)]
    w2b = [zbv(164, 2), zbv(166, 2)]
    w1b4 = [zbv(128, 2), zbv(130, 2), zbv(132, 2), zbv(134, 2)]
    w2b4 = [zbv(136, 2), zbv(138, 2), zbv(140, 2), zbv(142, 2)]

    PBK = [psum(f"pbk{i}", [128, 512], F32) for i in range(8)]
    PTB = [PBK[6][:].bitcast(BF16), PBK[7][:].bitcast(BF16)]

    left = nc.sbuf_bytes_remaining
    assert left >= 0, f"SBUF over budget by {-left} bytes/partition"

    for s in ENGINES:
        B.sem[s] = es.enter_context(nc.semaphore(f"sem_{s}"))
    for s in ["m1s0", "m1s1", "m1s2", "m1s3", "m2s0", "m2s1", "m2s2", "m2s3"]:
        B.sem[s] = es.enter_context(nc.semaphore(f"sem_{s}"))

    dbg_out = {}

    def add_dbg(name, shape, dt=F32):
        if name in dbg:
            dbg_out[name] = nc.dram_tensor("dbg_" + name, shape, dt, kind="ExternalOutput").ap()

    add_dbg("xn", [T, C]); add_dbg("qx", [T, C], BF16); add_dbg("sl", [128, NT])
    add_dbg("srb", [128, C]); add_dbg("qrhs", [C, C], BF16); add_dbg("xffn", [T, C])
    add_dbg("qs", [128, NC * T], BF16); add_dbg("ks", [128, NC * T], BF16)
    add_dbg("v0", [128, NT * C], BF16)
    add_dbg("negc", [128, NT * H]); add_dbg("ao", [NC, 128, T], BF16)
    add_dbg("zall", [128, NT * H]); add_dbg("x1", [T, C]); add_dbg("wte", [128, NT * E])
    add_dbg("qrT", [128, NC * T], BF16); add_dbg("ekl", [128, 15 * C], BF16)
    add_dbg("x2", [T, C]); add_dbg("xnT", [128, NC * T], BF16)
    add_dbg("kgT", [128, NC * G], BF16); add_dbg("iS8", [128, NC])
    add_dbg("kv2", [2, 128, 128], BF16); add_dbg("hT0", [128, NF * 512], BF16)
    add_dbg("srfranspose", [128, NC])
    add_dbg("src", [128, NC]); add_dbg("s8d", [8, 128]); add_dbg("rhsfd", [128, NC * C])
    add_dbg("ao1", [NC, 128, T], BF16); add_dbg("attn00", [128, 4 * T], BF16)
    add_dbg("aoev0", [128, 512], BF16)
    add_dbg("glog", [128, NT * E]); add_dbg("xnT1", [128, NC * T], BF16); add_dbg("loT1", [128, NC * T], BF16)

    ln_eps_col = cvec[:, 0:1]
    rms_eps_col = cvec[:, 1:2]

    def chT(tens, cc, c0, n, p0=0, np_=128):
        return tens[p0:p0 + np_, cc * T + c0: cc * T + c0 + n]

    def tp(e, out_ap, in_ap, idn, first, last):
        return e.matmul(out_ap, in_ap, idn, start=first, stop=last, is_transpose=True)

    ld0 = []
    for dst, src in [(ident, ident_d), (identb, identb_d), (onesrow, onesrow_d),
                     (onesrowb, onesrowb_d), (onescolb, onescolb_d), (cvec, cvec_d),
                     (selallb, selallb_d), (selz, selz_d),
                     (qnw, qnw_d), (knw, knw_d), (b1c, b1c_d)]:
        ld0.append(B.do("sync", lambda e, d=dst, s=src: e.dma_start(out=d[:], in_=s[:]), tok=True))
    t_consts = B.tok("sync")
    t_selj = B.do("sync", lambda e: e.dma_start(out=selj, in_=selj_d[:]), tok=True)
    t_rhs = B.do("sync", lambda e: e.dma_start(
        out=rhsf.rearrange("p (cc c) -> p cc c", cc=NC),
        in_=rhs_d.rearrange("(cc p) c -> p cc c", p=128)), tok=True)
    t_gw = B.do("sync", lambda e: e.dma_start(
        out=gsml[:].rearrange("p (cc c) -> p cc c", cc=NC),
        in_=gw_d.rearrange("(cc p) c -> p cc c", p=128)), tok=True)
    t_pb = B.do("gpsimd", lambda e: e.dma_start(out=pb_sb[:], in_=pb_d[:]), tok=True)

    if p0cut >= 2:
        B.waits("vector", [t_consts])
        t_r127 = B.do("vector", lambda e: e.reciprocal(r127[:], cvec[:, 2:3]), tok=True)
        B.wait("vector", t_r127)
        B.waits("vector", [t_gw])
        t_ghi = B.do("vector", lambda e: e.tensor_copy(ghi[:], gsml[:]), tok=True)
        B.wait("vector", t_ghi)
        t_glo = B.do("vector", lambda e: e.tensor_tensor(glo[:], gsml[:], ghi[:], ALU.subtract), tok=True)

    B.waits("tensor", [t_rhs, t_consts])
    for jc in range(NC if p0cut >= 3 else 0):
        for cc in range(NC):
            bank, off = PBK[cc // 4], (cc % 4) * 128
            B.do("tensor", lambda e, b=bank, o=off, c=cc, j=jc: tp(e,
                b[:, o:o + 128], rhsf[:, c * C + j * 128:c * C + (j + 1) * 128], ident[:],
                c % 4 == 0, c % 4 == 3))
        tpe = B.mark("tensor")
        B.waits("vector", [tpe])
        B.do("vector", lambda e: e.tensor_copy(p0rr[:, 0:512], PBK[0][:]))
        tev = B.do("vector", lambda e: e.tensor_copy(p0rt[:, 0:512], PBK[1][:]), tok=True)
        B.waits("tensor", [tev])
        B.do("vector", lambda e: e.reduce_max(colsA[:, 0:1], p0rr[:, 0:512], AX.X, apply_absolute_value=True))
        B.do("vector", lambda e: e.reduce_max(colsA[:, 1:2], p0rt[:, 0:512], AX.X, apply_absolute_value=True))
        B.do("vector", lambda e, j=jc: e.tensor_tensor(
            sr_cols[:, j:j + 1], colsA[:, 0:1], colsA[:, 1:2], ALU.max))
    if p0cut >= 4:
        tsr = B.do("vector", lambda e: e.tensor_scalar(
            sr_cols[:], sr_cols[:], 1e-12, r127[:], ALU.max, ALU.mult), tok=True)
        B.waits("tensor", [tsr])
        tpe = B.do("tensor", lambda e: e.transpose(PBK[2][0:8, 0:128], sr_cols[:], ident[:]), tok=True)
        B.waits("vector", [tpe])
        tev = B.do("vector", lambda e: e.tensor_copy(s8[:], PBK[2][0:8, 0:128]), tok=True)
        B.waits("tensor", [tev, t_selj])
        for jc in range(NC):
            bank = PBK[3 + (jc % 2)]
            tpe = B.do("tensor", lambda e, b=bank, j=jc: e.matmul(
                b[:, 0:128], selj[0:8, j * 128:(j + 1) * 128], s8[:], start=True, stop=True), tok=True)
            B.waits("vector", [tpe])
            tev = B.do("vector", lambda e, b=bank, j=jc: e.tensor_copy(
                sr_b[:, j * 128:(j + 1) * 128], b[:, 0:128]), tok=True)
            B.waits("tensor", [tev])
        t_isr = B.do("vector", lambda e: e.reciprocal(isr_b, sr_b), tok=True)
        B.wait("vector", t_isr)

    def emit_round(dst_ap, R0=None, R1=None, R2=None):
        R0 = R0 if R0 is not None else rr0
        R1 = R1 if R1 is not None else rr
        R2 = R2 if R2 is not None else rt
        B.do("vector", lambda e: e.tensor_scalar(R0, R0, 127.25, -127.25, ALU.min, ALU.max))
        B.do("vector", lambda e: e.tensor_scalar(R0, R0, 128.5, None, ALU.add))
        B.do("vector", lambda e: e.tensor_scalar(R2, R0, 128.0, 128.0, ALU.is_ge, ALU.mult))
        B.do("vector", lambda e: e.tensor_tensor(R1, R0, R2, ALU.subtract))
        for b in (64.0, 32.0, 16.0, 8.0, 4.0, 2.0, 1.0):
            B.do("vector", lambda e, b=b: e.tensor_scalar(R2, R1, b, b, ALU.is_ge, ALU.mult))
            B.do("vector", lambda e: e.tensor_tensor(R1, R1, R2, ALU.subtract))
        B.do("vector", lambda e: e.tensor_tensor(R2, R0, R1, ALU.subtract))
        return B.do("vector", lambda e: e.tensor_scalar(dst_ap, R2, 128.0, None, ALU.subtract), tok=True)

    for cc in range(NC if p0cut >= 5 else 0):
        B.do("vector", lambda e, c=cc: e.tensor_tensor(
            p0rr0, rhsf[:, c * C:(c + 1) * C], isr_b, ALU.mult))
        emit_round(qrhs[:, cc * C:(cc + 1) * C], p0rr0, p0rr, p0rt)
    t_qrhs = B.tok("vector")

    if "srb" in dbg:
        B.waits("sync", [t_isr])
        B.do("sync", lambda e: e.dma_start(out=dbg_out["srb"], in_=sr_b))
    if "src" in dbg:
        B.waits("sync", [t_isr])
        B.do("sync", lambda e: e.dma_start(out=dbg_out["src"], in_=sr_cols[:]))
    if "s8d" in dbg:
        B.waits("sync", [t_isr])
        B.do("sync", lambda e: e.dma_start(out=dbg_out["s8d"], in_=s8[:]))
    if "rhsfd" in dbg:
        B.waits("sync", [t_rhs])
        B.do("sync", lambda e: e.dma_start(out=dbg_out["rhsfd"], in_=rhsf))
    if "qrhs" in dbg:
        B.waits("sync", [t_qrhs])
        B.do("sync", lambda e: e.dma_start(
            out=dbg_out["qrhs"].rearrange("(cc p) c -> p cc c", p=128),
            in_=qrhs[:].rearrange("p (cc c) -> p cc c", cc=NC)))

    def emit_ln(xbuf, xn_out, dep_toks, cb, prev_tsq):
        B.waits("vector", dep_toks + ([prev_tsq] if prev_tsq else []))
        B.do("vector", lambda e: e.bn_stats(stats[:, 0:6], xbuf[:, 0:512]))
        B.do("vector", lambda e: e.bn_stats(stats[:, 6:12], xbuf[:, 512:1024]))
        tagg = B.do("vector", lambda e, cb=cb: e.bn_aggr(colsA[:, cb:cb + 2], stats[:, 0:12]), tok=True)
        B.waits("scalar", [tagg])
        tsq = B.do("scalar", lambda e, cb=cb: e.activation(
            colsA[:, cb + 2:cb + 3], colsA[:, cb + 1:cb + 2], AF.Sqrt,
            bias=ln_eps_col, scale=1.0), tok=True)
        B.waits("vector", [tsq])
        trc = B.do("vector", lambda e, cb=cb: e.reciprocal(
            colsA[:, cb + 3:cb + 4], colsA[:, cb + 2:cb + 3]), tok=True)
        B.wait("vector", trc)
        tn = B.do("vector", lambda e, cb=cb: e.tensor_scalar(
            xn_out, xbuf, colsA[:, cb:cb + 1], colsA[:, cb + 3:cb + 4],
            ALU.subtract, ALU.mult), tok=True)
        return tn, tsq

    if stop >= 1:
        prev_tsq = None
        ev1_tok = {}
        tp0s, tp1s = {}, {}
        t_xn_toks = [None] * NT
        for tc in range(NT):
            bi = tc % 2
            cb = bi * 4
            B.waits("sync", [t_qrhs] + ([t_xn_toks[tc - 2]] if tc >= 2 else []))
            t_xld = B.do("sync", lambda e, t=tc, b=bi: e.dma_start(
                out=xt[b], in_=x_d[t * 128:(t + 1) * 128, :]), tok=True)
            t_xn, prev_tsq = emit_ln(xt[bi], xn_s[bi], [t_xld], cb, prev_tsq)
            t_xn_toks[tc] = t_xn
            B.do("vector", lambda e, b=bi: e.reduce_max(
                colsB[:, 0:1], xn_s[b], AX.X, apply_absolute_value=True))
            tsl = B.do("vector", lambda e, t=tc: e.tensor_scalar(
                sl_all[:, t:t + 1], colsB[:, 0:1], 1e-12, r127[:], ALU.max, ALU.mult), tok=True)
            B.wait("vector", tsl)
            tisl = B.do("vector", lambda e, t=tc: e.reciprocal(
                isl_all[:, t:t + 1], sl_all[:, t:t + 1]), tok=True)
            B.wait("vector", tisl)
            B.waits("vector", [tp1s.get(tc - 2)])
            B.do("vector", lambda e, b=bi, t=tc: e.tensor_scalar(
                rr0, xn_s[b], isl_all[:, t:t + 1], None, ALU.mult))
            tq = emit_round(tb4[2 + bi][:])
            B.waits("scalar", [t_xn, tp0s.get(tc - 2)])
            txb = B.do("scalar", lambda e, b=bi: e.copy(tb4[b][:], xn_s[b]), tok=True)
            B.waits("tensor", [txb, tq, t_consts, ev1_tok.get(tc - 1)])
            for cc in range(NC):
                B.do("tensor", lambda e, c=cc, b=bi: tp(e,
                    PTB[0][:, c * 128:(c + 1) * 128], tb4[b][:, c * 128:(c + 1) * 128], identb[:],
                    c == 0, c == NC - 1))
            tp0s[tc] = B.mark("tensor")
            for cc in range(NC):
                B.do("tensor", lambda e, c=cc, b=bi: tp(e,
                    PTB[1][:, c * 128:(c + 1) * 128], tb4[2 + b][:, c * 128:(c + 1) * 128], identb[:],
                    c == 0, c == NC - 1))
            tp1s[tc] = B.mark("tensor")
            B.waits("vector", [tp0s[tc], tp1s[tc]])
            B.do("vector", lambda e, t=tc: e.tensor_copy(
                xnT.rearrange("p (cc tt) -> p cc tt", cc=NC)[:, :, t * 128:(t + 1) * 128],
                PTB[0][:].rearrange("p (cc tt) -> p cc tt", cc=NC)))
            ev1_tok[tc] = B.do("vector", lambda e, t=tc: e.tensor_copy(
                qxT.rearrange("p (cc tt) -> p cc tt", cc=NC)[:, :, t * 128:(t + 1) * 128],
                PTB[1][:].rearrange("p (cc tt) -> p cc tt", cc=NC)), tok=True)
            if "xn" in dbg:
                B.waits("sync", [t_xn])
                B.do("sync", lambda e, t=tc, b=bi: e.dma_start(
                    out=dbg_out["xn"][t * 128:(t + 1) * 128, :], in_=xn_s[b]))
            if "qx" in dbg:
                B.waits("sync", [tq])
                B.do("sync", lambda e, t=tc, b=bi: e.dma_start(
                    out=dbg_out["qx"][t * 128:(t + 1) * 128, :], in_=tb4[2 + b][:]))
        if "sl" in dbg:
            B.waits("sync", [B.tok("vector")])
            B.do("sync", lambda e: e.dma_start(out=dbg_out["sl"], in_=sl_all[:]))
        if "xnT" in dbg:
            B.waits("sync", [B.tok("vector")])
            B.do("sync", lambda e: e.dma_start(out=dbg_out["xnT"], in_=xnT))

    if stop >= 2:
        prev_ffn = {}
        for tc in range(NT):
            bb = (tc % 2) * 2
            bi = tc % 2
            deps = [t_qrhs, ev1_tok[tc]]
            if tc >= 2:
                deps.append(prev_ffn[tc - 2][1])
            B.waits("tensor", deps)
            for cc in range(NC):
                for jb in range(2):
                    B.do("tensor", lambda e, c=cc, j=jb, b=bb, t=tc: e.matmul(
                        PBK[b + j][:], chT(qxT, c, t * 128, 128),
                        qrhs[:, c * C + j * 512:c * C + (j + 1) * 512],
                        start=(c == 0), stop=(c == NC - 1)))
            tmm = B.mark("tensor")
            B.waits("sync", [t_xn_toks[14 + bi]] + ([prev_ffn[tc - 2][0]] if tc >= 2 else []))
            t_xld = B.do("sync", lambda e, t=tc, b=bi: e.dma_start(
                out=xt[b], in_=x_d[t * 128:(t + 1) * 128, :]), tok=True)
            B.waits("vector", [tmm, t_xld])
            for jb in range(2):
                B.do("vector", lambda e, j=jb, b=bb, t=tc: e.scalar_tensor_tensor(
                    rt[:, j * 512:(j + 1) * 512], PBK[b + j][:], sl_all[:, t:t + 1],
                    sr_b[:, j * 512:(j + 1) * 512], ALU.mult, ALU.mult))
            tev = B.do("vector", lambda e, b=bi: e.tensor_tensor(
                xt[b], xt[b], rt, ALU.add), tok=True)
            B.waits("sync", [tev])
            tst = B.do("sync", lambda e, t=tc, b=bi: e.dma_start(
                out=xs_d[t * 128:(t + 1) * 128, :], in_=xt[b]), tok=True)
            prev_ffn[tc] = (tst, tev)
            if "xffn" in dbg:
                B.do("sync", lambda e, t=tc, b=bi: e.dma_start(
                    out=dbg_out["xffn"][t * 128:(t + 1) * 128, :], in_=xt[b]))
        t_ffn_done_s = B.tok("sync")
        t_ffn_done_v = B.tok("vector")

    if stop >= 3:
        def rms_block(src_f32, w_rep, dst_bf, extra_scale, dep_tok, sq_buf):
            B.waits("scalar", [dep_tok])
            tsq = B.do("scalar", lambda e: e.activation(sq_buf, src_f32, AF.Square,
                                                        bias=0.0, scale=1.0), tok=True)
            B.waits("vector", [tsq])
            trd = B.do("vector", lambda e: e.tensor_reduce(
                colsB[:, 0:16].rearrange("p (h o) -> p h o", o=1),
                sq_buf.rearrange("p (h d) -> p h d", h=H), AX.X, ALU.add), tok=True)
            B.waits("scalar", [trd])
            ts1 = B.do("scalar", lambda e: e.activation(
                colsA[:, 0:8], colsB[:, 0:8], AF.Sqrt, bias=rms_eps_col, scale=1.0 / DH), tok=True)
            ts2 = B.do("scalar", lambda e: e.activation(
                stats[:, 0:8], colsB[:, 8:16], AF.Sqrt, bias=rms_eps_col, scale=1.0 / DH), tok=True)
            B.waits("vector", [ts1, ts2])
            B.do("vector", lambda e: e.reciprocal(stats[:, 8:16], colsA[:, 0:8]))
            trc = B.do("vector", lambda e: e.reciprocal(colsB[:, 0:8], stats[:, 0:8]), tok=True)
            B.wait("vector", trc)
            if extra_scale != 1.0:
                B.do("vector", lambda e: e.tensor_scalar(
                    stats[:, 8:16], stats[:, 8:16], extra_scale, None, ALU.mult))
                tsc = B.do("vector", lambda e: e.tensor_scalar(
                    colsB[:, 0:8], colsB[:, 0:8], extra_scale, None, ALU.mult), tok=True)
                B.wait("vector", tsc)
            B.do("vector", lambda e: e.tensor_tensor(
                sq_buf.rearrange("p (h d) -> p h d", h=H), src_f32.rearrange("p (h d) -> p h d", h=H),
                w_rep[:].unsqueeze(1).broadcast_to([128, H, DH]),
                ALU.mult))
            B.do("vector", lambda e: e.tensor_tensor(
                dst_bf.rearrange("p (h d) -> p h d", h=H)[:, 0:8],
                sq_buf.rearrange("p (h d) -> p h d", h=H)[:, 0:8],
                stats[:, 8:16].unsqueeze(2).broadcast_to([128, 8, DH]),
                ALU.mult))
            return B.do("vector", lambda e: e.tensor_tensor(
                dst_bf.rearrange("p (h d) -> p h d", h=H)[:, 8:16],
                sq_buf.rearrange("p (h d) -> p h d", h=H)[:, 8:16],
                colsB[:, 0:8].unsqueeze(2).broadcast_to([128, 8, DH]),
                ALU.mult), tok=True)

        def ssq_block(src_bf, dst_cols, sq_buf, dep_tok):
            B.waits("scalar", [dep_tok])
            tsq = B.do("scalar", lambda e: e.activation(sq_buf, src_bf, AF.Square,
                                                        bias=0.0, scale=1.0), tok=True)
            B.waits("vector", [tsq])
            return B.do("vector", lambda e: e.tensor_reduce(
                dst_cols.rearrange("p (h o) -> p h o", o=1),
                sq_buf.rearrange("p (h d) -> p h d", h=H), AX.X, ALU.add), tok=True)

        B.waits("gpsimd", [t_ffn_done_v, t_ffn_done_s])
        t_wqk = B.do("gpsimd", lambda e: e.dma_start(
            out=wqkb[:].rearrange("p (cc c) -> p cc c", cc=NC),
            in_=qkvwT_d[:, 0:2 * C].rearrange("(cc p) c -> p cc c", p=128)), tok=True)

        prev_qkv = {}
        tqk_ev = {}
        for tc in range(NT):
            bi = tc % 2
            deps = [ev1_tok[NT - 1], t_ffn_done_v, t_wqk]
            if tc >= 2:
                deps.append(prev_qkv[tc - 2])
            B.waits("tensor", deps)
            for cc in range(NC):
                for part in range(2):
                    for jb in range(2):
                        B.do("tensor", lambda e, c=cc, p=part, j=jb, t=tc: e.matmul(
                            PBK[p * 2 + j][:], chT(xnT, c, t * 128, 128),
                            wqkb[:, c * 2 * C + p * C + j * 512:c * 2 * C + p * C + (j + 1) * 512],
                            start=(c == 0), stop=(c == NC - 1)))
            tmm = B.mark("tensor")
            B.waits("vector", [tmm])
            B.do("vector", lambda e: e.tensor_copy(p3qf[:, 0:512], PBK[0][:]))
            B.do("vector", lambda e: e.tensor_copy(p3qf[:, 512:1024], PBK[1][:]))
            B.do("vector", lambda e: e.tensor_copy(p3kf[:, 0:512], PBK[2][:]))
            tevv = B.do("vector", lambda e: e.tensor_copy(p3kf[:, 512:1024], PBK[3][:]), tok=True)
            prev_qkv[tc] = tevv

            tqs = rms_block(p3qf, qnw, tb4[2 + bi][:], 0.125, tevv, p3sq)
            tssq = ssq_block(tb4[2 + bi][:], ssq_q[:, tc * H:(tc + 1) * H], p3sq, tqs)
            tks = rms_block(p3kf, knw, tb4[bi][:], 1.0, tssq, p3sq)
            tssk = ssq_block(tb4[bi][:], ssq_k[:, tc * H:(tc + 1) * H], p3sq, tks)

            B.waits("tensor", [tqs, tks, tqk_ev.get(tc - 1)])
            for cc in range(NC):
                B.do("tensor", lambda e, c=cc, b=bi: tp(e,
                    PTB[0][:, c * 128:(c + 1) * 128], tb4[2 + b][:, c * 128:(c + 1) * 128], identb[:],
                    c == 0, c == NC - 1))
            tq0 = B.mark("tensor")
            for cc in range(NC):
                B.do("tensor", lambda e, c=cc, b=bi: tp(e,
                    PTB[1][:, c * 128:(c + 1) * 128], tb4[b][:, c * 128:(c + 1) * 128], identb[:],
                    c == 0, c == NC - 1))
            tq1 = B.mark("tensor")
            B.waits("vector", [tq0, tq1, tssk])
            B.do("vector", lambda e, t=tc: e.tensor_copy(
                qsT.rearrange("p (cc tt) -> p cc tt", cc=NC)[:, :, t * 128:(t + 1) * 128],
                PTB[0][:].rearrange("p (cc tt) -> p cc tt", cc=NC)))
            tqk_ev[tc] = B.do("vector", lambda e, t=tc: e.tensor_copy(
                ksT.rearrange("p (cc tt) -> p cc tt", cc=NC)[:, :, t * 128:(t + 1) * 128],
                PTB[1][:].rearrange("p (cc tt) -> p cc tt", cc=NC)), tok=True)
        t_p3a_v = B.tok("vector")

        t_vw = [None] * 8
        t_vw_used = {}
        B.waits("gpsimd", [t_ffn_done_v])
        for v0 in (0, 1):
            t_vw[v0] = B.do("gpsimd", lambda e, v=v0, w=v0: e.dma_start(
                out=vwstr[w][:], in_=vwr_d[v]), tok=True)
        prev_vev = {}
        for vd in range(8):
            wb = vd % 2
            for tb in range(4):
                key = (vd, tb)
                B.waits("tensor", [t_vw[vd], t_p3a_v if vd == 0 and tb == 0 else None,
                                   prev_vev.get((vd, tb - 2)) or prev_vev.get((vd - 1, 2 + tb))])
                bank = PBK[(vd * 4 + tb) % 2]
                for cc in range(NC):
                    B.do("tensor", lambda e, c=cc, v=vd, t=tb, w=wb, bk=bank: e.matmul(
                        bk[:], vwstr[w][:, c * 128:(c + 1) * 128],
                        chT(xnT, c, t * 512, 512),
                        start=(c == 0), stop=(c == NC - 1)))
                tmm = B.mark("tensor")
                if tb == 3:
                    t_vw_used[vd] = tmm
                B.waits("vector", [tmm])
                tev = B.do("vector", lambda e, bk=bank: e.tensor_copy(aoev[:], bk[:]), tok=True)
                B.waits("tensor", [tev])
                for q4 in range(4):
                    B.do("tensor", lambda e, q=q4: tp(e,
                        PTB[0][:, q * 128:(q + 1) * 128], aoev[:, q * 128:(q + 1) * 128],
                        identb[:], q == 0, q == 3))
                tpe = B.mark("tensor")
                B.waits("vector", [tpe])
                tev2 = B.do("vector", lambda e, v=vd, t=tb: e.tensor_copy(
                    v_sb[:].rearrange("p (tt c) -> p tt c", tt=NT)[:, t * 4:(t + 1) * 4,
                                                                  v * 128:(v + 1) * 128],
                    PTB[0][:, 0:512].rearrange("p (q c) -> p q c", q=4)), tok=True)
                B.waits("tensor", [tev2])
                prev_vev[key] = tev2
            if vd + 2 < 8:
                B.waits("gpsimd", [t_vw_used[vd]])
                t_vw[vd + 2] = B.do("gpsimd", lambda e, v=vd + 2, w=(vd % 2): e.dma_start(
                    out=vwstr[w][:], in_=vwr_d[v]), tok=True)
        t_p3_done = B.tok("vector")

        B.do("vector", lambda e: e.tensor_copy(stats[:, 0:16], ssq_k[:, 0:16]))
        tmax = None
        for tc in range(1, NT):
            tmax = B.do("vector", lambda e, t=tc: e.tensor_tensor(
                stats[:, 0:16], stats[:, 0:16], ssq_k[:, t * H:(t + 1) * H], ALU.max), tok=True)
        B.waits("tensor", [tmax])
        tpe = B.do("tensor", lambda e: e.transpose(PBK[0][0:16, 0:128], stats[:, 0:16], ident[:]), tok=True)
        B.waits("vector", [tpe])
        tev = B.do("vector", lambda e: e.tensor_copy(rt[0:16, 0:128], PBK[0][0:16, 0:128]), tok=True)
        B.wait("vector", tev)
        tred = B.do("vector", lambda e: e.reduce_max(rt[0:16, 128:129], rt[0:16, 0:128], AX.X), tok=True)
        B.waits("tensor", [tred])
        tpe = B.do("tensor", lambda e: e.transpose(
            PBK[1][0:1, 0:16], rt[0:16, 128:129], ident[0:16, 0:16]), tok=True)
        B.waits("vector", [tpe])
        tev = B.do("vector", lambda e: e.tensor_copy(rt[0:1, 256:272], PBK[1][0:1, 0:16]), tok=True)
        B.wait("vector", tev)
        B.do("vector", lambda e: e.tensor_copy(stats[:, 0:16], ssq_q[:, 0:16]))
        tmaxq = None
        for tc in range(1, NT):
            tmaxq = B.do("vector", lambda e, t=tc: e.tensor_tensor(
                stats[:, 0:16], stats[:, 0:16], ssq_q[:, t * H:(t + 1) * H], ALU.max), tok=True)
        B.waits("tensor", [tmaxq])
        tpe = B.do("tensor", lambda e: e.transpose(PBK[0][0:16, 0:128], stats[:, 0:16], ident[:]), tok=True)
        B.waits("vector", [tpe])
        tev = B.do("vector", lambda e: e.tensor_copy(rt[0:16, 0:128], PBK[0][0:16, 0:128]), tok=True)
        B.wait("vector", tev)
        tred = B.do("vector", lambda e: e.reduce_max(rt[0:16, 128:129], rt[0:16, 0:128], AX.X), tok=True)
        B.waits("tensor", [tred])
        tpe = B.do("tensor", lambda e: e.transpose(
            PBK[1][0:1, 0:16], rt[0:16, 128:129], ident[0:16, 0:16]), tok=True)
        B.waits("vector", [tpe])
        tev = B.do("vector", lambda e: e.tensor_copy(rt[0:1, 272:288], PBK[1][0:1, 0:16]), tok=True)
        B.wait("vector", tev)
        B.do("vector", lambda e: e.tensor_scalar(
            rt[0:1, 288:304], rt[0:1, 256:272], 0.0625, None, ALU.mult))
        B.do("vector", lambda e: e.scalar_tensor_tensor(
            rt[0:1, 288:304], rt[0:1, 272:288], 4.0, rt[0:1, 288:304], ALU.mult, ALU.add))
        tcr = B.do("vector", lambda e: e.tensor_scalar(
            rt[0:1, 288:304], rt[0:1, 288:304], -1.0, None, ALU.mult), tok=True)
        B.waits("tensor", [tcr])
        tpe = B.do("tensor", lambda e: e.matmul(
            PBK[2][:, 0:16], onesrow[:], rt[0:1, 288:304], start=True, stop=True), tok=True)
        B.waits("vector", [tpe])
        t_negc2 = B.do("vector", lambda e: e.tensor_copy(negc[:, 0:16], PBK[2][:, 0:16]), tok=True)

        if "qs" in dbg:
            B.waits("sync", [B.tok("vector")])
            B.do("sync", lambda e: e.dma_start(out=dbg_out["qs"], in_=qsT))
        if "ks" in dbg:
            B.waits("sync", [B.tok("vector")])
            B.do("sync", lambda e: e.dma_start(out=dbg_out["ks"], in_=ksT))
        if "v0" in dbg:
            B.waits("sync", [B.tok("vector")])
            B.do("sync", lambda e: e.dma_start(out=dbg_out["v0"], in_=v_sb))
        if "negc" in dbg:
            B.waits("sync", [t_negc2])
            B.do("sync", lambda e: e.dma_start(out=dbg_out["negc"], in_=negc[:]))

    if stop >= 4:
        t_aodma = {}
        t_exp = {}
        t_avz = {}
        last_av_ev = None
        zbanks = [2, 3, 6, 7]
        for h in range(H):
            hc, off = h // 2, (h % 2) * 64
            avb = 4 + (h % 2)
            for ib in range(4):
                key = (h, ib)
                for jc in range(16):
                    sbk = PBK[jc % 2]
                    slot = jc % 8
                    B.waits("tensor", [t_negc2,
                                       t_p3_done if (h == 0 and ib == 0 and jc == 0) else None,
                                       t_exp.get((h, ib, jc - 2))
                                       or t_exp.get((h, ib - 1, 14 + jc))
                                       or t_exp.get((h - 1, 3, 14 + jc))])
                    tS = B.do("tensor", lambda e, j=jc, i=ib, hc2=hc, o=off, bk=sbk: e.matmul(
                        bk[:], chT(ksT, hc2, j * 128, 128, o, 64),
                        chT(qsT, hc2, i * 512, 512, o, 64),
                        start=True, stop=True), tok=True)
                    B.waits("scalar", [tS, t_avz.get((h, ib, jc - 8))
                                       or t_avz.get((h, ib - 1, 8 + jc))
                                       or t_avz.get((h - 1, 3, 8 + jc))])
                    t_exp[(h, ib, jc)] = B.do("scalar", lambda e, s=slot, hh=h, bk=sbk: e.activation(
                        attnT_s[:, s * 512:(s + 1) * 512], bk[:], AF.Exp,
                        bias=negc[:, hh:hh + 1], scale=1.0), tok=True)
                    B.waits("tensor", [t_exp[(h, ib, jc)],
                                       last_av_ev if jc == 0 else None])
                    B.do("tensor", lambda e, j=jc, s=slot, hh=h, o=off, b=avb: e.matmul(
                        PBK[b][o:o + 64, :], v_sb[:, j * C + hh * DH:j * C + (hh + 1) * DH],
                        attnT_s[:, s * 512:(s + 1) * 512],
                        start=(j == 0), stop=(j == 15),
                        **({"tile_position": (0, 64)} if o else {})))
                    B.do("tensor", lambda e, j=jc, s=slot, hh=h, i=ib: e.matmul(
                        PBK[zbanks[i]][0:16, 0:512], selz[:, hh * 16:(hh + 1) * 16],
                        attnT_s[:, s * 512:(s + 1) * 512],
                        start=(hh == 0 and j == 0), stop=(hh == H - 1 and j == 15)))
                    t_avz[(h, ib, jc)] = B.mark("tensor")
                tavd = t_avz[(h, ib, 15)]
                B.waits("vector", [tavd, t_aodma.get((h, ib - 1)) if ib > 0 else t_aodma.get((h - 1, 3))])
                tev = B.do("vector", lambda e, o=off, b=avb: e.tensor_copy(
                    aoev[o:o + 64, :], PBK[b][o:o + 64, :]), tok=True)
                last_av_ev = tev
                B.waits("sync", [tev])
                t_aodma[key] = B.do("sync", lambda e, hcc=hc, o=off, i=ib: e.dma_start(
                    out=ao_d[hcc, o:o + 64, i * 512:(i + 1) * 512], in_=aoev[o:o + 64, :]), tok=True)
        t_zev = {}
        for ib in range(4):
            B.waits("vector", [t_avz[(H - 1, ib, 15)]])
            B.do("vector", lambda e, i=ib: e.reciprocal(
                PBK[zbanks[i]][0:16, 0:512], PBK[zbanks[i]][0:16, 0:512]))
            t_zev[ib] = B.do("vector", lambda e, i=ib: e.tensor_copy(
                izTb[:, i * 512:(i + 1) * 512], PBK[zbanks[i]][0:16, 0:512]), tok=True)
        t_izT = t_zev[3]

        if "ao" in dbg:
            B.waits("sync", [t_aodma[(H - 1, 3)]])
            B.do("sync", lambda e: e.dma_start(out=dbg_out["ao"], in_=ao_d))
    if stop >= 5:
        B.waits("gpsimd", [B.tok("tensor")])
        t_projw = B.do("gpsimd", lambda e: e.dma_start(
            out=projw[:].rearrange("p (cc c) -> p cc c", cc=NC),
            in_=projT_d.rearrange("(cc p) c -> p cc c", p=128)), tok=True)
        prev_p5 = {}
        t_hi_ev = {}
        t_lo_used = {}
        t_lnq5 = {}
        t_izb_used = None
        t_xn1_toks = [None] * NT
        for tc in range(NT):
            bi = tc % 2
            cb = bi * 4
            B.waits("sync", [t_aodma[(H - 1, 3)]] + ([prev_p5[tc - 2][0]] if tc >= 2 else []))
            tld = B.do("sync", lambda e, t=tc, b=bi: e.dma_start(
                out=aold[b][:].rearrange("p (hc t) -> p hc t", hc=NC),
                in_=ao_d[:, :, t * 128:(t + 1) * 128].rearrange("hc p t -> p hc t")), tok=True)
            B.waits("tensor", [t_izT, tld] + ([t_izb_used] if t_izb_used else []))
            for hcc in range(NC):
                B.do("tensor", lambda e, hc2=hcc, t=tc: e.matmul(
                    PBK[5][:, 0:128], selallb[:, hc2 * 128:(hc2 + 1) * 128],
                    izTb[:, t * 128:(t + 1) * 128], start=True, stop=True))
                tpe = B.mark("tensor")
                B.waits("vector", [tpe])
                tev = B.do("vector", lambda e, hc2=hcc: e.tensor_copy(
                    izbb[:, hc2 * 128:(hc2 + 1) * 128], PBK[5][:, 0:128]), tok=True)
                B.waits("tensor", [tev])
            tsc = B.do("vector", lambda e, b=bi: e.tensor_tensor(
                aold[b][:], aold[b][:], izbb[:, 0:1024], ALU.mult), tok=True)
            t_izb_used = tsc
            B.waits("tensor", [tsc, t_projw, t_pb]
                    + ([prev_p5[tc - 2][1]] if tc >= 2 else []))
            for jb in range(2):
                B.do("tensor", lambda e, j=jb: e.matmul(
                    PBK[j][:], onesrowb[:], pb_sb[0:1, j * 512:(j + 1) * 512],
                    start=True, stop=False))
            for hcc in range(NC):
                for jb in range(2):
                    B.do("tensor", lambda e, hc2=hcc, j=jb, b=bi: e.matmul(
                        PBK[j][:], aold[b][:, hc2 * 128:(hc2 + 1) * 128],
                        projw[:, hc2 * C + j * 512:hc2 * C + (j + 1) * 512],
                        start=False, stop=(hc2 == NC - 1)))
            tmm = B.mark("tensor")
            B.waits("sync", [t_ffn_done_s] + ([prev_p5[tc - 2][2]] if tc >= 2 else []))
            txld = B.do("sync", lambda e, t=tc, b=bi: e.dma_start(
                out=xt[b], in_=xs_d[t * 128:(t + 1) * 128, :]), tok=True)
            B.waits("vector", [tmm, txld])
            B.do("vector", lambda e, b=bi: e.tensor_tensor(
                xt[b][:, 0:512], xt[b][:, 0:512], PBK[0][:], ALU.add))
            tx1 = B.do("vector", lambda e, b=bi: e.tensor_tensor(
                xt[b][:, 512:1024], xt[b][:, 512:1024], PBK[1][:], ALU.add), tok=True)
            B.waits("sync", [tx1])
            tst = B.do("sync", lambda e, t=tc, b=bi: e.dma_start(
                out=xs_d[t * 128:(t + 1) * 128, :], in_=xt[b]), tok=True)
            if "x1" in dbg:
                B.do("sync", lambda e, t=tc, b=bi: e.dma_start(
                    out=dbg_out["x1"][t * 128:(t + 1) * 128, :], in_=xt[b]))
            t_xn1, _p = emit_ln(xt[bi], xn_s[bi], [tx1], cb, t_lnq5.get(tc - 2))
            t_lnq5[tc] = _p
            t_xn1_toks[tc] = t_xn1
            B.waits("scalar", [t_xn1, t_hi_ev.get(tc - 2)])
            thi = B.do("scalar", lambda e, b=bi: e.copy(tb4[b][:], xn_s[b]), tok=True)
            B.waits("vector", [thi])
            tlo = B.do("vector", lambda e, b=bi: e.tensor_tensor(
                tb4[2 + b][:], xn_s[b], tb4[b][:], ALU.subtract), tok=True)
            B.waits("tensor", [thi, tlo, t_hi_ev.get(tc - 1)])
            for cc in range(NC):
                B.do("tensor", lambda e, c=cc, b=bi: tp(e,
                    PTB[0][:, c * 128:(c + 1) * 128], tb4[b][:, c * 128:(c + 1) * 128], identb[:],
                    c == 0, c == NC - 1))
            th0 = B.mark("tensor")
            for cc in range(NC):
                B.do("tensor", lambda e, c=cc, b=bi: tp(e,
                    PTB[1][:, c * 128:(c + 1) * 128], tb4[2 + b][:, c * 128:(c + 1) * 128], identb[:],
                    c == 0, c == NC - 1))
            th1 = B.mark("tensor")
            B.waits("vector", [th0, th1])
            B.do("vector", lambda e, t=tc: e.tensor_copy(
                xnT.rearrange("p (cc tt) -> p cc tt", cc=NC)[:, :, t * 128:(t + 1) * 128],
                PTB[0][:].rearrange("p (cc tt) -> p cc tt", cc=NC)))
            t_hi_ev[tc] = B.do("vector", lambda e, t=tc: e.tensor_copy(
                loT.rearrange("p (cc tt) -> p cc tt", cc=NC)[:, :, t * 128:(t + 1) * 128],
                PTB[1][:].rearrange("p (cc tt) -> p cc tt", cc=NC)), tok=True)
            prev_p5[tc] = (tld, tmm, tst)
        t_p5_v = B.tok("vector")
        if "xnT1" in dbg:
            B.waits("sync", [t_p5_v])
            B.do("sync", lambda e: e.dma_start(out=dbg_out["xnT1"], in_=xnT))
        if "loT1" in dbg:
            B.waits("sync", [t_p5_v])
            B.do("sync", lambda e: e.dma_start(out=dbg_out["loT1"], in_=loT))
        t_p5_s = B.tok("sync")

    if stop >= 6:
        t_wte = [None] * NT
        prev_g = {}
        for tc in range(NT):
            B.waits("tensor", [t_hi_ev[NT - 1], t_glo] + ([prev_g.get(tc - 2)] if tc >= 2 else []))
            gbk = PBK[2 + tc % 2]
            for cc in range(NC):
                B.do("tensor", lambda e, c=cc, t=tc, bk=gbk: e.matmul(
                    bk[:, 0:8], chT(xnT, c, t * 128, 128), ghi[:, c * E:(c + 1) * E],
                    start=(c == 0), stop=False))
                B.do("tensor", lambda e, c=cc, t=tc, bk=gbk: e.matmul(
                    bk[:, 0:8], chT(xnT, c, t * 128, 128), glo[:, c * E:(c + 1) * E],
                    start=False, stop=False))
            for cc in range(NC):
                B.do("tensor", lambda e, c=cc, t=tc, bk=gbk: e.matmul(
                    bk[:, 0:8], chT(loT, c, t * 128, 128), ghi[:, c * E:(c + 1) * E],
                    start=False, stop=(c == NC - 1)))
            tgm = B.mark("tensor")
            B.waits("vector", [tgm])
            tz0 = B.do("vector", lambda e, bk=gbk: e.tensor_copy(zgbuf[:, 0:8], bk[:, 0:8]), tok=True)
            prev_g[tc] = tz0
            B.wait("vector", tz0)
            if "glog" in dbg:
                B.do("vector", lambda e, t=tc: e.tensor_copy(
                    wte[:, 0:1] if False else zs2[:], zgbuf[:, 0:8]))
                B.waits("sync", [B.tok("vector")])
                B.do("sync", lambda e, t=tc: e.dma_start(
                    out=dbg_out["glog"][:, t * E:(t + 1) * E], in_=zgbuf[:, 0:8]))
            t1 = B.do("vector", lambda e: e.reduce_max(zgbuf[:, 8:9], zgbuf[:, 0:8], AX.X), tok=True)
            B.wait("vector", t1)
            t2 = B.do("vector", lambda e: e.tensor_scalar(
                zgbuf[:, 16:24], zgbuf[:, 0:8], zgbuf[:, 8:9], -1e30, ALU.is_equal, ALU.mult), tok=True)
            B.wait("vector", t2)
            t3 = B.do("vector", lambda e: e.tensor_tensor(
                zgbuf[:, 16:24], zgbuf[:, 16:24], zgbuf[:, 0:8], ALU.add), tok=True)
            B.wait("vector", t3)
            t4 = B.do("vector", lambda e: e.reduce_max(zgbuf[:, 9:10], zgbuf[:, 16:24], AX.X), tok=True)
            B.wait("vector", t4)
            t5 = B.do("vector", lambda e: e.tensor_scalar(
                zgbuf[:, 10:11], zgbuf[:, 8:9], -1.0, None, ALU.mult), tok=True)
            B.waits("scalar", [t5])
            t6 = B.do("scalar", lambda e: e.activation(
                zgbuf[:, 16:24], zgbuf[:, 0:8], AF.Exp, bias=zgbuf[:, 10:11], scale=1.0), tok=True)
            B.waits("vector", [t6])
            t7 = B.do("vector", lambda e: e.tensor_scalar(
                zs1[:], zgbuf[:, 0:8], zgbuf[:, 9:10], None, ALU.is_ge), tok=True)
            B.wait("vector", t7)
            t8 = B.do("vector", lambda e: e.tensor_tensor(
                zs2[:], zgbuf[:, 16:24], zs1[:], ALU.mult), tok=True)
            B.wait("vector", t8)
            t9 = B.do("vector", lambda e: e.tensor_reduce(zgbuf[:, 11:12], zs2[:], AX.X, ALU.add), tok=True)
            B.wait("vector", t9)
            t10 = B.do("vector", lambda e: e.reciprocal(zgbuf[:, 12:13], zgbuf[:, 11:12]), tok=True)
            B.wait("vector", t10)
            t_wte[tc] = B.do("vector", lambda e, t=tc: e.tensor_scalar(
                wte[:, t * E:(t + 1) * E], zs2[:], zgbuf[:, 12:13], None, ALU.mult), tok=True)
        t_p6_v = B.tok("vector")
        if "wte" in dbg:
            B.waits("sync", [t_p6_v])
            B.do("sync", lambda e: e.dma_start(out=dbg_out["wte"], in_=wte[:]))

    if stop >= 7:
        t_cs = B.do("sync", lambda e: e.dma_start(out=cosv, in_=cos_d[:]), tok=True)
        t_sn = B.do("sync", lambda e: e.dma_start(out=sinv, in_=sin_d[:]), tok=True)
        B.waits("sync", [t_p5_v, t_p6_v])

        def rope_tile(tc, src_f32, dst_bf, dep_toks):
            cs = cosv[:, tc * 32:(tc + 1) * 32].unsqueeze(1).broadcast_to([128, H, 32])
            sn = sinv[:, tc * 32:(tc + 1) * 32].unsqueeze(1).broadcast_to([128, H, 32])
            s3 = src_f32.rearrange("p (h d) -> p h d", h=H)
            d3 = dst_bf.rearrange("p (h d) -> p h d", h=H)
            rA = ropeA2.rearrange("p (h d) -> p h d", h=H, d=32)
            rB = ropeB2.rearrange("p (h d) -> p h d", h=H, d=32)
            B.waits("vector", dep_toks)
            B.do("vector", lambda e: e.tensor_tensor(rA, s3[:, :, 0:32], cs, ALU.mult))
            B.do("vector", lambda e: e.tensor_tensor(rB, s3[:, :, 32:64], sn, ALU.mult))
            B.do("vector", lambda e: e.tensor_tensor(d3[:, :, 0:32], rA, rB, ALU.subtract))
            B.do("vector", lambda e: e.tensor_tensor(rA, s3[:, :, 0:32], sn, ALU.mult))
            B.do("vector", lambda e: e.tensor_tensor(rB, s3[:, :, 32:64], cs, ALU.mult))
            return B.do("vector", lambda e: e.tensor_tensor(d3[:, :, 32:64], rA, rB, ALU.add), tok=True)

        B.waits("gpsimd", [t_p5_v, t_p6_v])
        t_lw = B.do("gpsimd", lambda e: e.dma_start(
            out=l1w[:].rearrange("p (cc c) -> p cc c", cc=NC),
            in_=wq_d.rearrange("(cc p) c -> p cc c", p=128)), tok=True)
        prev_q7 = {}
        tq7_ev = {}
        tq0s = {}
        for tc in range(NT):
            bi = tc % 2
            qb0 = 2 * (tc % 2)
            B.waits("tensor", [t_lw] + ([prev_q7.get(tc - 2)]))
            for cc in range(NC):
                for jb in range(2):
                    B.do("tensor", lambda e, c=cc, j=jb, t=tc, b0=qb0: e.matmul(
                        PBK[b0 + j][:], chT(xnT, c, t * 128, 128),
                        l1w[:, c * C + j * 512:c * C + (j + 1) * 512],
                        start=(c == 0), stop=(c == NC - 1)))
            tmm = B.mark("tensor")
            B.waits("vector", [tmm, tq7_ev.get(tc - 2)])
            B.do("vector", lambda e, b0=qb0: e.tensor_copy(l1qf[:, 0:512], PBK[b0][:]))
            tqe = B.do("vector", lambda e, b0=qb0: e.tensor_copy(l1qf[:, 512:1024], PBK[b0 + 1][:]), tok=True)
            prev_q7[tc] = tqe
            trp = rope_tile(tc, l1qf, tb4[bi][:], [tqe, tq0s.get(tc - 2)])
            B.waits("tensor", [trp, tq7_ev.get(tc - 1)])
            for cc in range(NC):
                B.do("tensor", lambda e, c=cc, b=bi: tp(e,
                    PTB[0][:, c * 128:(c + 1) * 128], tb4[b][:, c * 128:(c + 1) * 128], identb[:],
                    c == 0, c == NC - 1))
            tq0 = B.mark("tensor")
            tq0s[tc] = tq0
            B.waits("vector", [tq0])
            tq7_ev[tc] = B.do("vector", lambda e, t=tc: e.tensor_copy(
                qrT.rearrange("p (cc tt) -> p cc tt", cc=NC)[:, :, t * 128:(t + 1) * 128],
                PTB[0][:].rearrange("p (cc tt) -> p cc tt", cc=NC)), tok=True)
        t_qr_done = B.tok("vector")
        if "qrT" in dbg:
            B.waits("sync", [t_qr_done])
            B.do("sync", lambda e: e.dma_start(out=dbg_out["qrT"], in_=qrT))

        B.waits("gpsimd", [B.tok("tensor")])
        t_lwk = B.do("gpsimd", lambda e: e.dma_start(
            out=l1w[:].rearrange("p (cc c) -> p cc c", cc=NC),
            in_=wk_d.rearrange("(cc p) c -> p cc c", p=128)), tok=True)
        tz_ekl = B.do("vector", lambda e: e.memset(ekl[0:76, 0:C], 0.0), tok=True)
        prev_k7 = {}
        t_kg_ev = {}
        t_exk = {}
        t_kgpe = {}
        t_ekl_toks = []
        for tc in range(NT):
            bi = tc % 2
            qb0 = 2 * (tc % 2)
            B.waits("tensor", [t_lwk, prev_q7[NT - 2] if tc == 0 else None,
                               prev_q7[NT - 1] if tc == 1 else None] + ([prev_k7.get(tc - 2)]))
            for cc in range(NC):
                for jb in range(2):
                    B.do("tensor", lambda e, c=cc, j=jb, t=tc, b0=qb0: e.matmul(
                        PBK[b0 + j][:], chT(xnT, c, t * 128, 128),
                        l1w[:, c * C + j * 512:c * C + (j + 1) * 512],
                        start=(c == 0), stop=(c == NC - 1)))
            tmm = B.mark("tensor")
            B.waits("vector", [tmm, prev_k7.get(tc - 2)])
            B.do("vector", lambda e, b0=qb0: e.tensor_copy(l1qf[:, 0:512], PBK[b0][:]))
            tke = B.do("vector", lambda e, b0=qb0: e.tensor_copy(l1qf[:, 512:1024], PBK[b0 + 1][:]), tok=True)
            prev_k7[tc] = tke
            trp = rope_tile(tc, l1qf, tb4[bi][:],
                            [tke, tz_ekl, t_exk.get(tc - 2), t_kgpe.get(tc - 2)])
            if tc >= 1:
                B.waits("scalar", [trp])
                tex = B.do("scalar", lambda e, t=tc, b=bi: e.activation(
                    ekl[:, (t - 1) * C:t * C], tb4[b][:], AF.Exp, bias=0.0, scale=1.0), tok=True)
                t_exk[tc] = tex
                if tc == 1:
                    B.waits("vector", [tex])
                    tex = B.do("vector", lambda e: e.memset(ekl[0:76, 0:C], 0.0), tok=True)
                t_ekl_toks.append(tex)
            if tc <= 1:
                B.waits("tensor", [trp, t_kg_ev.get(tc - 1)])
                nrow = 128 if tc == 0 else 76
                for cc in range(NC):
                    B.do("tensor", lambda e, c=cc, b=bi, n=nrow: tp(e,
                        PTB[0][:, c * 128:c * 128 + n],
                        tb4[b][0:n, c * 128:(c + 1) * 128], identb[0:n, 0:n],
                        c == 0, c == NC - 1))
                tkp = B.mark("tensor")
                t_kgpe[tc] = tkp
                B.waits("vector", [tkp])
                co = 0 if tc == 0 else 128
                nrow2 = 128 if tc == 0 else 76
                t_kg_ev[tc] = B.do("vector", lambda e, c0=co, n=nrow2: e.tensor_copy(
                    kgT[:].rearrange("p (hc g) -> p hc g", hc=NC)[:, :, c0:c0 + n],
                    PTB[0][:].rearrange("p (hc g) -> p hc g", hc=NC)[:, :, 0:n]), tok=True)
        t_ekl_done_s = B.tok("scalar")
        t_ekl_done_v = B.tok("vector")
        t_kg_done = B.tok("vector")
        if "ekl" in dbg:
            B.waits("sync", [t_ekl_done_s, t_kg_done])
            B.do("sync", lambda e: e.dma_start(out=dbg_out["ekl"], in_=ekl[:, 0:15 * C]))
        if "kgT" in dbg:
            B.waits("sync", [t_kg_done])
            B.do("sync", lambda e: e.dma_start(
                out=dbg_out["kgT"].rearrange("p (hc g) -> p hc g", hc=NC),
                in_=kgT[:].rearrange("p (hc g) -> p hc g", hc=NC)[:, :, 0:G]))

        t_vw1 = [None] * 8
        t_vw1_used = {}
        B.waits("gpsimd", [B.tok("tensor")])
        for v0 in (0, 1):
            t_vw1[v0] = B.do("gpsimd", lambda e, v=v0, w=v0: e.dma_start(
                out=vw1str[w][:], in_=vwr1_d[v]), tok=True)
        prev_vev1 = {}
        for vd in range(8):
            wb = vd % 2
            for tb in range(4):
                key = (vd, tb)
                B.waits("tensor", [t_vw1[vd],
                                   prev_k7[NT - 2] if (vd == 0 and tb == 0) else None,
                                   prev_k7[NT - 1] if (vd == 0 and tb == 0) else None,
                                   prev_vev1.get((vd, tb - 2)) or prev_vev1.get((vd - 1, 2 + tb))])
                bank = PBK[(vd * 4 + tb) % 2 + 2]
                for cc in range(NC):
                    B.do("tensor", lambda e, c=cc, t=tb, w=wb, bk=bank: e.matmul(
                        bk[:], vw1str[w][:, c * 128:(c + 1) * 128],
                        chT(xnT, c, t * 512, 512),
                        start=(c == 0), stop=(c == NC - 1)))
                tmm = B.mark("tensor")
                if tb == 3:
                    t_vw1_used[vd] = tmm
                B.waits("vector", [tmm])
                tev = B.do("vector", lambda e, bk=bank: e.tensor_copy(aoev[:], bk[:]), tok=True)
                B.waits("tensor", [tev])
                for q4 in range(4):
                    B.do("tensor", lambda e, q=q4: tp(e,
                        PTB[0][:, q * 128:(q + 1) * 128], aoev[:, q * 128:(q + 1) * 128],
                        identb[:], q == 0, q == 3))
                tpe = B.mark("tensor")
                B.waits("vector", [tpe])
                tev2 = B.do("vector", lambda e, v=vd, t=tb: e.tensor_copy(
                    v_sb[:].rearrange("p (tt c) -> p tt c", tt=NT)[:, t * 4:(t + 1) * 4,
                                                                  v * 128:(v + 1) * 128],
                    PTB[0][:, 0:512].rearrange("p (q c) -> p q c", q=4)), tok=True)
                B.waits("tensor", [tev2])
                prev_vev1[key] = tev2
            if vd + 2 < 8:
                B.waits("gpsimd", [t_vw1_used[vd]])
                t_vw1[vd + 2] = B.do("gpsimd", lambda e, v=vd + 2, w=(vd % 2): e.dma_start(
                    out=vw1str[w][:], in_=vwr1_d[v]), tok=True)
        t_v1_done = B.tok("vector")
        if "v0" in dbg and stop >= 7:
            B.waits("sync", [t_v1_done])
            B.do("sync", lambda e: e.dma_start(out=dbg_out["v0"], in_=v_sb))

    if stop >= 8:
        tS_ev = {}
        for hc in range(NC):
            B.waits("tensor", [t_ekl_done_s, t_ekl_done_v, t_consts] + ([tS_ev.get(hc - 1)]))
            for lt in range(15):
                B.do("tensor", lambda e, l=lt, c=hc: e.matmul(
                    PBK[4][:, 0:1], ekl[:, l * C + c * 128:l * C + (c + 1) * 128],
                    onescolb[:], start=(l == 0), stop=(l == 14)))
            tS = B.mark("tensor")
            B.waits("vector", [tS])
            tS_ev[hc] = B.do("vector", lambda e, c=hc: e.tensor_copy(
                iS8[:, c:c + 1], PBK[4][:, 0:1]), tok=True)
        B.wait("vector", tS_ev[NC - 1])
        t_iS = B.do("vector", lambda e: e.reciprocal(iS8[:], iS8[:]), tok=True)
        B.wait("vector", t_iS)
        if "iS8" in dbg:
            B.waits("sync", [t_iS])
            B.do("sync", lambda e: e.dma_start(out=dbg_out["iS8"], in_=iS8[:]))

        t_lodma = {}
        t_kvev = {}
        t_kvev_s = {}
        t_tlm = {}
        t_tle = {}
        for hc in range(NC):
            kb = kv2[hc % 2]
            B.waits("vector", [t_tlm.get((hc - 2, 3))])
            tzm = B.do("vector", lambda e, k=kb: e.memset(k[:], 0.0), tok=True)
            for sub in range(2):
                h = 2 * hc + sub
                off = sub * 64
                bk = PBK[5 - sub]
                B.waits("tensor", [t_v1_done, tzm, t_kvev_s.get((hc - 1, sub)),
                                   tS_ev[NC - 1] if (hc == 0 and sub == 1) else None])
                for lt in range(15):
                    B.do("tensor", lambda e, l=lt, hh=h, o=off, bb=bk: e.matmul(
                        bb[o:o + 64, o:o + 64],
                        v_sb[:, (l + 1) * C + hh * DH:(l + 1) * C + (hh + 1) * DH],
                        ekl[:, l * C + hh * DH:l * C + (hh + 1) * DH],
                        start=(l == 0), stop=(l == 14),
                        **({"tile_position": (0, 64)} if o else {})))
                tkv = B.mark("tensor")
                B.waits("vector", [tkv])
                tev = B.do("vector", lambda e, k=kb, o=off, bb=bk: e.tensor_copy(
                    k[o:o + 64, o:o + 64], bb[o:o + 64, o:o + 64]), tok=True)
                t_kvev_s[(hc, sub)] = tev
                t_kvev[hc] = tev
            if "kv2" in dbg and hc <= 1:
                B.waits("sync", [t_kvev[hc]])
                B.do("sync", lambda e, c=hc: e.dma_start(out=dbg_out["kv2"][c], in_=kv2[c][:]))
            blocks = [(G, 512), (716, 512), (1228, 512), (1740, 308)]
            for nb, (c0, w) in enumerate(blocks):
                B.waits("tensor", [t_kvev[hc], t_qr_done if hc == 0 and nb == 0 else None,
                                   t_tle.get((hc, nb - 2)) or t_tle.get((hc - 1, nb + 2))])
                tlm = B.do("tensor", lambda e, k=kb, c=hc, c0=c0, w=w, n=nb: e.matmul(
                    PBK[n % 2][:, 0:w], k[:], chT(qrT, c, c0, w),
                    start=True, stop=True), tok=True)
                t_tlm[(hc, nb)] = tlm
                B.waits("vector", [tlm])
                tle = B.do("vector", lambda e, c=hc, w=w, n=nb: e.tensor_scalar(
                    aoev[:, 0:w], PBK[n % 2][:, 0:w], iS8[:, c:c + 1], None, ALU.mult), tok=True)
                t_tle[(hc, nb)] = tle
                if "aoev0" in dbg and hc == 0 and nb == 0:
                    B.waits("sync", [tle])
                    tdmp = B.do("sync", lambda e: e.dma_start(
                        out=dbg_out["aoev0"], in_=aoev[:]), tok=True)
                    B.waits("vector", [tdmp])
                B.waits("sync", [tle])
                t_lodma[(hc, nb)] = B.do("sync", lambda e, c=hc, c0=c0, w=w: e.dma_start(
                    out=ao_d[c, :, c0:c0 + w], in_=aoev[:, 0:w]), tok=True)
                B.waits("vector", [t_lodma[(hc, nb)]])
        t_p8_pe = B.tok("tensor")

    if stop >= 9:
        t_godma = {}
        for h in range(H):
            hc, off = h // 2, (h % 2) * 64
            B.waits("tensor", [t_kg_done, t_qr_done, t_p8_pe if h == 0 else None,
                               t_godma.get(h - 2)])
            B.do("tensor", lambda e, c=hc, o=off: e.matmul(
                PBK[2][:, 0:G], chT(qrT, c, 0, 128, o, 64),
                kgT[o:o + 64, c * 256:c * 256 + G], start=True, stop=True))
            tg1 = B.do("tensor", lambda e, c=hc, o=off: e.matmul(
                PBK[3][0:76, 0:G], chT(qrT, c, 128, 76, o, 64),
                kgT[o:o + 64, c * 256:c * 256 + G], start=True, stop=True), tok=True)
            B.waits("vector", [tg1])
            tm1 = B.do("vector", lambda e: e.reduce_max(zgbuf[:, 0:1], PBK[2][:, 0:G], AX.X), tok=True)
            tm2 = B.do("vector", lambda e: e.reduce_max(zgbuf[0:76, 1:2], PBK[3][0:76, 0:G], AX.X), tok=True)
            B.waits("vector", [tm1, tm2])
            tn1 = B.do("vector", lambda e: e.tensor_scalar(
                zgbuf[:, 2:3], zgbuf[:, 0:1], -0.125, None, ALU.mult), tok=True)
            tn2 = B.do("vector", lambda e: e.tensor_scalar(
                zgbuf[0:76, 3:4], zgbuf[0:76, 1:2], -0.125, None, ALU.mult), tok=True)
            B.waits("scalar", [tn1, tn2])
            te1 = B.do("scalar", lambda e: e.activation(
                aoev[:, 0:G], PBK[2][:, 0:G], AF.Exp, bias=zgbuf[:, 2:3], scale=0.125,
                accum_out=zgbuf[:, 4:5]), tok=True)
            te2 = B.do("scalar", lambda e: e.activation(
                aoev[0:76, 256:256 + G], PBK[3][0:76, 0:G], AF.Exp, bias=zgbuf[0:76, 3:4],
                scale=0.125, accum_out=zgbuf[0:76, 5:6]), tok=True)
            B.waits("vector", [te1, te2])
            tr1 = B.do("vector", lambda e: e.reciprocal(zgbuf[:, 6:7], zgbuf[:, 4:5]), tok=True)
            tr2 = B.do("vector", lambda e: e.reciprocal(zgbuf[0:76, 7:8], zgbuf[0:76, 5:6]), tok=True)
            B.waits("vector", [tr1, tr2])
            ts1 = B.do("vector", lambda e: e.tensor_scalar(
                aoev[:, 0:G], aoev[:, 0:G], zgbuf[:, 6:7], None, ALU.mult), tok=True)
            ts2 = B.do("vector", lambda e: e.tensor_scalar(
                aoev[0:76, 256:256 + G], aoev[0:76, 256:256 + G], zgbuf[0:76, 7:8],
                None, ALU.mult), tok=True)
            B.waits("tensor", [ts1, ts2])
            B.do("tensor", lambda e: tp(e, PTB[0][:, 0:128], aoev[:, 0:128], identb[:], True, False))
            B.do("tensor", lambda e: tp(e, PTB[0][:, 128:204], aoev[0:76, 256:256 + 128],
                                        identb[0:76, 0:76], False, True))
            B.do("tensor", lambda e: tp(e, PTB[1][0:76, 0:128], aoev[:, 128:204], identb[:], True, False))
            tg2 = B.do("tensor", lambda e: tp(e, PTB[1][0:76, 128:204],
                                              aoev[0:76, 256 + 128:256 + 204],
                                              identb[0:76, 0:76], False, True), tok=True)
            B.waits("vector", [tg2])
            tga = B.do("vector", lambda e: e.tensor_copy(agT0[:, 0:G], PTB[0][:, 0:G]), tok=True)
            tgb = B.do("vector", lambda e: e.tensor_copy(agT1[:, 0:G], PTB[1][0:76, 0:G]), tok=True)
            B.waits("tensor", [tga, tgb])
            B.do("tensor", lambda e, hh=h, o=off: e.matmul(
                PBK[4][o:o + 64, 0:G], v_sb[:, 0 * C + hh * DH:0 * C + (hh + 1) * DH],
                agT0[:, 0:G], start=True, stop=False,
                **({"tile_position": (0, 64)} if o else {})))
            tgo = B.do("tensor", lambda e, hh=h, o=off: e.matmul(
                PBK[4][o:o + 64, 0:G], v_sb[0:76, 1 * C + hh * DH:1 * C + (hh + 1) * DH],
                agT1[0:76, 0:G], start=False, stop=True,
                **({"tile_position": (0, 64)} if o else {})), tok=True)
            B.waits("vector", [tgo])
            tge = B.do("vector", lambda e, o=off: e.tensor_copy(
                aoev[o:o + 64, 256:256 + G], PBK[4][o:o + 64, 0:G]), tok=True)
            B.waits("sync", [tge])
            t_godma[h] = B.do("sync", lambda e, c=hc, o=off: e.dma_start(
                out=ao_d[c, o:o + 64, 0:G], in_=aoev[o:o + 64, 256:256 + G]), tok=True)
            B.waits("vector", [t_godma[h]])
            B.waits("scalar", [t_godma[h]])
        t_p9_pe = B.tok("tensor")
        t_p9_sc = B.tok("scalar")
        if "ao1" in dbg:
            B.waits("sync", [t_godma[H - 1], t_lodma[(NC - 1, 3)]])
            B.do("sync", lambda e: e.dma_start(out=dbg_out["ao1"], in_=ao_d))

    if stop >= 10:
        B.waits("gpsimd", [t_p9_pe])
        t_wo = B.do("gpsimd", lambda e: e.dma_start(
            out=woutw[:].rearrange("p (cc c) -> p cc c", cc=NC),
            in_=wo_d.rearrange("(cc p) c -> p cc c", p=128)), tok=True)
        prev_w = {}
        for tc in range(NT):
            bi = tc % 2
            jp = 2 * (tc % 2)
            B.waits("sync", [t_godma[H - 1], t_lodma[(NC - 1, 3)]]
                    + ([prev_w[tc - 2][0]] if tc >= 2 else []))
            tld = B.do("sync", lambda e, t=tc, b=bi: e.dma_start(
                out=aold[b][:].rearrange("p (hc t) -> p hc t", hc=NC),
                in_=ao_d[:, :, t * 128:(t + 1) * 128].rearrange("hc p t -> p hc t")), tok=True)
            B.waits("tensor", [t_wo, tld, t_p9_pe,
                               t_tle[(NC - 1, 3)] if tc == 0 else None,
                               t_p9_sc if tc == 1 else None]
                    + ([prev_w[tc - 2][3]] if tc >= 2 else []))
            for hcc in range(NC):
                for jb in range(2):
                    B.do("tensor", lambda e, hc2=hcc, j=jb, b=bi, p=jp: e.matmul(
                        PBK[p + j][:], aold[b][:, hc2 * 128:(hc2 + 1) * 128],
                        woutw[:, hc2 * C + j * 512:hc2 * C + (j + 1) * 512],
                        start=(hc2 == 0), stop=(hc2 == NC - 1)))
            tmm = B.mark("tensor")
            B.waits("sync", [t_p5_s] + ([prev_w[tc - 1][3]] if tc >= 1 else []))
            txld = B.do("sync", lambda e, t=tc: e.dma_start(
                out=xt10, in_=xs_d[t * 128:(t + 1) * 128, :]), tok=True)
            B.waits("vector", [tmm, txld, t_p8_pe if tc == 0 else None])
            B.do("vector", lambda e, t=tc, p=jp: e.tensor_tensor(
                x_sbf[:, t * C:t * C + 512], xt10[:, 0:512], PBK[p][:], ALU.add))
            tx2 = B.do("vector", lambda e, t=tc, p=jp: e.tensor_tensor(
                x_sbf[:, t * C + 512:(t + 1) * C], xt10[:, 512:1024], PBK[p + 1][:], ALU.add), tok=True)
            tst = tx2
            prev_w[tc] = (tld, tmm, txld, tx2)
            if "x2" in dbg:
                B.waits("sync", [tx2])
                B.do("sync", lambda e, t=tc: e.dma_start(
                    out=dbg_out["x2"][t * 128:(t + 1) * 128, :], in_=x_sbf[:, t * C:(t + 1) * C]))
        t_p10_v = B.tok("vector")

    if stop >= 11:
        t_b2 = {}
        t_w1ld = {}
        t_w2ld = {}
        t_gelu = {}
        t_yev = {}
        t_p1g = {}
        t_ph2mm = {}
        t_bias = {}
        for ei in range(E):
            B.waits("gpsimd", [t_p10_v if ei == 0 else None, t_bias.get((ei - 1, 3))])
            t_b2[ei] = B.do("gpsimd", lambda e, x=ei: e.dma_start(
                out=b2row[:], in_=b2_d[x:x + 1, :]), tok=True)
            for tb in range(4):
                for fc in range(NF):
                    fls = [0, 1, 2, 3] if fc == 0 else ([fc + 3] if fc + 3 < NF else [])
                    for fl in fls:
                        prev_use = (t_p1g.get((ei, tb, fl - 4))
                                    or t_p1g.get((ei, tb - 1, NF - 4 + fl))
                                    or t_p1g.get((ei - 1, 3, NF - 4 + fl)))
                        B.waits("gpsimd", [prev_use,
                                           t_p10_v if (ei == 0 and tb == 0 and fc == 0) else None])
                        t_w1ld[(ei, tb, fl)] = B.do("gpsimd", lambda e, x=ei, f=fl: e.dma_start(
                            out=w1b4[f % 4][:], in_=w1_d[x, f]), sem=f"m1s{fl % 4}")
                    bank = PBK[4 + (fc % 4)]
                    B.waits("tensor", [t_w1ld[(ei, tb, fc)],
                                       t_gelu.get((ei, tb, fc - 4))
                                       or t_gelu.get((ei, tb - 1, NF - 4 + fc))
                                       or t_gelu.get((ei - 1, 3, NF - 4 + fc)),
                                       (t_yev.get((ei, tb - 1)) or t_yev.get((ei - 1, 3)))
                                       if fc < 4 else None])
                    for cc in range(NC):
                        B.do("tensor", lambda e, c=cc, f=fc, t=tb, bk=bank: e.matmul(
                            bk[:], w1b4[f % 4][:, c * 128:(c + 1) * 128],
                            chT(xnT, c, t * 512, 512),
                            start=(c == 0), stop=(c == NC - 1)))
                    t_p1g[(ei, tb, fc)] = B.mark("tensor")
                    B.waits("scalar", [t_p1g[(ei, tb, fc)]])
                    t_gelu[(ei, tb, fc)] = B.do("scalar", lambda e, x=ei, f=fc, bk=bank: e.activation(
                        hT[:, f * 512:(f + 1) * 512], bk[:], AF.Gelu,
                        bias=b1c[:, x * NF + f:x * NF + f + 1], scale=1.0), tok=True)
                B.waits("tensor", [t_b2[ei], t_gelu[(ei, tb, NF - 1)], t_gelu[(ei, tb, NF - 2)],
                                   t_gelu[(ei, tb, NF - 3)], t_gelu[(ei, tb, NF - 4)],
                                   t_yev.get((ei, tb - 1)) or t_yev.get((ei - 1, 3))])
                for tc4 in range(4):
                    for jb in range(2):
                        B.do("tensor", lambda e, t4=tc4, j=jb: e.matmul(
                            PBK[t4 * 2 + j][:], onesrowb[:], b2row[0:1, j * 512:(j + 1) * 512],
                            start=True, stop=False))
                t_bias[(ei, tb)] = B.mark("tensor")
                for fc in range(NF):
                    fls = [0, 1, 2, 3] if fc == 0 else ([fc + 3] if fc + 3 < NF else [])
                    for fl in fls:
                        prev_use = (t_ph2mm.get((ei, tb, fl - 4))
                                    or t_ph2mm.get((ei, tb - 1, NF - 4 + fl))
                                    or t_ph2mm.get((ei - 1, 3, NF - 4 + fl)))
                        B.waits("gpsimd", [prev_use])
                        t_w2ld[(ei, tb, fl)] = B.do("gpsimd", lambda e, x=ei, f=fl: e.dma_start(
                            out=w2b4[f % 4][:], in_=w2_d[x, f * 128:(f + 1) * 128, :]), sem=f"m2s{fl % 4}")
                    B.waits("tensor", [t_w2ld[(ei, tb, fc)]])
                    for tc4 in range(4):
                        for jb in range(2):
                            B.do("tensor", lambda e, f=fc, t4=tc4, j=jb: e.matmul(
                                PBK[t4 * 2 + j][:],
                                hT[:, f * 512 + t4 * 128:f * 512 + (t4 + 1) * 128],
                                w2b4[f % 4][:, j * 512:(j + 1) * 512],
                                start=False, stop=(f == NF - 1)))
                    t_ph2mm[(ei, tb, fc)] = B.mark("tensor")
                B.waits("vector", [t_ph2mm[(ei, tb, NF - 1)], t_p10_v])
                for tc4 in range(4):
                    tg = tb * 4 + tc4
                    for jb in range(2):
                        B.do("vector", lambda e, t=tg, t4=tc4, j=jb, x=ei: e.scalar_tensor_tensor(
                            x_sbf[:, t * C + j * 512:t * C + (j + 1) * 512],
                            PBK[t4 * 2 + j][:], wte[:, t * E + x:t * E + x + 1],
                            x_sbf[:, t * C + j * 512:t * C + (j + 1) * 512],
                            ALU.mult, ALU.add))
                t_yev[(ei, tb)] = B.mark("vector")
        t_moe_v = B.tok("vector")
        if "hT0" in dbg:
            B.waits("sync", [t_gelu[(0, 0, NF - 1)]])
            B.do("sync", lambda e: e.dma_start(out=dbg_out["hT0"], in_=hT))

    if stop >= 12:
        B.waits("sync", [t_moe_v])
        for tc in range(NT):
            B.do("sync", lambda e, t=tc: e.dma_start(
                out=out_d[t * 128:(t + 1) * 128, :], in_=x_sbf[:, t * C:(t + 1) * C]),
                tok=(tc == NT - 1))
        B.wait("sync", B.tok("sync"))

    if stop < 12:
        B.waits("sync", [B.tok("vector"), B.tok("tensor"), B.tok("scalar"), B.tok("gpsimd")])
        if stop >= 2:
            for tc in range(NT):
                B.do("sync", lambda e, t=tc: e.dma_start(
                    out=out_d[t * 128:(t + 1) * 128, :], in_=xs_d[t * 128:(t + 1) * 128, :]))
            B.do("sync", lambda e: e.dma_start(out=out_d[0:1, 0:1], in_=xs_d[0:1, 0:1]), tok=True)
        else:
            tz = B.do("vector", lambda e: e.memset(rr0, 0.0), tok=True)
            B.waits("sync", [tz])
            for tc in range(NT):
                B.do("sync", lambda e, t=tc: e.dma_start(
                    out=out_d[t * 128:(t + 1) * 128, :], in_=rr0), tok=(tc == NT - 1))
        B.wait("sync", B.tok("sync"))

    with nc.Block() as block:
        B.emit(block)
    es.close()
    return nc, D, dbg_out


def _host_inputs(inputs, n_cores=8):
    f32 = np.float32
    bf16 = ml_dtypes.bfloat16
    x = np.ascontiguousarray(inputs["L0_attn_x"])
    qkvwT = np.ascontiguousarray(np.asarray(inputs["L0_attn_qkv_w"]).T)
    projT = np.ascontiguousarray(np.asarray(inputs["L0_attn_proj_w"]).T)
    pb = np.ascontiguousarray(np.asarray(inputs["L0_attn_proj_b"]).reshape(1, C))
    qnw = np.ascontiguousarray(np.broadcast_to(inputs["L0_attn_q_norm_w"], (128, DH)))
    knw = np.ascontiguousarray(np.broadcast_to(inputs["L0_attn_k_norm_w"], (128, DH)))
    rhs = np.ascontiguousarray(inputs["L0_ffn_rhs"])
    wq = np.ascontiguousarray(inputs["L1_attn_Wq"])
    wk = np.ascontiguousarray(inputs["L1_attn_Wk"])
    wv = np.ascontiguousarray(inputs["L1_attn_Wv"])
    wo = np.ascontiguousarray(inputs["L1_attn_Wout"])
    gw = np.ascontiguousarray(inputs["L1_ffn_gate_w"])
    w1 = np.asarray(inputs["L1_ffn_w1"]).reshape(E, NC, 128, NF, 128).transpose(0, 3, 2, 1, 4)
    w1 = np.ascontiguousarray(w1.reshape(E, NF, 128, NC * 128))
    b1 = np.asarray(inputs["L1_ffn_b1"]).reshape(E, NF, 128).transpose(2, 0, 1)
    b1c = np.ascontiguousarray(b1.reshape(128, E * NF))
    w2 = np.ascontiguousarray(inputs["L1_ffn_w2"])
    vw = np.asarray(inputs["L0_attn_qkv_w"]).T[:, 2 * C:3 * C]
    vwr = np.ascontiguousarray(vw.reshape(NC, 128, 8, 128).transpose(2, 1, 0, 3).reshape(8, 128, NC * 128))
    vw1 = np.asarray(inputs["L1_attn_Wv"])
    vwr1 = np.ascontiguousarray(vw1.reshape(NC, 128, 8, 128).transpose(2, 1, 0, 3).reshape(8, 128, NC * 128))
    b2 = np.ascontiguousarray(inputs["L1_ffn_b2"])

    inv = (f32(1.0) / (f32(10000.0) ** (np.arange(0, DH, 2, dtype=f32) / f32(DH)))).astype(f32)
    fr = (np.arange(T, dtype=f32)[:, None] * inv[None, :]).astype(f32)
    cos = np.cos(fr.astype(np.float64)).astype(f32)
    sin = np.sin(fr.astype(np.float64)).astype(f32)
    cos_l = np.ascontiguousarray(cos.reshape(NT, 128, 32).transpose(1, 0, 2).reshape(128, NT * 32))
    sin_l = np.ascontiguousarray(sin.reshape(NT, 128, 32).transpose(1, 0, 2).reshape(128, NT * 32))

    selallb = np.zeros((16, NC * 128), bf16)
    for hc in range(NC):
        for p in range(128):
            selallb[2 * hc + p // 64, hc * 128 + p] = 1.0
    selj = np.zeros((8, NC * 128), f32)
    for jc in range(NC):
        selj[jc, jc * 128:(jc + 1) * 128] = 1.0
    selz = np.zeros((128, 16 * 16), bf16)
    for h in range(16):
        selz[:, h * 16 + h] = 1.0
    cvec = np.zeros((128, 8), f32)
    cvec[:, 0] = 1e-5
    cvec[:, 1] = 1e-6
    cvec[:, 2] = 127.0

    shared = dict(qkvwT=qkvwT, projT=projT, pb=pb, d_qnw=qnw, d_knw=knw, rhs=rhs, vwr=vwr, vwr1=vwr1,
                  wq=wq, wk=wk, wv=wv, wo=wo, gw=gw, w1=w1, d_b1c=b1c, w2=w2, b2=b2,
                  cos=cos_l, sin=sin_l, d_ident=np.eye(128, dtype=f32),
                  d_identb=np.eye(128, dtype=bf16),
                  d_onesrow=np.ones((1, 128), f32), d_onesrowb=np.ones((1, 128), bf16),
                  d_onescolb=np.ones((128, 1), bf16),
                  d_selallb=selallb, d_selj=selj, d_selz=selz, d_cvec=cvec)
    in_maps = []
    for i in range(n_cores):
        m = dict(shared)
        m["x"] = np.ascontiguousarray(x[i])
        in_maps.append(m)
    return in_maps


def kernel(**inputs):
    nc, D, dbg_out = build_program()
    in_maps = _host_inputs(inputs)
    res = run_bass_kernel_spmd(nc, in_maps, core_ids=list(range(8)))
    return np.stack([np.asarray(r["out"]) for r in res.results], axis=0)


# version 09
# speedup vs baseline: 1.5189x; 1.0629x over previous
import numpy as np
import ml_dtypes
import concourse.bass as bass
import concourse.mybir as mybir
from concourse.bass_utils import run_bass_kernel_spmd
from contextlib import ExitStack

F32 = mybir.dt.float32
BF16 = mybir.dt.bfloat16
AF = mybir.ActivationFunctionType
ALU = mybir.AluOpType
AX = mybir.AxisListType

T, NT, C, NC = 2048, 16, 1024, 8
H, DH, G, E, FF, NF = 16, 64, 204, 8, 4096, 32
LTOK = T - G
ENGINES = ("sync", "gpsimd", "tensor", "vector", "scalar")
KB = 512


class Build:
    def __init__(self, nc):
        self.nc = nc
        self.q = {e: [] for e in ENGINES}
        self.val = {e: 0 for e in ENGINES}
        self.step = {"sync": 16, "gpsimd": 16, "tensor": 1, "vector": 1, "scalar": 1}
        self.sem = {}

    def wait(self, eng, tok):
        if tok is not None:
            self.q[eng].append(("w", tok[0], tok[1]))

    def waits(self, eng, toks):
        best = {}
        for t in toks:
            if t is None:
                continue
            best[t[0]] = max(best.get(t[0], 0), t[1])
        for s, v in best.items():
            self.q[eng].append(("w", s, v))

    def do(self, eng, fn, tok=False, sem=None):
        if eng in ("vector", "scalar"):
            self.q[eng].append(("w", eng, self.val[eng]))
            self.val[eng] += self.step[eng]
            self.q[eng].append(("i", fn, self.step[eng]))
            return (eng, self.val[eng])
        if tok or eng in ("sync", "gpsimd"):
            if sem is not None:
                self.val[sem] = self.val.get(sem, 0) + 16
                self.q[eng].append(("I", fn, 16, sem))
                return (sem, self.val[sem])
            self.q[eng].append(("w", eng, self.val[eng]))
            self.val[eng] += self.step[eng]
            self.q[eng].append(("i", fn, self.step[eng]))
            return (eng, self.val[eng])
        self.q[eng].append(("e", fn))
        return None

    def tok(self, eng):
        return (eng, self.val[eng])

    def mark(self, eng):
        for i in range(len(self.q[eng]) - 1, -1, -1):
            item = self.q[eng][i]
            if item[0] == "w":
                continue
            if item[0] == "i":
                return (eng, self._val_at(eng, i))
            self.val[eng] += self.step[eng]
            self.q[eng][i] = ("i", item[1], self.step[eng])
            return (eng, self.val[eng])
        raise RuntimeError("mark: no instruction on " + eng)

    def _val_at(self, eng, idx):
        v = 0
        for item in self.q[eng][:idx + 1]:
            if item[0] == "i":
                v += item[2]
        return v

    def emit(self, block):
        sem = self.sem
        for eng in ENGINES:
            queue = self.q[eng]

            def body(e, queue=queue):
                for item in queue:
                    if item[0] == "w":
                        e.wait_ge(sem[item[1]], item[2])
                    elif item[0] == "e":
                        item[1](e)
                    elif item[0] == "I":
                        item[1](e).then_inc(sem[item[3]], item[2])
                    else:
                        item[1](e).then_inc(sem[eng], item[2])
            getattr(block, eng)(body)


def build_program(debug=(), stop=99):
    dbg = set(debug)
    p0cut = next((int(d[5:]) for d in dbg if d.startswith("p0cut")), 99)
    nc = bass.Bass("TRN2", target_bir_lowering=False, detect_race_conditions=False)
    D = {}

    def din(name, shape, dt=F32):
        D[name] = nc.dram_tensor(name, shape, dt, kind="ExternalInput").ap()
        return D[name]

    x_d = din("x", [T, C])
    qkvwT_d = din("qkvwT", [C, 3 * C])
    projT_d = din("projT", [C, C])
    pb_d = din("pb", [1, C])
    qnw_d = din("d_qnw", [128, DH])
    knw_d = din("d_knw", [128, DH])
    rhs_d = din("rhs", [C, C])
    wq_d = din("wq", [C, C])
    wk_d = din("wk", [C, C])
    wv_d = din("wv", [C, C])
    wo_d = din("wo", [C, C])
    gw_d = din("gw", [C, E])
    vwr_d = din("vwr", [8, 128, NC * 128])
    vwr1_d = din("vwr1", [8, 128, NC * 128])
    w1_d = din("w1", [E, NF, 128, NC * 128])
    b1c_d = din("d_b1c", [128, E * NF])
    w2_d = din("w2", [E, FF, C])
    b2_d = din("b2", [E, C])
    cos_d = din("cos", [128, NT * 32])
    sin_d = din("sin", [128, NT * 32])
    ident_d = din("d_ident", [128, 128])
    identb_d = din("d_identb", [128, 128], BF16)
    onesrow_d = din("d_onesrow", [1, 128])
    onesrowb_d = din("d_onesrowb", [1, 128], BF16)
    onescolb_d = din("d_onescolb", [128, 1], BF16)
    selallb_d = din("d_selallb", [16, NC * 128], BF16)
    selz_d = din("d_selz", [128, H * 16], BF16)
    selj_d = din("d_selj", [8, NC * 128])
    cvec_d = din("d_cvec", [128, 8])
    xs_d = nc.dram_tensor("xs", [T, C], F32, kind="Internal").ap()
    ao_d = nc.dram_tensor("ao", [NC, 128, T], BF16, kind="Internal").ap()
    out_d = nc.dram_tensor("out", [T, C], F32, kind="ExternalOutput").ap()
    D["out"] = out_d

    es = ExitStack()

    def sb(name, shape, dt):
        return es.enter_context(nc.sbuf_tensor(name, shape, dt))

    def psum(name, shape, dt=F32):
        return es.enter_context(nc.psum_tensor(name, shape, dt))

    B = Build(nc)

    ident = sb("ident", [128, 128], F32)
    identb = sb("identb", [128, 128], BF16)
    onesrow = sb("onesrow", [1, 128], F32)
    onesrowb = sb("onesrowb", [1, 128], BF16)
    onescolb = sb("onescolb", [128, 1], BF16)
    cvec = sb("cvec", [128, 8], F32)
    r127 = sb("r127", [128, 1], F32)
    selallb = sb("selallb", [16, NC * 128], BF16)
    selz = sb("selz", [128, H * 16], BF16)
    qnw = sb("qnw", [128, DH], F32)
    knw = sb("knw", [128, DH], F32)
    pb_sb = sb("pb_sb", [1, C], BF16)
    b2row = sb("b2row", [1, C], BF16)
    b1c = sb("b1c", [128, E * NF], F32)
    gsml = sb("gsml", [128, NC * E], F32)
    ghi = sb("ghi", [128, NC * E], BF16)
    glo = sb("glo", [128, NC * E], BF16)
    wte = sb("wte", [128, NT * E], F32)
    stats = sb("stats", [128, 16], F32)
    colsA = sb("colsA", [128, 8], F32)
    colsB = sb("colsB", [128, 16], F32)
    sl_all = sb("sl_all", [128, NT], F32)
    isl_all = sb("isl_all", [128, NT], F32)
    ssq_q = sb("ssq_q", [128, NT * H], F32)
    ssq_k = sb("ssq_k", [128, NT * H], F32)
    negc = sb("negc", [128, NT * H], F32)
    zall = sb("zall", [128, NT * H], F32)
    zpart = sb("zpart", [128, 8], F32)
    m2b = sb("m2b", [128, H], F32)
    s8 = sb("s8", [8, 128], F32)
    sr_cols = sb("sr_cols", [128, NC], F32)
    agT0 = sb("agT0", [128, G], BF16)
    agT1 = sb("agT1", [76, G], BF16)
    kv2 = [sb(f"kv2_{i}", [128, 128], BF16) for i in range(2)]
    iS8 = sb("iS8", [128, NC], F32)
    zgbuf = sb("zgbuf", [128, 24], F32)
    zs1 = sb("zs1", [128, 8], F32)
    zs2 = sb("zs2", [128, 8], F32)
    aoev = sb("aoev", [128, 512], BF16)
    tb4 = [sb(f"tb4_{i}", [128, C], BF16) for i in range(4)]

    AR = sb("AR", [128, 176 * KB], BF16)

    def zbv(kb_off, kb_len):
        return AR[:, kb_off * KB:(kb_off + kb_len) * KB]

    def zfv(kb_off, kb_len):
        return AR[:, kb_off * KB:(kb_off + kb_len) * KB].bitcast(F32)

    xnT = zbv(0, 32)
    attn_s = zbv(0, 16)
    attnT_s = zbv(16, 8)
    izTb = zbv(160, 4)[0:16, 0:T]
    izbb = zbv(164, 4)
    qxT = zbv(32, 32)
    qsT = zbv(32, 32)
    loT = zbv(32, 32)
    qrT = zbv(32, 32)
    qrhs = zbv(64, 16)
    ksT = zbv(64, 32)
    ekl = zbv(64, 30)
    x_sbf = zfv(32, 64)
    v_sb = zbv(96, 32)
    projw = zbv(96, 16)
    woutw = zbv(96, 16)
    aold = [zbv(112, 2), zbv(114, 2)]
    hT = zbv(96, 32)
    rhsf = zfv(128, 32)
    xt = [zfv(128, 4), zfv(132, 4)]
    xn_s = [zfv(136, 4), zfv(140, 4)]
    rr0 = zfv(144, 4)
    rr = zfv(148, 4)
    rt = zfv(152, 4)
    spare4 = zfv(156, 4)
    qf = zfv(144, 4)
    kf = zfv(148, 4)
    sq_s = zfv(152, 4)
    ropeA = zfv(156, 2)
    ropeB = zfv(158, 2)
    cos_sb = zfv(128, 2)
    sin_sb = zfv(130, 2)
    p0rr0 = zfv(32, 4)
    p0rr = zfv(36, 4)
    p0rt = zfv(40, 4)
    sr_b = zfv(160, 4)
    isr_b = zfv(164, 4)
    selj = zfv(168, 4)[0:8, 0:1024]
    wqk = zfv(128, 32)
    wqkb = zbv(128, 32)
    vwstr = [zbv(160, 2), zbv(162, 2)]
    p3qf = zfv(96, 4)
    p3kf = zfv(100, 4)
    p3sq = zfv(104, 4)
    wstr = [zbv(160, 6), zbv(166, 6)]
    l1w = zbv(144, 16)
    l1qf = zfv(132, 4)
    ropeA2 = zfv(136, 2)
    ropeB2 = zfv(138, 2)
    cosv = zfv(128, 2)
    sinv = zfv(130, 2)
    xt10 = zfv(132, 4)
    vw1str = [zbv(140, 2), zbv(142, 2)]
    kgT = zbv(172, 4)
    w1b = [zbv(160, 2), zbv(162, 2)]
    w2b = [zbv(164, 2), zbv(166, 2)]
    w1b4 = [zbv(128, 2), zbv(130, 2), zbv(132, 2), zbv(134, 2)]
    w2b4 = [zbv(136, 2), zbv(138, 2), zbv(140, 2), zbv(142, 2)]

    PBK = [psum(f"pbk{i}", [128, 512], F32) for i in range(8)]
    PTB = [PBK[6][:].bitcast(BF16), PBK[7][:].bitcast(BF16)]

    left = nc.sbuf_bytes_remaining
    assert left >= 0, f"SBUF over budget by {-left} bytes/partition"

    for s in ENGINES:
        B.sem[s] = es.enter_context(nc.semaphore(f"sem_{s}"))
    for s in ["m1s0", "m1s1", "m1s2", "m1s3", "m2s0", "m2s1", "m2s2", "m2s3"]:
        B.sem[s] = es.enter_context(nc.semaphore(f"sem_{s}"))

    dbg_out = {}

    def add_dbg(name, shape, dt=F32):
        if name in dbg:
            dbg_out[name] = nc.dram_tensor("dbg_" + name, shape, dt, kind="ExternalOutput").ap()

    add_dbg("xn", [T, C]); add_dbg("qx", [T, C], BF16); add_dbg("sl", [128, NT])
    add_dbg("srb", [128, C]); add_dbg("qrhs", [C, C], BF16); add_dbg("xffn", [T, C])
    add_dbg("qs", [128, NC * T], BF16); add_dbg("ks", [128, NC * T], BF16)
    add_dbg("v0", [128, NT * C], BF16)
    add_dbg("negc", [128, NT * H]); add_dbg("ao", [NC, 128, T], BF16)
    add_dbg("zall", [128, NT * H]); add_dbg("x1", [T, C]); add_dbg("wte", [128, NT * E])
    add_dbg("qrT", [128, NC * T], BF16); add_dbg("ekl", [128, 15 * C], BF16)
    add_dbg("x2", [T, C]); add_dbg("xnT", [128, NC * T], BF16)
    add_dbg("kgT", [128, NC * G], BF16); add_dbg("iS8", [128, NC])
    add_dbg("kv2", [2, 128, 128], BF16); add_dbg("hT0", [128, NF * 512], BF16)
    add_dbg("srfranspose", [128, NC])
    add_dbg("src", [128, NC]); add_dbg("s8d", [8, 128]); add_dbg("rhsfd", [128, NC * C])
    add_dbg("ao1", [NC, 128, T], BF16); add_dbg("attn00", [128, 4 * T], BF16)
    add_dbg("aoev0", [128, 512], BF16)
    add_dbg("glog", [128, NT * E]); add_dbg("xnT1", [128, NC * T], BF16); add_dbg("loT1", [128, NC * T], BF16)

    ln_eps_col = cvec[:, 0:1]
    rms_eps_col = cvec[:, 1:2]

    def chT(tens, cc, c0, n, p0=0, np_=128):
        return tens[p0:p0 + np_, cc * T + c0: cc * T + c0 + n]

    def tp(e, out_ap, in_ap, idn, first, last):
        return e.matmul(out_ap, in_ap, idn, start=first, stop=last, is_transpose=True)

    ld0 = []
    for dst, src in [(ident, ident_d), (identb, identb_d), (onesrow, onesrow_d),
                     (onesrowb, onesrowb_d), (onescolb, onescolb_d), (cvec, cvec_d),
                     (selallb, selallb_d), (selz, selz_d),
                     (qnw, qnw_d), (knw, knw_d), (b1c, b1c_d)]:
        ld0.append(B.do("sync", lambda e, d=dst, s=src: e.dma_start(out=d[:], in_=s[:]), tok=True))
    t_consts = B.tok("sync")
    t_selj = B.do("sync", lambda e: e.dma_start(out=selj, in_=selj_d[:]), tok=True)
    t_rhs = B.do("sync", lambda e: e.dma_start(
        out=rhsf.rearrange("p (cc c) -> p cc c", cc=NC),
        in_=rhs_d.rearrange("(cc p) c -> p cc c", p=128)), tok=True)
    t_gw = B.do("sync", lambda e: e.dma_start(
        out=gsml[:].rearrange("p (cc c) -> p cc c", cc=NC),
        in_=gw_d.rearrange("(cc p) c -> p cc c", p=128)), tok=True)
    t_pb = B.do("gpsimd", lambda e: e.dma_start(out=pb_sb[:], in_=pb_d[:]), tok=True)

    if p0cut >= 2:
        B.waits("vector", [t_consts])
        t_r127 = B.do("vector", lambda e: e.reciprocal(r127[:], cvec[:, 2:3]), tok=True)
        B.wait("vector", t_r127)
        B.waits("vector", [t_gw])
        t_ghi = B.do("vector", lambda e: e.tensor_copy(ghi[:], gsml[:]), tok=True)
        B.wait("vector", t_ghi)
        t_glo = B.do("vector", lambda e: e.tensor_tensor(glo[:], gsml[:], ghi[:], ALU.subtract), tok=True)

    B.waits("tensor", [t_rhs, t_consts])
    for jc in range(NC if p0cut >= 3 else 0):
        for cc in range(NC):
            bank, off = PBK[cc // 4], (cc % 4) * 128
            B.do("tensor", lambda e, b=bank, o=off, c=cc, j=jc: tp(e,
                b[:, o:o + 128], rhsf[:, c * C + j * 128:c * C + (j + 1) * 128], ident[:],
                c % 4 == 0, c % 4 == 3))
        tpe = B.mark("tensor")
        B.waits("vector", [tpe])
        B.do("vector", lambda e: e.tensor_copy(p0rr[:, 0:512], PBK[0][:]))
        tev = B.do("vector", lambda e: e.tensor_copy(p0rt[:, 0:512], PBK[1][:]), tok=True)
        B.waits("tensor", [tev])
        B.do("vector", lambda e: e.reduce_max(colsA[:, 0:1], p0rr[:, 0:512], AX.X, apply_absolute_value=True))
        B.do("vector", lambda e: e.reduce_max(colsA[:, 1:2], p0rt[:, 0:512], AX.X, apply_absolute_value=True))
        B.do("vector", lambda e, j=jc: e.tensor_tensor(
            sr_cols[:, j:j + 1], colsA[:, 0:1], colsA[:, 1:2], ALU.max))
    if p0cut >= 4:
        tsr = B.do("vector", lambda e: e.tensor_scalar(
            sr_cols[:], sr_cols[:], 1e-12, r127[:], ALU.max, ALU.mult), tok=True)
        B.waits("tensor", [tsr])
        tpe = B.do("tensor", lambda e: e.transpose(PBK[2][0:8, 0:128], sr_cols[:], ident[:]), tok=True)
        B.waits("vector", [tpe])
        tev = B.do("vector", lambda e: e.tensor_copy(s8[:], PBK[2][0:8, 0:128]), tok=True)
        B.waits("tensor", [tev, t_selj])
        for jc in range(NC):
            bank = PBK[3 + (jc % 2)]
            tpe = B.do("tensor", lambda e, b=bank, j=jc: e.matmul(
                b[:, 0:128], selj[0:8, j * 128:(j + 1) * 128], s8[:], start=True, stop=True), tok=True)
            B.waits("vector", [tpe])
            tev = B.do("vector", lambda e, b=bank, j=jc: e.tensor_copy(
                sr_b[:, j * 128:(j + 1) * 128], b[:, 0:128]), tok=True)
            B.waits("tensor", [tev])
        t_isr = B.do("vector", lambda e: e.reciprocal(isr_b, sr_b), tok=True)
        B.wait("vector", t_isr)

    def emit_round(dst_ap, R0=None, R1=None, R2=None):
        R0 = R0 if R0 is not None else rr0
        R1 = R1 if R1 is not None else rr
        R2 = R2 if R2 is not None else rt
        B.do("vector", lambda e: e.tensor_scalar(R0, R0, 127.25, -127.25, ALU.min, ALU.max))
        B.do("vector", lambda e: e.tensor_scalar(R0, R0, 128.5, None, ALU.add))
        B.do("vector", lambda e: e.tensor_scalar(R2, R0, 128.0, 128.0, ALU.is_ge, ALU.mult))
        B.do("vector", lambda e: e.tensor_tensor(R1, R0, R2, ALU.subtract))
        for b in (64.0, 32.0, 16.0, 8.0, 4.0, 2.0, 1.0):
            B.do("vector", lambda e, b=b: e.tensor_scalar(R2, R1, b, b, ALU.is_ge, ALU.mult))
            B.do("vector", lambda e: e.tensor_tensor(R1, R1, R2, ALU.subtract))
        B.do("vector", lambda e: e.tensor_tensor(R2, R0, R1, ALU.subtract))
        return B.do("vector", lambda e: e.tensor_scalar(dst_ap, R2, 128.0, None, ALU.subtract), tok=True)

    for cc in range(NC if p0cut >= 5 else 0):
        B.do("vector", lambda e, c=cc: e.tensor_tensor(
            p0rr0, rhsf[:, c * C:(c + 1) * C], isr_b, ALU.mult))
        emit_round(qrhs[:, cc * C:(cc + 1) * C], p0rr0, p0rr, p0rt)
    t_qrhs = B.tok("vector")

    if "srb" in dbg:
        B.waits("sync", [t_isr])
        B.do("sync", lambda e: e.dma_start(out=dbg_out["srb"], in_=sr_b))
    if "src" in dbg:
        B.waits("sync", [t_isr])
        B.do("sync", lambda e: e.dma_start(out=dbg_out["src"], in_=sr_cols[:]))
    if "s8d" in dbg:
        B.waits("sync", [t_isr])
        B.do("sync", lambda e: e.dma_start(out=dbg_out["s8d"], in_=s8[:]))
    if "rhsfd" in dbg:
        B.waits("sync", [t_rhs])
        B.do("sync", lambda e: e.dma_start(out=dbg_out["rhsfd"], in_=rhsf))
    if "qrhs" in dbg:
        B.waits("sync", [t_qrhs])
        B.do("sync", lambda e: e.dma_start(
            out=dbg_out["qrhs"].rearrange("(cc p) c -> p cc c", p=128),
            in_=qrhs[:].rearrange("p (cc c) -> p cc c", cc=NC)))

    def emit_ln(xbuf, xn_out, dep_toks, cb, prev_tsq):
        B.waits("vector", dep_toks + ([prev_tsq] if prev_tsq else []))
        B.do("vector", lambda e: e.bn_stats(stats[:, 0:6], xbuf[:, 0:512]))
        B.do("vector", lambda e: e.bn_stats(stats[:, 6:12], xbuf[:, 512:1024]))
        tagg = B.do("vector", lambda e, cb=cb: e.bn_aggr(colsA[:, cb:cb + 2], stats[:, 0:12]), tok=True)
        B.waits("scalar", [tagg])
        tsq = B.do("scalar", lambda e, cb=cb: e.activation(
            colsA[:, cb + 2:cb + 3], colsA[:, cb + 1:cb + 2], AF.Sqrt,
            bias=ln_eps_col, scale=1.0), tok=True)
        B.waits("vector", [tsq])
        trc = B.do("vector", lambda e, cb=cb: e.reciprocal(
            colsA[:, cb + 3:cb + 4], colsA[:, cb + 2:cb + 3]), tok=True)
        B.wait("vector", trc)
        tn = B.do("vector", lambda e, cb=cb: e.tensor_scalar(
            xn_out, xbuf, colsA[:, cb:cb + 1], colsA[:, cb + 3:cb + 4],
            ALU.subtract, ALU.mult), tok=True)
        return tn, tsq

    if stop >= 1:
        prev_tsq = None
        ev1_tok = {}
        tp0s, tp1s = {}, {}
        t_xn_toks = [None] * NT
        prev_ffn = {}
        for tc in range(NT + 1):
            if tc < NT:
                bi = tc % 2
                cb = bi * 4
                B.waits("sync", [t_qrhs, t_xn_toks[tc - 2] if tc >= 2 else None,
                                 prev_ffn[tc - 2][0] if tc >= 2 else None])
                t_xld = B.do("sync", lambda e, t=tc, b=bi: e.dma_start(
                    out=xt[b], in_=x_d[t * 128:(t + 1) * 128, :]), tok=True)
                t_xn, prev_tsq = emit_ln(xt[bi], xn_s[bi], [t_xld], cb, prev_tsq)
                t_xn_toks[tc] = t_xn
                B.do("vector", lambda e, b=bi: e.reduce_max(
                    colsB[:, 0:1], xn_s[b], AX.X, apply_absolute_value=True))
                tsl = B.do("vector", lambda e, t=tc: e.tensor_scalar(
                    sl_all[:, t:t + 1], colsB[:, 0:1], 1e-12, r127[:], ALU.max, ALU.mult), tok=True)
                B.wait("vector", tsl)
                tisl = B.do("vector", lambda e, t=tc: e.reciprocal(
                    isl_all[:, t:t + 1], sl_all[:, t:t + 1]), tok=True)
                B.wait("vector", tisl)
                B.waits("vector", [tp1s.get(tc - 2)])
                B.do("vector", lambda e, b=bi, t=tc: e.tensor_scalar(
                    rr0, xn_s[b], isl_all[:, t:t + 1], None, ALU.mult))
                tq = emit_round(tb4[2 + bi][:])
                B.waits("scalar", [t_xn, tp0s.get(tc - 2)])
                txb = B.do("scalar", lambda e, b=bi: e.copy(tb4[b][:], xn_s[b]), tok=True)
                B.waits("tensor", [txb, tq, t_consts, ev1_tok.get(tc - 1)])
                for cc in range(NC):
                    B.do("tensor", lambda e, c=cc, b=bi: tp(e,
                        PTB[0][:, c * 128:(c + 1) * 128], tb4[b][:, c * 128:(c + 1) * 128], identb[:],
                        c == 0, c == NC - 1))
                tp0s[tc] = B.mark("tensor")
                for cc in range(NC):
                    B.do("tensor", lambda e, c=cc, b=bi: tp(e,
                        PTB[1][:, c * 128:(c + 1) * 128], tb4[2 + b][:, c * 128:(c + 1) * 128], identb[:],
                        c == 0, c == NC - 1))
                tp1s[tc] = B.mark("tensor")
                B.waits("vector", [tp0s[tc], tp1s[tc]])
                B.do("vector", lambda e, t=tc: e.tensor_copy(
                    xnT.rearrange("p (cc tt) -> p cc tt", cc=NC)[:, :, t * 128:(t + 1) * 128],
                    PTB[0][:].rearrange("p (cc tt) -> p cc tt", cc=NC)))
                ev1_tok[tc] = B.do("vector", lambda e, t=tc: e.tensor_copy(
                    qxT.rearrange("p (cc tt) -> p cc tt", cc=NC)[:, :, t * 128:(t + 1) * 128],
                    PTB[1][:].rearrange("p (cc tt) -> p cc tt", cc=NC)), tok=True)
                if "xn" in dbg:
                    B.waits("sync", [t_xn])
                    B.do("sync", lambda e, t=tc, b=bi: e.dma_start(
                        out=dbg_out["xn"][t * 128:(t + 1) * 128, :], in_=xn_s[b]))
                if "qx" in dbg:
                    B.waits("sync", [tq])
                    B.do("sync", lambda e, t=tc, b=bi: e.dma_start(
                        out=dbg_out["qx"][t * 128:(t + 1) * 128, :], in_=tb4[2 + b][:]))
            if stop >= 2 and tc >= 1:
                t2 = tc - 1
                b2 = t2 % 2
                bb = b2 * 2
                deps = [t_qrhs, ev1_tok[t2]]
                if t2 >= 2:
                    deps.append(prev_ffn[t2 - 2][1])
                B.waits("tensor", deps)
                for cc in range(NC):
                    for jb in range(2):
                        B.do("tensor", lambda e, c=cc, j=jb, b=bb, t=t2: e.matmul(
                            PBK[b + j][:], chT(qxT, c, t * 128, 128),
                            qrhs[:, c * C + j * 512:c * C + (j + 1) * 512],
                            start=(c == 0), stop=(c == NC - 1)))
                tmm = B.mark("tensor")
                B.waits("vector", [tmm])
                for jb in range(2):
                    B.do("vector", lambda e, j=jb, b=bb, t=t2: e.scalar_tensor_tensor(
                        rt[:, j * 512:(j + 1) * 512], PBK[b + j][:], sl_all[:, t:t + 1],
                        sr_b[:, j * 512:(j + 1) * 512], ALU.mult, ALU.mult))
                tev = B.do("vector", lambda e, b=b2: e.tensor_tensor(
                    xt[b], xt[b], rt, ALU.add), tok=True)
                B.waits("sync", [tev])
                tst = B.do("sync", lambda e, t=t2, b=b2: e.dma_start(
                    out=xs_d[t * 128:(t + 1) * 128, :], in_=xt[b]), tok=True)
                prev_ffn[t2] = (tst, tev)
                if "xffn" in dbg:
                    B.do("sync", lambda e, t=t2, b=b2: e.dma_start(
                        out=dbg_out["xffn"][t * 128:(t + 1) * 128, :], in_=xt[b]))
        if "sl" in dbg:
            B.waits("sync", [B.tok("vector")])
            B.do("sync", lambda e: e.dma_start(out=dbg_out["sl"], in_=sl_all[:]))
        if "xnT" in dbg:
            B.waits("sync", [B.tok("vector")])
            B.do("sync", lambda e: e.dma_start(out=dbg_out["xnT"], in_=xnT))
    if stop >= 2:
        t_ffn_done_s = B.tok("sync")
        t_ffn_done_v = B.tok("vector")

    if stop >= 3:
        def rms_block(src_f32, w_rep, dst_bf, extra_scale, dep_tok, sq_buf):
            B.waits("scalar", [dep_tok])
            tsq = B.do("scalar", lambda e: e.activation(sq_buf, src_f32, AF.Square,
                                                        bias=0.0, scale=1.0), tok=True)
            B.waits("vector", [tsq])
            trd = B.do("vector", lambda e: e.tensor_reduce(
                colsB[:, 0:16].rearrange("p (h o) -> p h o", o=1),
                sq_buf.rearrange("p (h d) -> p h d", h=H), AX.X, ALU.add), tok=True)
            B.waits("scalar", [trd])
            ts1 = B.do("scalar", lambda e: e.activation(
                colsA[:, 0:8], colsB[:, 0:8], AF.Sqrt, bias=rms_eps_col, scale=1.0 / DH), tok=True)
            ts2 = B.do("scalar", lambda e: e.activation(
                stats[:, 0:8], colsB[:, 8:16], AF.Sqrt, bias=rms_eps_col, scale=1.0 / DH), tok=True)
            B.waits("vector", [ts1, ts2])
            B.do("vector", lambda e: e.reciprocal(stats[:, 8:16], colsA[:, 0:8]))
            trc = B.do("vector", lambda e: e.reciprocal(colsB[:, 0:8], stats[:, 0:8]), tok=True)
            B.wait("vector", trc)
            if extra_scale != 1.0:
                B.do("vector", lambda e: e.tensor_scalar(
                    stats[:, 8:16], stats[:, 8:16], extra_scale, None, ALU.mult))
                tsc = B.do("vector", lambda e: e.tensor_scalar(
                    colsB[:, 0:8], colsB[:, 0:8], extra_scale, None, ALU.mult), tok=True)
                B.wait("vector", tsc)
            B.do("vector", lambda e: e.tensor_tensor(
                sq_buf.rearrange("p (h d) -> p h d", h=H), src_f32.rearrange("p (h d) -> p h d", h=H),
                w_rep[:].unsqueeze(1).broadcast_to([128, H, DH]),
                ALU.mult))
            B.do("vector", lambda e: e.tensor_tensor(
                dst_bf.rearrange("p (h d) -> p h d", h=H)[:, 0:8],
                sq_buf.rearrange("p (h d) -> p h d", h=H)[:, 0:8],
                stats[:, 8:16].unsqueeze(2).broadcast_to([128, 8, DH]),
                ALU.mult))
            return B.do("vector", lambda e: e.tensor_tensor(
                dst_bf.rearrange("p (h d) -> p h d", h=H)[:, 8:16],
                sq_buf.rearrange("p (h d) -> p h d", h=H)[:, 8:16],
                colsB[:, 0:8].unsqueeze(2).broadcast_to([128, 8, DH]),
                ALU.mult), tok=True)

        def ssq_block(src_bf, dst_cols, sq_buf, dep_tok):
            B.waits("scalar", [dep_tok])
            tsq = B.do("scalar", lambda e: e.activation(sq_buf, src_bf, AF.Square,
                                                        bias=0.0, scale=1.0), tok=True)
            B.waits("vector", [tsq])
            return B.do("vector", lambda e: e.tensor_reduce(
                dst_cols.rearrange("p (h o) -> p h o", o=1),
                sq_buf.rearrange("p (h d) -> p h d", h=H), AX.X, ALU.add), tok=True)

        B.waits("gpsimd", [t_ffn_done_v, t_ffn_done_s])
        t_wqk = B.do("gpsimd", lambda e: e.dma_start(
            out=wqkb[:].rearrange("p (cc c) -> p cc c", cc=NC),
            in_=qkvwT_d[:, 0:2 * C].rearrange("(cc p) c -> p cc c", p=128)), tok=True)

        prev_qkv = {}
        tqk_ev = {}
        for tc in range(NT):
            bi = tc % 2
            deps = [ev1_tok[NT - 1], t_ffn_done_v, t_wqk]
            if tc >= 2:
                deps.append(prev_qkv[tc - 2])
            B.waits("tensor", deps)
            for cc in range(NC):
                for part in range(2):
                    for jb in range(2):
                        B.do("tensor", lambda e, c=cc, p=part, j=jb, t=tc: e.matmul(
                            PBK[p * 2 + j][:], chT(xnT, c, t * 128, 128),
                            wqkb[:, c * 2 * C + p * C + j * 512:c * 2 * C + p * C + (j + 1) * 512],
                            start=(c == 0), stop=(c == NC - 1)))
            tmm = B.mark("tensor")
            B.waits("vector", [tmm])
            B.do("vector", lambda e: e.tensor_copy(p3qf[:, 0:512], PBK[0][:]))
            B.do("vector", lambda e: e.tensor_copy(p3qf[:, 512:1024], PBK[1][:]))
            B.do("vector", lambda e: e.tensor_copy(p3kf[:, 0:512], PBK[2][:]))
            tevv = B.do("vector", lambda e: e.tensor_copy(p3kf[:, 512:1024], PBK[3][:]), tok=True)
            prev_qkv[tc] = tevv

            tqs = rms_block(p3qf, qnw, tb4[2 + bi][:], 0.125, tevv, p3sq)
            tssq = ssq_block(tb4[2 + bi][:], ssq_q[:, tc * H:(tc + 1) * H], p3sq, tqs)
            tks = rms_block(p3kf, knw, tb4[bi][:], 1.0, tssq, p3sq)
            tssk = ssq_block(tb4[bi][:], ssq_k[:, tc * H:(tc + 1) * H], p3sq, tks)

            B.waits("tensor", [tqs, tks, tqk_ev.get(tc - 1)])
            for cc in range(NC):
                B.do("tensor", lambda e, c=cc, b=bi: tp(e,
                    PTB[0][:, c * 128:(c + 1) * 128], tb4[2 + b][:, c * 128:(c + 1) * 128], identb[:],
                    c == 0, c == NC - 1))
            tq0 = B.mark("tensor")
            for cc in range(NC):
                B.do("tensor", lambda e, c=cc, b=bi: tp(e,
                    PTB[1][:, c * 128:(c + 1) * 128], tb4[b][:, c * 128:(c + 1) * 128], identb[:],
                    c == 0, c == NC - 1))
            tq1 = B.mark("tensor")
            B.waits("vector", [tq0, tq1, tssk])
            B.do("vector", lambda e, t=tc: e.tensor_copy(
                qsT.rearrange("p (cc tt) -> p cc tt", cc=NC)[:, :, t * 128:(t + 1) * 128],
                PTB[0][:].rearrange("p (cc tt) -> p cc tt", cc=NC)))
            tqk_ev[tc] = B.do("vector", lambda e, t=tc: e.tensor_copy(
                ksT.rearrange("p (cc tt) -> p cc tt", cc=NC)[:, :, t * 128:(t + 1) * 128],
                PTB[1][:].rearrange("p (cc tt) -> p cc tt", cc=NC)), tok=True)
        t_p3a_v = B.tok("vector")

        t_vw = [None] * 8
        t_vw_used = {}
        B.waits("gpsimd", [t_ffn_done_v])
        for v0 in (0, 1):
            t_vw[v0] = B.do("gpsimd", lambda e, v=v0, w=v0: e.dma_start(
                out=vwstr[w][:], in_=vwr_d[v]), tok=True)
        prev_vev = {}
        for vd in range(8):
            wb = vd % 2
            for tb in range(4):
                key = (vd, tb)
                B.waits("tensor", [t_vw[vd], t_p3a_v if vd == 0 and tb == 0 else None,
                                   prev_vev.get((vd, tb - 2)) or prev_vev.get((vd - 1, 2 + tb))])
                bank = PBK[(vd * 4 + tb) % 2]
                for cc in range(NC):
                    B.do("tensor", lambda e, c=cc, v=vd, t=tb, w=wb, bk=bank: e.matmul(
                        bk[:], vwstr[w][:, c * 128:(c + 1) * 128],
                        chT(xnT, c, t * 512, 512),
                        start=(c == 0), stop=(c == NC - 1)))
                tmm = B.mark("tensor")
                if tb == 3:
                    t_vw_used[vd] = tmm
                B.waits("vector", [tmm])
                tev = B.do("vector", lambda e, bk=bank: e.tensor_copy(aoev[:], bk[:]), tok=True)
                B.waits("tensor", [tev])
                for q4 in range(4):
                    B.do("tensor", lambda e, q=q4: tp(e,
                        PTB[0][:, q * 128:(q + 1) * 128], aoev[:, q * 128:(q + 1) * 128],
                        identb[:], q == 0, q == 3))
                tpe = B.mark("tensor")
                B.waits("vector", [tpe])
                tev2 = B.do("vector", lambda e, v=vd, t=tb: e.tensor_copy(
                    v_sb[:].rearrange("p (tt c) -> p tt c", tt=NT)[:, t * 4:(t + 1) * 4,
                                                                  v * 128:(v + 1) * 128],
                    PTB[0][:, 0:512].rearrange("p (q c) -> p q c", q=4)), tok=True)
                B.waits("tensor", [tev2])
                prev_vev[key] = tev2
            if vd + 2 < 8:
                B.waits("gpsimd", [t_vw_used[vd]])
                t_vw[vd + 2] = B.do("gpsimd", lambda e, v=vd + 2, w=(vd % 2): e.dma_start(
                    out=vwstr[w][:], in_=vwr_d[v]), tok=True)
        t_p3_done = B.tok("vector")

        B.do("vector", lambda e: e.tensor_copy(stats[:, 0:16], ssq_k[:, 0:16]))
        tmax = None
        for tc in range(1, NT):
            tmax = B.do("vector", lambda e, t=tc: e.tensor_tensor(
                stats[:, 0:16], stats[:, 0:16], ssq_k[:, t * H:(t + 1) * H], ALU.max), tok=True)
        B.waits("tensor", [tmax])
        tpe = B.do("tensor", lambda e: e.transpose(PBK[0][0:16, 0:128], stats[:, 0:16], ident[:]), tok=True)
        B.waits("vector", [tpe])
        tev = B.do("vector", lambda e: e.tensor_copy(rt[0:16, 0:128], PBK[0][0:16, 0:128]), tok=True)
        B.wait("vector", tev)
        tred = B.do("vector", lambda e: e.reduce_max(rt[0:16, 128:129], rt[0:16, 0:128], AX.X), tok=True)
        B.waits("tensor", [tred])
        tpe = B.do("tensor", lambda e: e.transpose(
            PBK[1][0:1, 0:16], rt[0:16, 128:129], ident[0:16, 0:16]), tok=True)
        B.waits("vector", [tpe])
        tev = B.do("vector", lambda e: e.tensor_copy(rt[0:1, 256:272], PBK[1][0:1, 0:16]), tok=True)
        B.wait("vector", tev)
        B.do("vector", lambda e: e.tensor_copy(stats[:, 0:16], ssq_q[:, 0:16]))
        tmaxq = None
        for tc in range(1, NT):
            tmaxq = B.do("vector", lambda e, t=tc: e.tensor_tensor(
                stats[:, 0:16], stats[:, 0:16], ssq_q[:, t * H:(t + 1) * H], ALU.max), tok=True)
        B.waits("tensor", [tmaxq])
        tpe = B.do("tensor", lambda e: e.transpose(PBK[0][0:16, 0:128], stats[:, 0:16], ident[:]), tok=True)
        B.waits("vector", [tpe])
        tev = B.do("vector", lambda e: e.tensor_copy(rt[0:16, 0:128], PBK[0][0:16, 0:128]), tok=True)
        B.wait("vector", tev)
        tred = B.do("vector", lambda e: e.reduce_max(rt[0:16, 128:129], rt[0:16, 0:128], AX.X), tok=True)
        B.waits("tensor", [tred])
        tpe = B.do("tensor", lambda e: e.transpose(
            PBK[1][0:1, 0:16], rt[0:16, 128:129], ident[0:16, 0:16]), tok=True)
        B.waits("vector", [tpe])
        tev = B.do("vector", lambda e: e.tensor_copy(rt[0:1, 272:288], PBK[1][0:1, 0:16]), tok=True)
        B.wait("vector", tev)
        B.do("vector", lambda e: e.tensor_scalar(
            rt[0:1, 288:304], rt[0:1, 256:272], 0.0625, None, ALU.mult))
        B.do("vector", lambda e: e.scalar_tensor_tensor(
            rt[0:1, 288:304], rt[0:1, 272:288], 4.0, rt[0:1, 288:304], ALU.mult, ALU.add))
        tcr = B.do("vector", lambda e: e.tensor_scalar(
            rt[0:1, 288:304], rt[0:1, 288:304], -1.0, None, ALU.mult), tok=True)
        B.waits("tensor", [tcr])
        tpe = B.do("tensor", lambda e: e.matmul(
            PBK[2][:, 0:16], onesrow[:], rt[0:1, 288:304], start=True, stop=True), tok=True)
        B.waits("vector", [tpe])
        t_negc2 = B.do("vector", lambda e: e.tensor_copy(negc[:, 0:16], PBK[2][:, 0:16]), tok=True)

        if "qs" in dbg:
            B.waits("sync", [B.tok("vector")])
            B.do("sync", lambda e: e.dma_start(out=dbg_out["qs"], in_=qsT))
        if "ks" in dbg:
            B.waits("sync", [B.tok("vector")])
            B.do("sync", lambda e: e.dma_start(out=dbg_out["ks"], in_=ksT))
        if "v0" in dbg:
            B.waits("sync", [B.tok("vector")])
            B.do("sync", lambda e: e.dma_start(out=dbg_out["v0"], in_=v_sb))
        if "negc" in dbg:
            B.waits("sync", [t_negc2])
            B.do("sync", lambda e: e.dma_start(out=dbg_out["negc"], in_=negc[:]))

    if stop >= 4:
        t_aodma = {}
        t_exp = {}
        t_avz = {}
        last_av_ev = None
        zbanks = [2, 3, 6, 7]
        for h in range(H):
            hc, off = h // 2, (h % 2) * 64
            avb = 4 + (h % 2)
            for ib in range(4):
                key = (h, ib)
                for step in range(18):
                    if step < 16:
                        jc = step
                        sbk = PBK[jc % 2]
                        slot = jc % 8
                        B.waits("tensor", [t_negc2,
                                           t_p3_done if (h == 0 and ib == 0 and jc == 0) else None,
                                           t_exp.get((h, ib, jc - 2))
                                           or t_exp.get((h, ib - 1, 14 + jc))
                                           or t_exp.get((h - 1, 3, 14 + jc))])
                        tS = B.do("tensor", lambda e, j=jc, i=ib, hc2=hc, o=off, bk=sbk: e.matmul(
                            bk[:], chT(ksT, hc2, j * 128, 128, o, 64),
                            chT(qsT, hc2, i * 512, 512, o, 64),
                            start=True, stop=True), tok=True)
                        B.waits("scalar", [tS, t_avz.get((h, ib, jc - 8))
                                           or t_avz.get((h, ib - 1, 8 + jc))
                                           or t_avz.get((h - 1, 3, 8 + jc))])
                        t_exp[(h, ib, jc)] = B.do("scalar", lambda e, s=slot, hh=h, bk=sbk: e.activation(
                            attnT_s[:, s * 512:(s + 1) * 512], bk[:], AF.Exp,
                            bias=negc[:, hh:hh + 1], scale=1.0), tok=True)
                    if step >= 2:
                        jc = step - 2
                        slot = jc % 8
                        B.waits("tensor", [t_exp[(h, ib, jc)],
                                           last_av_ev if jc == 0 else None])
                        B.do("tensor", lambda e, j=jc, s=slot, hh=h, o=off, b=avb: e.matmul(
                            PBK[b][o:o + 64, :], v_sb[:, j * C + hh * DH:j * C + (hh + 1) * DH],
                            attnT_s[:, s * 512:(s + 1) * 512],
                            start=(j == 0), stop=(j == 15),
                            **({"tile_position": (0, 64)} if o else {})))
                        B.do("tensor", lambda e, j=jc, s=slot, hh=h, i=ib: e.matmul(
                            PBK[zbanks[i]][0:16, 0:512], selz[:, hh * 16:(hh + 1) * 16],
                            attnT_s[:, s * 512:(s + 1) * 512],
                            start=(hh == 0 and j == 0), stop=(hh == H - 1 and j == 15)))
                        t_avz[(h, ib, jc)] = B.mark("tensor")
                tavd = t_avz[(h, ib, 15)]
                B.waits("vector", [tavd, t_aodma.get((h, ib - 1)) if ib > 0 else t_aodma.get((h - 1, 3))])
                tev = B.do("vector", lambda e, o=off, b=avb: e.tensor_copy(
                    aoev[o:o + 64, :], PBK[b][o:o + 64, :]), tok=True)
                last_av_ev = tev
                B.waits("sync", [tev])
                t_aodma[key] = B.do("sync", lambda e, hcc=hc, o=off, i=ib: e.dma_start(
                    out=ao_d[hcc, o:o + 64, i * 512:(i + 1) * 512], in_=aoev[o:o + 64, :]), tok=True)
        t_zev = {}
        for ib in range(4):
            B.waits("vector", [t_avz[(H - 1, ib, 15)]])
            B.do("vector", lambda e, i=ib: e.reciprocal(
                PBK[zbanks[i]][0:16, 0:512], PBK[zbanks[i]][0:16, 0:512]))
            t_zev[ib] = B.do("vector", lambda e, i=ib: e.tensor_copy(
                izTb[:, i * 512:(i + 1) * 512], PBK[zbanks[i]][0:16, 0:512]), tok=True)
        t_izT = t_zev[3]

        if "ao" in dbg:
            B.waits("sync", [t_aodma[(H - 1, 3)]])
            B.do("sync", lambda e: e.dma_start(out=dbg_out["ao"], in_=ao_d))
    if stop >= 5:
        B.waits("gpsimd", [B.tok("tensor")])
        t_projw = B.do("gpsimd", lambda e: e.dma_start(
            out=projw[:].rearrange("p (cc c) -> p cc c", cc=NC),
            in_=projT_d.rearrange("(cc p) c -> p cc c", p=128)), tok=True)
        prev_p5 = {}
        t_hi_ev = {}
        t_lo_used = {}
        t_lnq5 = {}
        t_izb_used = None
        t_xn1_toks = [None] * NT
        for tc in range(NT):
            bi = tc % 2
            cb = bi * 4
            B.waits("sync", [t_aodma[(H - 1, 3)]] + ([prev_p5[tc - 2][0]] if tc >= 2 else []))
            tld = B.do("sync", lambda e, t=tc, b=bi: e.dma_start(
                out=aold[b][:].rearrange("p (hc t) -> p hc t", hc=NC),
                in_=ao_d[:, :, t * 128:(t + 1) * 128].rearrange("hc p t -> p hc t")), tok=True)
            B.waits("tensor", [t_izT, tld] + ([t_izb_used] if t_izb_used else []))
            for hcc in range(NC):
                B.do("tensor", lambda e, hc2=hcc, t=tc: e.matmul(
                    PBK[5][:, 0:128], selallb[:, hc2 * 128:(hc2 + 1) * 128],
                    izTb[:, t * 128:(t + 1) * 128], start=True, stop=True))
                tpe = B.mark("tensor")
                B.waits("vector", [tpe])
                tev = B.do("vector", lambda e, hc2=hcc: e.tensor_copy(
                    izbb[:, hc2 * 128:(hc2 + 1) * 128], PBK[5][:, 0:128]), tok=True)
                B.waits("tensor", [tev])
            tsc = B.do("vector", lambda e, b=bi: e.tensor_tensor(
                aold[b][:], aold[b][:], izbb[:, 0:1024], ALU.mult), tok=True)
            t_izb_used = tsc
            B.waits("tensor", [tsc, t_projw, t_pb]
                    + ([prev_p5[tc - 2][1]] if tc >= 2 else []))
            for jb in range(2):
                B.do("tensor", lambda e, j=jb: e.matmul(
                    PBK[j][:], onesrowb[:], pb_sb[0:1, j * 512:(j + 1) * 512],
                    start=True, stop=False))
            for hcc in range(NC):
                for jb in range(2):
                    B.do("tensor", lambda e, hc2=hcc, j=jb, b=bi: e.matmul(
                        PBK[j][:], aold[b][:, hc2 * 128:(hc2 + 1) * 128],
                        projw[:, hc2 * C + j * 512:hc2 * C + (j + 1) * 512],
                        start=False, stop=(hc2 == NC - 1)))
            tmm = B.mark("tensor")
            B.waits("sync", [t_ffn_done_s] + ([prev_p5[tc - 2][2]] if tc >= 2 else []))
            txld = B.do("sync", lambda e, t=tc, b=bi: e.dma_start(
                out=xt[b], in_=xs_d[t * 128:(t + 1) * 128, :]), tok=True)
            B.waits("vector", [tmm, txld])
            B.do("vector", lambda e, b=bi: e.tensor_tensor(
                xt[b][:, 0:512], xt[b][:, 0:512], PBK[0][:], ALU.add))
            tx1 = B.do("vector", lambda e, b=bi: e.tensor_tensor(
                xt[b][:, 512:1024], xt[b][:, 512:1024], PBK[1][:], ALU.add), tok=True)
            B.waits("sync", [tx1])
            tst = B.do("sync", lambda e, t=tc, b=bi: e.dma_start(
                out=xs_d[t * 128:(t + 1) * 128, :], in_=xt[b]), tok=True)
            if "x1" in dbg:
                B.do("sync", lambda e, t=tc, b=bi: e.dma_start(
                    out=dbg_out["x1"][t * 128:(t + 1) * 128, :], in_=xt[b]))
            t_xn1, _p = emit_ln(xt[bi], xn_s[bi], [tx1], cb, t_lnq5.get(tc - 2))
            t_lnq5[tc] = _p
            t_xn1_toks[tc] = t_xn1
            B.waits("scalar", [t_xn1, t_hi_ev.get(tc - 2)])
            thi = B.do("scalar", lambda e, b=bi: e.copy(tb4[b][:], xn_s[b]), tok=True)
            B.waits("vector", [thi])
            tlo = B.do("vector", lambda e, b=bi: e.tensor_tensor(
                tb4[2 + b][:], xn_s[b], tb4[b][:], ALU.subtract), tok=True)
            B.waits("tensor", [thi, tlo, t_hi_ev.get(tc - 1)])
            for cc in range(NC):
                B.do("tensor", lambda e, c=cc, b=bi: tp(e,
                    PTB[0][:, c * 128:(c + 1) * 128], tb4[b][:, c * 128:(c + 1) * 128], identb[:],
                    c == 0, c == NC - 1))
            th0 = B.mark("tensor")
            for cc in range(NC):
                B.do("tensor", lambda e, c=cc, b=bi: tp(e,
                    PTB[1][:, c * 128:(c + 1) * 128], tb4[2 + b][:, c * 128:(c + 1) * 128], identb[:],
                    c == 0, c == NC - 1))
            th1 = B.mark("tensor")
            B.waits("vector", [th0, th1])
            B.do("vector", lambda e, t=tc: e.tensor_copy(
                xnT.rearrange("p (cc tt) -> p cc tt", cc=NC)[:, :, t * 128:(t + 1) * 128],
                PTB[0][:].rearrange("p (cc tt) -> p cc tt", cc=NC)))
            t_hi_ev[tc] = B.do("vector", lambda e, t=tc: e.tensor_copy(
                loT.rearrange("p (cc tt) -> p cc tt", cc=NC)[:, :, t * 128:(t + 1) * 128],
                PTB[1][:].rearrange("p (cc tt) -> p cc tt", cc=NC)), tok=True)
            prev_p5[tc] = (tld, tmm, tst)
        t_p5_v = B.tok("vector")
        if "xnT1" in dbg:
            B.waits("sync", [t_p5_v])
            B.do("sync", lambda e: e.dma_start(out=dbg_out["xnT1"], in_=xnT))
        if "loT1" in dbg:
            B.waits("sync", [t_p5_v])
            B.do("sync", lambda e: e.dma_start(out=dbg_out["loT1"], in_=loT))
        t_p5_s = B.tok("sync")

    if stop >= 6:
        t_wte = [None] * NT
        prev_g = {}
        for tc in range(NT):
            B.waits("tensor", [t_hi_ev[NT - 1], t_glo] + ([prev_g.get(tc - 2)] if tc >= 2 else []))
            gbk = PBK[2 + tc % 2]
            for cc in range(NC):
                B.do("tensor", lambda e, c=cc, t=tc, bk=gbk: e.matmul(
                    bk[:, 0:8], chT(xnT, c, t * 128, 128), ghi[:, c * E:(c + 1) * E],
                    start=(c == 0), stop=False))
                B.do("tensor", lambda e, c=cc, t=tc, bk=gbk: e.matmul(
                    bk[:, 0:8], chT(xnT, c, t * 128, 128), glo[:, c * E:(c + 1) * E],
                    start=False, stop=False))
            for cc in range(NC):
                B.do("tensor", lambda e, c=cc, t=tc, bk=gbk: e.matmul(
                    bk[:, 0:8], chT(loT, c, t * 128, 128), ghi[:, c * E:(c + 1) * E],
                    start=False, stop=(c == NC - 1)))
            tgm = B.mark("tensor")
            B.waits("vector", [tgm])
            tz0 = B.do("vector", lambda e, bk=gbk: e.tensor_copy(zgbuf[:, 0:8], bk[:, 0:8]), tok=True)
            prev_g[tc] = tz0
            B.wait("vector", tz0)
            if "glog" in dbg:
                B.do("vector", lambda e, t=tc: e.tensor_copy(
                    wte[:, 0:1] if False else zs2[:], zgbuf[:, 0:8]))
                B.waits("sync", [B.tok("vector")])
                B.do("sync", lambda e, t=tc: e.dma_start(
                    out=dbg_out["glog"][:, t * E:(t + 1) * E], in_=zgbuf[:, 0:8]))
            t1 = B.do("vector", lambda e: e.reduce_max(zgbuf[:, 8:9], zgbuf[:, 0:8], AX.X), tok=True)
            B.wait("vector", t1)
            t2 = B.do("vector", lambda e: e.tensor_scalar(
                zgbuf[:, 16:24], zgbuf[:, 0:8], zgbuf[:, 8:9], -1e30, ALU.is_equal, ALU.mult), tok=True)
            B.wait("vector", t2)
            t3 = B.do("vector", lambda e: e.tensor_tensor(
                zgbuf[:, 16:24], zgbuf[:, 16:24], zgbuf[:, 0:8], ALU.add), tok=True)
            B.wait("vector", t3)
            t4 = B.do("vector", lambda e: e.reduce_max(zgbuf[:, 9:10], zgbuf[:, 16:24], AX.X), tok=True)
            B.wait("vector", t4)
            t5 = B.do("vector", lambda e: e.tensor_scalar(
                zgbuf[:, 10:11], zgbuf[:, 8:9], -1.0, None, ALU.mult), tok=True)
            B.waits("scalar", [t5])
            t6 = B.do("scalar", lambda e: e.activation(
                zgbuf[:, 16:24], zgbuf[:, 0:8], AF.Exp, bias=zgbuf[:, 10:11], scale=1.0), tok=True)
            B.waits("vector", [t6])
            t7 = B.do("vector", lambda e: e.tensor_scalar(
                zs1[:], zgbuf[:, 0:8], zgbuf[:, 9:10], None, ALU.is_ge), tok=True)
            B.wait("vector", t7)
            t8 = B.do("vector", lambda e: e.tensor_tensor(
                zs2[:], zgbuf[:, 16:24], zs1[:], ALU.mult), tok=True)
            B.wait("vector", t8)
            t9 = B.do("vector", lambda e: e.tensor_reduce(zgbuf[:, 11:12], zs2[:], AX.X, ALU.add), tok=True)
            B.wait("vector", t9)
            t10 = B.do("vector", lambda e: e.reciprocal(zgbuf[:, 12:13], zgbuf[:, 11:12]), tok=True)
            B.wait("vector", t10)
            t_wte[tc] = B.do("vector", lambda e, t=tc: e.tensor_scalar(
                wte[:, t * E:(t + 1) * E], zs2[:], zgbuf[:, 12:13], None, ALU.mult), tok=True)
        t_p6_v = B.tok("vector")
        if "wte" in dbg:
            B.waits("sync", [t_p6_v])
            B.do("sync", lambda e: e.dma_start(out=dbg_out["wte"], in_=wte[:]))

    if stop >= 7:
        t_cs = B.do("sync", lambda e: e.dma_start(out=cosv, in_=cos_d[:]), tok=True)
        t_sn = B.do("sync", lambda e: e.dma_start(out=sinv, in_=sin_d[:]), tok=True)
        B.waits("sync", [t_p5_v, t_p6_v])

        def rope_tile(tc, src_f32, dst_bf, dep_toks):
            cs = cosv[:, tc * 32:(tc + 1) * 32].unsqueeze(1).broadcast_to([128, H, 32])
            sn = sinv[:, tc * 32:(tc + 1) * 32].unsqueeze(1).broadcast_to([128, H, 32])
            s3 = src_f32.rearrange("p (h d) -> p h d", h=H)
            d3 = dst_bf.rearrange("p (h d) -> p h d", h=H)
            rA = ropeA2.rearrange("p (h d) -> p h d", h=H, d=32)
            rB = ropeB2.rearrange("p (h d) -> p h d", h=H, d=32)
            B.waits("vector", dep_toks)
            B.do("vector", lambda e: e.tensor_tensor(rA, s3[:, :, 0:32], cs, ALU.mult))
            B.do("vector", lambda e: e.tensor_tensor(rB, s3[:, :, 32:64], sn, ALU.mult))
            B.do("vector", lambda e: e.tensor_tensor(d3[:, :, 0:32], rA, rB, ALU.subtract))
            B.do("vector", lambda e: e.tensor_tensor(rA, s3[:, :, 0:32], sn, ALU.mult))
            B.do("vector", lambda e: e.tensor_tensor(rB, s3[:, :, 32:64], cs, ALU.mult))
            return B.do("vector", lambda e: e.tensor_tensor(d3[:, :, 32:64], rA, rB, ALU.add), tok=True)

        B.waits("gpsimd", [t_p5_v, t_p6_v])
        t_lw = B.do("gpsimd", lambda e: e.dma_start(
            out=l1w[:].rearrange("p (cc c) -> p cc c", cc=NC),
            in_=wq_d.rearrange("(cc p) c -> p cc c", p=128)), tok=True)
        prev_q7 = {}
        tq7_ev = {}
        tq0s = {}
        for tc in range(NT):
            bi = tc % 2
            qb0 = 2 * (tc % 2)
            B.waits("tensor", [t_lw] + ([prev_q7.get(tc - 2)]))
            for cc in range(NC):
                for jb in range(2):
                    B.do("tensor", lambda e, c=cc, j=jb, t=tc, b0=qb0: e.matmul(
                        PBK[b0 + j][:], chT(xnT, c, t * 128, 128),
                        l1w[:, c * C + j * 512:c * C + (j + 1) * 512],
                        start=(c == 0), stop=(c == NC - 1)))
            tmm = B.mark("tensor")
            B.waits("vector", [tmm, tq7_ev.get(tc - 2)])
            B.do("vector", lambda e, b0=qb0: e.tensor_copy(l1qf[:, 0:512], PBK[b0][:]))
            tqe = B.do("vector", lambda e, b0=qb0: e.tensor_copy(l1qf[:, 512:1024], PBK[b0 + 1][:]), tok=True)
            prev_q7[tc] = tqe
            trp = rope_tile(tc, l1qf, tb4[bi][:], [tqe, tq0s.get(tc - 2)])
            B.waits("tensor", [trp, tq7_ev.get(tc - 1)])
            for cc in range(NC):
                B.do("tensor", lambda e, c=cc, b=bi: tp(e,
                    PTB[0][:, c * 128:(c + 1) * 128], tb4[b][:, c * 128:(c + 1) * 128], identb[:],
                    c == 0, c == NC - 1))
            tq0 = B.mark("tensor")
            tq0s[tc] = tq0
            B.waits("vector", [tq0])
            tq7_ev[tc] = B.do("vector", lambda e, t=tc: e.tensor_copy(
                qrT.rearrange("p (cc tt) -> p cc tt", cc=NC)[:, :, t * 128:(t + 1) * 128],
                PTB[0][:].rearrange("p (cc tt) -> p cc tt", cc=NC)), tok=True)
        t_qr_done = B.tok("vector")
        if "qrT" in dbg:
            B.waits("sync", [t_qr_done])
            B.do("sync", lambda e: e.dma_start(out=dbg_out["qrT"], in_=qrT))

        B.waits("gpsimd", [B.tok("tensor")])
        t_lwk = B.do("gpsimd", lambda e: e.dma_start(
            out=l1w[:].rearrange("p (cc c) -> p cc c", cc=NC),
            in_=wk_d.rearrange("(cc p) c -> p cc c", p=128)), tok=True)
        tz_ekl = B.do("vector", lambda e: e.memset(ekl[0:76, 0:C], 0.0), tok=True)
        prev_k7 = {}
        t_kg_ev = {}
        t_exk = {}
        t_kgpe = {}
        t_ekl_toks = []
        for tc in range(NT):
            bi = tc % 2
            qb0 = 2 * (tc % 2)
            B.waits("tensor", [t_lwk, prev_q7[NT - 2] if tc == 0 else None,
                               prev_q7[NT - 1] if tc == 1 else None] + ([prev_k7.get(tc - 2)]))
            for cc in range(NC):
                for jb in range(2):
                    B.do("tensor", lambda e, c=cc, j=jb, t=tc, b0=qb0: e.matmul(
                        PBK[b0 + j][:], chT(xnT, c, t * 128, 128),
                        l1w[:, c * C + j * 512:c * C + (j + 1) * 512],
                        start=(c == 0), stop=(c == NC - 1)))
            tmm = B.mark("tensor")
            B.waits("vector", [tmm, prev_k7.get(tc - 2)])
            B.do("vector", lambda e, b0=qb0: e.tensor_copy(l1qf[:, 0:512], PBK[b0][:]))
            tke = B.do("vector", lambda e, b0=qb0: e.tensor_copy(l1qf[:, 512:1024], PBK[b0 + 1][:]), tok=True)
            prev_k7[tc] = tke
            trp = rope_tile(tc, l1qf, tb4[bi][:],
                            [tke, tz_ekl, t_exk.get(tc - 2), t_kgpe.get(tc - 2)])
            if tc >= 1:
                B.waits("scalar", [trp])
                tex = B.do("scalar", lambda e, t=tc, b=bi: e.activation(
                    ekl[:, (t - 1) * C:t * C], tb4[b][:], AF.Exp, bias=0.0, scale=1.0), tok=True)
                t_exk[tc] = tex
                if tc == 1:
                    B.waits("vector", [tex])
                    tex = B.do("vector", lambda e: e.memset(ekl[0:76, 0:C], 0.0), tok=True)
                t_ekl_toks.append(tex)
            if tc <= 1:
                B.waits("tensor", [trp, t_kg_ev.get(tc - 1)])
                nrow = 128 if tc == 0 else 76
                for cc in range(NC):
                    B.do("tensor", lambda e, c=cc, b=bi, n=nrow: tp(e,
                        PTB[0][:, c * 128:c * 128 + n],
                        tb4[b][0:n, c * 128:(c + 1) * 128], identb[0:n, 0:n],
                        c == 0, c == NC - 1))
                tkp = B.mark("tensor")
                t_kgpe[tc] = tkp
                B.waits("vector", [tkp])
                co = 0 if tc == 0 else 128
                nrow2 = 128 if tc == 0 else 76
                t_kg_ev[tc] = B.do("vector", lambda e, c0=co, n=nrow2: e.tensor_copy(
                    kgT[:].rearrange("p (hc g) -> p hc g", hc=NC)[:, :, c0:c0 + n],
                    PTB[0][:].rearrange("p (hc g) -> p hc g", hc=NC)[:, :, 0:n]), tok=True)
        t_ekl_done_s = B.tok("scalar")
        t_ekl_done_v = B.tok("vector")
        t_kg_done = B.tok("vector")
        if "ekl" in dbg:
            B.waits("sync", [t_ekl_done_s, t_kg_done])
            B.do("sync", lambda e: e.dma_start(out=dbg_out["ekl"], in_=ekl[:, 0:15 * C]))
        if "kgT" in dbg:
            B.waits("sync", [t_kg_done])
            B.do("sync", lambda e: e.dma_start(
                out=dbg_out["kgT"].rearrange("p (hc g) -> p hc g", hc=NC),
                in_=kgT[:].rearrange("p (hc g) -> p hc g", hc=NC)[:, :, 0:G]))

        t_vw1 = [None] * 8
        t_vw1_used = {}
        B.waits("gpsimd", [B.tok("tensor")])
        for v0 in (0, 1):
            t_vw1[v0] = B.do("gpsimd", lambda e, v=v0, w=v0: e.dma_start(
                out=vw1str[w][:], in_=vwr1_d[v]), tok=True)
        prev_vev1 = {}
        for vd in range(8):
            wb = vd % 2
            for tb in range(4):
                key = (vd, tb)
                B.waits("tensor", [t_vw1[vd],
                                   prev_k7[NT - 2] if (vd == 0 and tb == 0) else None,
                                   prev_k7[NT - 1] if (vd == 0 and tb == 0) else None,
                                   prev_vev1.get((vd, tb - 2)) or prev_vev1.get((vd - 1, 2 + tb))])
                bank = PBK[(vd * 4 + tb) % 2 + 2]
                for cc in range(NC):
                    B.do("tensor", lambda e, c=cc, t=tb, w=wb, bk=bank: e.matmul(
                        bk[:], vw1str[w][:, c * 128:(c + 1) * 128],
                        chT(xnT, c, t * 512, 512),
                        start=(c == 0), stop=(c == NC - 1)))
                tmm = B.mark("tensor")
                if tb == 3:
                    t_vw1_used[vd] = tmm
                B.waits("vector", [tmm])
                tev = B.do("vector", lambda e, bk=bank: e.tensor_copy(aoev[:], bk[:]), tok=True)
                B.waits("tensor", [tev])
                for q4 in range(4):
                    B.do("tensor", lambda e, q=q4: tp(e,
                        PTB[0][:, q * 128:(q + 1) * 128], aoev[:, q * 128:(q + 1) * 128],
                        identb[:], q == 0, q == 3))
                tpe = B.mark("tensor")
                B.waits("vector", [tpe])
                tev2 = B.do("vector", lambda e, v=vd, t=tb: e.tensor_copy(
                    v_sb[:].rearrange("p (tt c) -> p tt c", tt=NT)[:, t * 4:(t + 1) * 4,
                                                                  v * 128:(v + 1) * 128],
                    PTB[0][:, 0:512].rearrange("p (q c) -> p q c", q=4)), tok=True)
                B.waits("tensor", [tev2])
                prev_vev1[key] = tev2
            if vd + 2 < 8:
                B.waits("gpsimd", [t_vw1_used[vd]])
                t_vw1[vd + 2] = B.do("gpsimd", lambda e, v=vd + 2, w=(vd % 2): e.dma_start(
                    out=vw1str[w][:], in_=vwr1_d[v]), tok=True)
        t_v1_done = B.tok("vector")
        if "v0" in dbg and stop >= 7:
            B.waits("sync", [t_v1_done])
            B.do("sync", lambda e: e.dma_start(out=dbg_out["v0"], in_=v_sb))

    if stop >= 8:
        tS_ev = {}
        for hc in range(NC):
            B.waits("tensor", [t_ekl_done_s, t_ekl_done_v, t_consts] + ([tS_ev.get(hc - 1)]))
            for lt in range(15):
                B.do("tensor", lambda e, l=lt, c=hc: e.matmul(
                    PBK[4][:, 0:1], ekl[:, l * C + c * 128:l * C + (c + 1) * 128],
                    onescolb[:], start=(l == 0), stop=(l == 14)))
            tS = B.mark("tensor")
            B.waits("vector", [tS])
            tS_ev[hc] = B.do("vector", lambda e, c=hc: e.tensor_copy(
                iS8[:, c:c + 1], PBK[4][:, 0:1]), tok=True)
        B.wait("vector", tS_ev[NC - 1])
        t_iS = B.do("vector", lambda e: e.reciprocal(iS8[:], iS8[:]), tok=True)
        B.wait("vector", t_iS)
        if "iS8" in dbg:
            B.waits("sync", [t_iS])
            B.do("sync", lambda e: e.dma_start(out=dbg_out["iS8"], in_=iS8[:]))

        t_lodma = {}
        t_kvev = {}
        t_kvev_s = {}
        t_tlm = {}
        t_tle = {}
        for hc in range(NC):
            kb = kv2[hc % 2]
            B.waits("vector", [t_tlm.get((hc - 2, 3))])
            tzm = B.do("vector", lambda e, k=kb: e.memset(k[:], 0.0), tok=True)
            for sub in range(2):
                h = 2 * hc + sub
                off = sub * 64
                bk = PBK[5 - sub]
                B.waits("tensor", [t_v1_done, tzm, t_kvev_s.get((hc - 1, sub)),
                                   tS_ev[NC - 1] if (hc == 0 and sub == 1) else None])
                for lt in range(15):
                    B.do("tensor", lambda e, l=lt, hh=h, o=off, bb=bk: e.matmul(
                        bb[o:o + 64, o:o + 64],
                        v_sb[:, (l + 1) * C + hh * DH:(l + 1) * C + (hh + 1) * DH],
                        ekl[:, l * C + hh * DH:l * C + (hh + 1) * DH],
                        start=(l == 0), stop=(l == 14),
                        **({"tile_position": (0, 64)} if o else {})))
                tkv = B.mark("tensor")
                B.waits("vector", [tkv])
                tev = B.do("vector", lambda e, k=kb, o=off, bb=bk: e.tensor_copy(
                    k[o:o + 64, o:o + 64], bb[o:o + 64, o:o + 64]), tok=True)
                t_kvev_s[(hc, sub)] = tev
                t_kvev[hc] = tev
            if "kv2" in dbg and hc <= 1:
                B.waits("sync", [t_kvev[hc]])
                B.do("sync", lambda e, c=hc: e.dma_start(out=dbg_out["kv2"][c], in_=kv2[c][:]))
            blocks = [(G, 512), (716, 512), (1228, 512), (1740, 308)]
            for nb, (c0, w) in enumerate(blocks):
                B.waits("tensor", [t_kvev[hc], t_qr_done if hc == 0 and nb == 0 else None,
                                   t_tle.get((hc, nb - 2)) or t_tle.get((hc - 1, nb + 2))])
                tlm = B.do("tensor", lambda e, k=kb, c=hc, c0=c0, w=w, n=nb: e.matmul(
                    PBK[n % 2][:, 0:w], k[:], chT(qrT, c, c0, w),
                    start=True, stop=True), tok=True)
                t_tlm[(hc, nb)] = tlm
                B.waits("vector", [tlm])
                tle = B.do("vector", lambda e, c=hc, w=w, n=nb: e.tensor_scalar(
                    aoev[:, 0:w], PBK[n % 2][:, 0:w], iS8[:, c:c + 1], None, ALU.mult), tok=True)
                t_tle[(hc, nb)] = tle
                if "aoev0" in dbg and hc == 0 and nb == 0:
                    B.waits("sync", [tle])
                    tdmp = B.do("sync", lambda e: e.dma_start(
                        out=dbg_out["aoev0"], in_=aoev[:]), tok=True)
                    B.waits("vector", [tdmp])
                B.waits("sync", [tle])
                t_lodma[(hc, nb)] = B.do("sync", lambda e, c=hc, c0=c0, w=w: e.dma_start(
                    out=ao_d[c, :, c0:c0 + w], in_=aoev[:, 0:w]), tok=True)
                B.waits("vector", [t_lodma[(hc, nb)]])
        t_p8_pe = B.tok("tensor")

    if stop >= 9:
        t_godma = {}
        for h in range(H):
            hc, off = h // 2, (h % 2) * 64
            B.waits("tensor", [t_kg_done, t_qr_done, t_p8_pe if h == 0 else None,
                               t_godma.get(h - 2)])
            B.do("tensor", lambda e, c=hc, o=off: e.matmul(
                PBK[2][:, 0:G], chT(qrT, c, 0, 128, o, 64),
                kgT[o:o + 64, c * 256:c * 256 + G], start=True, stop=True))
            tg1 = B.do("tensor", lambda e, c=hc, o=off: e.matmul(
                PBK[3][0:76, 0:G], chT(qrT, c, 128, 76, o, 64),
                kgT[o:o + 64, c * 256:c * 256 + G], start=True, stop=True), tok=True)
            B.waits("vector", [tg1])
            tm1 = B.do("vector", lambda e: e.reduce_max(zgbuf[:, 0:1], PBK[2][:, 0:G], AX.X), tok=True)
            tm2 = B.do("vector", lambda e: e.reduce_max(zgbuf[0:76, 1:2], PBK[3][0:76, 0:G], AX.X), tok=True)
            B.waits("vector", [tm1, tm2])
            tn1 = B.do("vector", lambda e: e.tensor_scalar(
                zgbuf[:, 2:3], zgbuf[:, 0:1], -0.125, None, ALU.mult), tok=True)
            tn2 = B.do("vector", lambda e: e.tensor_scalar(
                zgbuf[0:76, 3:4], zgbuf[0:76, 1:2], -0.125, None, ALU.mult), tok=True)
            B.waits("scalar", [tn1, tn2])
            te1 = B.do("scalar", lambda e: e.activation(
                aoev[:, 0:G], PBK[2][:, 0:G], AF.Exp, bias=zgbuf[:, 2:3], scale=0.125,
                accum_out=zgbuf[:, 4:5]), tok=True)
            te2 = B.do("scalar", lambda e: e.activation(
                aoev[0:76, 256:256 + G], PBK[3][0:76, 0:G], AF.Exp, bias=zgbuf[0:76, 3:4],
                scale=0.125, accum_out=zgbuf[0:76, 5:6]), tok=True)
            B.waits("vector", [te1, te2])
            tr1 = B.do("vector", lambda e: e.reciprocal(zgbuf[:, 6:7], zgbuf[:, 4:5]), tok=True)
            tr2 = B.do("vector", lambda e: e.reciprocal(zgbuf[0:76, 7:8], zgbuf[0:76, 5:6]), tok=True)
            B.waits("vector", [tr1, tr2])
            ts1 = B.do("vector", lambda e: e.tensor_scalar(
                aoev[:, 0:G], aoev[:, 0:G], zgbuf[:, 6:7], None, ALU.mult), tok=True)
            ts2 = B.do("vector", lambda e: e.tensor_scalar(
                aoev[0:76, 256:256 + G], aoev[0:76, 256:256 + G], zgbuf[0:76, 7:8],
                None, ALU.mult), tok=True)
            B.waits("tensor", [ts1, ts2])
            B.do("tensor", lambda e: tp(e, PTB[0][:, 0:128], aoev[:, 0:128], identb[:], True, False))
            B.do("tensor", lambda e: tp(e, PTB[0][:, 128:204], aoev[0:76, 256:256 + 128],
                                        identb[0:76, 0:76], False, True))
            B.do("tensor", lambda e: tp(e, PTB[1][0:76, 0:128], aoev[:, 128:204], identb[:], True, False))
            tg2 = B.do("tensor", lambda e: tp(e, PTB[1][0:76, 128:204],
                                              aoev[0:76, 256 + 128:256 + 204],
                                              identb[0:76, 0:76], False, True), tok=True)
            B.waits("vector", [tg2])
            tga = B.do("vector", lambda e: e.tensor_copy(agT0[:, 0:G], PTB[0][:, 0:G]), tok=True)
            tgb = B.do("vector", lambda e: e.tensor_copy(agT1[:, 0:G], PTB[1][0:76, 0:G]), tok=True)
            B.waits("tensor", [tga, tgb])
            B.do("tensor", lambda e, hh=h, o=off: e.matmul(
                PBK[4][o:o + 64, 0:G], v_sb[:, 0 * C + hh * DH:0 * C + (hh + 1) * DH],
                agT0[:, 0:G], start=True, stop=False,
                **({"tile_position": (0, 64)} if o else {})))
            tgo = B.do("tensor", lambda e, hh=h, o=off: e.matmul(
                PBK[4][o:o + 64, 0:G], v_sb[0:76, 1 * C + hh * DH:1 * C + (hh + 1) * DH],
                agT1[0:76, 0:G], start=False, stop=True,
                **({"tile_position": (0, 64)} if o else {})), tok=True)
            B.waits("vector", [tgo])
            tge = B.do("vector", lambda e, o=off: e.tensor_copy(
                aoev[o:o + 64, 256:256 + G], PBK[4][o:o + 64, 0:G]), tok=True)
            B.waits("sync", [tge])
            t_godma[h] = B.do("sync", lambda e, c=hc, o=off: e.dma_start(
                out=ao_d[c, o:o + 64, 0:G], in_=aoev[o:o + 64, 256:256 + G]), tok=True)
            B.waits("vector", [t_godma[h]])
            B.waits("scalar", [t_godma[h]])
        t_p9_pe = B.tok("tensor")
        t_p9_sc = B.tok("scalar")
        if "ao1" in dbg:
            B.waits("sync", [t_godma[H - 1], t_lodma[(NC - 1, 3)]])
            B.do("sync", lambda e: e.dma_start(out=dbg_out["ao1"], in_=ao_d))

    if stop >= 10:
        B.waits("gpsimd", [t_p9_pe])
        t_wo = B.do("gpsimd", lambda e: e.dma_start(
            out=woutw[:].rearrange("p (cc c) -> p cc c", cc=NC),
            in_=wo_d.rearrange("(cc p) c -> p cc c", p=128)), tok=True)
        prev_w = {}
        for tc in range(NT):
            bi = tc % 2
            jp = 2 * (tc % 2)
            B.waits("sync", [t_godma[H - 1], t_lodma[(NC - 1, 3)]]
                    + ([prev_w[tc - 2][0]] if tc >= 2 else []))
            tld = B.do("sync", lambda e, t=tc, b=bi: e.dma_start(
                out=aold[b][:].rearrange("p (hc t) -> p hc t", hc=NC),
                in_=ao_d[:, :, t * 128:(t + 1) * 128].rearrange("hc p t -> p hc t")), tok=True)
            B.waits("tensor", [t_wo, tld, t_p9_pe,
                               t_tle[(NC - 1, 3)] if tc == 0 else None,
                               t_p9_sc if tc == 1 else None]
                    + ([prev_w[tc - 2][3]] if tc >= 2 else []))
            for hcc in range(NC):
                for jb in range(2):
                    B.do("tensor", lambda e, hc2=hcc, j=jb, b=bi, p=jp: e.matmul(
                        PBK[p + j][:], aold[b][:, hc2 * 128:(hc2 + 1) * 128],
                        woutw[:, hc2 * C + j * 512:hc2 * C + (j + 1) * 512],
                        start=(hc2 == 0), stop=(hc2 == NC - 1)))
            tmm = B.mark("tensor")
            B.waits("sync", [t_p5_s] + ([prev_w[tc - 1][3]] if tc >= 1 else []))
            txld = B.do("sync", lambda e, t=tc: e.dma_start(
                out=xt10, in_=xs_d[t * 128:(t + 1) * 128, :]), tok=True)
            B.waits("vector", [tmm, txld, t_p8_pe if tc == 0 else None])
            B.do("vector", lambda e, t=tc, p=jp: e.tensor_tensor(
                x_sbf[:, t * C:t * C + 512], xt10[:, 0:512], PBK[p][:], ALU.add))
            tx2 = B.do("vector", lambda e, t=tc, p=jp: e.tensor_tensor(
                x_sbf[:, t * C + 512:(t + 1) * C], xt10[:, 512:1024], PBK[p + 1][:], ALU.add), tok=True)
            tst = tx2
            prev_w[tc] = (tld, tmm, txld, tx2)
            if "x2" in dbg:
                B.waits("sync", [tx2])
                B.do("sync", lambda e, t=tc: e.dma_start(
                    out=dbg_out["x2"][t * 128:(t + 1) * 128, :], in_=x_sbf[:, t * C:(t + 1) * C]))
        t_p10_v = B.tok("vector")

    if stop >= 11:
        t_b2 = {}
        t_w1ld = {}
        t_w2ld = {}
        t_gelu = {}
        t_yev = {}
        t_p1g = {}
        t_ph2mm = {}
        t_bias = {}
        for ei in range(E):
            B.waits("gpsimd", [t_p10_v if ei == 0 else None, t_bias.get((ei - 1, 3))])
            t_b2[ei] = B.do("gpsimd", lambda e, x=ei: e.dma_start(
                out=b2row[:], in_=b2_d[x:x + 1, :]), tok=True)
            for tb in range(4):
                for fc in range(NF):
                    fls = [0, 1, 2, 3] if fc == 0 else ([fc + 3] if fc + 3 < NF else [])
                    for fl in fls:
                        prev_use = (t_p1g.get((ei, tb, fl - 4))
                                    or t_p1g.get((ei, tb - 1, NF - 4 + fl))
                                    or t_p1g.get((ei - 1, 3, NF - 4 + fl)))
                        B.waits("gpsimd", [prev_use,
                                           t_p10_v if (ei == 0 and tb == 0 and fc == 0) else None])
                        t_w1ld[(ei, tb, fl)] = B.do("gpsimd", lambda e, x=ei, f=fl: e.dma_start(
                            out=w1b4[f % 4][:], in_=w1_d[x, f]), sem=f"m1s{fl % 4}")
                    bank = PBK[4 + (fc % 4)]
                    B.waits("tensor", [t_w1ld[(ei, tb, fc)],
                                       t_gelu.get((ei, tb, fc - 4))
                                       or t_gelu.get((ei, tb - 1, NF - 4 + fc))
                                       or t_gelu.get((ei - 1, 3, NF - 4 + fc)),
                                       (t_yev.get((ei, tb - 1)) or t_yev.get((ei - 1, 3)))
                                       if fc < 4 else None])
                    for cc in range(NC):
                        B.do("tensor", lambda e, c=cc, f=fc, t=tb, bk=bank: e.matmul(
                            bk[:], w1b4[f % 4][:, c * 128:(c + 1) * 128],
                            chT(xnT, c, t * 512, 512),
                            start=(c == 0), stop=(c == NC - 1)))
                    t_p1g[(ei, tb, fc)] = B.mark("tensor")
                    B.waits("scalar", [t_p1g[(ei, tb, fc)]])
                    t_gelu[(ei, tb, fc)] = B.do("scalar", lambda e, x=ei, f=fc, bk=bank: e.activation(
                        hT[:, f * 512:(f + 1) * 512], bk[:], AF.Gelu,
                        bias=b1c[:, x * NF + f:x * NF + f + 1], scale=1.0), tok=True)
                B.waits("tensor", [t_b2[ei], t_gelu[(ei, tb, NF - 1)], t_gelu[(ei, tb, NF - 2)],
                                   t_gelu[(ei, tb, NF - 3)], t_gelu[(ei, tb, NF - 4)],
                                   t_yev.get((ei, tb - 1)) or t_yev.get((ei - 1, 3))])
                for tc4 in range(4):
                    for jb in range(2):
                        B.do("tensor", lambda e, t4=tc4, j=jb: e.matmul(
                            PBK[t4 * 2 + j][:], onesrowb[:], b2row[0:1, j * 512:(j + 1) * 512],
                            start=True, stop=False))
                t_bias[(ei, tb)] = B.mark("tensor")
                for fc in range(NF):
                    fls = [0, 1, 2, 3] if fc == 0 else ([fc + 3] if fc + 3 < NF else [])
                    for fl in fls:
                        prev_use = (t_ph2mm.get((ei, tb, fl - 4))
                                    or t_ph2mm.get((ei, tb - 1, NF - 4 + fl))
                                    or t_ph2mm.get((ei - 1, 3, NF - 4 + fl)))
                        B.waits("gpsimd", [prev_use])
                        t_w2ld[(ei, tb, fl)] = B.do("gpsimd", lambda e, x=ei, f=fl: e.dma_start(
                            out=w2b4[f % 4][:], in_=w2_d[x, f * 128:(f + 1) * 128, :]), sem=f"m2s{fl % 4}")
                    B.waits("tensor", [t_w2ld[(ei, tb, fc)]])
                    for tc4 in range(4):
                        for jb in range(2):
                            B.do("tensor", lambda e, f=fc, t4=tc4, j=jb: e.matmul(
                                PBK[t4 * 2 + j][:],
                                hT[:, f * 512 + t4 * 128:f * 512 + (t4 + 1) * 128],
                                w2b4[f % 4][:, j * 512:(j + 1) * 512],
                                start=False, stop=(f == NF - 1)))
                    t_ph2mm[(ei, tb, fc)] = B.mark("tensor")
                B.waits("vector", [t_ph2mm[(ei, tb, NF - 1)], t_p10_v])
                for tc4 in range(4):
                    tg = tb * 4 + tc4
                    for jb in range(2):
                        B.do("vector", lambda e, t=tg, t4=tc4, j=jb, x=ei: e.scalar_tensor_tensor(
                            x_sbf[:, t * C + j * 512:t * C + (j + 1) * 512],
                            PBK[t4 * 2 + j][:], wte[:, t * E + x:t * E + x + 1],
                            x_sbf[:, t * C + j * 512:t * C + (j + 1) * 512],
                            ALU.mult, ALU.add))
                t_yev[(ei, tb)] = B.mark("vector")
        t_moe_v = B.tok("vector")
        if "hT0" in dbg:
            B.waits("sync", [t_gelu[(0, 0, NF - 1)]])
            B.do("sync", lambda e: e.dma_start(out=dbg_out["hT0"], in_=hT))

    if stop >= 12:
        B.waits("sync", [t_moe_v])
        for tc in range(NT):
            B.do("sync", lambda e, t=tc: e.dma_start(
                out=out_d[t * 128:(t + 1) * 128, :], in_=x_sbf[:, t * C:(t + 1) * C]),
                tok=(tc == NT - 1))
        B.wait("sync", B.tok("sync"))

    if stop < 12:
        B.waits("sync", [B.tok("vector"), B.tok("tensor"), B.tok("scalar"), B.tok("gpsimd")])
        if stop >= 2:
            for tc in range(NT):
                B.do("sync", lambda e, t=tc: e.dma_start(
                    out=out_d[t * 128:(t + 1) * 128, :], in_=xs_d[t * 128:(t + 1) * 128, :]))
            B.do("sync", lambda e: e.dma_start(out=out_d[0:1, 0:1], in_=xs_d[0:1, 0:1]), tok=True)
        else:
            tz = B.do("vector", lambda e: e.memset(rr0, 0.0), tok=True)
            B.waits("sync", [tz])
            for tc in range(NT):
                B.do("sync", lambda e, t=tc: e.dma_start(
                    out=out_d[t * 128:(t + 1) * 128, :], in_=rr0), tok=(tc == NT - 1))
        B.wait("sync", B.tok("sync"))

    with nc.Block() as block:
        B.emit(block)
    es.close()
    return nc, D, dbg_out


def _host_inputs(inputs, n_cores=8):
    f32 = np.float32
    bf16 = ml_dtypes.bfloat16
    x = np.ascontiguousarray(inputs["L0_attn_x"])
    qkvwT = np.ascontiguousarray(np.asarray(inputs["L0_attn_qkv_w"]).T)
    projT = np.ascontiguousarray(np.asarray(inputs["L0_attn_proj_w"]).T)
    pb = np.ascontiguousarray(np.asarray(inputs["L0_attn_proj_b"]).reshape(1, C))
    qnw = np.ascontiguousarray(np.broadcast_to(inputs["L0_attn_q_norm_w"], (128, DH)))
    knw = np.ascontiguousarray(np.broadcast_to(inputs["L0_attn_k_norm_w"], (128, DH)))
    rhs = np.ascontiguousarray(inputs["L0_ffn_rhs"])
    wq = np.ascontiguousarray(inputs["L1_attn_Wq"])
    wk = np.ascontiguousarray(inputs["L1_attn_Wk"])
    wv = np.ascontiguousarray(inputs["L1_attn_Wv"])
    wo = np.ascontiguousarray(inputs["L1_attn_Wout"])
    gw = np.ascontiguousarray(inputs["L1_ffn_gate_w"])
    w1 = np.asarray(inputs["L1_ffn_w1"]).reshape(E, NC, 128, NF, 128).transpose(0, 3, 2, 1, 4)
    w1 = np.ascontiguousarray(w1.reshape(E, NF, 128, NC * 128))
    b1 = np.asarray(inputs["L1_ffn_b1"]).reshape(E, NF, 128).transpose(2, 0, 1)
    b1c = np.ascontiguousarray(b1.reshape(128, E * NF))
    w2 = np.ascontiguousarray(inputs["L1_ffn_w2"])
    vw = np.asarray(inputs["L0_attn_qkv_w"]).T[:, 2 * C:3 * C]
    vwr = np.ascontiguousarray(vw.reshape(NC, 128, 8, 128).transpose(2, 1, 0, 3).reshape(8, 128, NC * 128))
    vw1 = np.asarray(inputs["L1_attn_Wv"])
    vwr1 = np.ascontiguousarray(vw1.reshape(NC, 128, 8, 128).transpose(2, 1, 0, 3).reshape(8, 128, NC * 128))
    b2 = np.ascontiguousarray(inputs["L1_ffn_b2"])

    inv = (f32(1.0) / (f32(10000.0) ** (np.arange(0, DH, 2, dtype=f32) / f32(DH)))).astype(f32)
    fr = (np.arange(T, dtype=f32)[:, None] * inv[None, :]).astype(f32)
    cos = np.cos(fr.astype(np.float64)).astype(f32)
    sin = np.sin(fr.astype(np.float64)).astype(f32)
    cos_l = np.ascontiguousarray(cos.reshape(NT, 128, 32).transpose(1, 0, 2).reshape(128, NT * 32))
    sin_l = np.ascontiguousarray(sin.reshape(NT, 128, 32).transpose(1, 0, 2).reshape(128, NT * 32))

    selallb = np.zeros((16, NC * 128), bf16)
    for hc in range(NC):
        for p in range(128):
            selallb[2 * hc + p // 64, hc * 128 + p] = 1.0
    selj = np.zeros((8, NC * 128), f32)
    for jc in range(NC):
        selj[jc, jc * 128:(jc + 1) * 128] = 1.0
    selz = np.zeros((128, 16 * 16), bf16)
    for h in range(16):
        selz[:, h * 16 + h] = 1.0
    cvec = np.zeros((128, 8), f32)
    cvec[:, 0] = 1e-5
    cvec[:, 1] = 1e-6
    cvec[:, 2] = 127.0

    shared = dict(qkvwT=qkvwT, projT=projT, pb=pb, d_qnw=qnw, d_knw=knw, rhs=rhs, vwr=vwr, vwr1=vwr1,
                  wq=wq, wk=wk, wv=wv, wo=wo, gw=gw, w1=w1, d_b1c=b1c, w2=w2, b2=b2,
                  cos=cos_l, sin=sin_l, d_ident=np.eye(128, dtype=f32),
                  d_identb=np.eye(128, dtype=bf16),
                  d_onesrow=np.ones((1, 128), f32), d_onesrowb=np.ones((1, 128), bf16),
                  d_onescolb=np.ones((128, 1), bf16),
                  d_selallb=selallb, d_selj=selj, d_selz=selz, d_cvec=cvec)
    in_maps = []
    for i in range(n_cores):
        m = dict(shared)
        m["x"] = np.ascontiguousarray(x[i])
        in_maps.append(m)
    return in_maps


def kernel(**inputs):
    nc, D, dbg_out = build_program()
    in_maps = _host_inputs(inputs)
    res = run_bass_kernel_spmd(nc, in_maps, core_ids=list(range(8)))
    return np.stack([np.asarray(r["out"]) for r in res.results], axis=0)


# version 10 (final)
# speedup vs baseline: 1.5663x; 1.0312x over previous
import numpy as np
import ml_dtypes
import concourse.bass as bass
import concourse.mybir as mybir
from concourse.bass_utils import run_bass_kernel_spmd
from contextlib import ExitStack

F32 = mybir.dt.float32
BF16 = mybir.dt.bfloat16
AF = mybir.ActivationFunctionType
ALU = mybir.AluOpType
AX = mybir.AxisListType

T, NT, C, NC = 2048, 16, 1024, 8
H, DH, G, E, FF, NF = 16, 64, 204, 8, 4096, 32
LTOK = T - G
ENGINES = ("sync", "gpsimd", "tensor", "vector", "scalar")
KB = 512


class Build:
    def __init__(self, nc):
        self.nc = nc
        self.q = {e: [] for e in ENGINES}
        self.val = {e: 0 for e in ENGINES}
        self.step = {"sync": 16, "gpsimd": 16, "tensor": 1, "vector": 1, "scalar": 1}
        self.sem = {}

    def wait(self, eng, tok):
        if tok is not None:
            self.q[eng].append(("w", tok[0], tok[1]))

    def waits(self, eng, toks):
        best = {}
        for t in toks:
            if t is None:
                continue
            best[t[0]] = max(best.get(t[0], 0), t[1])
        for s, v in best.items():
            self.q[eng].append(("w", s, v))

    def do(self, eng, fn, tok=False, sem=None):
        if eng in ("vector", "scalar"):
            self.q[eng].append(("w", eng, self.val[eng]))
            self.val[eng] += self.step[eng]
            self.q[eng].append(("i", fn, self.step[eng]))
            return (eng, self.val[eng])
        if tok or eng in ("sync", "gpsimd"):
            if sem is not None:
                self.val[sem] = self.val.get(sem, 0) + 16
                self.q[eng].append(("I", fn, 16, sem))
                return (sem, self.val[sem])
            self.q[eng].append(("w", eng, self.val[eng]))
            self.val[eng] += self.step[eng]
            self.q[eng].append(("i", fn, self.step[eng]))
            return (eng, self.val[eng])
        self.q[eng].append(("e", fn))
        return None

    def tok(self, eng):
        return (eng, self.val[eng])

    def mark(self, eng):
        for i in range(len(self.q[eng]) - 1, -1, -1):
            item = self.q[eng][i]
            if item[0] == "w":
                continue
            if item[0] == "i":
                return (eng, self._val_at(eng, i))
            self.val[eng] += self.step[eng]
            self.q[eng][i] = ("i", item[1], self.step[eng])
            return (eng, self.val[eng])
        raise RuntimeError("mark: no instruction on " + eng)

    def _val_at(self, eng, idx):
        v = 0
        for item in self.q[eng][:idx + 1]:
            if item[0] == "i":
                v += item[2]
        return v

    def emit(self, block):
        sem = self.sem
        for eng in ENGINES:
            queue = self.q[eng]

            def body(e, queue=queue):
                for item in queue:
                    if item[0] == "w":
                        e.wait_ge(sem[item[1]], item[2])
                    elif item[0] == "e":
                        item[1](e)
                    elif item[0] == "I":
                        item[1](e).then_inc(sem[item[3]], item[2])
                    else:
                        item[1](e).then_inc(sem[eng], item[2])
            getattr(block, eng)(body)


def build_program(debug=(), stop=99):
    dbg = set(debug)
    p0cut = next((int(d[5:]) for d in dbg if d.startswith("p0cut")), 99)
    nc = bass.Bass("TRN2", target_bir_lowering=False, detect_race_conditions=False)
    D = {}

    def din(name, shape, dt=F32):
        D[name] = nc.dram_tensor(name, shape, dt, kind="ExternalInput").ap()
        return D[name]

    x_d = din("x", [T, C])
    qkvwT_d = din("qkvwT", [C, 3 * C])
    projT_d = din("projT", [C, C])
    pb_d = din("pb", [1, C])
    qnw_d = din("d_qnw", [128, DH])
    knw_d = din("d_knw", [128, DH])
    rhs_d = din("rhs", [C, C])
    wq_d = din("wq", [C, C])
    wk_d = din("wk", [C, C])
    wv_d = din("wv", [C, C])
    wo_d = din("wo", [C, C])
    gw_d = din("gw", [C, E])
    vwr_d = din("vwr", [8, 128, NC * 128])
    vwr1_d = din("vwr1", [8, 128, NC * 128])
    w1_d = din("w1", [E, NF, 128, NC * 128])
    b1c_d = din("d_b1c", [128, E * NF])
    w2_d = din("w2", [E, FF, C])
    b2_d = din("b2", [E, C])
    cos_d = din("cos", [128, NT * 32])
    sin_d = din("sin", [128, NT * 32])
    ident_d = din("d_ident", [128, 128])
    identb_d = din("d_identb", [128, 128], BF16)
    onesrow_d = din("d_onesrow", [1, 128])
    onesrowb_d = din("d_onesrowb", [1, 128], BF16)
    onescolb_d = din("d_onescolb", [128, 1], BF16)
    selallb_d = din("d_selallb", [16, NC * 128], BF16)
    selz_d = din("d_selz", [128, H * 16], BF16)
    selj_d = din("d_selj", [8, NC * 128])
    cvec_d = din("d_cvec", [128, 8])
    xs_d = nc.dram_tensor("xs", [T, C], F32, kind="Internal").ap()
    ao_d = nc.dram_tensor("ao", [NC, 128, T], BF16, kind="Internal").ap()
    out_d = nc.dram_tensor("out", [T, C], F32, kind="ExternalOutput").ap()
    D["out"] = out_d

    es = ExitStack()

    def sb(name, shape, dt):
        return es.enter_context(nc.sbuf_tensor(name, shape, dt))

    def psum(name, shape, dt=F32):
        return es.enter_context(nc.psum_tensor(name, shape, dt))

    B = Build(nc)

    ident = sb("ident", [128, 128], F32)
    identb = sb("identb", [128, 128], BF16)
    onesrow = sb("onesrow", [1, 128], F32)
    onesrowb = sb("onesrowb", [1, 128], BF16)
    onescolb = sb("onescolb", [128, 1], BF16)
    cvec = sb("cvec", [128, 8], F32)
    r127 = sb("r127", [128, 1], F32)
    selallb = sb("selallb", [16, NC * 128], BF16)
    selz = sb("selz", [128, H * 16], BF16)
    qnw = sb("qnw", [128, DH], F32)
    knw = sb("knw", [128, DH], F32)
    pb_sb = sb("pb_sb", [1, C], BF16)
    b2row = sb("b2row", [1, C], BF16)
    b1c = sb("b1c", [128, E * NF], F32)
    gsml = sb("gsml", [128, NC * E], F32)
    ghi = sb("ghi", [128, NC * E], BF16)
    glo = sb("glo", [128, NC * E], BF16)
    wte = sb("wte", [128, NT * E], F32)
    stats = sb("stats", [128, 16], F32)
    colsA = sb("colsA", [128, 8], F32)
    colsB = sb("colsB", [128, 16], F32)
    sl_all = sb("sl_all", [128, NT], F32)
    isl_all = sb("isl_all", [128, NT], F32)
    ssq_q = sb("ssq_q", [128, NT * H], F32)
    ssq_k = sb("ssq_k", [128, NT * H], F32)
    negc = sb("negc", [128, NT * H], F32)
    zall = sb("zall", [128, NT * H], F32)
    zpart = sb("zpart", [128, 8], F32)
    m2b = sb("m2b", [128, H], F32)
    s8 = sb("s8", [8, 128], F32)
    sr_cols = sb("sr_cols", [128, NC], F32)
    agT0 = sb("agT0", [128, G], BF16)
    agT1 = sb("agT1", [76, G], BF16)
    kv2 = [sb(f"kv2_{i}", [128, 128], BF16) for i in range(2)]
    iS8 = sb("iS8", [128, NC], F32)
    zgbuf = sb("zgbuf", [128, 24], F32)
    zs1 = sb("zs1", [128, 8], F32)
    zs2 = sb("zs2", [128, 8], F32)
    aoev = sb("aoev", [128, 512], BF16)
    tb4 = [sb(f"tb4_{i}", [128, C], BF16) for i in range(4)]

    AR = sb("AR", [128, 176 * KB], BF16)

    def zbv(kb_off, kb_len):
        return AR[:, kb_off * KB:(kb_off + kb_len) * KB]

    def zfv(kb_off, kb_len):
        return AR[:, kb_off * KB:(kb_off + kb_len) * KB].bitcast(F32)

    xnT = zbv(0, 32)
    attn_s = zbv(0, 16)
    attnT_s = zbv(16, 8)
    izTb = zbv(160, 4)[0:16, 0:T]
    izbb = zbv(164, 4)
    qxT = zbv(32, 32)
    qsT = zbv(32, 32)
    loT = zbv(32, 32)
    qrT = zbv(32, 32)
    qrhs = zbv(64, 16)
    ksT = zbv(64, 32)
    ekl = zbv(64, 30)
    x_sbf = zfv(32, 64)
    v_sb = zbv(96, 32)
    projw = zbv(96, 16)
    woutw = zbv(96, 16)
    aold = [zbv(112, 2), zbv(114, 2)]
    hT = zbv(96, 32)
    rhsf = zfv(128, 32)
    xt = [zfv(80, 4), zfv(84, 4)]
    xn_s = [zfv(88, 4), zfv(92, 4)]
    rr0 = zfv(96, 4)
    rr = zfv(100, 4)
    rt = zfv(104, 4)
    spare4 = zfv(156, 4)
    qf = zfv(144, 4)
    kf = zfv(148, 4)
    sq_s = zfv(152, 4)
    ropeA = zfv(156, 2)
    ropeB = zfv(158, 2)
    cos_sb = zfv(128, 2)
    sin_sb = zfv(130, 2)
    p0rr0 = zfv(32, 4)
    p0rr = zfv(36, 4)
    p0rt = zfv(40, 4)
    sr_b = zfv(160, 4)
    isr_b = zfv(164, 4)
    selj = zfv(168, 4)[0:8, 0:1024]
    wqk = zfv(128, 32)
    wqkb = zbv(128, 32)
    vwstr = [zbv(160, 2), zbv(162, 2)]
    p3qf = zfv(96, 4)
    p3kf = zfv(100, 4)
    p3sq = zfv(104, 4)
    wstr = [zbv(160, 6), zbv(166, 6)]
    l1w = zbv(144, 16)
    l1qf = zfv(132, 4)
    ropeA2 = zfv(136, 2)
    ropeB2 = zfv(138, 2)
    cosv = zfv(128, 2)
    sinv = zfv(130, 2)
    xt10 = zfv(132, 4)
    vw1str = [zbv(140, 2), zbv(142, 2)]
    kgT = zbv(172, 4)
    stbv = zfv(172, 4)
    w1b = [zbv(160, 2), zbv(162, 2)]
    w2b = [zbv(164, 2), zbv(166, 2)]
    w1b4 = [zbv(128, 2), zbv(130, 2), zbv(132, 2), zbv(134, 2)]
    w2b4 = [zbv(136, 2), zbv(138, 2), zbv(140, 2), zbv(142, 2)]

    PBK = [psum(f"pbk{i}", [128, 512], F32) for i in range(8)]
    PTB = [PBK[6][:].bitcast(BF16), PBK[7][:].bitcast(BF16)]

    left = nc.sbuf_bytes_remaining
    assert left >= 0, f"SBUF over budget by {-left} bytes/partition"

    for s in ENGINES:
        B.sem[s] = es.enter_context(nc.semaphore(f"sem_{s}"))
    for s in ["m1s0", "m1s1", "m1s2", "m1s3", "m2s0", "m2s1", "m2s2", "m2s3"]:
        B.sem[s] = es.enter_context(nc.semaphore(f"sem_{s}"))

    dbg_out = {}

    def add_dbg(name, shape, dt=F32):
        if name in dbg:
            dbg_out[name] = nc.dram_tensor("dbg_" + name, shape, dt, kind="ExternalOutput").ap()

    add_dbg("xn", [T, C]); add_dbg("qx", [T, C], BF16); add_dbg("sl", [128, NT])
    add_dbg("srb", [128, C]); add_dbg("qrhs", [C, C], BF16); add_dbg("xffn", [T, C])
    add_dbg("qs", [128, NC * T], BF16); add_dbg("ks", [128, NC * T], BF16)
    add_dbg("v0", [128, NT * C], BF16)
    add_dbg("negc", [128, NT * H]); add_dbg("ao", [NC, 128, T], BF16)
    add_dbg("zall", [128, NT * H]); add_dbg("x1", [T, C]); add_dbg("wte", [128, NT * E])
    add_dbg("qrT", [128, NC * T], BF16); add_dbg("ekl", [128, 15 * C], BF16)
    add_dbg("x2", [T, C]); add_dbg("xnT", [128, NC * T], BF16)
    add_dbg("kgT", [128, NC * G], BF16); add_dbg("iS8", [128, NC])
    add_dbg("kv2", [2, 128, 128], BF16); add_dbg("hT0", [128, NF * 512], BF16)
    add_dbg("srfranspose", [128, NC])
    add_dbg("src", [128, NC]); add_dbg("s8d", [8, 128]); add_dbg("rhsfd", [128, NC * C])
    add_dbg("ao1", [NC, 128, T], BF16); add_dbg("attn00", [128, 4 * T], BF16)
    add_dbg("aoev0", [128, 512], BF16)
    add_dbg("glog", [128, NT * E]); add_dbg("xnT1", [128, NC * T], BF16); add_dbg("loT1", [128, NC * T], BF16)

    ln_eps_col = cvec[:, 0:1]
    rms_eps_col = cvec[:, 1:2]

    def chT(tens, cc, c0, n, p0=0, np_=128):
        return tens[p0:p0 + np_, cc * T + c0: cc * T + c0 + n]

    def tp(e, out_ap, in_ap, idn, first, last):
        return e.matmul(out_ap, in_ap, idn, start=first, stop=last, is_transpose=True)

    ld0 = []
    for dst, src in [(ident, ident_d), (identb, identb_d), (onesrow, onesrow_d),
                     (onesrowb, onesrowb_d), (onescolb, onescolb_d), (cvec, cvec_d),
                     (selallb, selallb_d), (selz, selz_d),
                     (qnw, qnw_d), (knw, knw_d), (b1c, b1c_d)]:
        ld0.append(B.do("sync", lambda e, d=dst, s=src: e.dma_start(out=d[:], in_=s[:]), tok=True))
    t_consts = B.tok("sync")
    t_selj = B.do("sync", lambda e: e.dma_start(out=selj, in_=selj_d[:]), tok=True)
    t_rhs = B.do("sync", lambda e: e.dma_start(
        out=rhsf.rearrange("p (cc c) -> p cc c", cc=NC),
        in_=rhs_d.rearrange("(cc p) c -> p cc c", p=128)), tok=True)
    t_gw = B.do("sync", lambda e: e.dma_start(
        out=gsml[:].rearrange("p (cc c) -> p cc c", cc=NC),
        in_=gw_d.rearrange("(cc p) c -> p cc c", p=128)), tok=True)
    t_pb = B.do("gpsimd", lambda e: e.dma_start(out=pb_sb[:], in_=pb_d[:]), tok=True)

    if p0cut >= 2:
        B.waits("vector", [t_consts])
        t_r127 = B.do("vector", lambda e: e.reciprocal(r127[:], cvec[:, 2:3]), tok=True)
        B.wait("vector", t_r127)
        B.waits("vector", [t_gw])
        t_ghi = B.do("vector", lambda e: e.tensor_copy(ghi[:], gsml[:]), tok=True)
        B.wait("vector", t_ghi)
        t_glo = B.do("vector", lambda e: e.tensor_tensor(glo[:], gsml[:], ghi[:], ALU.subtract), tok=True)

    B.waits("tensor", [t_rhs, t_consts])
    for jc in range(NC if p0cut >= 3 else 0):
        for cc in range(NC):
            bank, off = PBK[cc // 4], (cc % 4) * 128
            B.do("tensor", lambda e, b=bank, o=off, c=cc, j=jc: tp(e,
                b[:, o:o + 128], rhsf[:, c * C + j * 128:c * C + (j + 1) * 128], ident[:],
                c % 4 == 0, c % 4 == 3))
        tpe = B.mark("tensor")
        B.waits("vector", [tpe])
        B.do("vector", lambda e: e.tensor_copy(p0rr[:, 0:512], PBK[0][:]))
        tev = B.do("vector", lambda e: e.tensor_copy(p0rt[:, 0:512], PBK[1][:]), tok=True)
        B.waits("tensor", [tev])
        B.do("vector", lambda e: e.reduce_max(colsA[:, 0:1], p0rr[:, 0:512], AX.X, apply_absolute_value=True))
        B.do("vector", lambda e: e.reduce_max(colsA[:, 1:2], p0rt[:, 0:512], AX.X, apply_absolute_value=True))
        B.do("vector", lambda e, j=jc: e.tensor_tensor(
            sr_cols[:, j:j + 1], colsA[:, 0:1], colsA[:, 1:2], ALU.max))
    if p0cut >= 4:
        tsr = B.do("vector", lambda e: e.tensor_scalar(
            sr_cols[:], sr_cols[:], 1e-12, r127[:], ALU.max, ALU.mult), tok=True)
        B.waits("tensor", [tsr])
        tpe = B.do("tensor", lambda e: e.transpose(PBK[2][0:8, 0:128], sr_cols[:], ident[:]), tok=True)
        B.waits("vector", [tpe])
        tev = B.do("vector", lambda e: e.tensor_copy(s8[:], PBK[2][0:8, 0:128]), tok=True)
        B.waits("tensor", [tev, t_selj])
        for jc in range(NC):
            bank = PBK[3 + (jc % 2)]
            tpe = B.do("tensor", lambda e, b=bank, j=jc: e.matmul(
                b[:, 0:128], selj[0:8, j * 128:(j + 1) * 128], s8[:], start=True, stop=True), tok=True)
            B.waits("vector", [tpe])
            tev = B.do("vector", lambda e, b=bank, j=jc: e.tensor_copy(
                sr_b[:, j * 128:(j + 1) * 128], b[:, 0:128]), tok=True)
            B.waits("tensor", [tev])
        t_isr = B.do("vector", lambda e: e.reciprocal(isr_b, sr_b), tok=True)
        B.wait("vector", t_isr)

    def emit_round(dst_ap, R0=None, R1=None, R2=None):
        R0 = R0 if R0 is not None else rr0
        R1 = R1 if R1 is not None else rr
        R2 = R2 if R2 is not None else rt
        B.do("vector", lambda e: e.tensor_scalar(R0, R0, 127.25, -127.25, ALU.min, ALU.max))
        B.do("vector", lambda e: e.tensor_scalar(R0, R0, 128.5, None, ALU.add))
        B.do("vector", lambda e: e.tensor_scalar(R2, R0, 128.0, 128.0, ALU.is_ge, ALU.mult))
        B.do("vector", lambda e: e.tensor_tensor(R1, R0, R2, ALU.subtract))
        for b in (64.0, 32.0, 16.0, 8.0, 4.0, 2.0, 1.0):
            B.do("vector", lambda e, b=b: e.tensor_scalar(R2, R1, b, b, ALU.is_ge, ALU.mult))
            B.do("vector", lambda e: e.tensor_tensor(R1, R1, R2, ALU.subtract))
        B.do("vector", lambda e: e.tensor_tensor(R2, R0, R1, ALU.subtract))
        return B.do("vector", lambda e: e.tensor_scalar(dst_ap, R2, 128.0, None, ALU.subtract), tok=True)

    for cc in range(NC if p0cut >= 5 else 0):
        B.do("vector", lambda e, c=cc: e.tensor_tensor(
            p0rr0, rhsf[:, c * C:(c + 1) * C], isr_b, ALU.mult))
        emit_round(qrhs[:, cc * C:(cc + 1) * C], p0rr0, p0rr, p0rt)
    t_qrhs = B.tok("vector")

    if "srb" in dbg:
        B.waits("sync", [t_isr])
        B.do("sync", lambda e: e.dma_start(out=dbg_out["srb"], in_=sr_b))
    if "src" in dbg:
        B.waits("sync", [t_isr])
        B.do("sync", lambda e: e.dma_start(out=dbg_out["src"], in_=sr_cols[:]))
    if "s8d" in dbg:
        B.waits("sync", [t_isr])
        B.do("sync", lambda e: e.dma_start(out=dbg_out["s8d"], in_=s8[:]))
    if "rhsfd" in dbg:
        B.waits("sync", [t_rhs])
        B.do("sync", lambda e: e.dma_start(out=dbg_out["rhsfd"], in_=rhsf))
    if "qrhs" in dbg:
        B.waits("sync", [t_qrhs])
        B.do("sync", lambda e: e.dma_start(
            out=dbg_out["qrhs"].rearrange("(cc p) c -> p cc c", p=128),
            in_=qrhs[:].rearrange("p (cc c) -> p cc c", cc=NC)))

    def emit_ln(xbuf, xn_out, dep_toks, cb, prev_tsq):
        B.waits("vector", dep_toks + ([prev_tsq] if prev_tsq else []))
        B.do("vector", lambda e: e.bn_stats(stats[:, 0:6], xbuf[:, 0:512]))
        B.do("vector", lambda e: e.bn_stats(stats[:, 6:12], xbuf[:, 512:1024]))
        tagg = B.do("vector", lambda e, cb=cb: e.bn_aggr(colsA[:, cb:cb + 2], stats[:, 0:12]), tok=True)
        B.waits("scalar", [tagg])
        tsq = B.do("scalar", lambda e, cb=cb: e.activation(
            colsA[:, cb + 2:cb + 3], colsA[:, cb + 1:cb + 2], AF.Sqrt,
            bias=ln_eps_col, scale=1.0), tok=True)
        B.waits("vector", [tsq])
        trc = B.do("vector", lambda e, cb=cb: e.reciprocal(
            colsA[:, cb + 3:cb + 4], colsA[:, cb + 2:cb + 3]), tok=True)
        B.wait("vector", trc)
        tn = B.do("vector", lambda e, cb=cb: e.tensor_scalar(
            xn_out, xbuf, colsA[:, cb:cb + 1], colsA[:, cb + 3:cb + 4],
            ALU.subtract, ALU.mult), tok=True)
        return tn, tsq

    if stop >= 1:
        prev_tsq = None
        ev1_tok = {}
        tp0s, tp1s = {}, {}
        t_xn_toks = [None] * NT
        prev_ffn = {}
        for tc in range(NT + 1):
            if tc < NT:
                bi = tc % 2
                cb = bi * 4
                B.waits("sync", [t_xn_toks[tc - 2] if tc >= 2 else None,
                                 prev_ffn[tc - 2][0] if tc >= 2 else None])
                t_xld = B.do("sync", lambda e, t=tc, b=bi: e.dma_start(
                    out=xt[b], in_=x_d[t * 128:(t + 1) * 128, :]), tok=True)
                t_xn, prev_tsq = emit_ln(xt[bi], xn_s[bi], [t_xld], cb, prev_tsq)
                t_xn_toks[tc] = t_xn
                B.do("vector", lambda e, b=bi: e.reduce_max(
                    colsB[:, 0:1], xn_s[b], AX.X, apply_absolute_value=True))
                tsl = B.do("vector", lambda e, t=tc: e.tensor_scalar(
                    sl_all[:, t:t + 1], colsB[:, 0:1], 1e-12, r127[:], ALU.max, ALU.mult), tok=True)
                B.wait("vector", tsl)
                tisl = B.do("vector", lambda e, t=tc: e.reciprocal(
                    isl_all[:, t:t + 1], sl_all[:, t:t + 1]), tok=True)
                B.wait("vector", tisl)
                B.waits("vector", [tp1s.get(tc - 2)])
                B.do("vector", lambda e, b=bi, t=tc: e.tensor_scalar(
                    rr0, xn_s[b], isl_all[:, t:t + 1], None, ALU.mult))
                tq = emit_round(tb4[2 + bi][:])
                B.waits("scalar", [t_xn, tp0s.get(tc - 2)])
                txb = B.do("scalar", lambda e, b=bi: e.copy(tb4[b][:], xn_s[b]), tok=True)
                B.waits("tensor", [txb, tq, t_consts, ev1_tok.get(tc - 1)])
                for cc in range(NC):
                    B.do("tensor", lambda e, c=cc, b=bi: tp(e,
                        PTB[0][:, c * 128:(c + 1) * 128], tb4[b][:, c * 128:(c + 1) * 128], identb[:],
                        c == 0, c == NC - 1))
                tp0s[tc] = B.mark("tensor")
                for cc in range(NC):
                    B.do("tensor", lambda e, c=cc, b=bi: tp(e,
                        PTB[1][:, c * 128:(c + 1) * 128], tb4[2 + b][:, c * 128:(c + 1) * 128], identb[:],
                        c == 0, c == NC - 1))
                tp1s[tc] = B.mark("tensor")
                B.waits("vector", [tp0s[tc], tp1s[tc]])
                B.do("vector", lambda e, t=tc: e.tensor_copy(
                    xnT.rearrange("p (cc tt) -> p cc tt", cc=NC)[:, :, t * 128:(t + 1) * 128],
                    PTB[0][:].rearrange("p (cc tt) -> p cc tt", cc=NC)))
                ev1_tok[tc] = B.do("vector", lambda e, t=tc: e.tensor_copy(
                    qxT.rearrange("p (cc tt) -> p cc tt", cc=NC)[:, :, t * 128:(t + 1) * 128],
                    PTB[1][:].rearrange("p (cc tt) -> p cc tt", cc=NC)), tok=True)
                if "xn" in dbg:
                    B.waits("sync", [t_xn])
                    B.do("sync", lambda e, t=tc, b=bi: e.dma_start(
                        out=dbg_out["xn"][t * 128:(t + 1) * 128, :], in_=xn_s[b]))
                if "qx" in dbg:
                    B.waits("sync", [tq])
                    B.do("sync", lambda e, t=tc, b=bi: e.dma_start(
                        out=dbg_out["qx"][t * 128:(t + 1) * 128, :], in_=tb4[2 + b][:]))
            if stop >= 2 and tc >= 1:
                t2 = tc - 1
                b2 = t2 % 2
                bb = b2 * 2
                deps = [t_qrhs, ev1_tok[t2]]
                if t2 >= 2:
                    deps.append(prev_ffn[t2 - 2][1])
                B.waits("tensor", deps)
                for cc in range(NC):
                    for jb in range(2):
                        B.do("tensor", lambda e, c=cc, j=jb, b=bb, t=t2: e.matmul(
                            PBK[b + j][:], chT(qxT, c, t * 128, 128),
                            qrhs[:, c * C + j * 512:c * C + (j + 1) * 512],
                            start=(c == 0), stop=(c == NC - 1)))
                tmm = B.mark("tensor")
                B.waits("vector", [tmm])
                for jb in range(2):
                    B.do("vector", lambda e, j=jb, b=bb, t=t2: e.scalar_tensor_tensor(
                        rt[:, j * 512:(j + 1) * 512], PBK[b + j][:], sl_all[:, t:t + 1],
                        sr_b[:, j * 512:(j + 1) * 512], ALU.mult, ALU.mult))
                tev = B.do("vector", lambda e, b=b2: e.tensor_tensor(
                    xt[b], xt[b], rt, ALU.add), tok=True)
                B.waits("sync", [tev])
                tst = B.do("sync", lambda e, t=t2, b=b2: e.dma_start(
                    out=xs_d[t * 128:(t + 1) * 128, :], in_=xt[b]), tok=True)
                prev_ffn[t2] = (tst, tev)
                if "xffn" in dbg:
                    B.do("sync", lambda e, t=t2, b=b2: e.dma_start(
                        out=dbg_out["xffn"][t * 128:(t + 1) * 128, :], in_=xt[b]))
        if "sl" in dbg:
            B.waits("sync", [B.tok("vector")])
            B.do("sync", lambda e: e.dma_start(out=dbg_out["sl"], in_=sl_all[:]))
        if "xnT" in dbg:
            B.waits("sync", [B.tok("vector")])
            B.do("sync", lambda e: e.dma_start(out=dbg_out["xnT"], in_=xnT))
    if stop >= 2:
        t_ffn_done_s = B.tok("sync")
        t_ffn_done_v = B.tok("vector")

    if stop >= 3:
        def rms_block(src_f32, w_rep, dst_bf, extra_scale, dep_tok, sq_buf):
            B.waits("scalar", [dep_tok])
            tsq = B.do("scalar", lambda e: e.activation(sq_buf, src_f32, AF.Square,
                                                        bias=0.0, scale=1.0), tok=True)
            B.waits("vector", [tsq])
            trd = B.do("vector", lambda e: e.tensor_reduce(
                colsB[:, 0:16].rearrange("p (h o) -> p h o", o=1),
                sq_buf.rearrange("p (h d) -> p h d", h=H), AX.X, ALU.add), tok=True)
            B.waits("scalar", [trd])
            ts1 = B.do("scalar", lambda e: e.activation(
                colsA[:, 0:8], colsB[:, 0:8], AF.Sqrt, bias=rms_eps_col, scale=1.0 / DH), tok=True)
            ts2 = B.do("scalar", lambda e: e.activation(
                stats[:, 0:8], colsB[:, 8:16], AF.Sqrt, bias=rms_eps_col, scale=1.0 / DH), tok=True)
            B.waits("vector", [ts1, ts2])
            B.do("vector", lambda e: e.reciprocal(stats[:, 8:16], colsA[:, 0:8]))
            trc = B.do("vector", lambda e: e.reciprocal(colsB[:, 0:8], stats[:, 0:8]), tok=True)
            B.wait("vector", trc)
            if extra_scale != 1.0:
                B.do("vector", lambda e: e.tensor_scalar(
                    stats[:, 8:16], stats[:, 8:16], extra_scale, None, ALU.mult))
                tsc = B.do("vector", lambda e: e.tensor_scalar(
                    colsB[:, 0:8], colsB[:, 0:8], extra_scale, None, ALU.mult), tok=True)
                B.wait("vector", tsc)
            B.do("vector", lambda e: e.tensor_tensor(
                sq_buf.rearrange("p (h d) -> p h d", h=H), src_f32.rearrange("p (h d) -> p h d", h=H),
                w_rep[:].unsqueeze(1).broadcast_to([128, H, DH]),
                ALU.mult))
            B.do("vector", lambda e: e.tensor_tensor(
                dst_bf.rearrange("p (h d) -> p h d", h=H)[:, 0:8],
                sq_buf.rearrange("p (h d) -> p h d", h=H)[:, 0:8],
                stats[:, 8:16].unsqueeze(2).broadcast_to([128, 8, DH]),
                ALU.mult))
            return B.do("vector", lambda e: e.tensor_tensor(
                dst_bf.rearrange("p (h d) -> p h d", h=H)[:, 8:16],
                sq_buf.rearrange("p (h d) -> p h d", h=H)[:, 8:16],
                colsB[:, 0:8].unsqueeze(2).broadcast_to([128, 8, DH]),
                ALU.mult), tok=True)

        def ssq_block(src_bf, dst_cols, sq_buf, dep_tok):
            B.waits("scalar", [dep_tok])
            tsq = B.do("scalar", lambda e: e.activation(sq_buf, src_bf, AF.Square,
                                                        bias=0.0, scale=1.0), tok=True)
            B.waits("vector", [tsq])
            return B.do("vector", lambda e: e.tensor_reduce(
                dst_cols.rearrange("p (h o) -> p h o", o=1),
                sq_buf.rearrange("p (h d) -> p h d", h=H), AX.X, ALU.add), tok=True)

        B.waits("gpsimd", [t_ffn_done_v, t_ffn_done_s])
        t_wqk = B.do("gpsimd", lambda e: e.dma_start(
            out=wqkb[:].rearrange("p (cc c) -> p cc c", cc=NC),
            in_=qkvwT_d[:, 0:2 * C].rearrange("(cc p) c -> p cc c", p=128)), tok=True)

        prev_qkv = {}
        tqk_ev = {}
        for tc in range(NT):
            bi = tc % 2
            deps = [ev1_tok[NT - 1], t_ffn_done_v, t_wqk]
            if tc >= 2:
                deps.append(prev_qkv[tc - 2])
            B.waits("tensor", deps)
            for cc in range(NC):
                for part in range(2):
                    for jb in range(2):
                        B.do("tensor", lambda e, c=cc, p=part, j=jb, t=tc: e.matmul(
                            PBK[p * 2 + j][:], chT(xnT, c, t * 128, 128),
                            wqkb[:, c * 2 * C + p * C + j * 512:c * 2 * C + p * C + (j + 1) * 512],
                            start=(c == 0), stop=(c == NC - 1)))
            tmm = B.mark("tensor")
            B.waits("vector", [tmm])
            B.do("vector", lambda e: e.tensor_copy(p3qf[:, 0:512], PBK[0][:]))
            B.do("vector", lambda e: e.tensor_copy(p3qf[:, 512:1024], PBK[1][:]))
            B.do("vector", lambda e: e.tensor_copy(p3kf[:, 0:512], PBK[2][:]))
            tevv = B.do("vector", lambda e: e.tensor_copy(p3kf[:, 512:1024], PBK[3][:]), tok=True)
            prev_qkv[tc] = tevv

            tqs = rms_block(p3qf, qnw, tb4[2 + bi][:], 0.125, tevv, p3sq)
            tssq = ssq_block(tb4[2 + bi][:], ssq_q[:, tc * H:(tc + 1) * H], p3sq, tqs)
            tks = rms_block(p3kf, knw, tb4[bi][:], 1.0, tssq, p3sq)
            tssk = ssq_block(tb4[bi][:], ssq_k[:, tc * H:(tc + 1) * H], p3sq, tks)

            B.waits("tensor", [tqs, tks, tqk_ev.get(tc - 1)])
            for cc in range(NC):
                B.do("tensor", lambda e, c=cc, b=bi: tp(e,
                    PTB[0][:, c * 128:(c + 1) * 128], tb4[2 + b][:, c * 128:(c + 1) * 128], identb[:],
                    c == 0, c == NC - 1))
            tq0 = B.mark("tensor")
            for cc in range(NC):
                B.do("tensor", lambda e, c=cc, b=bi: tp(e,
                    PTB[1][:, c * 128:(c + 1) * 128], tb4[b][:, c * 128:(c + 1) * 128], identb[:],
                    c == 0, c == NC - 1))
            tq1 = B.mark("tensor")
            B.waits("vector", [tq0, tq1, tssk])
            B.do("vector", lambda e, t=tc: e.tensor_copy(
                qsT.rearrange("p (cc tt) -> p cc tt", cc=NC)[:, :, t * 128:(t + 1) * 128],
                PTB[0][:].rearrange("p (cc tt) -> p cc tt", cc=NC)))
            tqk_ev[tc] = B.do("vector", lambda e, t=tc: e.tensor_copy(
                ksT.rearrange("p (cc tt) -> p cc tt", cc=NC)[:, :, t * 128:(t + 1) * 128],
                PTB[1][:].rearrange("p (cc tt) -> p cc tt", cc=NC)), tok=True)
        t_p3a_v = B.tok("vector")

        t_vw = [None] * 8
        t_vw_used = {}
        B.waits("gpsimd", [t_ffn_done_v])
        for v0 in (0, 1):
            t_vw[v0] = B.do("gpsimd", lambda e, v=v0, w=v0: e.dma_start(
                out=vwstr[w][:], in_=vwr_d[v]), tok=True)
        prev_vev = {}
        for vd in range(8):
            wb = vd % 2
            for tb in range(4):
                key = (vd, tb)
                B.waits("tensor", [t_vw[vd], t_p3a_v if vd == 0 and tb == 0 else None,
                                   prev_vev.get((vd, tb - 2)) or prev_vev.get((vd - 1, 2 + tb))])
                bank = PBK[(vd * 4 + tb) % 2]
                for cc in range(NC):
                    B.do("tensor", lambda e, c=cc, v=vd, t=tb, w=wb, bk=bank: e.matmul(
                        bk[:], vwstr[w][:, c * 128:(c + 1) * 128],
                        chT(xnT, c, t * 512, 512),
                        start=(c == 0), stop=(c == NC - 1)))
                tmm = B.mark("tensor")
                if tb == 3:
                    t_vw_used[vd] = tmm
                B.waits("vector", [tmm])
                tev = B.do("vector", lambda e, bk=bank: e.tensor_copy(aoev[:], bk[:]), tok=True)
                B.waits("tensor", [tev])
                for q4 in range(4):
                    B.do("tensor", lambda e, q=q4: tp(e,
                        PTB[0][:, q * 128:(q + 1) * 128], aoev[:, q * 128:(q + 1) * 128],
                        identb[:], q == 0, q == 3))
                tpe = B.mark("tensor")
                B.waits("vector", [tpe])
                tev2 = B.do("vector", lambda e, v=vd, t=tb: e.tensor_copy(
                    v_sb[:].rearrange("p (tt c) -> p tt c", tt=NT)[:, t * 4:(t + 1) * 4,
                                                                  v * 128:(v + 1) * 128],
                    PTB[0][:, 0:512].rearrange("p (q c) -> p q c", q=4)), tok=True)
                B.waits("tensor", [tev2])
                prev_vev[key] = tev2
            if vd + 2 < 8:
                B.waits("gpsimd", [t_vw_used[vd]])
                t_vw[vd + 2] = B.do("gpsimd", lambda e, v=vd + 2, w=(vd % 2): e.dma_start(
                    out=vwstr[w][:], in_=vwr_d[v]), tok=True)
        t_p3_done = B.tok("vector")

        B.do("vector", lambda e: e.tensor_copy(stats[:, 0:16], ssq_k[:, 0:16]))
        tmax = None
        for tc in range(1, NT):
            tmax = B.do("vector", lambda e, t=tc: e.tensor_tensor(
                stats[:, 0:16], stats[:, 0:16], ssq_k[:, t * H:(t + 1) * H], ALU.max), tok=True)
        B.waits("tensor", [tmax])
        tpe = B.do("tensor", lambda e: e.transpose(PBK[0][0:16, 0:128], stats[:, 0:16], ident[:]), tok=True)
        B.waits("vector", [tpe])
        tev = B.do("vector", lambda e: e.tensor_copy(stbv[0:16, 0:128], PBK[0][0:16, 0:128]), tok=True)
        B.wait("vector", tev)
        tred = B.do("vector", lambda e: e.reduce_max(stbv[0:16, 128:129], stbv[0:16, 0:128], AX.X), tok=True)
        B.waits("tensor", [tred])
        tpe = B.do("tensor", lambda e: e.transpose(
            PBK[1][0:1, 0:16], stbv[0:16, 128:129], ident[0:16, 0:16]), tok=True)
        B.waits("vector", [tpe])
        tev = B.do("vector", lambda e: e.tensor_copy(stbv[0:1, 256:272], PBK[1][0:1, 0:16]), tok=True)
        B.wait("vector", tev)
        B.do("vector", lambda e: e.tensor_copy(stats[:, 0:16], ssq_q[:, 0:16]))
        tmaxq = None
        for tc in range(1, NT):
            tmaxq = B.do("vector", lambda e, t=tc: e.tensor_tensor(
                stats[:, 0:16], stats[:, 0:16], ssq_q[:, t * H:(t + 1) * H], ALU.max), tok=True)
        B.waits("tensor", [tmaxq])
        tpe = B.do("tensor", lambda e: e.transpose(PBK[0][0:16, 0:128], stats[:, 0:16], ident[:]), tok=True)
        B.waits("vector", [tpe])
        tev = B.do("vector", lambda e: e.tensor_copy(stbv[0:16, 0:128], PBK[0][0:16, 0:128]), tok=True)
        B.wait("vector", tev)
        tred = B.do("vector", lambda e: e.reduce_max(stbv[0:16, 128:129], stbv[0:16, 0:128], AX.X), tok=True)
        B.waits("tensor", [tred])
        tpe = B.do("tensor", lambda e: e.transpose(
            PBK[1][0:1, 0:16], stbv[0:16, 128:129], ident[0:16, 0:16]), tok=True)
        B.waits("vector", [tpe])
        tev = B.do("vector", lambda e: e.tensor_copy(stbv[0:1, 272:288], PBK[1][0:1, 0:16]), tok=True)
        B.wait("vector", tev)
        B.do("vector", lambda e: e.tensor_scalar(
            stbv[0:1, 288:304], stbv[0:1, 256:272], 0.0625, None, ALU.mult))
        B.do("vector", lambda e: e.scalar_tensor_tensor(
            stbv[0:1, 288:304], stbv[0:1, 272:288], 4.0, stbv[0:1, 288:304], ALU.mult, ALU.add))
        tcr = B.do("vector", lambda e: e.tensor_scalar(
            stbv[0:1, 288:304], stbv[0:1, 288:304], -1.0, None, ALU.mult), tok=True)
        B.waits("tensor", [tcr])
        tpe = B.do("tensor", lambda e: e.matmul(
            PBK[2][:, 0:16], onesrow[:], stbv[0:1, 288:304], start=True, stop=True), tok=True)
        B.waits("vector", [tpe])
        t_negc2 = B.do("vector", lambda e: e.tensor_copy(negc[:, 0:16], PBK[2][:, 0:16]), tok=True)

        if "qs" in dbg:
            B.waits("sync", [B.tok("vector")])
            B.do("sync", lambda e: e.dma_start(out=dbg_out["qs"], in_=qsT))
        if "ks" in dbg:
            B.waits("sync", [B.tok("vector")])
            B.do("sync", lambda e: e.dma_start(out=dbg_out["ks"], in_=ksT))
        if "v0" in dbg:
            B.waits("sync", [B.tok("vector")])
            B.do("sync", lambda e: e.dma_start(out=dbg_out["v0"], in_=v_sb))
        if "negc" in dbg:
            B.waits("sync", [t_negc2])
            B.do("sync", lambda e: e.dma_start(out=dbg_out["negc"], in_=negc[:]))

    if stop >= 4:
        t_aodma = {}
        t_exp = {}
        t_avz = {}
        last_av_ev = None
        zbanks = [2, 3, 6, 7]
        for h in range(H):
            hc, off = h // 2, (h % 2) * 64
            avb = 4 + (h % 2)
            for ib in range(4):
                key = (h, ib)
                for step in range(18):
                    if step < 16:
                        jc = step
                        sbk = PBK[jc % 2]
                        slot = jc % 8
                        B.waits("tensor", [t_negc2,
                                           t_p3_done if (h == 0 and ib == 0 and jc == 0) else None,
                                           t_exp.get((h, ib, jc - 2))
                                           or t_exp.get((h, ib - 1, 14 + jc))
                                           or t_exp.get((h - 1, 3, 14 + jc))])
                        tS = B.do("tensor", lambda e, j=jc, i=ib, hc2=hc, o=off, bk=sbk: e.matmul(
                            bk[:], chT(ksT, hc2, j * 128, 128, o, 64),
                            chT(qsT, hc2, i * 512, 512, o, 64),
                            start=True, stop=True), tok=True)
                        B.waits("scalar", [tS, t_avz.get((h, ib, jc - 8))
                                           or t_avz.get((h, ib - 1, 8 + jc))
                                           or t_avz.get((h - 1, 3, 8 + jc))])
                        t_exp[(h, ib, jc)] = B.do("scalar", lambda e, s=slot, hh=h, bk=sbk: e.activation(
                            attnT_s[:, s * 512:(s + 1) * 512], bk[:], AF.Exp,
                            bias=negc[:, hh:hh + 1], scale=1.0), tok=True)
                    if step >= 2:
                        jc = step - 2
                        slot = jc % 8
                        B.waits("tensor", [t_exp[(h, ib, jc)],
                                           last_av_ev if jc == 0 else None])
                        B.do("tensor", lambda e, j=jc, s=slot, hh=h, o=off, b=avb: e.matmul(
                            PBK[b][o:o + 64, :], v_sb[:, j * C + hh * DH:j * C + (hh + 1) * DH],
                            attnT_s[:, s * 512:(s + 1) * 512],
                            start=(j == 0), stop=(j == 15),
                            **({"tile_position": (0, 64)} if o else {})))
                        B.do("tensor", lambda e, j=jc, s=slot, hh=h, i=ib: e.matmul(
                            PBK[zbanks[i]][0:16, 0:512], selz[:, hh * 16:(hh + 1) * 16],
                            attnT_s[:, s * 512:(s + 1) * 512],
                            start=(hh == 0 and j == 0), stop=(hh == H - 1 and j == 15)))
                        t_avz[(h, ib, jc)] = B.mark("tensor")
                tavd = t_avz[(h, ib, 15)]
                B.waits("vector", [tavd, t_aodma.get((h, ib - 1)) if ib > 0 else t_aodma.get((h - 1, 3))])
                tev = B.do("vector", lambda e, o=off, b=avb: e.tensor_copy(
                    aoev[o:o + 64, :], PBK[b][o:o + 64, :]), tok=True)
                last_av_ev = tev
                B.waits("sync", [tev])
                t_aodma[key] = B.do("sync", lambda e, hcc=hc, o=off, i=ib: e.dma_start(
                    out=ao_d[hcc, o:o + 64, i * 512:(i + 1) * 512], in_=aoev[o:o + 64, :]), tok=True)
        t_zev = {}
        for ib in range(4):
            B.waits("vector", [t_avz[(H - 1, ib, 15)]])
            B.do("vector", lambda e, i=ib: e.reciprocal(
                PBK[zbanks[i]][0:16, 0:512], PBK[zbanks[i]][0:16, 0:512]))
            t_zev[ib] = B.do("vector", lambda e, i=ib: e.tensor_copy(
                izTb[:, i * 512:(i + 1) * 512], PBK[zbanks[i]][0:16, 0:512]), tok=True)
        t_izT = t_zev[3]

        if "ao" in dbg:
            B.waits("sync", [t_aodma[(H - 1, 3)]])
            B.do("sync", lambda e: e.dma_start(out=dbg_out["ao"], in_=ao_d))
    if stop >= 5:
        B.waits("gpsimd", [B.tok("tensor")])
        t_projw = B.do("gpsimd", lambda e: e.dma_start(
            out=projw[:].rearrange("p (cc c) -> p cc c", cc=NC),
            in_=projT_d.rearrange("(cc p) c -> p cc c", p=128)), tok=True)
        prev_p5 = {}
        t_hi_ev = {}
        t_lo_used = {}
        t_lnq5 = {}
        t_izb_used = None
        t_xn1_toks = [None] * NT
        for tc in range(NT):
            bi = tc % 2
            cb = bi * 4
            B.waits("sync", [t_aodma[(H - 1, 3)]] + ([prev_p5[tc - 2][0]] if tc >= 2 else []))
            tld = B.do("sync", lambda e, t=tc, b=bi: e.dma_start(
                out=aold[b][:].rearrange("p (hc t) -> p hc t", hc=NC),
                in_=ao_d[:, :, t * 128:(t + 1) * 128].rearrange("hc p t -> p hc t")), tok=True)
            B.waits("tensor", [t_izT, tld] + ([t_izb_used] if t_izb_used else []))
            for hcc in range(NC):
                B.do("tensor", lambda e, hc2=hcc, t=tc: e.matmul(
                    PBK[5][:, 0:128], selallb[:, hc2 * 128:(hc2 + 1) * 128],
                    izTb[:, t * 128:(t + 1) * 128], start=True, stop=True))
                tpe = B.mark("tensor")
                B.waits("vector", [tpe])
                tev = B.do("vector", lambda e, hc2=hcc: e.tensor_copy(
                    izbb[:, hc2 * 128:(hc2 + 1) * 128], PBK[5][:, 0:128]), tok=True)
                B.waits("tensor", [tev])
            tsc = B.do("vector", lambda e, b=bi: e.tensor_tensor(
                aold[b][:], aold[b][:], izbb[:, 0:1024], ALU.mult), tok=True)
            t_izb_used = tsc
            B.waits("tensor", [tsc, t_projw, t_pb]
                    + ([prev_p5[tc - 2][1]] if tc >= 2 else []))
            for jb in range(2):
                B.do("tensor", lambda e, j=jb: e.matmul(
                    PBK[j][:], onesrowb[:], pb_sb[0:1, j * 512:(j + 1) * 512],
                    start=True, stop=False))
            for hcc in range(NC):
                for jb in range(2):
                    B.do("tensor", lambda e, hc2=hcc, j=jb, b=bi: e.matmul(
                        PBK[j][:], aold[b][:, hc2 * 128:(hc2 + 1) * 128],
                        projw[:, hc2 * C + j * 512:hc2 * C + (j + 1) * 512],
                        start=False, stop=(hc2 == NC - 1)))
            tmm = B.mark("tensor")
            B.waits("sync", [t_ffn_done_s] + ([prev_p5[tc - 2][2]] if tc >= 2 else []))
            txld = B.do("sync", lambda e, t=tc, b=bi: e.dma_start(
                out=xt[b], in_=xs_d[t * 128:(t + 1) * 128, :]), tok=True)
            B.waits("vector", [tmm, txld])
            B.do("vector", lambda e, b=bi: e.tensor_tensor(
                xt[b][:, 0:512], xt[b][:, 0:512], PBK[0][:], ALU.add))
            tx1 = B.do("vector", lambda e, b=bi: e.tensor_tensor(
                xt[b][:, 512:1024], xt[b][:, 512:1024], PBK[1][:], ALU.add), tok=True)
            B.waits("sync", [tx1])
            tst = B.do("sync", lambda e, t=tc, b=bi: e.dma_start(
                out=xs_d[t * 128:(t + 1) * 128, :], in_=xt[b]), tok=True)
            if "x1" in dbg:
                B.do("sync", lambda e, t=tc, b=bi: e.dma_start(
                    out=dbg_out["x1"][t * 128:(t + 1) * 128, :], in_=xt[b]))
            t_xn1, _p = emit_ln(xt[bi], xn_s[bi], [tx1], cb, t_lnq5.get(tc - 2))
            t_lnq5[tc] = _p
            t_xn1_toks[tc] = t_xn1
            B.waits("scalar", [t_xn1, t_hi_ev.get(tc - 2)])
            thi = B.do("scalar", lambda e, b=bi: e.copy(tb4[b][:], xn_s[b]), tok=True)
            B.waits("vector", [thi])
            tlo = B.do("vector", lambda e, b=bi: e.tensor_tensor(
                tb4[2 + b][:], xn_s[b], tb4[b][:], ALU.subtract), tok=True)
            B.waits("tensor", [thi, tlo, t_hi_ev.get(tc - 1)])
            for cc in range(NC):
                B.do("tensor", lambda e, c=cc, b=bi: tp(e,
                    PTB[0][:, c * 128:(c + 1) * 128], tb4[b][:, c * 128:(c + 1) * 128], identb[:],
                    c == 0, c == NC - 1))
            th0 = B.mark("tensor")
            for cc in range(NC):
                B.do("tensor", lambda e, c=cc, b=bi: tp(e,
                    PTB[1][:, c * 128:(c + 1) * 128], tb4[2 + b][:, c * 128:(c + 1) * 128], identb[:],
                    c == 0, c == NC - 1))
            th1 = B.mark("tensor")
            B.waits("vector", [th0, th1])
            B.do("vector", lambda e, t=tc: e.tensor_copy(
                xnT.rearrange("p (cc tt) -> p cc tt", cc=NC)[:, :, t * 128:(t + 1) * 128],
                PTB[0][:].rearrange("p (cc tt) -> p cc tt", cc=NC)))
            t_hi_ev[tc] = B.do("vector", lambda e, t=tc: e.tensor_copy(
                loT.rearrange("p (cc tt) -> p cc tt", cc=NC)[:, :, t * 128:(t + 1) * 128],
                PTB[1][:].rearrange("p (cc tt) -> p cc tt", cc=NC)), tok=True)
            prev_p5[tc] = (tld, tmm, tst)
        t_p5_v = B.tok("vector")
        if "xnT1" in dbg:
            B.waits("sync", [t_p5_v])
            B.do("sync", lambda e: e.dma_start(out=dbg_out["xnT1"], in_=xnT))
        if "loT1" in dbg:
            B.waits("sync", [t_p5_v])
            B.do("sync", lambda e: e.dma_start(out=dbg_out["loT1"], in_=loT))
        t_p5_s = B.tok("sync")

    if stop >= 6:
        t_wte = [None] * NT
        prev_g = {}
        for tc in range(NT):
            B.waits("tensor", [t_hi_ev[NT - 1], t_glo] + ([prev_g.get(tc - 2)] if tc >= 2 else []))
            gbk = PBK[2 + tc % 2]
            for cc in range(NC):
                B.do("tensor", lambda e, c=cc, t=tc, bk=gbk: e.matmul(
                    bk[:, 0:8], chT(xnT, c, t * 128, 128), ghi[:, c * E:(c + 1) * E],
                    start=(c == 0), stop=False))
                B.do("tensor", lambda e, c=cc, t=tc, bk=gbk: e.matmul(
                    bk[:, 0:8], chT(xnT, c, t * 128, 128), glo[:, c * E:(c + 1) * E],
                    start=False, stop=False))
            for cc in range(NC):
                B.do("tensor", lambda e, c=cc, t=tc, bk=gbk: e.matmul(
                    bk[:, 0:8], chT(loT, c, t * 128, 128), ghi[:, c * E:(c + 1) * E],
                    start=False, stop=(c == NC - 1)))
            tgm = B.mark("tensor")
            B.waits("vector", [tgm])
            tz0 = B.do("vector", lambda e, bk=gbk: e.tensor_copy(zgbuf[:, 0:8], bk[:, 0:8]), tok=True)
            prev_g[tc] = tz0
            B.wait("vector", tz0)
            if "glog" in dbg:
                B.do("vector", lambda e, t=tc: e.tensor_copy(
                    wte[:, 0:1] if False else zs2[:], zgbuf[:, 0:8]))
                B.waits("sync", [B.tok("vector")])
                B.do("sync", lambda e, t=tc: e.dma_start(
                    out=dbg_out["glog"][:, t * E:(t + 1) * E], in_=zgbuf[:, 0:8]))
            t1 = B.do("vector", lambda e: e.reduce_max(zgbuf[:, 8:9], zgbuf[:, 0:8], AX.X), tok=True)
            B.wait("vector", t1)
            t2 = B.do("vector", lambda e: e.tensor_scalar(
                zgbuf[:, 16:24], zgbuf[:, 0:8], zgbuf[:, 8:9], -1e30, ALU.is_equal, ALU.mult), tok=True)
            B.wait("vector", t2)
            t3 = B.do("vector", lambda e: e.tensor_tensor(
                zgbuf[:, 16:24], zgbuf[:, 16:24], zgbuf[:, 0:8], ALU.add), tok=True)
            B.wait("vector", t3)
            t4 = B.do("vector", lambda e: e.reduce_max(zgbuf[:, 9:10], zgbuf[:, 16:24], AX.X), tok=True)
            B.wait("vector", t4)
            t5 = B.do("vector", lambda e: e.tensor_scalar(
                zgbuf[:, 10:11], zgbuf[:, 8:9], -1.0, None, ALU.mult), tok=True)
            B.waits("scalar", [t5])
            t6 = B.do("scalar", lambda e: e.activation(
                zgbuf[:, 16:24], zgbuf[:, 0:8], AF.Exp, bias=zgbuf[:, 10:11], scale=1.0), tok=True)
            B.waits("vector", [t6])
            t7 = B.do("vector", lambda e: e.tensor_scalar(
                zs1[:], zgbuf[:, 0:8], zgbuf[:, 9:10], None, ALU.is_ge), tok=True)
            B.wait("vector", t7)
            t8 = B.do("vector", lambda e: e.tensor_tensor(
                zs2[:], zgbuf[:, 16:24], zs1[:], ALU.mult), tok=True)
            B.wait("vector", t8)
            t9 = B.do("vector", lambda e: e.tensor_reduce(zgbuf[:, 11:12], zs2[:], AX.X, ALU.add), tok=True)
            B.wait("vector", t9)
            t10 = B.do("vector", lambda e: e.reciprocal(zgbuf[:, 12:13], zgbuf[:, 11:12]), tok=True)
            B.wait("vector", t10)
            t_wte[tc] = B.do("vector", lambda e, t=tc: e.tensor_scalar(
                wte[:, t * E:(t + 1) * E], zs2[:], zgbuf[:, 12:13], None, ALU.mult), tok=True)
        t_p6_v = B.tok("vector")
        if "wte" in dbg:
            B.waits("sync", [t_p6_v])
            B.do("sync", lambda e: e.dma_start(out=dbg_out["wte"], in_=wte[:]))

    if stop >= 7:
        t_cs = B.do("sync", lambda e: e.dma_start(out=cosv, in_=cos_d[:]), tok=True)
        t_sn = B.do("sync", lambda e: e.dma_start(out=sinv, in_=sin_d[:]), tok=True)
        B.waits("sync", [t_p5_v, t_p6_v])

        def rope_tile(tc, src_f32, dst_bf, dep_toks):
            cs = cosv[:, tc * 32:(tc + 1) * 32].unsqueeze(1).broadcast_to([128, H, 32])
            sn = sinv[:, tc * 32:(tc + 1) * 32].unsqueeze(1).broadcast_to([128, H, 32])
            s3 = src_f32.rearrange("p (h d) -> p h d", h=H)
            d3 = dst_bf.rearrange("p (h d) -> p h d", h=H)
            rA = ropeA2.rearrange("p (h d) -> p h d", h=H, d=32)
            rB = ropeB2.rearrange("p (h d) -> p h d", h=H, d=32)
            B.waits("vector", dep_toks)
            B.do("vector", lambda e: e.tensor_tensor(rA, s3[:, :, 0:32], cs, ALU.mult))
            B.do("vector", lambda e: e.tensor_tensor(rB, s3[:, :, 32:64], sn, ALU.mult))
            B.do("vector", lambda e: e.tensor_tensor(d3[:, :, 0:32], rA, rB, ALU.subtract))
            B.do("vector", lambda e: e.tensor_tensor(rA, s3[:, :, 0:32], sn, ALU.mult))
            B.do("vector", lambda e: e.tensor_tensor(rB, s3[:, :, 32:64], cs, ALU.mult))
            return B.do("vector", lambda e: e.tensor_tensor(d3[:, :, 32:64], rA, rB, ALU.add), tok=True)

        B.waits("gpsimd", [t_p5_v, t_p6_v])
        t_lw = B.do("gpsimd", lambda e: e.dma_start(
            out=l1w[:].rearrange("p (cc c) -> p cc c", cc=NC),
            in_=wq_d.rearrange("(cc p) c -> p cc c", p=128)), tok=True)
        prev_q7 = {}
        tq7_ev = {}
        tq0s = {}
        for tc in range(NT):
            bi = tc % 2
            qb0 = 2 * (tc % 2)
            B.waits("tensor", [t_lw] + ([prev_q7.get(tc - 2)]))
            for cc in range(NC):
                for jb in range(2):
                    B.do("tensor", lambda e, c=cc, j=jb, t=tc, b0=qb0: e.matmul(
                        PBK[b0 + j][:], chT(xnT, c, t * 128, 128),
                        l1w[:, c * C + j * 512:c * C + (j + 1) * 512],
                        start=(c == 0), stop=(c == NC - 1)))
            tmm = B.mark("tensor")
            B.waits("vector", [tmm, tq7_ev.get(tc - 2)])
            B.do("vector", lambda e, b0=qb0: e.tensor_copy(l1qf[:, 0:512], PBK[b0][:]))
            tqe = B.do("vector", lambda e, b0=qb0: e.tensor_copy(l1qf[:, 512:1024], PBK[b0 + 1][:]), tok=True)
            prev_q7[tc] = tqe
            trp = rope_tile(tc, l1qf, tb4[bi][:], [tqe, tq0s.get(tc - 2)])
            B.waits("tensor", [trp, tq7_ev.get(tc - 1)])
            for cc in range(NC):
                B.do("tensor", lambda e, c=cc, b=bi: tp(e,
                    PTB[0][:, c * 128:(c + 1) * 128], tb4[b][:, c * 128:(c + 1) * 128], identb[:],
                    c == 0, c == NC - 1))
            tq0 = B.mark("tensor")
            tq0s[tc] = tq0
            B.waits("vector", [tq0])
            tq7_ev[tc] = B.do("vector", lambda e, t=tc: e.tensor_copy(
                qrT.rearrange("p (cc tt) -> p cc tt", cc=NC)[:, :, t * 128:(t + 1) * 128],
                PTB[0][:].rearrange("p (cc tt) -> p cc tt", cc=NC)), tok=True)
        t_qr_done = B.tok("vector")
        if "qrT" in dbg:
            B.waits("sync", [t_qr_done])
            B.do("sync", lambda e: e.dma_start(out=dbg_out["qrT"], in_=qrT))

        B.waits("gpsimd", [B.tok("tensor")])
        t_lwk = B.do("gpsimd", lambda e: e.dma_start(
            out=l1w[:].rearrange("p (cc c) -> p cc c", cc=NC),
            in_=wk_d.rearrange("(cc p) c -> p cc c", p=128)), tok=True)
        tz_ekl = B.do("vector", lambda e: e.memset(ekl[0:76, 0:C], 0.0), tok=True)
        prev_k7 = {}
        t_kg_ev = {}
        t_exk = {}
        t_kgpe = {}
        t_ekl_toks = []
        for tc in range(NT):
            bi = tc % 2
            qb0 = 2 * (tc % 2)
            B.waits("tensor", [t_lwk, prev_q7[NT - 2] if tc == 0 else None,
                               prev_q7[NT - 1] if tc == 1 else None] + ([prev_k7.get(tc - 2)]))
            for cc in range(NC):
                for jb in range(2):
                    B.do("tensor", lambda e, c=cc, j=jb, t=tc, b0=qb0: e.matmul(
                        PBK[b0 + j][:], chT(xnT, c, t * 128, 128),
                        l1w[:, c * C + j * 512:c * C + (j + 1) * 512],
                        start=(c == 0), stop=(c == NC - 1)))
            tmm = B.mark("tensor")
            B.waits("vector", [tmm, prev_k7.get(tc - 2)])
            B.do("vector", lambda e, b0=qb0: e.tensor_copy(l1qf[:, 0:512], PBK[b0][:]))
            tke = B.do("vector", lambda e, b0=qb0: e.tensor_copy(l1qf[:, 512:1024], PBK[b0 + 1][:]), tok=True)
            prev_k7[tc] = tke
            trp = rope_tile(tc, l1qf, tb4[bi][:],
                            [tke, tz_ekl, t_exk.get(tc - 2), t_kgpe.get(tc - 2)])
            if tc >= 1:
                B.waits("scalar", [trp])
                tex = B.do("scalar", lambda e, t=tc, b=bi: e.activation(
                    ekl[:, (t - 1) * C:t * C], tb4[b][:], AF.Exp, bias=0.0, scale=1.0), tok=True)
                t_exk[tc] = tex
                if tc == 1:
                    B.waits("vector", [tex])
                    tex = B.do("vector", lambda e: e.memset(ekl[0:76, 0:C], 0.0), tok=True)
                t_ekl_toks.append(tex)
            if tc <= 1:
                B.waits("tensor", [trp, t_kg_ev.get(tc - 1)])
                nrow = 128 if tc == 0 else 76
                for cc in range(NC):
                    B.do("tensor", lambda e, c=cc, b=bi, n=nrow: tp(e,
                        PTB[0][:, c * 128:c * 128 + n],
                        tb4[b][0:n, c * 128:(c + 1) * 128], identb[0:n, 0:n],
                        c == 0, c == NC - 1))
                tkp = B.mark("tensor")
                t_kgpe[tc] = tkp
                B.waits("vector", [tkp])
                co = 0 if tc == 0 else 128
                nrow2 = 128 if tc == 0 else 76
                t_kg_ev[tc] = B.do("vector", lambda e, c0=co, n=nrow2: e.tensor_copy(
                    kgT[:].rearrange("p (hc g) -> p hc g", hc=NC)[:, :, c0:c0 + n],
                    PTB[0][:].rearrange("p (hc g) -> p hc g", hc=NC)[:, :, 0:n]), tok=True)
        t_ekl_done_s = B.tok("scalar")
        t_ekl_done_v = B.tok("vector")
        t_kg_done = B.tok("vector")
        if "ekl" in dbg:
            B.waits("sync", [t_ekl_done_s, t_kg_done])
            B.do("sync", lambda e: e.dma_start(out=dbg_out["ekl"], in_=ekl[:, 0:15 * C]))
        if "kgT" in dbg:
            B.waits("sync", [t_kg_done])
            B.do("sync", lambda e: e.dma_start(
                out=dbg_out["kgT"].rearrange("p (hc g) -> p hc g", hc=NC),
                in_=kgT[:].rearrange("p (hc g) -> p hc g", hc=NC)[:, :, 0:G]))

        t_vw1 = [None] * 8
        t_vw1_used = {}
        B.waits("gpsimd", [B.tok("tensor")])
        for v0 in (0, 1):
            t_vw1[v0] = B.do("gpsimd", lambda e, v=v0, w=v0: e.dma_start(
                out=vw1str[w][:], in_=vwr1_d[v]), tok=True)
        prev_vev1 = {}
        for vd in range(8):
            wb = vd % 2
            for tb in range(4):
                key = (vd, tb)
                B.waits("tensor", [t_vw1[vd],
                                   prev_k7[NT - 2] if (vd == 0 and tb == 0) else None,
                                   prev_k7[NT - 1] if (vd == 0 and tb == 0) else None,
                                   prev_vev1.get((vd, tb - 2)) or prev_vev1.get((vd - 1, 2 + tb))])
                bank = PBK[(vd * 4 + tb) % 2 + 2]
                for cc in range(NC):
                    B.do("tensor", lambda e, c=cc, t=tb, w=wb, bk=bank: e.matmul(
                        bk[:], vw1str[w][:, c * 128:(c + 1) * 128],
                        chT(xnT, c, t * 512, 512),
                        start=(c == 0), stop=(c == NC - 1)))
                tmm = B.mark("tensor")
                if tb == 3:
                    t_vw1_used[vd] = tmm
                B.waits("vector", [tmm])
                tev = B.do("vector", lambda e, bk=bank: e.tensor_copy(aoev[:], bk[:]), tok=True)
                B.waits("tensor", [tev])
                for q4 in range(4):
                    B.do("tensor", lambda e, q=q4: tp(e,
                        PTB[0][:, q * 128:(q + 1) * 128], aoev[:, q * 128:(q + 1) * 128],
                        identb[:], q == 0, q == 3))
                tpe = B.mark("tensor")
                B.waits("vector", [tpe])
                tev2 = B.do("vector", lambda e, v=vd, t=tb: e.tensor_copy(
                    v_sb[:].rearrange("p (tt c) -> p tt c", tt=NT)[:, t * 4:(t + 1) * 4,
                                                                  v * 128:(v + 1) * 128],
                    PTB[0][:, 0:512].rearrange("p (q c) -> p q c", q=4)), tok=True)
                B.waits("tensor", [tev2])
                prev_vev1[key] = tev2
            if vd + 2 < 8:
                B.waits("gpsimd", [t_vw1_used[vd]])
                t_vw1[vd + 2] = B.do("gpsimd", lambda e, v=vd + 2, w=(vd % 2): e.dma_start(
                    out=vw1str[w][:], in_=vwr1_d[v]), tok=True)
        t_v1_done = B.tok("vector")
        if "v0" in dbg and stop >= 7:
            B.waits("sync", [t_v1_done])
            B.do("sync", lambda e: e.dma_start(out=dbg_out["v0"], in_=v_sb))

    if stop >= 8:
        tS_ev = {}
        for hc in range(NC):
            B.waits("tensor", [t_ekl_done_s, t_ekl_done_v, t_consts] + ([tS_ev.get(hc - 1)]))
            for lt in range(15):
                B.do("tensor", lambda e, l=lt, c=hc: e.matmul(
                    PBK[4][:, 0:1], ekl[:, l * C + c * 128:l * C + (c + 1) * 128],
                    onescolb[:], start=(l == 0), stop=(l == 14)))
            tS = B.mark("tensor")
            B.waits("vector", [tS])
            tS_ev[hc] = B.do("vector", lambda e, c=hc: e.tensor_copy(
                iS8[:, c:c + 1], PBK[4][:, 0:1]), tok=True)
        B.wait("vector", tS_ev[NC - 1])
        t_iS = B.do("vector", lambda e: e.reciprocal(iS8[:], iS8[:]), tok=True)
        B.wait("vector", t_iS)
        if "iS8" in dbg:
            B.waits("sync", [t_iS])
            B.do("sync", lambda e: e.dma_start(out=dbg_out["iS8"], in_=iS8[:]))

        t_lodma = {}
        t_kvev = {}
        t_kvev_s = {}
        t_tlm = {}
        t_tle = {}
        for hc in range(NC):
            kb = kv2[hc % 2]
            B.waits("vector", [t_tlm.get((hc - 2, 3))])
            tzm = B.do("vector", lambda e, k=kb: e.memset(k[:], 0.0), tok=True)
            for sub in range(2):
                h = 2 * hc + sub
                off = sub * 64
                bk = PBK[5 - sub]
                B.waits("tensor", [t_v1_done, tzm, t_kvev_s.get((hc - 1, sub)),
                                   tS_ev[NC - 1] if (hc == 0 and sub == 1) else None])
                for lt in range(15):
                    B.do("tensor", lambda e, l=lt, hh=h, o=off, bb=bk: e.matmul(
                        bb[o:o + 64, o:o + 64],
                        v_sb[:, (l + 1) * C + hh * DH:(l + 1) * C + (hh + 1) * DH],
                        ekl[:, l * C + hh * DH:l * C + (hh + 1) * DH],
                        start=(l == 0), stop=(l == 14),
                        **({"tile_position": (0, 64)} if o else {})))
                tkv = B.mark("tensor")
                B.waits("vector", [tkv])
                tev = B.do("vector", lambda e, k=kb, o=off, bb=bk: e.tensor_copy(
                    k[o:o + 64, o:o + 64], bb[o:o + 64, o:o + 64]), tok=True)
                t_kvev_s[(hc, sub)] = tev
                t_kvev[hc] = tev
            if "kv2" in dbg and hc <= 1:
                B.waits("sync", [t_kvev[hc]])
                B.do("sync", lambda e, c=hc: e.dma_start(out=dbg_out["kv2"][c], in_=kv2[c][:]))
            blocks = [(G, 512), (716, 512), (1228, 512), (1740, 308)]
            for nb, (c0, w) in enumerate(blocks):
                B.waits("tensor", [t_kvev[hc], t_qr_done if hc == 0 and nb == 0 else None,
                                   t_tle.get((hc, nb - 2)) or t_tle.get((hc - 1, nb + 2))])
                tlm = B.do("tensor", lambda e, k=kb, c=hc, c0=c0, w=w, n=nb: e.matmul(
                    PBK[n % 2][:, 0:w], k[:], chT(qrT, c, c0, w),
                    start=True, stop=True), tok=True)
                t_tlm[(hc, nb)] = tlm
                B.waits("vector", [tlm])
                tle = B.do("vector", lambda e, c=hc, w=w, n=nb: e.tensor_scalar(
                    aoev[:, 0:w], PBK[n % 2][:, 0:w], iS8[:, c:c + 1], None, ALU.mult), tok=True)
                t_tle[(hc, nb)] = tle
                if "aoev0" in dbg and hc == 0 and nb == 0:
                    B.waits("sync", [tle])
                    tdmp = B.do("sync", lambda e: e.dma_start(
                        out=dbg_out["aoev0"], in_=aoev[:]), tok=True)
                    B.waits("vector", [tdmp])
                B.waits("sync", [tle])
                t_lodma[(hc, nb)] = B.do("sync", lambda e, c=hc, c0=c0, w=w: e.dma_start(
                    out=ao_d[c, :, c0:c0 + w], in_=aoev[:, 0:w]), tok=True)
                B.waits("vector", [t_lodma[(hc, nb)]])
        t_p8_pe = B.tok("tensor")

    if stop >= 9:
        t_godma = {}
        for h in range(H):
            hc, off = h // 2, (h % 2) * 64
            B.waits("tensor", [t_kg_done, t_qr_done, t_p8_pe if h == 0 else None,
                               t_godma.get(h - 2)])
            B.do("tensor", lambda e, c=hc, o=off: e.matmul(
                PBK[2][:, 0:G], chT(qrT, c, 0, 128, o, 64),
                kgT[o:o + 64, c * 256:c * 256 + G], start=True, stop=True))
            tg1 = B.do("tensor", lambda e, c=hc, o=off: e.matmul(
                PBK[3][0:76, 0:G], chT(qrT, c, 128, 76, o, 64),
                kgT[o:o + 64, c * 256:c * 256 + G], start=True, stop=True), tok=True)
            B.waits("vector", [tg1])
            tm1 = B.do("vector", lambda e: e.reduce_max(zgbuf[:, 0:1], PBK[2][:, 0:G], AX.X), tok=True)
            tm2 = B.do("vector", lambda e: e.reduce_max(zgbuf[0:76, 1:2], PBK[3][0:76, 0:G], AX.X), tok=True)
            B.waits("vector", [tm1, tm2])
            tn1 = B.do("vector", lambda e: e.tensor_scalar(
                zgbuf[:, 2:3], zgbuf[:, 0:1], -0.125, None, ALU.mult), tok=True)
            tn2 = B.do("vector", lambda e: e.tensor_scalar(
                zgbuf[0:76, 3:4], zgbuf[0:76, 1:2], -0.125, None, ALU.mult), tok=True)
            B.waits("scalar", [tn1, tn2])
            te1 = B.do("scalar", lambda e: e.activation(
                aoev[:, 0:G], PBK[2][:, 0:G], AF.Exp, bias=zgbuf[:, 2:3], scale=0.125,
                accum_out=zgbuf[:, 4:5]), tok=True)
            te2 = B.do("scalar", lambda e: e.activation(
                aoev[0:76, 256:256 + G], PBK[3][0:76, 0:G], AF.Exp, bias=zgbuf[0:76, 3:4],
                scale=0.125, accum_out=zgbuf[0:76, 5:6]), tok=True)
            B.waits("vector", [te1, te2])
            tr1 = B.do("vector", lambda e: e.reciprocal(zgbuf[:, 6:7], zgbuf[:, 4:5]), tok=True)
            tr2 = B.do("vector", lambda e: e.reciprocal(zgbuf[0:76, 7:8], zgbuf[0:76, 5:6]), tok=True)
            B.waits("vector", [tr1, tr2])
            ts1 = B.do("vector", lambda e: e.tensor_scalar(
                aoev[:, 0:G], aoev[:, 0:G], zgbuf[:, 6:7], None, ALU.mult), tok=True)
            ts2 = B.do("vector", lambda e: e.tensor_scalar(
                aoev[0:76, 256:256 + G], aoev[0:76, 256:256 + G], zgbuf[0:76, 7:8],
                None, ALU.mult), tok=True)
            B.waits("tensor", [ts1, ts2])
            B.do("tensor", lambda e: tp(e, PTB[0][:, 0:128], aoev[:, 0:128], identb[:], True, False))
            B.do("tensor", lambda e: tp(e, PTB[0][:, 128:204], aoev[0:76, 256:256 + 128],
                                        identb[0:76, 0:76], False, True))
            B.do("tensor", lambda e: tp(e, PTB[1][0:76, 0:128], aoev[:, 128:204], identb[:], True, False))
            tg2 = B.do("tensor", lambda e: tp(e, PTB[1][0:76, 128:204],
                                              aoev[0:76, 256 + 128:256 + 204],
                                              identb[0:76, 0:76], False, True), tok=True)
            B.waits("vector", [tg2])
            tga = B.do("vector", lambda e: e.tensor_copy(agT0[:, 0:G], PTB[0][:, 0:G]), tok=True)
            tgb = B.do("vector", lambda e: e.tensor_copy(agT1[:, 0:G], PTB[1][0:76, 0:G]), tok=True)
            B.waits("tensor", [tga, tgb])
            B.do("tensor", lambda e, hh=h, o=off: e.matmul(
                PBK[4][o:o + 64, 0:G], v_sb[:, 0 * C + hh * DH:0 * C + (hh + 1) * DH],
                agT0[:, 0:G], start=True, stop=False,
                **({"tile_position": (0, 64)} if o else {})))
            tgo = B.do("tensor", lambda e, hh=h, o=off: e.matmul(
                PBK[4][o:o + 64, 0:G], v_sb[0:76, 1 * C + hh * DH:1 * C + (hh + 1) * DH],
                agT1[0:76, 0:G], start=False, stop=True,
                **({"tile_position": (0, 64)} if o else {})), tok=True)
            B.waits("vector", [tgo])
            tge = B.do("vector", lambda e, o=off: e.tensor_copy(
                aoev[o:o + 64, 256:256 + G], PBK[4][o:o + 64, 0:G]), tok=True)
            B.waits("sync", [tge])
            t_godma[h] = B.do("sync", lambda e, c=hc, o=off: e.dma_start(
                out=ao_d[c, o:o + 64, 0:G], in_=aoev[o:o + 64, 256:256 + G]), tok=True)
            B.waits("vector", [t_godma[h]])
            B.waits("scalar", [t_godma[h]])
        t_p9_pe = B.tok("tensor")
        t_p9_sc = B.tok("scalar")
        if "ao1" in dbg:
            B.waits("sync", [t_godma[H - 1], t_lodma[(NC - 1, 3)]])
            B.do("sync", lambda e: e.dma_start(out=dbg_out["ao1"], in_=ao_d))

    if stop >= 10:
        B.waits("gpsimd", [t_p9_pe])
        t_wo = B.do("gpsimd", lambda e: e.dma_start(
            out=woutw[:].rearrange("p (cc c) -> p cc c", cc=NC),
            in_=wo_d.rearrange("(cc p) c -> p cc c", p=128)), tok=True)
        prev_w = {}
        for tc in range(NT):
            bi = tc % 2
            jp = 2 * (tc % 2)
            B.waits("sync", [t_godma[H - 1], t_lodma[(NC - 1, 3)]]
                    + ([prev_w[tc - 2][0]] if tc >= 2 else []))
            tld = B.do("sync", lambda e, t=tc, b=bi: e.dma_start(
                out=aold[b][:].rearrange("p (hc t) -> p hc t", hc=NC),
                in_=ao_d[:, :, t * 128:(t + 1) * 128].rearrange("hc p t -> p hc t")), tok=True)
            B.waits("tensor", [t_wo, tld, t_p9_pe,
                               t_tle[(NC - 1, 3)] if tc == 0 else None,
                               t_p9_sc if tc == 1 else None]
                    + ([prev_w[tc - 2][3]] if tc >= 2 else []))
            for hcc in range(NC):
                for jb in range(2):
                    B.do("tensor", lambda e, hc2=hcc, j=jb, b=bi, p=jp: e.matmul(
                        PBK[p + j][:], aold[b][:, hc2 * 128:(hc2 + 1) * 128],
                        woutw[:, hc2 * C + j * 512:hc2 * C + (j + 1) * 512],
                        start=(hc2 == 0), stop=(hc2 == NC - 1)))
            tmm = B.mark("tensor")
            B.waits("sync", [t_p5_s] + ([prev_w[tc - 1][3]] if tc >= 1 else []))
            txld = B.do("sync", lambda e, t=tc: e.dma_start(
                out=xt10, in_=xs_d[t * 128:(t + 1) * 128, :]), tok=True)
            B.waits("vector", [tmm, txld, t_p8_pe if tc == 0 else None])
            B.do("vector", lambda e, t=tc, p=jp: e.tensor_tensor(
                x_sbf[:, t * C:t * C + 512], xt10[:, 0:512], PBK[p][:], ALU.add))
            tx2 = B.do("vector", lambda e, t=tc, p=jp: e.tensor_tensor(
                x_sbf[:, t * C + 512:(t + 1) * C], xt10[:, 512:1024], PBK[p + 1][:], ALU.add), tok=True)
            tst = tx2
            prev_w[tc] = (tld, tmm, txld, tx2)
            if "x2" in dbg:
                B.waits("sync", [tx2])
                B.do("sync", lambda e, t=tc: e.dma_start(
                    out=dbg_out["x2"][t * 128:(t + 1) * 128, :], in_=x_sbf[:, t * C:(t + 1) * C]))
        t_p10_v = B.tok("vector")

    if stop >= 11:
        t_b2 = {}
        t_w1ld = {}
        t_w2ld = {}
        t_gelu = {}
        t_yev = {}
        t_p1g = {}
        t_ph2mm = {}
        t_bias = {}
        for ei in range(E):
            B.waits("gpsimd", [t_p10_v if ei == 0 else None, t_bias.get((ei - 1, 3))])
            t_b2[ei] = B.do("gpsimd", lambda e, x=ei: e.dma_start(
                out=b2row[:], in_=b2_d[x:x + 1, :]), tok=True)
            for tb in range(4):
                for fc in range(NF):
                    fls = [0, 1, 2, 3] if fc == 0 else ([fc + 3] if fc + 3 < NF else [])
                    for fl in fls:
                        prev_use = (t_p1g.get((ei, tb, fl - 4))
                                    or t_p1g.get((ei, tb - 1, NF - 4 + fl))
                                    or t_p1g.get((ei - 1, 3, NF - 4 + fl)))
                        B.waits("gpsimd", [prev_use,
                                           t_p10_v if (ei == 0 and tb == 0 and fc == 0) else None])
                        t_w1ld[(ei, tb, fl)] = B.do("gpsimd", lambda e, x=ei, f=fl: e.dma_start(
                            out=w1b4[f % 4][:], in_=w1_d[x, f]), sem=f"m1s{fl % 4}")
                    bank = PBK[4 + (fc % 4)]
                    B.waits("tensor", [t_w1ld[(ei, tb, fc)],
                                       t_gelu.get((ei, tb, fc - 4))
                                       or t_gelu.get((ei, tb - 1, NF - 4 + fc))
                                       or t_gelu.get((ei - 1, 3, NF - 4 + fc)),
                                       (t_yev.get((ei, tb - 1)) or t_yev.get((ei - 1, 3)))
                                       if fc < 4 else None])
                    for cc in range(NC):
                        B.do("tensor", lambda e, c=cc, f=fc, t=tb, bk=bank: e.matmul(
                            bk[:], w1b4[f % 4][:, c * 128:(c + 1) * 128],
                            chT(xnT, c, t * 512, 512),
                            start=(c == 0), stop=(c == NC - 1)))
                    t_p1g[(ei, tb, fc)] = B.mark("tensor")
                    B.waits("scalar", [t_p1g[(ei, tb, fc)]])
                    t_gelu[(ei, tb, fc)] = B.do("scalar", lambda e, x=ei, f=fc, bk=bank: e.activation(
                        hT[:, f * 512:(f + 1) * 512], bk[:], AF.Gelu,
                        bias=b1c[:, x * NF + f:x * NF + f + 1], scale=1.0), tok=True)
                B.waits("tensor", [t_b2[ei], t_gelu[(ei, tb, NF - 1)], t_gelu[(ei, tb, NF - 2)],
                                   t_gelu[(ei, tb, NF - 3)], t_gelu[(ei, tb, NF - 4)],
                                   t_yev.get((ei, tb - 1)) or t_yev.get((ei - 1, 3))])
                for tc4 in range(4):
                    for jb in range(2):
                        B.do("tensor", lambda e, t4=tc4, j=jb: e.matmul(
                            PBK[t4 * 2 + j][:], onesrowb[:], b2row[0:1, j * 512:(j + 1) * 512],
                            start=True, stop=False))
                t_bias[(ei, tb)] = B.mark("tensor")
                for fc in range(NF):
                    fls = [0, 1, 2, 3] if fc == 0 else ([fc + 3] if fc + 3 < NF else [])
                    for fl in fls:
                        prev_use = (t_ph2mm.get((ei, tb, fl - 4))
                                    or t_ph2mm.get((ei, tb - 1, NF - 4 + fl))
                                    or t_ph2mm.get((ei - 1, 3, NF - 4 + fl)))
                        B.waits("gpsimd", [prev_use])
                        t_w2ld[(ei, tb, fl)] = B.do("gpsimd", lambda e, x=ei, f=fl: e.dma_start(
                            out=w2b4[f % 4][:], in_=w2_d[x, f * 128:(f + 1) * 128, :]), sem=f"m2s{fl % 4}")
                    B.waits("tensor", [t_w2ld[(ei, tb, fc)]])
                    for tc4 in range(4):
                        for jb in range(2):
                            B.do("tensor", lambda e, f=fc, t4=tc4, j=jb: e.matmul(
                                PBK[t4 * 2 + j][:],
                                hT[:, f * 512 + t4 * 128:f * 512 + (t4 + 1) * 128],
                                w2b4[f % 4][:, j * 512:(j + 1) * 512],
                                start=False, stop=(f == NF - 1)))
                    t_ph2mm[(ei, tb, fc)] = B.mark("tensor")
                B.waits("vector", [t_ph2mm[(ei, tb, NF - 1)], t_p10_v])
                for tc4 in range(4):
                    tg = tb * 4 + tc4
                    for jb in range(2):
                        B.do("vector", lambda e, t=tg, t4=tc4, j=jb, x=ei: e.scalar_tensor_tensor(
                            x_sbf[:, t * C + j * 512:t * C + (j + 1) * 512],
                            PBK[t4 * 2 + j][:], wte[:, t * E + x:t * E + x + 1],
                            x_sbf[:, t * C + j * 512:t * C + (j + 1) * 512],
                            ALU.mult, ALU.add))
                t_yev[(ei, tb)] = B.mark("vector")
        t_moe_v = B.tok("vector")
        if "hT0" in dbg:
            B.waits("sync", [t_gelu[(0, 0, NF - 1)]])
            B.do("sync", lambda e: e.dma_start(out=dbg_out["hT0"], in_=hT))

    if stop >= 12:
        B.waits("sync", [t_moe_v])
        for tc in range(NT):
            B.do("sync", lambda e, t=tc: e.dma_start(
                out=out_d[t * 128:(t + 1) * 128, :], in_=x_sbf[:, t * C:(t + 1) * C]),
                tok=(tc == NT - 1))
        B.wait("sync", B.tok("sync"))

    if stop < 12:
        B.waits("sync", [B.tok("vector"), B.tok("tensor"), B.tok("scalar"), B.tok("gpsimd")])
        if stop >= 2:
            for tc in range(NT):
                B.do("sync", lambda e, t=tc: e.dma_start(
                    out=out_d[t * 128:(t + 1) * 128, :], in_=xs_d[t * 128:(t + 1) * 128, :]))
            B.do("sync", lambda e: e.dma_start(out=out_d[0:1, 0:1], in_=xs_d[0:1, 0:1]), tok=True)
        else:
            tz = B.do("vector", lambda e: e.memset(rr0, 0.0), tok=True)
            B.waits("sync", [tz])
            for tc in range(NT):
                B.do("sync", lambda e, t=tc: e.dma_start(
                    out=out_d[t * 128:(t + 1) * 128, :], in_=rr0), tok=(tc == NT - 1))
        B.wait("sync", B.tok("sync"))

    with nc.Block() as block:
        B.emit(block)
    es.close()
    return nc, D, dbg_out


def _host_inputs(inputs, n_cores=8):
    f32 = np.float32
    bf16 = ml_dtypes.bfloat16
    x = np.ascontiguousarray(inputs["L0_attn_x"])
    qkvwT = np.ascontiguousarray(np.asarray(inputs["L0_attn_qkv_w"]).T)
    projT = np.ascontiguousarray(np.asarray(inputs["L0_attn_proj_w"]).T)
    pb = np.ascontiguousarray(np.asarray(inputs["L0_attn_proj_b"]).reshape(1, C))
    qnw = np.ascontiguousarray(np.broadcast_to(inputs["L0_attn_q_norm_w"], (128, DH)))
    knw = np.ascontiguousarray(np.broadcast_to(inputs["L0_attn_k_norm_w"], (128, DH)))
    rhs = np.ascontiguousarray(inputs["L0_ffn_rhs"])
    wq = np.ascontiguousarray(inputs["L1_attn_Wq"])
    wk = np.ascontiguousarray(inputs["L1_attn_Wk"])
    wv = np.ascontiguousarray(inputs["L1_attn_Wv"])
    wo = np.ascontiguousarray(inputs["L1_attn_Wout"])
    gw = np.ascontiguousarray(inputs["L1_ffn_gate_w"])
    w1 = np.asarray(inputs["L1_ffn_w1"]).reshape(E, NC, 128, NF, 128).transpose(0, 3, 2, 1, 4)
    w1 = np.ascontiguousarray(w1.reshape(E, NF, 128, NC * 128))
    b1 = np.asarray(inputs["L1_ffn_b1"]).reshape(E, NF, 128).transpose(2, 0, 1)
    b1c = np.ascontiguousarray(b1.reshape(128, E * NF))
    w2 = np.ascontiguousarray(inputs["L1_ffn_w2"])
    vw = np.asarray(inputs["L0_attn_qkv_w"]).T[:, 2 * C:3 * C]
    vwr = np.ascontiguousarray(vw.reshape(NC, 128, 8, 128).transpose(2, 1, 0, 3).reshape(8, 128, NC * 128))
    vw1 = np.asarray(inputs["L1_attn_Wv"])
    vwr1 = np.ascontiguousarray(vw1.reshape(NC, 128, 8, 128).transpose(2, 1, 0, 3).reshape(8, 128, NC * 128))
    b2 = np.ascontiguousarray(inputs["L1_ffn_b2"])

    inv = (f32(1.0) / (f32(10000.0) ** (np.arange(0, DH, 2, dtype=f32) / f32(DH)))).astype(f32)
    fr = (np.arange(T, dtype=f32)[:, None] * inv[None, :]).astype(f32)
    cos = np.cos(fr.astype(np.float64)).astype(f32)
    sin = np.sin(fr.astype(np.float64)).astype(f32)
    cos_l = np.ascontiguousarray(cos.reshape(NT, 128, 32).transpose(1, 0, 2).reshape(128, NT * 32))
    sin_l = np.ascontiguousarray(sin.reshape(NT, 128, 32).transpose(1, 0, 2).reshape(128, NT * 32))

    selallb = np.zeros((16, NC * 128), bf16)
    for hc in range(NC):
        for p in range(128):
            selallb[2 * hc + p // 64, hc * 128 + p] = 1.0
    selj = np.zeros((8, NC * 128), f32)
    for jc in range(NC):
        selj[jc, jc * 128:(jc + 1) * 128] = 1.0
    selz = np.zeros((128, 16 * 16), bf16)
    for h in range(16):
        selz[:, h * 16 + h] = 1.0
    cvec = np.zeros((128, 8), f32)
    cvec[:, 0] = 1e-5
    cvec[:, 1] = 1e-6
    cvec[:, 2] = 127.0

    shared = dict(qkvwT=qkvwT, projT=projT, pb=pb, d_qnw=qnw, d_knw=knw, rhs=rhs, vwr=vwr, vwr1=vwr1,
                  wq=wq, wk=wk, wv=wv, wo=wo, gw=gw, w1=w1, d_b1c=b1c, w2=w2, b2=b2,
                  cos=cos_l, sin=sin_l, d_ident=np.eye(128, dtype=f32),
                  d_identb=np.eye(128, dtype=bf16),
                  d_onesrow=np.ones((1, 128), f32), d_onesrowb=np.ones((1, 128), bf16),
                  d_onescolb=np.ones((128, 1), bf16),
                  d_selallb=selallb, d_selj=selj, d_selz=selz, d_cvec=cvec)
    in_maps = []
    for i in range(n_cores):
        m = dict(shared)
        m["x"] = np.ascontiguousarray(x[i])
        in_maps.append(m)
    return in_maps


def kernel(**inputs):
    nc, D, dbg_out = build_program()
    in_maps = _host_inputs(inputs)
    res = run_bass_kernel_spmd(nc, in_maps, core_ids=list(range(8)))
    return np.stack([np.asarray(r["out"]) for r in res.results], axis=0)
